# version 16
# baseline (speedup 1.0000x reference)
import contextlib
import numpy as np
import concourse.bass as bass
import concourse.mybir as mybir
from concourse.bass_utils import run_bass_kernel_spmd

F32 = mybir.dt.float32
BF16 = mybir.dt.bfloat16
I8 = mybir.dt.int8
AF = mybir.ActivationFunctionType
ALU = mybir.AluOpType
AX = mybir.AxisListType

RSQRT2 = float(2.0 ** (-0.5))
LN_EPS = 1e-5
EPS_NORM = 1e-8
EPS_Q = 1e-5

N_CORES = 8


class Cfg:
    def __init__(self, S=2048, D=1024, HID=4096, NSEQ=4, H=8, AG=512, T1=256, T2=512):
        self.S, self.D, self.HID, self.NSEQ, self.H = S, D, HID, NSEQ, H
        self.DH = 64
        self.AG, self.T1, self.T2 = AG, T1, T2
        self.DC = D // 128
        self.QK = self.H * self.DH
        self.QC = self.QK // 128
        self.HC = HID // 128
        self.KC = S // 128
        self.NG = S // AG
        self.NT1 = S // T1
        self.NT2 = S // T2
        self.NP = self.H // 2
        self.VW = self.H * 65 + 8
        self.DBG = False


CFG = Cfg()


class Buf:
    __slots__ = ("name", "writer", "readers")

    def __init__(self, name):
        self.name = name
        self.writer = None
        self.readers = []


class Em:
    ENGINES = ("tensor", "vector", "scalar", "gpsimd", "sync")
    NSLOT = {"sync": 12, "scalar": 8, "gpsimd": 4, "vector": 2}

    def __init__(self, nc):
        self.nc = nc
        self.TRACKS = list(self.ENGINES)
        self.dma_slots = {}
        for q, n in self.NSLOT.items():
            self.dma_slots[q] = [f"{q}_dma{i}" for i in range(n)]
            self.TRACKS += self.dma_slots[q]
        self.sems = {}
        self.cnt = {t: 0 for t in self.TRACKS}
        self.waited = {e: {t: 0 for t in self.TRACKS} for e in self.ENGINES}
        self.eng = {"tensor": nc.tensor, "vector": nc.vector, "scalar": nc.scalar,
                    "gpsimd": nc.gpsimd, "sync": nc.sync}
        self.dma_idx = {q: 0 for q in self.NSLOT}
        self.bufs = {}

    def B(self, name):
        if name not in self.bufs:
            self.bufs[name] = Buf(name)
        return self.bufs[name]

    def init_sems(self, ctx):
        for t in self.TRACKS:
            self.sems[t] = ctx.enter_context(self.nc.semaphore(f"sem_{t}"))

    def _wait(self, eng, dep_eng, dep_cnt):
        if dep_cnt <= 0 or self.waited[eng][dep_eng] >= dep_cnt:
            return
        self.eng[eng].wait_ge(self.sems[dep_eng], dep_cnt)
        self.waited[eng][dep_eng] = dep_cnt

    def emit(self, eng, fn, reads=(), writes=(), inc=1, track=None):
        if track is None:
            track = eng
        for b in reads:
            if b.writer is not None:
                self._wait(eng, b.writer[0], b.writer[1])
        for b in writes:
            if b.writer is not None:
                self._wait(eng, b.writer[0], b.writer[1])
            for tok in b.readers:
                self._wait(eng, tok[0], tok[1])
        inst = fn(self.eng[eng])
        self.cnt[track] += inc
        inst.then_inc(self.sems[track], inc)
        tok = (track, self.cnt[track])
        for b in reads:
            b.readers.append(tok)
        for b in writes:
            b.writer = tok
            b.readers = []
        return tok

    def dma(self, queue, out_ap, in_ap, reads=(), writes=(), slow=False):
        slot = self.dma_slots[queue][self.dma_idx[queue] % self.NSLOT[queue]]
        self.dma_idx[queue] += 1
        self._wait(queue, slot, self.cnt[slot])
        if slow:
            return self.emit(queue, lambda e: e.dma_start(
                out=out_ap, in_=in_ap, allow_slow_non_contiguous=True),
                reads=reads, writes=writes, inc=16, track=slot)
        return self.emit(queue, lambda e: e.dma_start(out=out_ap, in_=in_ap),
                         reads=reads, writes=writes, inc=16, track=slot)

    def barrier(self):
        snap = dict(self.cnt)
        for e in self.ENGINES:
            for t in self.TRACKS:
                if t != e:
                    self._wait(e, t, snap[t])

    def finish(self):
        snap = dict(self.cnt)
        for t in self.TRACKS:
            if t != "gpsimd":
                self._wait("gpsimd", t, snap[t])


def build_program(c: Cfg):
    nc = bass.Bass("TRN2", target_bir_lowering=False)
    S, D, HID, NSEQ, H = c.S, c.D, c.HID, c.NSEQ, c.H
    QK, DC, QC, HC, KC, VW = c.QK, c.DC, c.QC, c.HC, c.KC, c.VW

    def din(name, shape, dt=F32):
        return nc.dram_tensor(name, list(shape), dt, kind="ExternalInput").ap()

    x_in = din("x", (NSEQ * S, D))
    wqk = din("wqk", (D, 2 * QK))
    bqk2 = din("bqk2", (2, QK))
    wv = din("wv", (D, QK))
    bv = din("bv", (1, QK))
    wout = din("wout", (QK, D))
    bout = din("bout", (1, D))
    ln_g = din("ln_g", (1, D))
    ln_b = din("ln_b", (1, D))
    n1w = din("n1w", (1, D))
    w1t = din("w1t", (D, HID))
    actsc = din("actsc", (1, 2))
    n2w = din("n2w", (1, HID))
    w2t = din("w2t", (HID, D))
    c_t = din("c_t", (D, NSEQ))
    wmod = din("wmod", (D, 6 * D))

    out_t = nc.dram_tensor("out", [NSEQ * S, D], F32, kind="ExternalOutput").ap()
    xmid_scr = nc.dram_tensor("xmid_scr", [NSEQ * S, D], F32).ap()
    mod_scr = nc.dram_tensor("mod_scr", [NSEQ * 6, D], F32).ap()
    w1q_scr = nc.dram_tensor("w1q_scr", [D, HID], BF16).ap()
    w2q_scr = nc.dram_tensor("w2q_scr", [HID, D], BF16).ap()
    xq2_scr = nc.dram_tensor("xq2_scr", [HID, S], BF16).ap()
    tiny_scr = nc.dram_tensor("tiny_scr", [64], F32).ap()
    dbg_scr = nc.dram_tensor("dbg_scr", [c.NG * 128, QC * c.AG], F32).ap() if c.DBG else None
    dbg2_scr = nc.dram_tensor("dbg2_scr", [(c.AG // 128) * 128, D], F32).ap() if c.DBG else None

    ctx = contextlib.ExitStack()
    em = Em(nc)

    with ctx:
        em.init_sems(ctx)
        B = em.B

        WQK = ctx.enter_context(nc.sbuf_tensor("WQK", [128, DC * 2 * QK], BF16))
        WV = ctx.enter_context(nc.sbuf_tensor("WV", [128, DC * QK], BF16))
        WOUT = ctx.enter_context(nc.sbuf_tensor("WOUT", [128, QC * D], BF16))
        IDN = ctx.enter_context(nc.sbuf_tensor("IDN", [128, 128], BF16))
        ONESF = ctx.enter_context(nc.sbuf_tensor("ONESF", [1, 128], F32))
        ONESC = ctx.enter_context(nc.sbuf_tensor("ONESC", [128, 1], F32))
        EPSCOL = ctx.enter_context(nc.sbuf_tensor("EPSCOL", [128, 1], F32))
        ZCOL = ctx.enter_context(nc.sbuf_tensor("ZCOL", [128, 1], F32))
        IND64 = ctx.enter_context(nc.sbuf_tensor("IND64", [128, 256], F32))
        BQS = ctx.enter_context(nc.sbuf_tensor("BQS", [128, QC], F32))
        BKS = ctx.enter_context(nc.sbuf_tensor("BKS", [128, QC], F32))
        BVC = ctx.enter_context(nc.sbuf_tensor("BVC", [128, QC], BF16))
        BOREP = ctx.enter_context(nc.sbuf_tensor("BOREP", [128, D], BF16))
        GREP = ctx.enter_context(nc.sbuf_tensor("GREP", [128, D], BF16))
        BREP = ctx.enter_context(nc.sbuf_tensor("BREP", [128, D], BF16))
        C1REP = ctx.enter_context(nc.sbuf_tensor("C1REP", [128, D], F32))
        C1IREP = ctx.enter_context(nc.sbuf_tensor("C1IREP", [128, D], F32))
        C2REP = ctx.enter_context(nc.sbuf_tensor("C2REP", [128, HID], BF16))
        ABCOL = ctx.enter_context(nc.sbuf_tensor("ABCOL", [128, 2], F32))
        SWCOL = ctx.enter_context(nc.sbuf_tensor("SWCOL", [128, 2], F32))
        IND8S = ctx.enter_context(nc.sbuf_tensor("IND8S", [128, QC * 8], BF16))
        NNB = ctx.enter_context(nc.sbuf_tensor("NNB", [128, c.NP], F32))
        S2COL = ctx.enter_context(nc.sbuf_tensor("S2COL", [128, S // 128], F32))
        STATC = ctx.enter_context(nc.sbuf_tensor("STATC", [128, 160], F32))

        P0 = ctx.enter_context(nc.psum_tensor("P0", [128, 1024], F32))
        P1 = ctx.enter_context(nc.psum_tensor("P1", [128, 1024], F32))
        P2A = ctx.enter_context(nc.psum_tensor("P2A", [128, 512], F32))
        P3 = ctx.enter_context(nc.psum_tensor("P3", [128, 512], F32))
        P4 = ctx.enter_context(nc.psum_tensor("P4", [128, 512], F32))
        PT = ctx.enter_context(nc.psum_tensor("PT", [128, 1024], BF16))
        bP0, bP1, bP2, bP3, bP4 = B("P0"), B("P1"), B("P2A"), B("P3"), B("P4")
        bPT = B("PT")
        bSTs = [B(f"STATC{k}") for k in range(4)]
        bSTm = [B(f"STATM{k}") for k in range(4)]
        bAK = B("AKCOLS")
        bST = B("STATC")

        def ts(eng, out, in0, s1, s2, op0, op1=None, reads=(), writes=()):
            if op1 is None:
                em.emit(eng, lambda e: e.tensor_scalar(out, in0, s1, None, op0=op0),
                        reads=reads, writes=writes)
            else:
                em.emit(eng, lambda e: e.tensor_scalar(out, in0, s1, s2, op0=op0, op1=op1),
                        reads=reads, writes=writes)

        def tt(eng, out, in0, in1, op, reads=(), writes=()):
            em.emit(eng, lambda e: e.tensor_tensor(out, in0, in1, op=op),
                    reads=reads, writes=writes)

        def stt(eng, out, in0, s, in1, op0, op1, reads=(), writes=()):
            em.emit(eng, lambda e: e.scalar_tensor_tensor(out, in0, s, in1, op0=op0,
                                                          op1=op1),
                    reads=reads, writes=writes)

        def act(eng, out, in0, func, bias=0.0, scale=1.0, accum=None, reads=(), writes=()):
            if accum is None:
                em.emit(eng, lambda e: e.activation(out, in0, func, bias=bias,
                                                    scale=scale),
                        reads=reads, writes=writes)
            else:
                em.emit(eng, lambda e: e.activation(out, in0, func, bias=bias,
                                                    scale=scale, accum_out=accum),
                        reads=reads, writes=writes)

        def cp(eng, out, in0, reads=(), writes=()):
            if eng == "scalar":
                em.emit(eng, lambda e: e.copy(out, in0), reads=reads, writes=writes)
            else:
                em.emit(eng, lambda e: e.tensor_copy(out, in0), reads=reads,
                        writes=writes)

        def red(eng, out, in0, op, absval=False, reads=(), writes=()):
            em.emit(eng, lambda e: e.tensor_reduce(out, in0, axis=AX.X, op=op,
                                                   apply_absolute_value=absval),
                    reads=reads, writes=writes)

        def mm(out, lhsT, rhs, start, stop, reads=(), writes=(), tp=None):
            if tp is None:
                em.emit("tensor", lambda e: e.matmul(out, lhsT, rhs, start=start,
                                                     stop=stop),
                        reads=reads, writes=writes)
            else:
                em.emit("tensor", lambda e: e.matmul(out, lhsT, rhs, start=start,
                                                     stop=stop, tile_position=tp),
                        reads=reads, writes=writes)

        def transp(out, in0, reads=(), writes=()):
            em.emit("tensor", lambda e: e.transpose(out, in0, IDN[:]),
                    reads=list(reads) + [B("IDN")], writes=writes)

        def bcast(queue, out_tile, src_t, src_off, n, reads=(), writes=()):
            src = bass.AP(src_t, src_off, [[0, 128], [1, n]])
            em.dma(queue, out_tile, src, reads=reads, writes=writes)

        def ln_bn(xa, width, rbuf, k):
            o = 32 * k
            bSk = bSTs[k]
            nparts = width // 512 if width > 512 else 1
            wper = width // nparts
            assert wper <= 512 and nparts <= 4
            for i in range(nparts):
                em.emit("vector", lambda e, i=i: e.bn_stats(
                    STATC[:, o + 6 * i:o + 6 * i + 6], xa[:, i * wper:(i + 1) * wper]),
                    reads=[rbuf], writes=[bSk])
            em.emit("vector", lambda e: e.bn_aggr(STATC[:, o + 12:o + 14],
                                                  STATC[:, o:o + 6 * nparts]),
                    reads=[bSk], writes=[bSk])

        def ln_sqrt(k):
            o = 32 * k
            act("scalar", STATC[:, o + 13:o + 14], STATC[:, o + 13:o + 14], AF.Sqrt,
                bias=EPSCOL[:, 0:1], reads=[bSTs[k], B("EPSCOL")], writes=[bSTs[k]])

        def ln_recip(k):
            o = 32 * k
            em.emit("vector", lambda e: e.reciprocal(STATC[:, o + 13:o + 14],
                                                     STATC[:, o + 13:o + 14]),
                    reads=[bSTs[k]], writes=[bSTs[k]])
            stt("vector", STATC[:, o + 14:o + 15], STATC[:, o + 12:o + 13], -1.0,
                STATC[:, o + 13:o + 14], ALU.mult, ALU.mult,
                reads=[bSTs[k]], writes=[bSTs[k]])

        def ln_apply(eng_out, in_ap, k, rbufs, wbufs):
            o = 32 * k
            act("scalar", eng_out, in_ap, AF.Identity, bias=STATC[:, o + 14:o + 15],
                scale=STATC[:, o + 13:o + 14], reads=list(rbufs) + [bSTs[k]],
                writes=wbufs)

        def ln_stats(xa, width, rbuf, k):
            ln_bn(xa, width, rbuf, k)
            ln_sqrt(k)
            ln_recip(k)

        em.emit("gpsimd", lambda e: e.iota(
            bass.AP(IDN, 0, [[128, 128], [1, 128]]), [[1, 128]], channel_multiplier=-1,
            allow_small_or_imprecise_dtypes=True), writes=[B("IDN")])
        ts("vector", IDN[:], IDN[:], 0.0, None, ALU.is_equal, reads=[B("IDN")],
           writes=[B("IDN")])
        em.emit("vector", lambda e: e.memset(ONESF[:], 1.0), writes=[B("ONESF")])
        em.emit("vector", lambda e: e.memset(ONESC[:], 1.0), writes=[B("ONESC")])
        em.emit("vector", lambda e: e.memset(EPSCOL[:], LN_EPS), writes=[B("EPSCOL")])
        em.emit("vector", lambda e: e.memset(ZCOL[:], 0.0), writes=[B("ZCOL")])
        em.emit("vector", lambda e: e.memset(IND64[:], 0.0), writes=[B("IND64")])
        em.emit("vector", lambda e: e.memset(IND64[64:65, 0:64], 1.0), writes=[B("IND64")])
        em.emit("vector", lambda e: e.memset(IND64[64:65, 192:256], 1.0), writes=[B("IND64")])
        em.emit("vector", lambda e: e.memset(IND8S[:], 0.0), writes=[B("IND8S")])
        for fc in range(QC):
            em.emit("vector", lambda e, fc=fc: e.memset(
                IND8S[0:64, fc * 8 + 2 * fc:fc * 8 + 2 * fc + 1], 1.0), writes=[B("IND8S")])
            em.emit("vector", lambda e, fc=fc: e.memset(
                IND8S[64:128, fc * 8 + 2 * fc + 1:fc * 8 + 2 * fc + 2], 1.0),
                writes=[B("IND8S")])

        for tile, tname, srcx, n in ((GREP, "GREP", ln_g, D), (BREP, "BREP", ln_b, D),
                                     (C2REP, "C2REP", n2w, HID),
                                     (BOREP, "BOREP", bout, D)):
            em.dma("gpsimd", tile[:], bass.AP(srcx.tensor, 0, [[0, 128], [1, n]]),
                   writes=[B(tname)])
        bcast("sync", C1REP[:], n1w.tensor, 0, D, writes=[B("C1REP")])
        bcast("sync", ABCOL[:, 0:2], actsc.tensor, 0, 2, writes=[B("ABCOL")])
        em.emit("vector", lambda e: e.reciprocal(C1IREP[:], C1REP[:]),
                reads=[B("C1REP")], writes=[B("C1IREP")])

        em.dma("gpsimd", bass.AP(BVC, 0, [[QC, 128], [1, QC]]),
               bass.AP(bv.tensor, 0, [[1, 128], [128, QC]]), writes=[B("BVC")],
               slow=True)
        em.dma("sync", bass.AP(BQS, 0, [[QC, 128], [1, QC]]),
               bass.AP(bqk2.tensor, 0, [[1, 128], [128, QC]]), writes=[B("BQS")],
               slow=True)
        em.dma("sync", bass.AP(BKS, 0, [[QC, 128], [1, QC]]),
               bass.AP(bqk2.tensor, QK, [[1, 128], [128, QC]]), writes=[B("BKS")],
               slow=True)
        ts("vector", BQS[:], BQS[:], 0.125, None, ALU.mult, reads=[B("BQS")],
           writes=[B("BQS")])

        em.dma("gpsimd",
               bass.AP(WQK, 0, [[DC * 2 * QK, 128], [2 * QK, DC], [1, 2 * QK]]),
               bass.AP(wqk.tensor, 0, [[2 * QK, 128], [128 * 2 * QK, DC], [1, 2 * QK]]),
               writes=[B("WQK")])
        em.dma("gpsimd",
               bass.AP(WV, 0, [[DC * QK, 128], [QK, DC], [1, QK]]),
               bass.AP(wv.tensor, 0, [[QK, 128], [128 * QK, DC], [1, QK]]),
               writes=[B("WV")])
        em.dma("gpsimd",
               bass.AP(WOUT, 0, [[QC * D, 128], [D, QC], [1, D]]),
               bass.AP(wout.tensor, 0, [[D, 128], [128 * D, QC], [1, D]]),
               writes=[B("WOUT")])

        arena = contextlib.ExitStack()
        BVTMP = arena.enter_context(nc.sbuf_tensor("BVTMP", [1, D], F32))
        for half in range(max(1, D // 512)):
            dw2 = min(512, D)
            for ci in range(QC):
                mm(P0[0:1, half * dw2:half * dw2 + dw2], BVC[:, ci:ci + 1],
                   WOUT[:, ci * D + half * dw2:ci * D + (half + 1) * dw2],
                   start=(ci == 0), stop=(ci == QC - 1),
                   reads=[B("BVC"), B("WOUT")], writes=[bP0])
        cp("vector", BVTMP[0:1, :], P0[0:1, 0:D], reads=[bP0], writes=[B("BVTMP")])
        for half in range(max(1, D // 512)):
            dw2 = min(512, D)
            mm(P0[:, half * dw2:half * dw2 + dw2], ONESF[0:1, :],
               BVTMP[0:1, half * dw2:half * dw2 + dw2],
               start=True, stop=True, reads=[B("BVTMP"), B("ONESF")], writes=[bP0])
            tt("vector", BOREP[:, half * dw2:half * dw2 + dw2],
               BOREP[:, half * dw2:half * dw2 + dw2],
               P0[:, half * dw2:half * dw2 + dw2], ALU.add,
               reads=[bP0], writes=[B("BOREP")])
        SILC = arena.enter_context(nc.sbuf_tensor("SILC", [128, DC * NSEQ], F32))
        MODT = arena.enter_context(nc.sbuf_tensor("MODT", [NSEQ, 6 * D], F32))
        WMS = arena.enter_context(nc.sbuf_tensor("WMS", [128, 2 * 512], F32))
        em.dma("sync", bass.AP(SILC, 0, [[DC * NSEQ, 128], [NSEQ, DC], [1, NSEQ]]),
               bass.AP(c_t.tensor, 0, [[NSEQ, 128], [128 * NSEQ, DC], [1, NSEQ]]),
               writes=[B("SILC")], slow=True)
        act("scalar", SILC[:], SILC[:], AF.Silu, reads=[B("SILC")], writes=[B("SILC")])
        for nt in range(6 * D // 512):
            for dc in range(DC):
                wslot = (nt * DC + dc) % 2
                bWM = B(f"WMS{wslot}")
                em.dma("sync", WMS[:, wslot * 512:wslot * 512 + 512],
                       bass.AP(wmod.tensor, dc * 128 * 6 * D + nt * 512,
                               [[6 * D, 128], [1, 512]]),
                       writes=[bWM])
                mm(P3[0:NSEQ, 0:512], SILC[:, dc * NSEQ:(dc + 1) * NSEQ],
                   WMS[:, wslot * 512:wslot * 512 + 512],
                   start=(dc == 0), stop=(dc == DC - 1),
                   reads=[B("SILC"), bWM], writes=[bP3])
            cp("vector", MODT[0:NSEQ, nt * 512:(nt + 1) * 512], P3[0:NSEQ, 0:512],
               reads=[bP3], writes=[B("MODT")])
        em.dma("scalar", bass.AP(mod_scr.tensor, 0, [[6 * D, NSEQ], [1, 6 * D]]),
               MODT[0:NSEQ, :], reads=[B("MODT")], writes=[B("mod_scr")])

        WQT = arena.enter_context(nc.sbuf_tensor("WQT", [128, HID], F32))
        WQA = arena.enter_context(nc.sbuf_tensor("WQA", [128, HID], F32))
        WQB = arena.enter_context(nc.sbuf_tensor("WQB", [128, HID], BF16))
        WQI = arena.enter_context(nc.sbuf_tensor("WQI", [128, HID], I8))
        WQC = arena.enter_context(nc.sbuf_tensor("WQC", [128, 4], F32))

        def wquant(w_ap, rows, cols, scr_t, idx):
            nch = rows // 128
            bW, bA, bC = B("WQT"), B("WQAa"), B("WQC")
            bI, bBf, bScr = B("WQI"), B("WQB"), B(f"qS{idx}")
            for ch in range(nch):
                em.dma("sync", WQT[:, 0:cols],
                       bass.AP(w_ap.tensor, ch * 128 * cols, [[cols, 128], [1, cols]]),
                       writes=[bW])
                red("vector", WQC[:, 1:2], WQT[:, 0:cols], ALU.add, absval=True,
                    reads=[bW], writes=[bC])
                if ch == 0:
                    cp("vector", WQC[:, 0:1], WQC[:, 1:2], reads=[bC], writes=[bA])
                else:
                    tt("vector", WQC[:, 0:1], WQC[:, 0:1], WQC[:, 1:2], ALU.add,
                       reads=[bC], writes=[bA])
            mm(P3[0:1, 0:1], WQC[:, 0:1], ONESC[:, 0:1], start=True, stop=True,
               reads=[bA, B("ONESC")], writes=[bP3])
            cp("vector", WQC[0:1, 2:3], P3[0:1, 0:1], reads=[bP3], writes=[bC])
            ts("vector", WQC[0:1, 2:3], WQC[0:1, 2:3], 1.0 / (rows * cols), EPS_Q,
               ALU.mult, ALU.max, reads=[bC], writes=[bC])
            mm(P3[:, 0:1], ONESF[0:1, :], WQC[0:1, 2:3], start=True, stop=True,
               reads=[bC, B("ONESF")], writes=[bP3])
            cp("vector", SWCOL[:, idx:idx + 1], P3[:, 0:1], reads=[bP3],
               writes=[B("SWCOL")])
            em.emit("vector", lambda e: e.reciprocal(WQC[:, 3:4], SWCOL[:, idx:idx + 1]),
                    reads=[B("SWCOL")], writes=[bC])
            for ch in range(nch):
                em.dma("sync", WQT[:, 0:cols],
                       bass.AP(w_ap.tensor, ch * 128 * cols, [[cols, 128], [1, cols]]),
                       writes=[bW])
                act("scalar", WQA[:, 0:cols], WQT[:, 0:cols], AF.Identity,
                    bias=ZCOL[:, 0:1], scale=WQC[:, 3:4],
                    reads=[bW, bC, B("ZCOL")], writes=[B("WQAa")])
                ts("vector", WQT[:, 0:cols], WQA[:, 0:cols], 0.5, None, ALU.is_gt,
                   reads=[B("WQAa")], writes=[bW])
                stt("vector", WQB[:, 0:cols], WQA[:, 0:cols], -0.5, WQT[:, 0:cols],
                    ALU.is_lt, ALU.subtract, reads=[B("WQAa"), bW], writes=[bBf])
                em.dma("scalar",
                       bass.AP(scr_t, ch * 128 * cols, [[cols, 128], [1, cols]]),
                       WQB[:, 0:cols], reads=[bBf], writes=[bScr])

        wquant(w1t, D, HID, w1q_scr.tensor, 0)
        wquant(w2t, HID, D, w2q_scr.tensor, 1)
        em.barrier()
        arena.close()

        for s in range(NSEQ):
            ar = contextlib.ExitStack()
            HBF = ar.enter_context(nc.sbuf_tensor(f"HBF{s}", [128, DC * S], BF16))
            KBF = ar.enter_context(nc.sbuf_tensor(f"KBF{s}", [128, QC * S], BF16))
            QBF = ar.enter_context(nc.sbuf_tensor(f"QBF{s}", [128, QC * S], BF16))
            VTM = ar.enter_context(nc.sbuf_tensor(f"VTM{s}", [128, KC * VW], BF16))
            XT = ar.enter_context(nc.sbuf_tensor(f"XT{s}", [128, 4 * D], F32))
            PBF = ar.enter_context(nc.sbuf_tensor(f"PBF{s}", [128, 2 * 1024], BF16))
            OBF = ar.enter_context(nc.sbuf_tensor(f"OBF{s}", [128, QC * c.AG], BF16))
            OST = ar.enter_context(nc.sbuf_tensor(f"OST{s}", [128, c.NP * c.AG], BF16))
            ORW = ar.enter_context(nc.sbuf_tensor(f"ORW{s}", [128, 2 * c.AG], BF16))
            RRC = ar.enter_context(nc.sbuf_tensor(f"RRC{s}", [128, 512], BF16))
            SMR = ar.enter_context(nc.sbuf_tensor(f"SMR{s}", [128, 2048], F32))
            EP1 = ar.enter_context(nc.sbuf_tensor(f"EP1{s}", [128, (c.AG // 128) * D], BF16))
            EP2 = ar.enter_context(nc.sbuf_tensor(f"EP2{s}", [128, (c.AG // 128) * D], BF16))
            CA1 = ar.enter_context(nc.sbuf_tensor(f"CA1{s}", [128, D], F32))
            SAR = ar.enter_context(nc.sbuf_tensor(f"SAR{s}", [128, D], BF16))
            GAR = ar.enter_context(nc.sbuf_tensor(f"GAR{s}", [128, D], BF16))
            GSC = ar.enter_context(nc.sbuf_tensor(f"GSC{s}", [128, D], F32))

            bHBF, bKBF, bQBF, bVTM = B(f"HBF{s}"), B(f"KBF{s}"), B(f"QBF{s}"), B(f"VTM{s}")
            bOBF, bOST, bORW = B(f"OBF{s}"), B(f"OST{s}"), B(f"ORW{s}")
            bXTs = [B(f"XT{s}_{i}") for i in range(4)]
            bPBF = [B(f"PBF{s}_0"), B(f"PBF{s}_1")]
            bRRC, bSMR = B(f"RRC{s}"), B(f"SMR{s}")
            bEP1, bEP2 = B(f"EP1{s}"), B(f"EP2{s}")
            bCA1, bSAR, bGAR, bGSC = B(f"CA1{s}"), B(f"SAR{s}"), B(f"GAR{s}"), B(f"GSC{s}")
            bNNB, bXMS = B("NNB"), B("xmid_scr")

            def xchunk(gbase_tok, tc, nsl=4):
                sl = tc % nsl
                em.dma("sync", XT[:, sl * D:(sl + 1) * D],
                       bass.AP(x_in.tensor, (gbase_tok + tc * 128) * D,
                               [[D, 128], [1, D]]),
                       writes=[bXTs[sl]])
                return XT[:, sl * D:(sl + 1) * D], bXTs[sl]

            bcast("sync", CA1[:], mod_scr.tensor, (s * 6 + 1) * D, D,
                  reads=[B("mod_scr")], writes=[bCA1])
            ts("vector", CA1[:], CA1[:], 1.0, None, ALU.add, reads=[bCA1], writes=[bCA1])
            em.dma("gpsimd", SAR[:],
                   bass.AP(mod_scr.tensor, (s * 6 + 0) * D, [[0, 128], [1, D]]),
                   reads=[B("mod_scr")], writes=[bSAR])
            bcast("sync", GSC[:], mod_scr.tensor, (s * 6 + 2) * D, D,
                  reads=[B("mod_scr")], writes=[bGSC])
            ts("vector", GAR[:], GSC[:], 0.0, 2.0, ALU.is_ge, ALU.mult, reads=[bGSC],
               writes=[bGAR])
            ts("vector", GAR[:], GAR[:], -1.0, None, ALU.add, reads=[bGAR], writes=[bGAR])
            act("scalar", GSC[:], GSC[:], AF.Abs, reads=[bGSC], writes=[bGSC])
            ts("vector", GSC[:], GSC[:], 0.25, None, ALU.max, reads=[bGSC],
               writes=[bGSC])
            tt("vector", GSC[:], GSC[:], GAR[:], ALU.mult, reads=[bGAR], writes=[bGSC])
            ts("vector", GAR[:], GSC[:], RSQRT2, None, ALU.mult, reads=[bGSC],
               writes=[bGAR])

            em.emit("vector", lambda e: e.memset(
                bass.AP(VTM, 64, [[KC * VW, 128], [VW, KC], [65, H], [1, 1]]), 1.0),
                writes=[bVTM])

            NCH = c.AG // 128
            for g in range(c.NG):
                gbase = s * S + g * c.AG
                xas = []
                for tc in range(NCH):
                    xa, bxa = xchunk(gbase, tc)
                    xas.append((xa, bxa))
                    k4 = tc % 4
                    o = 32 * k4
                    ln_bn(xa, D, bxa, k4)
                for tc in range(NCH):
                    ln_sqrt(tc % 4)
                for tc in range(NCH):
                    ln_recip(tc % 4)
                for tc in range(NCH):
                    k4 = tc % 4
                    o = 32 * k4
                    xa, bxa = xas[tc]
                    bEP1c = B(f"EP1{s}_{tc}")
                    bEP2c = B(f"EP2{s}_{tc}")
                    ha = EP2[:, tc * D:(tc + 1) * D]
                    ln_apply(ha, xa, k4, [bxa], [bEP2c])
                    ea = EP1[:, tc * D:(tc + 1) * D]
                    tt("vector", ea, ha, CA1[:], ALU.mult, reads=[bEP2c, bCA1],
                       writes=[bEP1c])
                    tt("vector", ha, ea, SAR[:], ALU.add, reads=[bEP1c, bSAR],
                       writes=[bEP2c])
                for tc in range(NCH):
                    bEP2c = B(f"EP2{s}_{tc}")
                    ha = EP2[:, tc * D:(tc + 1) * D]
                    for j in range(DC):
                        transp(PT[:, (j % 8) * 128:(j % 8) * 128 + 128],
                               ha[:, j * 128:(j + 1) * 128],
                               reads=[bEP2c], writes=[bPT])
                    g4 = min(4, DC)
                    for half in range(DC // g4):
                        cp("scalar",
                           bass.AP(HBF, half * g4 * S + g * c.AG + tc * 128,
                                   [[DC * S, 128], [S, g4], [1, 128]]),
                           bass.AP(PT, half * g4 * 128, [[1024, 128], [128, g4], [1, 128]]),
                           reads=[bPT], writes=[bHBF])
                for fc2 in range(0, QC, 2):
                    for dc in range(DC):
                        for pi in range(2):
                            fc = fc2 + pi
                            pk, bpk = (P0, bP0) if pi == 0 else (P1, bP1)
                            mm(pk[:, 0:c.AG],
                               WQK[:, dc * 2 * QK + QK + fc * 128:
                                   dc * 2 * QK + QK + (fc + 1) * 128],
                               HBF[:, dc * S + g * c.AG:dc * S + (g + 1) * c.AG],
                               start=(dc == 0), stop=(dc == DC - 1),
                               reads=[B("WQK"), bHBF], writes=[bpk])
                    for pi in range(2):
                        fc = fc2 + pi
                        pk, bpk = (P0, bP0) if pi == 0 else (P1, bP1)
                        act("scalar", KBF[:, fc * S + g * c.AG:fc * S + (g + 1) * c.AG],
                            pk[:, 0:c.AG], AF.Identity, bias=BKS[:, fc:fc + 1],
                            reads=[bpk, B("BKS")], writes=[bKBF])
                for tc2 in range(0, NCH, 2):
                    npi = 2 if tc2 + 1 < NCH else 1
                    for dc in range(DC):
                        for pi in range(npi):
                            tc = tc2 + pi
                            pv, bpv = (P0, bP0) if pi == 0 else (P1, bP1)
                            mm(pv[:, 0:QK],
                               HBF[:, dc * S + g * c.AG + tc * 128:
                                   dc * S + g * c.AG + (tc + 1) * 128],
                               WV[:, dc * QK:(dc + 1) * QK],
                               start=(dc == 0), stop=(dc == DC - 1),
                               reads=[bHBF, B("WV")], writes=[bpv])
                    for pi in range(npi):
                        tc = tc2 + pi
                        pv, bpv = (P0, bP0) if pi == 0 else (P1, bP1)
                        kc = g * NCH + tc
                        cp("scalar",
                           bass.AP(VTM, kc * VW, [[KC * VW, 128], [65, H], [1, 64]]),
                           bass.AP(pv, 0, [[1024, 128], [64, H], [1, 64]]),
                           reads=[bpv], writes=[bVTM])

            for g in range(c.NG):
                for fc2 in range(0, QC, 2):
                    for dc in range(DC):
                        for pi in range(2):
                            fc = fc2 + pi
                            pq, bpq = (P0, bP0) if pi == 0 else (P1, bP1)
                            mm(pq[:, 0:c.AG],
                               WQK[:, dc * 2 * QK + fc * 128:dc * 2 * QK + (fc + 1) * 128],
                               HBF[:, dc * S + g * c.AG:dc * S + (g + 1) * c.AG],
                               start=(dc == 0), stop=(dc == DC - 1),
                               reads=[B("WQK"), bHBF], writes=[bpq])
                    for pi in range(2):
                        fc = fc2 + pi
                        pq, bpq = (P0, bP0) if pi == 0 else (P1, bP1)
                        act("scalar", QBF[:, fc * S + g * c.AG:fc * S + (g + 1) * c.AG],
                            pq[:, 0:c.AG], AF.Identity, bias=BQS[:, fc:fc + 1],
                            scale=0.125, reads=[bpq, B("BQS")], writes=[bQBF])
            for qsrc, bsrc, coli in ((QBF, bQBF, 22), (KBF, bKBF, 24)):
                first = True
                for fc in range(QC):
                    act("scalar", EP1[:, 0:S], qsrc[:, fc * S:(fc + 1) * S], AF.Square,
                        reads=[bsrc], writes=[B(f"EP1{s}_0"), B(f"EP1{s}_1")])
                    for ct in range((S + 511) // 512):
                        ctw = min(512, S - ct * 512)
                        mm(P3[0:8, 0:ctw], IND8S[:, fc * 8:(fc + 1) * 8],
                           EP1[:, ct * 512:ct * 512 + ctw],
                           start=True, stop=True,
                           reads=[B(f"EP1{s}_0"), B(f"EP1{s}_1"), B("IND8S")],
                           writes=[bP3])
                        red("vector", STATC[0:8, coli + 1:coli + 2], P3[0:8, 0:ctw],
                            ALU.max, reads=[bP3], writes=[bSTs[0]])
                        if first:
                            cp("vector", STATC[0:8, coli:coli + 1],
                               STATC[0:8, coli + 1:coli + 2], reads=[bSTs[0]],
                               writes=[bSTs[0]])
                            first = False
                        else:
                            tt("vector", STATC[0:8, coli:coli + 1],
                               STATC[0:8, coli:coli + 1], STATC[0:8, coli + 1:coli + 2],
                               ALU.max, reads=[bSTs[0]], writes=[bSTs[0]])
            tt("vector", STATC[0:8, 26:27], STATC[0:8, 22:23], STATC[0:8, 24:25],
               ALU.mult, reads=[bSTs[0]], writes=[bSTs[0]])
            em.dma("scalar", bass.AP(tiny_scr.tensor, 0, [[1, 8], [1, 1]]),
                   STATC[0:8, 26:27], reads=[bSTs[0]], writes=[B("tiny")], slow=True)
            em.dma("scalar", SMR[0:1, 0:8],
                   bass.AP(tiny_scr.tensor, 0, [[8, 1], [1, 8]]),
                   reads=[B("tiny")], writes=[bSMR], slow=True)
            act("scalar", SMR[0:1, 0:8], SMR[0:1, 0:8], AF.Sqrt, reads=[bSMR],
                writes=[bSMR])
            tt("vector", SMR[0:1, 8:8 + c.NP], bass.AP(SMR, 0, [[2048, 1], [2, c.NP]]),
               bass.AP(SMR, 1, [[2048, 1], [2, c.NP]]), ALU.max, reads=[bSMR],
               writes=[bSMR])
            ts("vector", SMR[0:1, 8:8 + c.NP], SMR[0:1, 8:8 + c.NP], -1.0, None,
               ALU.mult, reads=[bSMR], writes=[bSMR])
            mm(P3[:, 0:c.NP], ONESF[0:1, :], SMR[0:1, 8:8 + c.NP], start=True,
               stop=True, reads=[bSMR, B("ONESF")], writes=[bP3])
            cp("vector", NNB[:, 0:c.NP], P3[:, 0:c.NP], reads=[bP3], writes=[bNNB])

            for g in range(c.NG):
                gbase = s * S + g * c.AG
                pend = []

                def _emit_norm(g, p, slot):
                    orw = ORW[:, slot * c.AG:(slot + 1) * c.AG]
                    em.emit("vector", lambda e: e.reciprocal(
                        SMR[64:65, slot * 1024:slot * 1024 + 2 * c.AG],
                        SMR[64:65, slot * 1024:slot * 1024 + 2 * c.AG]),
                        reads=[bSMR], writes=[bSMR])
                    mm(P2A[:, 0:c.AG], IND64[64:65, 0:128],
                       SMR[64:65, slot * 1024:slot * 1024 + c.AG],
                       start=True, stop=False,
                       reads=[bSMR, B("IND64")], writes=[bP2])
                    mm(P2A[:, 0:c.AG], IND64[64:65, 128:256],
                       SMR[64:65, slot * 1024 + c.AG:slot * 1024 + 2 * c.AG],
                       start=False, stop=True,
                       reads=[bSMR, B("IND64")], writes=[bP2])
                    cp("vector", RRC[0:64, 0:c.AG], P2A[0:64, 0:c.AG], reads=[bP2],
                       writes=[bRRC])
                    tt("vector", OBF[0:64, p * c.AG:(p + 1) * c.AG],
                       orw[0:64, :], RRC[0:64, 0:c.AG], ALU.mult,
                       reads=[bORW, bRRC], writes=[bOBF])
                    tt("vector", OST[0:64, p * c.AG:(p + 1) * c.AG],
                       OST[0:64, p * c.AG:(p + 1) * c.AG],
                       P2A[64:128, 0:c.AG], ALU.mult,
                       reads=[bP2], writes=[bOST])

                for p in range(c.NP):
                    hA, hB = 2 * p, 2 * p + 1
                    for j in range(KC):
                        psc, bsc = (P0, bP0) if j % 2 == 0 else (P1, bP1)
                        pbh = j % 2
                        mm(psc[:, 0:c.AG],
                           KBF[0:64, p * S + j * 128:p * S + (j + 1) * 128],
                           QBF[0:64, p * S + g * c.AG:p * S + (g + 1) * c.AG],
                           start=True, stop=True, tp=(0, 0),
                           reads=[bKBF, bQBF], writes=[bsc])
                        mm(psc[:, 512:512 + c.AG],
                           KBF[64:128, p * S + j * 128:p * S + (j + 1) * 128],
                           QBF[64:128, p * S + g * c.AG:p * S + (g + 1) * c.AG],
                           start=True, stop=True, tp=(64, 0),
                           reads=[bKBF, bQBF], writes=[bsc])
                        if j == 2 and pend:
                            pp, ss = pend.pop(0)
                            _emit_norm(g, pp, ss)
                        if c.AG == 512:
                            act("scalar", PBF[:, pbh * 1024:pbh * 1024 + 1024],
                                psc[:, 0:1024], AF.Exp, bias=NNB[:, p:p + 1],
                                reads=[bsc, bNNB], writes=[bPBF[pbh]])
                        else:
                            act("scalar", PBF[:, pbh * 1024:pbh * 1024 + c.AG],
                                psc[:, 0:c.AG], AF.Exp, bias=NNB[:, p:p + 1],
                                reads=[bsc, bNNB], writes=[bPBF[pbh]])
                            act("scalar",
                                PBF[:, pbh * 1024 + 512:pbh * 1024 + 512 + c.AG],
                                psc[:, 512:512 + c.AG], AF.Exp, bias=NNB[:, p:p + 1],
                                reads=[bsc, bNNB], writes=[bPBF[pbh]])
                        if j >= 1:
                            jm = j - 1
                            pbm = jm % 2
                            mm(P3[0:65, 0:c.AG],
                               VTM[:, jm * VW + 65 * hA:jm * VW + 65 * hA + 65],
                               PBF[:, pbm * 1024:pbm * 1024 + c.AG],
                               start=(jm == 0), stop=False,
                               reads=[bVTM, bPBF[pbm]], writes=[bP3])
                            mm(P4[0:65, 0:c.AG],
                               VTM[:, jm * VW + 65 * hB:jm * VW + 65 * hB + 65],
                               PBF[:, pbm * 1024 + 512:pbm * 1024 + 512 + c.AG],
                               start=(jm == 0), stop=False,
                               reads=[bVTM, bPBF[pbm]], writes=[bP4])
                    jm = KC - 1
                    pbm = jm % 2
                    mm(P3[0:65, 0:c.AG],
                       VTM[:, jm * VW + 65 * hA:jm * VW + 65 * hA + 65],
                       PBF[:, pbm * 1024:pbm * 1024 + c.AG],
                       start=False, stop=True,
                       reads=[bVTM, bPBF[pbm]], writes=[bP3])
                    mm(P4[0:65, 0:c.AG],
                       VTM[:, jm * VW + 65 * hB:jm * VW + 65 * hB + 65],
                       PBF[:, pbm * 1024 + 512:pbm * 1024 + 512 + c.AG],
                       start=False, stop=True,
                       reads=[bVTM, bPBF[pbm]], writes=[bP4])
                    slot = p % 2
                    orw = ORW[:, slot * c.AG:(slot + 1) * c.AG]
                    cp("scalar", orw[0:65, :], P3[0:65, 0:c.AG], reads=[bP3],
                       writes=[bORW])
                    cp("scalar", SMR[64:65, slot * 1024:slot * 1024 + c.AG],
                       P3[64:65, 0:c.AG], reads=[bP3], writes=[bSMR])
                    cp("scalar", OST[0:65, p * c.AG:(p + 1) * c.AG],
                       P4[0:65, 0:c.AG], reads=[bP4], writes=[bOST])
                    cp("scalar", SMR[64:65, slot * 1024 + c.AG:slot * 1024 + 2 * c.AG],
                       P4[64:65, 0:c.AG], reads=[bP4], writes=[bSMR])
                    pend.append((p, slot))
                while pend:
                    pp, ss = pend.pop(0)
                    _emit_norm(g, pp, ss)
                em.dma("scalar",
                       bass.AP(OBF, 64 * QC * c.AG,
                               [[QC * c.AG, 64], [c.AG, c.NP], [1, c.AG]]),
                       bass.AP(OST, 0, [[c.NP * c.AG, 64], [c.AG, c.NP], [1, c.AG]]),
                       reads=[bOST], writes=[bOBF])
                if c.DBG and s == 0:
                    em.dma("gpsimd",
                           bass.AP(dbg_scr.tensor, g * 128 * QC * c.AG,
                                   [[QC * c.AG, 128], [1, QC * c.AG]]),
                           OBF[:, :], reads=[bOBF], writes=[B("dbg")])
                for tc in range(NCH):
                    py, bpy = (P0, bP0) if tc % 2 == 0 else (P1, bP1)
                    nh = max(1, D // 512)
                    dw = D // nh
                    for ci in range(QC):
                        for half in range(nh):
                            mm(py[:, half * dw:(half + 1) * dw],
                               OBF[:, ci * c.AG + tc * 128:ci * c.AG + (tc + 1) * 128],
                               WOUT[:, ci * D + half * dw:ci * D + (half + 1) * dw],
                               start=(ci == 0), stop=(ci == QC - 1),
                               reads=[bOBF, B("WOUT")], writes=[bpy])
                    k4 = tc % 4
                    o = 32 * k4
                    bSk = bSTs[k4]
                    bEP1c = B(f"EP1{s}_{tc}")
                    bEP2c = B(f"EP2{s}_{tc}")
                    e1 = EP1[:, tc * D:(tc + 1) * D]
                    e2 = EP2[:, tc * D:(tc + 1) * D]
                    xa, bxa = xchunk(gbase, tc)
                    tt("vector", e1, py[:, 0:D], BOREP[:], ALU.add,
                       reads=[bpy, B("BOREP")], writes=[bEP1c])
                    if c.DBG and s == 0 and g == 0:
                        em.dma("gpsimd",
                               bass.AP(dbg2_scr.tensor, tc * 128 * D, [[D, 128], [1, D]]),
                               e1, reads=[bEP1c], writes=[B("dbg2")])
                    ln_bn(e1, D, bEP1c, k4)
                    ln_sqrt(k4)
                    ln_recip(k4)
                    ln_apply(e2, e1, k4, [bEP1c], [bEP2c])
                    tt("vector", e1, e2, GREP[:], ALU.mult, reads=[bEP2c, B("GREP")],
                       writes=[bEP1c])
                    tt("vector", e2, e1, BREP[:], ALU.add, reads=[bEP1c, B("BREP")],
                       writes=[bEP2c])
                    for j in range(DC):
                        transp(PT[:, (j % 8) * 128:(j % 8) * 128 + 128],
                               HBF[:, j * S + g * c.AG + tc * 128:
                                   j * S + g * c.AG + (tc + 1) * 128],
                               reads=[bHBF], writes=[bPT])
                    tt("vector", e1, e2, PT[:, 0:D], ALU.add, reads=[bEP2c, bPT],
                       writes=[bEP1c])
                    tt("vector", e2, e1, GAR[:], ALU.mult, reads=[bEP1c, bGAR],
                       writes=[bEP2c])
                    tt("gpsimd", xa, xa, e2, ALU.add, reads=[bEP2c], writes=[bxa])
                    em.dma("scalar",
                           bass.AP(xmid_scr.tensor, (gbase + tc * 128) * D,
                                   [[D, 128], [1, D]]),
                           xa, reads=[bxa], writes=[bXMS])

            em.barrier()
            ar.close()

            br = contextlib.ExitStack()
            W1S = br.enter_context(nc.sbuf_tensor(f"W1S{s}", [128, DC * HID], BF16))
            CB1C1 = br.enter_context(nc.sbuf_tensor(f"CB1{s}", [128, D], BF16))
            SBC1 = br.enter_context(nc.sbuf_tensor(f"SBC1{s}", [128, D], BF16))
            ZC = br.enter_context(nc.sbuf_tensor(f"ZC{s}", [128, (c.T1 // 128) * D], F32))
            DS1 = br.enter_context(nc.sbuf_tensor(f"DS1{s}", [128, D], F32))
            DS2 = br.enter_context(nc.sbuf_tensor(f"DS2{s}", [128, D], BF16))
            DI8 = br.enter_context(nc.sbuf_tensor(f"DI8{s}", [128, D], I8))
            XQ1 = br.enter_context(nc.sbuf_tensor(f"XQ1{s}", [128, 3 * DC * c.T1], BF16))
            XMB = br.enter_context(nc.sbuf_tensor(f"XMB{s}", [128, 2 * (c.T1 // 128) * D], F32))
            HA = br.enter_context(nc.sbuf_tensor(f"HA{s}", [128, (c.T1 // 128) * HID], BF16))
            HBt = br.enter_context(nc.sbuf_tensor(f"HBt{s}", [128, (c.T1 // 128) * HID], BF16))
            HI8 = br.enter_context(nc.sbuf_tensor(f"HI8{s}", [128, HID], I8))
            XQ2S = br.enter_context(nc.sbuf_tensor(f"XQ2S{s}", [128, 2048], BF16))

            bW1, bCB, bSB = B(f"W1S{s}"), B(f"CB1{s}"), B(f"SBC1{s}")
            bXQ1, bXMB = B(f"XQ1{s}"), B(f"XMB{s}")
            bXQscr, bS2 = B("xq2_scr"), B("S2COL")

            bcast("gpsimd", CB1C1[:], mod_scr.tensor, (s * 6 + 4) * D, D,
                  reads=[B("mod_scr")], writes=[bCB])
            ts("vector", CB1C1[:], CB1C1[:], 1.0, None, ALU.add, reads=[bCB],
               writes=[bCB])
            tt("vector", CB1C1[:], CB1C1[:], C1REP[:], ALU.mult, reads=[B("C1REP")],
               writes=[bCB])
            bcast("sync", DS1[:, 0:D], mod_scr.tensor, (s * 6 + 3) * D, D,
                  reads=[B("mod_scr")], writes=[B(f"DS1{s}_0")])
            tt("vector", SBC1[:], DS1[:, 0:D], C1REP[:], ALU.mult,
               reads=[B(f"DS1{s}_0"), B("C1REP")], writes=[bSB])
            for hh, q_ in ((0, "sync"), (1, "gpsimd")):
                em.dma(q_,
                       bass.AP(W1S, hh * (DC // 2) * HID,
                               [[DC * HID, 128], [HID, DC // 2], [1, HID]]),
                       bass.AP(w1q_scr.tensor, hh * (DC // 2) * 128 * HID,
                               [[HID, 128], [128 * HID, DC // 2], [1, HID]]),
                       reads=[B("qS0")], writes=[bW1])

            def xq2_finish(t1, tc):
                bHBc = B(f"HBt{s}_{tc}")
                ua = HBt[:, tc * HID:(tc + 1) * HID]
                g8 = min(8, HC)
                for quad in range(HC // g8):
                    for j in range(g8):
                        transp(PT[:, j * 128:(j + 1) * 128],
                               ua[:, (quad * g8 + j) * 128:(quad * g8 + j + 1) * 128],
                               reads=[bHBc], writes=[bPT])
                    qs = quad % 2
                    bXQ2c = B(f"XQ2S{s}_{qs}")
                    cp("scalar",
                       bass.AP(XQ2S, qs * 1024, [[2048, 128], [128, g8], [1, 128]]),
                       bass.AP(PT, 0, [[1024, 128], [128, g8], [1, 128]]),
                       reads=[bPT], writes=[bXQ2c])
                    em.dma("scalar",
                           bass.AP(xq2_scr.tensor,
                                   (quad * g8 * 128) * S + t1 * c.T1 + tc * 128,
                                   [[S, 128], [128 * S, g8], [1, 128]]),
                           bass.AP(XQ2S, qs * 1024, [[2048, 128], [128, g8], [1, 128]]),
                           reads=[bXQ2c], writes=[bXQscr])

            NC1 = c.T1 // 128

            def b1_dstage(t1):
                ts_ = t1 % 2
                sl = t1 % 3
                tb = s * S + t1 * c.T1
                em.dma("sync",
                       bass.AP(XMB, ts_ * NC1 * D,
                               [[2 * NC1 * D, 128], [D, NC1], [1, D]]),
                       bass.AP(xmid_scr.tensor, tb * D,
                               [[D, 128], [128 * D, NC1], [1, D]]),
                       reads=[bXMS], writes=[B(f"XMB{s}_{ts_}")])
                for tc in range(NC1):
                    k4 = (t1 * NC1 + tc) % 4
                    ln_bn(XMB[:, (ts_ * NC1 + tc) * D:(ts_ * NC1 + tc + 1) * D], D,
                          B(f"XMB{s}_{ts_}"), k4)
                for tc in range(NC1):
                    ln_sqrt((t1 * NC1 + tc) % 4)
                for tc in range(NC1):
                    ln_recip((t1 * NC1 + tc) % 4)
                for tc in range(NC1):
                    k4 = (t1 * NC1 + tc) % 4
                    o = 32 * k4
                    bSk = bSTs[k4]
                    tcol = t1 * NC1 + tc
                    bZCc = B(f"ZC{s}_{tc}")
                    bD1c, bD2c = B(f"DS1{s}"), B(f"DS2{s}")
                    bDIc = B(f"DI8{s}")
                    xa = XMB[:, (ts_ * NC1 + tc) * D:(ts_ * NC1 + tc + 1) * D]
                    za = ZC[:, tc * D:(tc + 1) * D]
                    d1 = DS1[:, 0:D]
                    d2 = DS2[:, 0:D]
                    di = DI8[:, 0:D]
                    xqt = d2
                    ak = 22 + (tcol // 4) % 2
                    ln_apply(za, xa, k4, [B(f"XMB{s}_{ts_}")], [bZCc])
                    tt("vector", za, za, CB1C1[:], ALU.mult, reads=[bCB], writes=[bZCc])
                    tt("vector", za, za, SBC1[:], ALU.add, reads=[bSB], writes=[bZCc])
                    tt("vector", d1, za, C1IREP[:], ALU.mult,
                       reads=[bZCc, B("C1IREP")], writes=[bD1c])
                    act("scalar", d2, d1, AF.Square, accum=STATC[:, o + 15:o + 16],
                        reads=[bD1c], writes=[bD2c, bSk])
                    ts("vector", STATC[:, o + 15:o + 16], STATC[:, o + 15:o + 16],
                       1.0 / D, EPS_NORM, ALU.mult, ALU.add, reads=[bSk], writes=[bSk])
                    em.emit("vector", lambda e, o=o: e.reciprocal(
                        STATC[:, o + 16:o + 17], STATC[:, o + 15:o + 16]),
                        reads=[bSk], writes=[bSk])
                    act("scalar", STATC[:, o + 16:o + 17], STATC[:, o + 16:o + 17],
                        AF.Sqrt, reads=[bSk], writes=[bSk])
                    red("vector", STATC[:, o + 17:o + 18], za, ALU.max, absval=True,
                        reads=[bZCc], writes=[bSk])
                    tt("vector", STATC[:, o + 17:o + 18], STATC[:, o + 17:o + 18],
                       STATC[:, o + 16:o + 17], ALU.mult, reads=[bSk], writes=[bSk])
                    ts("vector", STATC[:, o + 17:o + 18], STATC[:, o + 17:o + 18],
                       EPS_Q, None, ALU.max, reads=[bSk], writes=[bSk])
                    em.emit("vector", lambda e, o=o: e.reciprocal(
                        STATC[:, o + 18:o + 19], STATC[:, o + 17:o + 18]),
                        reads=[bSk], writes=[bSk])
                    tt("vector", STATC[:, o + 19:o + 20], STATC[:, o + 18:o + 19],
                       STATC[:, o + 16:o + 17], ALU.mult, reads=[bSk], writes=[bSk])
                    ts("vector", STATC[:, o + 19:o + 20], STATC[:, o + 19:o + 20],
                       127.0, None, ALU.mult, reads=[bSk], writes=[bSk])
                    act("scalar", za, za, AF.Identity, bias=ZCOL[:, 0:1],
                        scale=STATC[:, o + 19:o + 20],
                        reads=[bSk, bZCc, B("ZCOL")], writes=[bZCc])
                    ts("vector", STATC[:, o + 20:o + 21], STATC[:, o + 17:o + 18],
                       1.0 / 127.0, None, ALU.mult, reads=[bSk], writes=[bSk])
                    tt("vector", STATC[:, o + 20:o + 21], STATC[:, o + 20:o + 21],
                       SWCOL[:, 0:1], ALU.mult, reads=[B("SWCOL")], writes=[bSk])
                    tt("vector", STATC[:, o + ak:o + ak + 1], STATC[:, o + 20:o + 21],
                       STATC[:, o + 20:o + 21], ALU.mult, reads=[bSk], writes=[bAK])
                    tt("vector", STATC[:, o + ak:o + ak + 1], STATC[:, o + ak:o + ak + 1],
                       ABCOL[:, 0:1], ALU.mult, reads=[B("ABCOL")], writes=[bAK])
                    cp("scalar", di, za, reads=[bZCc], writes=[bDIc])
                    tt("vector", d1, za, di, ALU.subtract, reads=[bZCc, bDIc],
                       writes=[bD1c])
                    stt("vector", d2, d1, 0.5, di, ALU.is_ge, ALU.add,
                        reads=[bD1c, bDIc], writes=[bD2c])
                    stt("vector", xqt, d1, -0.5, d2, ALU.is_lt, ALU.subtract,
                        reads=[bD1c, bD2c], writes=[bD2c])
                    for j in range(DC):
                        transp(PT[:, (j % 8) * 128:(j % 8) * 128 + 128],
                               xqt[:, j * 128:(j + 1) * 128],
                               reads=[bD2c], writes=[bPT])
                    g4 = min(4, DC)
                    for half in range(DC // g4):
                        cp("scalar",
                           bass.AP(XQ1, sl * DC * c.T1 + half * g4 * c.T1 + tc * 128,
                                   [[3 * DC * c.T1, 128], [c.T1, g4], [1, 128]]),
                           bass.AP(PT, half * g4 * 128, [[1024, 128], [128, g4], [1, 128]]),
                           reads=[bPT], writes=[B(f"XQ1{s}_{sl}")])

            def b1_mstage(t1):
                sl = t1 % 3
                for tc in range(NC1):
                    bHAc, bHBc = B(f"HA{s}_{tc}"), B(f"HBt{s}_{tc}")
                    for hb2 in range(0, HID // 512, 2):
                        pairs = [(0, hb2)]
                        if hb2 + 1 < HID // 512:
                            pairs.append((1, hb2 + 1))
                        for dc in range(DC):
                            for pi, hb in pairs:
                                ph, bph = (P0, bP0) if pi == 0 else (P1, bP1)
                                mm(ph[:, 0:512],
                                   XQ1[:, sl * DC * c.T1 + dc * c.T1 + tc * 128:
                                       sl * DC * c.T1 + dc * c.T1 + (tc + 1) * 128],
                                   W1S[:, dc * HID + hb * 512:dc * HID + (hb + 1) * 512],
                                   start=(dc == 0), stop=(dc == DC - 1),
                                   reads=[B(f"XQ1{s}_{sl}"), bW1], writes=[bph])
                        for pi, hb in pairs:
                            ph, bph = (P0, bP0) if pi == 0 else (P1, bP1)
                            hs = HA[:, tc * HID + hb * 512:tc * HID + (hb + 1) * 512]
                            act("scalar", hs, ph[:, 0:512], AF.Relu, reads=[bph],
                                writes=[bHAc])
                            us = HBt[:, tc * HID + hb * 512:tc * HID + (hb + 1) * 512]
                            act("scalar", us, hs, AF.Square, reads=[bHAc],
                                writes=[bHBc])
                for tc in range(NC1):
                    k4 = (t1 * NC1 + tc) % 4
                    o = 32 * k4
                    tcol = t1 * NC1 + tc
                    ak = 22 + (tcol // 4) % 2
                    bSm = bSTm[k4]
                    bHAc, bHBc, bHIc = B(f"HA{s}_{tc}"), B(f"HBt{s}_{tc}"), B(f"HI8{s}")
                    ua = HBt[:, tc * HID:(tc + 1) * HID]
                    aa = HA[:, tc * HID:(tc + 1) * HID]
                    ia = HI8[:, 0:HID]
                    act("scalar", aa, ua, AF.Identity, bias=ABCOL[:, 1:2],
                        scale=STATC[:, o + ak:o + ak + 1],
                        reads=[bHBc, bAK, B("ABCOL")], writes=[bHAc])
                    act("scalar", ua, aa, AF.Square, accum=STATC[:, o + 24:o + 25],
                        reads=[bHAc], writes=[bHBc, bSm])
                    ts("vector", STATC[:, o + 24:o + 25], STATC[:, o + 24:o + 25],
                       1.0 / HID, EPS_NORM, ALU.mult, ALU.add, reads=[bSm], writes=[bSm])
                    em.emit("vector", lambda e, o=o: e.reciprocal(
                        STATC[:, o + 25:o + 26], STATC[:, o + 24:o + 25]),
                        reads=[bSm], writes=[bSm])
                    act("scalar", STATC[:, o + 25:o + 26], STATC[:, o + 25:o + 26],
                        AF.Sqrt, reads=[bSm], writes=[bSm])
                    tt("vector", ua, aa, C2REP[:], ALU.mult, reads=[bHAc, B("C2REP")],
                       writes=[bHBc])
                    red("vector", STATC[:, o + 26:o + 27], ua, ALU.max, absval=True,
                        reads=[bHBc], writes=[bSm])
                    tt("vector", STATC[:, o + 26:o + 27], STATC[:, o + 26:o + 27],
                       STATC[:, o + 25:o + 26], ALU.mult, reads=[bSm], writes=[bSm])
                    ts("vector", STATC[:, o + 26:o + 27], STATC[:, o + 26:o + 27],
                       EPS_Q, None, ALU.max, reads=[bSm], writes=[bSm])
                    em.emit("vector", lambda e, o=o: e.reciprocal(
                        STATC[:, o + 27:o + 28], STATC[:, o + 26:o + 27]),
                        reads=[bSm], writes=[bSm])
                    tt("vector", STATC[:, o + 28:o + 29], STATC[:, o + 27:o + 28],
                       STATC[:, o + 25:o + 26], ALU.mult, reads=[bSm], writes=[bSm])
                    ts("vector", STATC[:, o + 28:o + 29], STATC[:, o + 28:o + 29],
                       127.0, None, ALU.mult, reads=[bSm], writes=[bSm])
                    act("scalar", aa, ua, AF.Identity, bias=ZCOL[:, 0:1],
                        scale=STATC[:, o + 28:o + 29],
                        reads=[bHBc, bSm, B("ZCOL")], writes=[bHAc])
                    ts("vector", S2COL[:, tcol:tcol + 1], STATC[:, o + 26:o + 27],
                       1.0 / 127.0, None, ALU.mult, reads=[bSm], writes=[bS2])
                    tt("vector", S2COL[:, tcol:tcol + 1], S2COL[:, tcol:tcol + 1],
                       SWCOL[:, 1:2], ALU.mult, reads=[B("SWCOL")], writes=[bS2])
                    cp("scalar", ia, aa, reads=[bHAc], writes=[bHIc])
                    tt("vector", ua, aa, ia, ALU.subtract, reads=[bHAc, bHIc],
                       writes=[bHBc])
                    stt("vector", aa, ua, 0.5, ia, ALU.is_ge, ALU.add,
                        reads=[bHBc, bHIc], writes=[bHAc])
                    stt("vector", ua, ua, -0.5, aa, ALU.is_lt, ALU.subtract,
                        reads=[bHAc], writes=[bHBc])

            b1_dstage(0)
            if c.NT1 > 1:
                b1_dstage(1)
            for t1 in range(c.NT1):
                if t1 >= 1:
                    for tc in range(NC1):
                        xq2_finish(t1 - 1, tc)
                b1_mstage(t1)
                if t1 + 2 < c.NT1:
                    b1_dstage(t1 + 2)
            for tc in range(c.T1 // 128):
                xq2_finish(c.NT1 - 1, tc)

            em.barrier()
            br.close()

            b2 = contextlib.ExitStack()
            W2S = b2.enter_context(nc.sbuf_tensor(f"W2S{s}", [128, HC * D], BF16))
            GBR = b2.enter_context(nc.sbuf_tensor(f"GBR{s}", [128, D], BF16))
            GB2 = b2.enter_context(nc.sbuf_tensor(f"GB2{s}", [128, D], F32))
            XQ2F = b2.enter_context(nc.sbuf_tensor(f"XQ2F{s}", [128, HC * c.T2], BF16))
            XMT2 = b2.enter_context(nc.sbuf_tensor(f"XMT2{s}", [128, (c.T2 // 128) * D], F32))
            OUTS = b2.enter_context(nc.sbuf_tensor(f"OUTS{s}", [128, (c.T2 // 128) * D], F32))
            bW2, bGBR, bGB2 = B(f"W2S{s}"), B(f"GBR{s}"), B(f"GB2{s}")
            bXM2 = B(f"XMT2{s}")
            bXQFh = [B(f"XQ2F{s}_0"), B(f"XQ2F{s}_1")]

            bcast("sync", GB2[:], mod_scr.tensor, (s * 6 + 5) * D, D,
                  reads=[B("mod_scr")], writes=[bGB2])
            ts("vector", GBR[:], GB2[:], 0.0, 2.0, ALU.is_ge, ALU.mult, reads=[bGB2],
               writes=[bGBR])
            ts("vector", GBR[:], GBR[:], -1.0, None, ALU.add, reads=[bGBR],
               writes=[bGBR])
            act("scalar", GB2[:], GB2[:], AF.Abs, reads=[bGB2], writes=[bGB2])
            ts("vector", GB2[:], GB2[:], 0.25, None, ALU.max, reads=[bGB2],
               writes=[bGB2])
            tt("vector", GB2[:], GB2[:], GBR[:], ALU.mult, reads=[bGBR], writes=[bGB2])
            cp("vector", GBR[:], GB2[:], reads=[bGB2], writes=[bGBR])
            for hh, q_ in ((0, "sync"), (1, "gpsimd")):
                em.dma(q_,
                       bass.AP(W2S, hh * (HC // 2) * D,
                               [[HC * D, 128], [D, HC // 2], [1, D]]),
                       bass.AP(w2q_scr.tensor, hh * (HC // 2) * 128 * D,
                               [[D, 128], [128 * D, HC // 2], [1, D]]),
                       reads=[B("qS1")], writes=[bW2])

            HC2 = HC // 2
            for t2 in range(c.NT2):
                tb = s * S + t2 * c.T2
                for hh in range(2):
                    em.dma("sync" if hh == 0 else "gpsimd",
                           bass.AP(XQ2F, hh * HC2 * c.T2,
                                   [[HC * c.T2, 128], [c.T2, HC2], [1, c.T2]]),
                           bass.AP(xq2_scr.tensor, hh * HC2 * 128 * S + t2 * c.T2,
                                   [[S, 128], [128 * S, HC2], [1, c.T2]]),
                           reads=[bXQscr], writes=[bXQFh[hh]])
                em.dma("scalar",
                       bass.AP(XMT2, 0, [[(c.T2 // 128) * D, 128], [D, c.T2 // 128], [1, D]]),
                       bass.AP(xmid_scr.tensor, tb * D,
                               [[D, 128], [128 * D, c.T2 // 128], [1, D]]),
                       reads=[bXMS], writes=[bXM2])
                for tc in range(c.T2 // 128):
                    py, bpy = (P0, bP0) if tc % 2 == 0 else (P1, bP1)
                    nh = max(1, D // 512)
                    dw = D // nh
                    for hc in range(HC):
                        for half in range(nh):
                            mm(py[:, half * dw:(half + 1) * dw],
                               XQ2F[:, hc * c.T2 + tc * 128:hc * c.T2 + (tc + 1) * 128],
                               W2S[:, hc * D + half * dw:hc * D + (half + 1) * dw],
                               start=(hc == 0), stop=(hc == HC - 1),
                               reads=[bXQFh[hc // HC2], bW2], writes=[bpy])
                    oa = OUTS[:, tc * D:(tc + 1) * D]
                    bOUc = B(f"OUTS{s}_{tc % 2}")
                    tcol = (t2 * c.T2) // 128 + tc
                    ts("vector", oa, py[:, 0:D], S2COL[:, tcol:tcol + 1], None,
                       ALU.mult, reads=[bpy, bS2], writes=[bOUc])
                    tt("vector", oa, oa, GBR[:], ALU.mult, reads=[bGBR], writes=[bOUc])
                    tt("vector", oa, oa, XMT2[:, tc * D:(tc + 1) * D], ALU.add,
                       reads=[bXM2], writes=[bOUc])
                    em.dma("scalar",
                           bass.AP(out_t.tensor, (tb + tc * 128) * D,
                                   [[D, 128], [1, D]]),
                           oa, reads=[bOUc], writes=[B("out")])

            em.barrier()
            b2.close()

        em.finish()

    return nc


_NC_CACHE = {}


def _get_nc():
    if "full" not in _NC_CACHE:
        _NC_CACHE["full"] = build_program(CFG)
    return _NC_CACHE["full"]


def make_inputs(core, cfg, L0_attn_x, L0_attn_W_qk, L0_attn_b_qk, L0_attn_W_v,
                L0_attn_b_v, L0_attn_W_out, L0_attn_b_out, L0_attn_ln_gamma,
                L0_attn_ln_beta, L0_ffn_norm1_w, L0_ffn_w1, L0_ffn_act_scale,
                L0_ffn_act_bias, L0_ffn_norm2_w, L0_ffn_w2, L0_c, L0_wmod):
    c = cfg
    D, QK = c.D, c.QK
    f32 = np.float32
    perm = np.arange(QK)
    h_idx, d_idx = perm // c.DH, perm % c.DH
    qcols = h_idx * 2 * c.DH + 2 * d_idx
    kcols = qcols + 1
    W_qk = np.asarray(L0_attn_W_qk, f32)
    b_qk = np.asarray(L0_attn_b_qk, f32)
    wqk_p = np.concatenate([W_qk[:, qcols], W_qk[:, kcols]], axis=1)
    bqk2 = np.stack([b_qk[qcols], b_qk[kcols]], axis=0)
    b0 = core * c.NSEQ
    x = np.asarray(L0_attn_x, f32)
    return {
        "x": np.ascontiguousarray(x[b0:b0 + c.NSEQ].reshape(c.NSEQ * c.S, D)),
        "wqk": np.ascontiguousarray(wqk_p),
        "bqk2": np.ascontiguousarray(bqk2),
        "wv": np.ascontiguousarray(np.asarray(L0_attn_W_v, f32)),
        "bv": np.ascontiguousarray(np.asarray(L0_attn_b_v, f32).reshape(1, QK)),
        "wout": np.ascontiguousarray(np.asarray(L0_attn_W_out, f32)),
        "bout": np.ascontiguousarray(np.asarray(L0_attn_b_out, f32).reshape(1, D)),
        "ln_g": np.ascontiguousarray(np.asarray(L0_attn_ln_gamma, f32).reshape(1, D)),
        "ln_b": np.ascontiguousarray(np.asarray(L0_attn_ln_beta, f32).reshape(1, D)),
        "n1w": np.ascontiguousarray(np.asarray(L0_ffn_norm1_w, f32).reshape(1, D)),
        "w1t": np.ascontiguousarray(np.asarray(L0_ffn_w1, f32).T),
        "actsc": np.ascontiguousarray(np.array(
            [[np.asarray(L0_ffn_act_scale, f32).reshape(()),
              np.asarray(L0_ffn_act_bias, f32).reshape(())]], f32)),
        "n2w": np.ascontiguousarray(np.asarray(L0_ffn_norm2_w, f32).reshape(1, c.HID)),
        "w2t": np.ascontiguousarray(np.asarray(L0_ffn_w2, f32).T),
        "c_t": np.ascontiguousarray(np.asarray(L0_c, f32)[b0:b0 + c.NSEQ].T),
        "wmod": np.ascontiguousarray(np.asarray(L0_wmod, f32)),
    }


def kernel(**inputs):
    c = CFG
    nc = _get_nc()
    in_maps = [make_inputs(core, c, **inputs) for core in range(N_CORES)]
    res = run_bass_kernel_spmd(nc, in_maps, core_ids=list(range(N_CORES)))
    outs = [np.asarray(r["out"]).reshape(c.NSEQ, c.S, c.D) for r in res.results]
    return np.concatenate(outs, axis=0)


# version 17
# speedup vs baseline: 1.0341x; 1.0341x over previous
import contextlib
import numpy as np
import concourse.bass as bass
import concourse.mybir as mybir
from concourse.bass_utils import run_bass_kernel_spmd

F32 = mybir.dt.float32
BF16 = mybir.dt.bfloat16
I8 = mybir.dt.int8
AF = mybir.ActivationFunctionType
ALU = mybir.AluOpType
AX = mybir.AxisListType

RSQRT2 = float(2.0 ** (-0.5))
LN_EPS = 1e-5
EPS_NORM = 1e-8
EPS_Q = 1e-5

N_CORES = 8


class Cfg:
    def __init__(self, S=2048, D=1024, HID=4096, NSEQ=4, H=8, AG=512, T1=256, T2=512):
        self.S, self.D, self.HID, self.NSEQ, self.H = S, D, HID, NSEQ, H
        self.DH = 64
        self.AG, self.T1, self.T2 = AG, T1, T2
        self.DC = D // 128
        self.QK = self.H * self.DH
        self.QC = self.QK // 128
        self.HC = HID // 128
        self.KC = S // 128
        self.NG = S // AG
        self.NT1 = S // T1
        self.NT2 = S // T2
        self.NP = self.H // 2
        self.VW = self.H * 65 + 8
        self.DBG = False


CFG = Cfg()


class Buf:
    __slots__ = ("name", "writer", "readers")

    def __init__(self, name):
        self.name = name
        self.writer = None
        self.readers = []


class Em:
    ENGINES = ("tensor", "vector", "scalar", "gpsimd", "sync")
    NSLOT = {"sync": 12, "scalar": 8, "gpsimd": 4, "vector": 2}

    def __init__(self, nc):
        self.nc = nc
        self.TRACKS = list(self.ENGINES)
        self.dma_slots = {}
        for q, n in self.NSLOT.items():
            self.dma_slots[q] = [f"{q}_dma{i}" for i in range(n)]
            self.TRACKS += self.dma_slots[q]
        self.sems = {}
        self.cnt = {t: 0 for t in self.TRACKS}
        self.waited = {e: {t: 0 for t in self.TRACKS} for e in self.ENGINES}
        self.eng = {"tensor": nc.tensor, "vector": nc.vector, "scalar": nc.scalar,
                    "gpsimd": nc.gpsimd, "sync": nc.sync}
        self.dma_idx = {q: 0 for q in self.NSLOT}
        self.bufs = {}

    def B(self, name):
        if name not in self.bufs:
            self.bufs[name] = Buf(name)
        return self.bufs[name]

    def init_sems(self, ctx):
        for t in self.TRACKS:
            self.sems[t] = ctx.enter_context(self.nc.semaphore(f"sem_{t}"))

    def _wait(self, eng, dep_eng, dep_cnt):
        if dep_cnt <= 0 or self.waited[eng][dep_eng] >= dep_cnt:
            return
        self.eng[eng].wait_ge(self.sems[dep_eng], dep_cnt)
        self.waited[eng][dep_eng] = dep_cnt

    def emit(self, eng, fn, reads=(), writes=(), inc=1, track=None):
        if track is None:
            track = eng
        for b in reads:
            if b.writer is not None:
                self._wait(eng, b.writer[0], b.writer[1])
        for b in writes:
            if b.writer is not None:
                self._wait(eng, b.writer[0], b.writer[1])
            for tok in b.readers:
                self._wait(eng, tok[0], tok[1])
        inst = fn(self.eng[eng])
        self.cnt[track] += inc
        inst.then_inc(self.sems[track], inc)
        tok = (track, self.cnt[track])
        for b in reads:
            b.readers.append(tok)
        for b in writes:
            b.writer = tok
            b.readers = []
        return tok

    def dma(self, queue, out_ap, in_ap, reads=(), writes=(), slow=False):
        slot = self.dma_slots[queue][self.dma_idx[queue] % self.NSLOT[queue]]
        self.dma_idx[queue] += 1
        self._wait(queue, slot, self.cnt[slot])
        if slow:
            return self.emit(queue, lambda e: e.dma_start(
                out=out_ap, in_=in_ap, allow_slow_non_contiguous=True),
                reads=reads, writes=writes, inc=16, track=slot)
        return self.emit(queue, lambda e: e.dma_start(out=out_ap, in_=in_ap),
                         reads=reads, writes=writes, inc=16, track=slot)

    def barrier(self):
        snap = dict(self.cnt)
        for e in self.ENGINES:
            for t in self.TRACKS:
                if t != e:
                    self._wait(e, t, snap[t])

    def finish(self):
        snap = dict(self.cnt)
        for t in self.TRACKS:
            if t != "gpsimd":
                self._wait("gpsimd", t, snap[t])


def build_program(c: Cfg):
    nc = bass.Bass("TRN2", target_bir_lowering=False)
    S, D, HID, NSEQ, H = c.S, c.D, c.HID, c.NSEQ, c.H
    QK, DC, QC, HC, KC, VW = c.QK, c.DC, c.QC, c.HC, c.KC, c.VW

    def din(name, shape, dt=F32):
        return nc.dram_tensor(name, list(shape), dt, kind="ExternalInput").ap()

    x_in = din("x", (NSEQ * S, D))
    wqk = din("wqk", (D, 2 * QK))
    bqk2 = din("bqk2", (2, QK))
    wv = din("wv", (D, QK))
    bv = din("bv", (1, QK))
    wout = din("wout", (QK, D))
    bout = din("bout", (1, D))
    ln_g = din("ln_g", (1, D))
    ln_b = din("ln_b", (1, D))
    n1w = din("n1w", (1, D))
    w1t = din("w1t", (D, HID))
    actsc = din("actsc", (1, 2))
    n2w = din("n2w", (1, HID))
    w2t = din("w2t", (HID, D))
    c_t = din("c_t", (D, NSEQ))
    wmod = din("wmod", (D, 6 * D))

    out_t = nc.dram_tensor("out", [NSEQ * S, D], F32, kind="ExternalOutput").ap()
    xmid_scr = nc.dram_tensor("xmid_scr", [NSEQ * S, D], F32).ap()
    mod_scr = nc.dram_tensor("mod_scr", [NSEQ * 6, D], F32).ap()
    w1q_scr = nc.dram_tensor("w1q_scr", [D, HID], BF16).ap()
    w2q_scr = nc.dram_tensor("w2q_scr", [HID, D], BF16).ap()
    xq2_scr = nc.dram_tensor("xq2_scr", [HID, S], BF16).ap()
    tiny_scr = nc.dram_tensor("tiny_scr", [64], F32).ap()
    dbg_scr = nc.dram_tensor("dbg_scr", [c.NG * 128, QC * c.AG], F32).ap() if c.DBG else None
    dbg2_scr = nc.dram_tensor("dbg2_scr", [(c.AG // 128) * 128, D], F32).ap() if c.DBG else None

    ctx = contextlib.ExitStack()
    em = Em(nc)

    with ctx:
        em.init_sems(ctx)
        B = em.B

        WQK = ctx.enter_context(nc.sbuf_tensor("WQK", [128, DC * 2 * QK], BF16))
        WV = ctx.enter_context(nc.sbuf_tensor("WV", [128, DC * QK], BF16))
        WOUT = ctx.enter_context(nc.sbuf_tensor("WOUT", [128, QC * D], BF16))
        IDN = ctx.enter_context(nc.sbuf_tensor("IDN", [128, 128], BF16))
        ONESF = ctx.enter_context(nc.sbuf_tensor("ONESF", [1, 128], F32))
        ONESC = ctx.enter_context(nc.sbuf_tensor("ONESC", [128, 1], F32))
        EPSCOL = ctx.enter_context(nc.sbuf_tensor("EPSCOL", [128, 1], F32))
        ZCOL = ctx.enter_context(nc.sbuf_tensor("ZCOL", [128, 1], F32))
        IND64 = ctx.enter_context(nc.sbuf_tensor("IND64", [128, 256], F32))
        BQS = ctx.enter_context(nc.sbuf_tensor("BQS", [128, QC], F32))
        BKS = ctx.enter_context(nc.sbuf_tensor("BKS", [128, QC], F32))
        BVC = ctx.enter_context(nc.sbuf_tensor("BVC", [128, QC], BF16))
        BOREP = ctx.enter_context(nc.sbuf_tensor("BOREP", [128, D], BF16))
        GREP = ctx.enter_context(nc.sbuf_tensor("GREP", [128, D], BF16))
        BREP = ctx.enter_context(nc.sbuf_tensor("BREP", [128, D], BF16))
        C1REP = ctx.enter_context(nc.sbuf_tensor("C1REP", [128, D], F32))
        C1IREP = ctx.enter_context(nc.sbuf_tensor("C1IREP", [128, D], F32))
        C2REP = ctx.enter_context(nc.sbuf_tensor("C2REP", [128, HID], BF16))
        ABCOL = ctx.enter_context(nc.sbuf_tensor("ABCOL", [128, 2], F32))
        SWCOL = ctx.enter_context(nc.sbuf_tensor("SWCOL", [128, 2], F32))
        IND8S = ctx.enter_context(nc.sbuf_tensor("IND8S", [128, QC * 8], BF16))
        NNB = ctx.enter_context(nc.sbuf_tensor("NNB", [128, c.NP], F32))
        S2COL = ctx.enter_context(nc.sbuf_tensor("S2COL", [128, S // 128], F32))
        STATC = ctx.enter_context(nc.sbuf_tensor("STATC", [128, 160], F32))

        P0 = ctx.enter_context(nc.psum_tensor("P0", [128, 1024], F32))
        P1 = ctx.enter_context(nc.psum_tensor("P1", [128, 1024], F32))
        P2A = ctx.enter_context(nc.psum_tensor("P2A", [128, 512], F32))
        P3 = ctx.enter_context(nc.psum_tensor("P3", [128, 512], F32))
        P4 = ctx.enter_context(nc.psum_tensor("P4", [128, 512], F32))
        PT = ctx.enter_context(nc.psum_tensor("PT", [128, 1024], BF16))
        bP0, bP1, bP2, bP3, bP4 = B("P0"), B("P1"), B("P2A"), B("P3"), B("P4")
        bPT = B("PT")
        bSTs = [B(f"STATC{k}") for k in range(4)]
        bSTm = [B(f"STATM{k}") for k in range(4)]
        bAK = B("AKCOLS")
        bST = B("STATC")

        def ts(eng, out, in0, s1, s2, op0, op1=None, reads=(), writes=()):
            if op1 is None:
                em.emit(eng, lambda e: e.tensor_scalar(out, in0, s1, None, op0=op0),
                        reads=reads, writes=writes)
            else:
                em.emit(eng, lambda e: e.tensor_scalar(out, in0, s1, s2, op0=op0, op1=op1),
                        reads=reads, writes=writes)

        def tt(eng, out, in0, in1, op, reads=(), writes=()):
            em.emit(eng, lambda e: e.tensor_tensor(out, in0, in1, op=op),
                    reads=reads, writes=writes)

        def stt(eng, out, in0, s, in1, op0, op1, reads=(), writes=()):
            em.emit(eng, lambda e: e.scalar_tensor_tensor(out, in0, s, in1, op0=op0,
                                                          op1=op1),
                    reads=reads, writes=writes)

        def act(eng, out, in0, func, bias=0.0, scale=1.0, accum=None, reads=(), writes=()):
            if accum is None:
                em.emit(eng, lambda e: e.activation(out, in0, func, bias=bias,
                                                    scale=scale),
                        reads=reads, writes=writes)
            else:
                em.emit(eng, lambda e: e.activation(out, in0, func, bias=bias,
                                                    scale=scale, accum_out=accum),
                        reads=reads, writes=writes)

        def cp(eng, out, in0, reads=(), writes=()):
            if eng == "scalar":
                em.emit(eng, lambda e: e.copy(out, in0), reads=reads, writes=writes)
            else:
                em.emit(eng, lambda e: e.tensor_copy(out, in0), reads=reads,
                        writes=writes)

        def red(eng, out, in0, op, absval=False, reads=(), writes=()):
            em.emit(eng, lambda e: e.tensor_reduce(out, in0, axis=AX.X, op=op,
                                                   apply_absolute_value=absval),
                    reads=reads, writes=writes)

        def mm(out, lhsT, rhs, start, stop, reads=(), writes=(), tp=None):
            if tp is None:
                em.emit("tensor", lambda e: e.matmul(out, lhsT, rhs, start=start,
                                                     stop=stop),
                        reads=reads, writes=writes)
            else:
                em.emit("tensor", lambda e: e.matmul(out, lhsT, rhs, start=start,
                                                     stop=stop, tile_position=tp),
                        reads=reads, writes=writes)

        def transp(out, in0, reads=(), writes=()):
            em.emit("tensor", lambda e: e.transpose(out, in0, IDN[:]),
                    reads=list(reads) + [B("IDN")], writes=writes)

        def bcast(queue, out_tile, src_t, src_off, n, reads=(), writes=()):
            src = bass.AP(src_t, src_off, [[0, 128], [1, n]])
            em.dma(queue, out_tile, src, reads=reads, writes=writes)

        def ln_bn(xa, width, rbuf, k):
            o = 32 * k
            bSk = bSTs[k]
            nparts = width // 512 if width > 512 else 1
            wper = width // nparts
            assert wper <= 512 and nparts <= 4
            for i in range(nparts):
                em.emit("vector", lambda e, i=i: e.bn_stats(
                    STATC[:, o + 6 * i:o + 6 * i + 6], xa[:, i * wper:(i + 1) * wper]),
                    reads=[rbuf], writes=[bSk])
            em.emit("vector", lambda e: e.bn_aggr(STATC[:, o + 12:o + 14],
                                                  STATC[:, o:o + 6 * nparts]),
                    reads=[bSk], writes=[bSk])

        def ln_sqrt(k):
            o = 32 * k
            act("scalar", STATC[:, o + 13:o + 14], STATC[:, o + 13:o + 14], AF.Sqrt,
                bias=EPSCOL[:, 0:1], reads=[bSTs[k], B("EPSCOL")], writes=[bSTs[k]])

        def ln_recip(k):
            o = 32 * k
            em.emit("vector", lambda e: e.reciprocal(STATC[:, o + 13:o + 14],
                                                     STATC[:, o + 13:o + 14]),
                    reads=[bSTs[k]], writes=[bSTs[k]])
            stt("vector", STATC[:, o + 14:o + 15], STATC[:, o + 12:o + 13], -1.0,
                STATC[:, o + 13:o + 14], ALU.mult, ALU.mult,
                reads=[bSTs[k]], writes=[bSTs[k]])

        def ln_apply(eng_out, in_ap, k, rbufs, wbufs):
            o = 32 * k
            act("scalar", eng_out, in_ap, AF.Identity, bias=STATC[:, o + 14:o + 15],
                scale=STATC[:, o + 13:o + 14], reads=list(rbufs) + [bSTs[k]],
                writes=wbufs)

        def ln_stats(xa, width, rbuf, k):
            ln_bn(xa, width, rbuf, k)
            ln_sqrt(k)
            ln_recip(k)

        em.emit("gpsimd", lambda e: e.iota(
            bass.AP(IDN, 0, [[128, 128], [1, 128]]), [[1, 128]], channel_multiplier=-1,
            allow_small_or_imprecise_dtypes=True), writes=[B("IDN")])
        ts("vector", IDN[:], IDN[:], 0.0, None, ALU.is_equal, reads=[B("IDN")],
           writes=[B("IDN")])
        em.emit("vector", lambda e: e.memset(ONESF[:], 1.0), writes=[B("ONESF")])
        em.emit("vector", lambda e: e.memset(ONESC[:], 1.0), writes=[B("ONESC")])
        em.emit("vector", lambda e: e.memset(EPSCOL[:], LN_EPS), writes=[B("EPSCOL")])
        em.emit("vector", lambda e: e.memset(ZCOL[:], 0.0), writes=[B("ZCOL")])
        em.emit("vector", lambda e: e.memset(IND64[:], 0.0), writes=[B("IND64")])
        em.emit("vector", lambda e: e.memset(IND64[64:65, 0:64], 1.0), writes=[B("IND64")])
        em.emit("vector", lambda e: e.memset(IND64[64:65, 192:256], 1.0), writes=[B("IND64")])
        em.emit("vector", lambda e: e.memset(IND8S[:], 0.0), writes=[B("IND8S")])
        for fc in range(QC):
            em.emit("vector", lambda e, fc=fc: e.memset(
                IND8S[0:64, fc * 8 + 2 * fc:fc * 8 + 2 * fc + 1], 1.0), writes=[B("IND8S")])
            em.emit("vector", lambda e, fc=fc: e.memset(
                IND8S[64:128, fc * 8 + 2 * fc + 1:fc * 8 + 2 * fc + 2], 1.0),
                writes=[B("IND8S")])

        for tile, tname, srcx, n in ((GREP, "GREP", ln_g, D), (BREP, "BREP", ln_b, D),
                                     (C2REP, "C2REP", n2w, HID),
                                     (BOREP, "BOREP", bout, D)):
            em.dma("gpsimd", tile[:], bass.AP(srcx.tensor, 0, [[0, 128], [1, n]]),
                   writes=[B(tname)])
        bcast("sync", C1REP[:], n1w.tensor, 0, D, writes=[B("C1REP")])
        bcast("sync", ABCOL[:, 0:2], actsc.tensor, 0, 2, writes=[B("ABCOL")])
        em.emit("vector", lambda e: e.reciprocal(C1IREP[:], C1REP[:]),
                reads=[B("C1REP")], writes=[B("C1IREP")])

        em.dma("gpsimd", bass.AP(BVC, 0, [[QC, 128], [1, QC]]),
               bass.AP(bv.tensor, 0, [[1, 128], [128, QC]]), writes=[B("BVC")],
               slow=True)
        em.dma("sync", bass.AP(BQS, 0, [[QC, 128], [1, QC]]),
               bass.AP(bqk2.tensor, 0, [[1, 128], [128, QC]]), writes=[B("BQS")],
               slow=True)
        em.dma("sync", bass.AP(BKS, 0, [[QC, 128], [1, QC]]),
               bass.AP(bqk2.tensor, QK, [[1, 128], [128, QC]]), writes=[B("BKS")],
               slow=True)
        ts("vector", BQS[:], BQS[:], 0.125, None, ALU.mult, reads=[B("BQS")],
           writes=[B("BQS")])

        em.dma("gpsimd",
               bass.AP(WQK, 0, [[DC * 2 * QK, 128], [2 * QK, DC], [1, 2 * QK]]),
               bass.AP(wqk.tensor, 0, [[2 * QK, 128], [128 * 2 * QK, DC], [1, 2 * QK]]),
               writes=[B("WQK")])
        em.dma("gpsimd",
               bass.AP(WV, 0, [[DC * QK, 128], [QK, DC], [1, QK]]),
               bass.AP(wv.tensor, 0, [[QK, 128], [128 * QK, DC], [1, QK]]),
               writes=[B("WV")])
        em.dma("gpsimd",
               bass.AP(WOUT, 0, [[QC * D, 128], [D, QC], [1, D]]),
               bass.AP(wout.tensor, 0, [[D, 128], [128 * D, QC], [1, D]]),
               writes=[B("WOUT")])

        arena = contextlib.ExitStack()
        BVTMP = arena.enter_context(nc.sbuf_tensor("BVTMP", [1, D], F32))
        for half in range(max(1, D // 512)):
            dw2 = min(512, D)
            for ci in range(QC):
                mm(P0[0:1, half * dw2:half * dw2 + dw2], BVC[:, ci:ci + 1],
                   WOUT[:, ci * D + half * dw2:ci * D + (half + 1) * dw2],
                   start=(ci == 0), stop=(ci == QC - 1),
                   reads=[B("BVC"), B("WOUT")], writes=[bP0])
        cp("vector", BVTMP[0:1, :], P0[0:1, 0:D], reads=[bP0], writes=[B("BVTMP")])
        for half in range(max(1, D // 512)):
            dw2 = min(512, D)
            mm(P0[:, half * dw2:half * dw2 + dw2], ONESF[0:1, :],
               BVTMP[0:1, half * dw2:half * dw2 + dw2],
               start=True, stop=True, reads=[B("BVTMP"), B("ONESF")], writes=[bP0])
            tt("vector", BOREP[:, half * dw2:half * dw2 + dw2],
               BOREP[:, half * dw2:half * dw2 + dw2],
               P0[:, half * dw2:half * dw2 + dw2], ALU.add,
               reads=[bP0], writes=[B("BOREP")])
        SILC = arena.enter_context(nc.sbuf_tensor("SILC", [128, DC * NSEQ], F32))
        MODT = arena.enter_context(nc.sbuf_tensor("MODT", [NSEQ, 6 * D], F32))
        WMS = arena.enter_context(nc.sbuf_tensor("WMS", [128, 2 * 512], F32))
        em.dma("sync", bass.AP(SILC, 0, [[DC * NSEQ, 128], [NSEQ, DC], [1, NSEQ]]),
               bass.AP(c_t.tensor, 0, [[NSEQ, 128], [128 * NSEQ, DC], [1, NSEQ]]),
               writes=[B("SILC")], slow=True)
        act("scalar", SILC[:], SILC[:], AF.Silu, reads=[B("SILC")], writes=[B("SILC")])
        for nt in range(6 * D // 512):
            for dc in range(DC):
                wslot = (nt * DC + dc) % 2
                bWM = B(f"WMS{wslot}")
                em.dma("sync", WMS[:, wslot * 512:wslot * 512 + 512],
                       bass.AP(wmod.tensor, dc * 128 * 6 * D + nt * 512,
                               [[6 * D, 128], [1, 512]]),
                       writes=[bWM])
                mm(P3[0:NSEQ, 0:512], SILC[:, dc * NSEQ:(dc + 1) * NSEQ],
                   WMS[:, wslot * 512:wslot * 512 + 512],
                   start=(dc == 0), stop=(dc == DC - 1),
                   reads=[B("SILC"), bWM], writes=[bP3])
            cp("vector", MODT[0:NSEQ, nt * 512:(nt + 1) * 512], P3[0:NSEQ, 0:512],
               reads=[bP3], writes=[B("MODT")])
        em.dma("scalar", bass.AP(mod_scr.tensor, 0, [[6 * D, NSEQ], [1, 6 * D]]),
               MODT[0:NSEQ, :], reads=[B("MODT")], writes=[B("mod_scr")])

        WQT = arena.enter_context(nc.sbuf_tensor("WQT", [128, HID], F32))
        WQA = arena.enter_context(nc.sbuf_tensor("WQA", [128, HID], F32))
        WQB = arena.enter_context(nc.sbuf_tensor("WQB", [128, HID], BF16))
        WQI = arena.enter_context(nc.sbuf_tensor("WQI", [128, HID], I8))
        WQC = arena.enter_context(nc.sbuf_tensor("WQC", [128, 4], F32))

        def wquant(w_ap, rows, cols, scr_t, idx):
            nch = rows // 128
            bW, bA, bC = B("WQT"), B("WQAa"), B("WQC")
            bI, bBf, bScr = B("WQI"), B("WQB"), B(f"qS{idx}")
            for ch in range(nch):
                em.dma("sync", WQT[:, 0:cols],
                       bass.AP(w_ap.tensor, ch * 128 * cols, [[cols, 128], [1, cols]]),
                       writes=[bW])
                red("vector", WQC[:, 1:2], WQT[:, 0:cols], ALU.add, absval=True,
                    reads=[bW], writes=[bC])
                if ch == 0:
                    cp("vector", WQC[:, 0:1], WQC[:, 1:2], reads=[bC], writes=[bA])
                else:
                    tt("vector", WQC[:, 0:1], WQC[:, 0:1], WQC[:, 1:2], ALU.add,
                       reads=[bC], writes=[bA])
            mm(P3[0:1, 0:1], WQC[:, 0:1], ONESC[:, 0:1], start=True, stop=True,
               reads=[bA, B("ONESC")], writes=[bP3])
            cp("vector", WQC[0:1, 2:3], P3[0:1, 0:1], reads=[bP3], writes=[bC])
            ts("vector", WQC[0:1, 2:3], WQC[0:1, 2:3], 1.0 / (rows * cols), EPS_Q,
               ALU.mult, ALU.max, reads=[bC], writes=[bC])
            mm(P3[:, 0:1], ONESF[0:1, :], WQC[0:1, 2:3], start=True, stop=True,
               reads=[bC, B("ONESF")], writes=[bP3])
            cp("vector", SWCOL[:, idx:idx + 1], P3[:, 0:1], reads=[bP3],
               writes=[B("SWCOL")])
            em.emit("vector", lambda e: e.reciprocal(WQC[:, 3:4], SWCOL[:, idx:idx + 1]),
                    reads=[B("SWCOL")], writes=[bC])
            for ch in range(nch):
                em.dma("sync", WQT[:, 0:cols],
                       bass.AP(w_ap.tensor, ch * 128 * cols, [[cols, 128], [1, cols]]),
                       writes=[bW])
                act("scalar", WQA[:, 0:cols], WQT[:, 0:cols], AF.Identity,
                    bias=ZCOL[:, 0:1], scale=WQC[:, 3:4],
                    reads=[bW, bC, B("ZCOL")], writes=[B("WQAa")])
                ts("vector", WQT[:, 0:cols], WQA[:, 0:cols], 0.5, None, ALU.is_gt,
                   reads=[B("WQAa")], writes=[bW])
                stt("vector", WQB[:, 0:cols], WQA[:, 0:cols], -0.5, WQT[:, 0:cols],
                    ALU.is_lt, ALU.subtract, reads=[B("WQAa"), bW], writes=[bBf])
                em.dma("scalar",
                       bass.AP(scr_t, ch * 128 * cols, [[cols, 128], [1, cols]]),
                       WQB[:, 0:cols], reads=[bBf], writes=[bScr])

        wquant(w1t, D, HID, w1q_scr.tensor, 0)
        wquant(w2t, HID, D, w2q_scr.tensor, 1)
        em.barrier()
        arena.close()

        for s in range(NSEQ):
            ar = contextlib.ExitStack()
            HBF = ar.enter_context(nc.sbuf_tensor(f"HBF{s}", [128, DC * S], BF16))
            KBF = ar.enter_context(nc.sbuf_tensor(f"KBF{s}", [128, QC * S], BF16))
            QBF = ar.enter_context(nc.sbuf_tensor(f"QBF{s}", [128, QC * S], BF16))
            VTM = ar.enter_context(nc.sbuf_tensor(f"VTM{s}", [128, KC * VW], BF16))
            XT = ar.enter_context(nc.sbuf_tensor(f"XT{s}", [128, 4 * D], F32))
            PBF = ar.enter_context(nc.sbuf_tensor(f"PBF{s}", [128, 2 * 1024], BF16))
            OBF = ar.enter_context(nc.sbuf_tensor(f"OBF{s}", [128, QC * c.AG], BF16))
            OST = ar.enter_context(nc.sbuf_tensor(f"OST{s}", [128, c.NP * c.AG], BF16))
            ORW = ar.enter_context(nc.sbuf_tensor(f"ORW{s}", [128, 2 * c.AG], BF16))
            RRC = ar.enter_context(nc.sbuf_tensor(f"RRC{s}", [128, 512], BF16))
            SMR = ar.enter_context(nc.sbuf_tensor(f"SMR{s}", [128, 2048], F32))
            EP1 = ar.enter_context(nc.sbuf_tensor(f"EP1{s}", [128, (c.AG // 128) * D], BF16))
            EP2 = ar.enter_context(nc.sbuf_tensor(f"EP2{s}", [128, (c.AG // 128) * D], BF16))
            CA1 = ar.enter_context(nc.sbuf_tensor(f"CA1{s}", [128, D], F32))
            SAR = ar.enter_context(nc.sbuf_tensor(f"SAR{s}", [128, D], BF16))
            GAR = ar.enter_context(nc.sbuf_tensor(f"GAR{s}", [128, D], BF16))
            GSC = ar.enter_context(nc.sbuf_tensor(f"GSC{s}", [128, D], F32))

            bHBF, bKBF, bQBF, bVTM = B(f"HBF{s}"), B(f"KBF{s}"), B(f"QBF{s}"), B(f"VTM{s}")
            bOBF, bOST, bORW = B(f"OBF{s}"), B(f"OST{s}"), B(f"ORW{s}")
            bXTs = [B(f"XT{s}_{i}") for i in range(4)]
            bPBF = [B(f"PBF{s}_0"), B(f"PBF{s}_1")]
            bRRC, bSMR = B(f"RRC{s}"), B(f"SMR{s}")
            bEP1, bEP2 = B(f"EP1{s}"), B(f"EP2{s}")
            bCA1, bSAR, bGAR, bGSC = B(f"CA1{s}"), B(f"SAR{s}"), B(f"GAR{s}"), B(f"GSC{s}")
            bNNB, bXMS = B("NNB"), B("xmid_scr")

            def xchunk(gbase_tok, tc, nsl=4):
                sl = tc % nsl
                em.dma("sync", XT[:, sl * D:(sl + 1) * D],
                       bass.AP(x_in.tensor, (gbase_tok + tc * 128) * D,
                               [[D, 128], [1, D]]),
                       writes=[bXTs[sl]])
                return XT[:, sl * D:(sl + 1) * D], bXTs[sl]

            bcast("sync", CA1[:], mod_scr.tensor, (s * 6 + 1) * D, D,
                  reads=[B("mod_scr")], writes=[bCA1])
            ts("vector", CA1[:], CA1[:], 1.0, None, ALU.add, reads=[bCA1], writes=[bCA1])
            em.dma("gpsimd", SAR[:],
                   bass.AP(mod_scr.tensor, (s * 6 + 0) * D, [[0, 128], [1, D]]),
                   reads=[B("mod_scr")], writes=[bSAR])
            bcast("sync", GSC[:], mod_scr.tensor, (s * 6 + 2) * D, D,
                  reads=[B("mod_scr")], writes=[bGSC])
            ts("vector", GAR[:], GSC[:], 0.0, 2.0, ALU.is_ge, ALU.mult, reads=[bGSC],
               writes=[bGAR])
            ts("vector", GAR[:], GAR[:], -1.0, None, ALU.add, reads=[bGAR], writes=[bGAR])
            act("scalar", GSC[:], GSC[:], AF.Abs, reads=[bGSC], writes=[bGSC])
            ts("vector", GSC[:], GSC[:], 0.25, None, ALU.max, reads=[bGSC],
               writes=[bGSC])
            tt("vector", GSC[:], GSC[:], GAR[:], ALU.mult, reads=[bGAR], writes=[bGSC])
            ts("vector", GAR[:], GSC[:], RSQRT2, None, ALU.mult, reads=[bGSC],
               writes=[bGAR])

            em.emit("vector", lambda e: e.memset(
                bass.AP(VTM, 64, [[KC * VW, 128], [VW, KC], [65, H], [1, 1]]), 1.0),
                writes=[bVTM])

            NCH = c.AG // 128
            for g in range(c.NG):
                gbase = s * S + g * c.AG
                xas = []
                for tc in range(NCH):
                    xa, bxa = xchunk(gbase, tc)
                    xas.append((xa, bxa))
                    k4 = tc % 4
                    o = 32 * k4
                    ln_bn(xa, D, bxa, k4)
                for tc in range(NCH):
                    ln_sqrt(tc % 4)
                for tc in range(NCH):
                    ln_recip(tc % 4)
                for tc in range(NCH):
                    k4 = tc % 4
                    o = 32 * k4
                    xa, bxa = xas[tc]
                    bEP1c = B(f"EP1{s}_{tc}")
                    bEP2c = B(f"EP2{s}_{tc}")
                    ha = EP2[:, tc * D:(tc + 1) * D]
                    ln_apply(ha, xa, k4, [bxa], [bEP2c])
                    ea = EP1[:, tc * D:(tc + 1) * D]
                    tt("vector", ea, ha, CA1[:], ALU.mult, reads=[bEP2c, bCA1],
                       writes=[bEP1c])
                    tt("vector", ha, ea, SAR[:], ALU.add, reads=[bEP1c, bSAR],
                       writes=[bEP2c])
                for tc in range(NCH):
                    bEP2c = B(f"EP2{s}_{tc}")
                    ha = EP2[:, tc * D:(tc + 1) * D]
                    for j in range(DC):
                        transp(PT[:, (j % 8) * 128:(j % 8) * 128 + 128],
                               ha[:, j * 128:(j + 1) * 128],
                               reads=[bEP2c], writes=[bPT])
                    g4 = min(4, DC)
                    for half in range(DC // g4):
                        cp("scalar",
                           bass.AP(HBF, half * g4 * S + g * c.AG + tc * 128,
                                   [[DC * S, 128], [S, g4], [1, 128]]),
                           bass.AP(PT, half * g4 * 128, [[1024, 128], [128, g4], [1, 128]]),
                           reads=[bPT], writes=[bHBF])
                for fc2 in range(0, QC, 2):
                    for dc in range(DC):
                        for pi in range(2):
                            fc = fc2 + pi
                            pk, bpk = (P0, bP0) if pi == 0 else (P1, bP1)
                            mm(pk[:, 0:c.AG],
                               WQK[:, dc * 2 * QK + QK + fc * 128:
                                   dc * 2 * QK + QK + (fc + 1) * 128],
                               HBF[:, dc * S + g * c.AG:dc * S + (g + 1) * c.AG],
                               start=(dc == 0), stop=(dc == DC - 1),
                               reads=[B("WQK"), bHBF], writes=[bpk])
                    for pi in range(2):
                        fc = fc2 + pi
                        pk, bpk = (P0, bP0) if pi == 0 else (P1, bP1)
                        act("scalar", KBF[:, fc * S + g * c.AG:fc * S + (g + 1) * c.AG],
                            pk[:, 0:c.AG], AF.Identity, bias=BKS[:, fc:fc + 1],
                            reads=[bpk, B("BKS")], writes=[bKBF])
                for tc2 in range(0, NCH, 2):
                    npi = 2 if tc2 + 1 < NCH else 1
                    for dc in range(DC):
                        for pi in range(npi):
                            tc = tc2 + pi
                            pv, bpv = (P0, bP0) if pi == 0 else (P1, bP1)
                            mm(pv[:, 0:QK],
                               HBF[:, dc * S + g * c.AG + tc * 128:
                                   dc * S + g * c.AG + (tc + 1) * 128],
                               WV[:, dc * QK:(dc + 1) * QK],
                               start=(dc == 0), stop=(dc == DC - 1),
                               reads=[bHBF, B("WV")], writes=[bpv])
                    for pi in range(npi):
                        tc = tc2 + pi
                        pv, bpv = (P0, bP0) if pi == 0 else (P1, bP1)
                        kc = g * NCH + tc
                        cp("scalar",
                           bass.AP(VTM, kc * VW, [[KC * VW, 128], [65, H], [1, 64]]),
                           bass.AP(pv, 0, [[1024, 128], [64, H], [1, 64]]),
                           reads=[bpv], writes=[bVTM])

            for g in range(c.NG):
                for fc2 in range(0, QC, 2):
                    for dc in range(DC):
                        for pi in range(2):
                            fc = fc2 + pi
                            pq, bpq = (P0, bP0) if pi == 0 else (P1, bP1)
                            mm(pq[:, 0:c.AG],
                               WQK[:, dc * 2 * QK + fc * 128:dc * 2 * QK + (fc + 1) * 128],
                               HBF[:, dc * S + g * c.AG:dc * S + (g + 1) * c.AG],
                               start=(dc == 0), stop=(dc == DC - 1),
                               reads=[B("WQK"), bHBF], writes=[bpq])
                    for pi in range(2):
                        fc = fc2 + pi
                        pq, bpq = (P0, bP0) if pi == 0 else (P1, bP1)
                        act("scalar", QBF[:, fc * S + g * c.AG:fc * S + (g + 1) * c.AG],
                            pq[:, 0:c.AG], AF.Identity, bias=BQS[:, fc:fc + 1],
                            scale=0.125, reads=[bpq, B("BQS")], writes=[bQBF])
            for qsrc, bsrc, coli in ((QBF, bQBF, 22), (KBF, bKBF, 24)):
                first = True
                for fc in range(QC):
                    act("scalar", EP1[:, 0:S], qsrc[:, fc * S:(fc + 1) * S], AF.Square,
                        reads=[bsrc], writes=[B(f"EP1{s}_0"), B(f"EP1{s}_1")])
                    for ct in range((S + 511) // 512):
                        ctw = min(512, S - ct * 512)
                        mm(P3[0:8, 0:ctw], IND8S[:, fc * 8:(fc + 1) * 8],
                           EP1[:, ct * 512:ct * 512 + ctw],
                           start=True, stop=True,
                           reads=[B(f"EP1{s}_0"), B(f"EP1{s}_1"), B("IND8S")],
                           writes=[bP3])
                        red("vector", STATC[0:8, coli + 1:coli + 2], P3[0:8, 0:ctw],
                            ALU.max, reads=[bP3], writes=[bSTs[0]])
                        if first:
                            cp("vector", STATC[0:8, coli:coli + 1],
                               STATC[0:8, coli + 1:coli + 2], reads=[bSTs[0]],
                               writes=[bSTs[0]])
                            first = False
                        else:
                            tt("vector", STATC[0:8, coli:coli + 1],
                               STATC[0:8, coli:coli + 1], STATC[0:8, coli + 1:coli + 2],
                               ALU.max, reads=[bSTs[0]], writes=[bSTs[0]])
            tt("vector", STATC[0:8, 26:27], STATC[0:8, 22:23], STATC[0:8, 24:25],
               ALU.mult, reads=[bSTs[0]], writes=[bSTs[0]])
            em.dma("scalar", bass.AP(tiny_scr.tensor, 0, [[1, 8], [1, 1]]),
                   STATC[0:8, 26:27], reads=[bSTs[0]], writes=[B("tiny")], slow=True)
            em.dma("scalar", SMR[0:1, 0:8],
                   bass.AP(tiny_scr.tensor, 0, [[8, 1], [1, 8]]),
                   reads=[B("tiny")], writes=[bSMR], slow=True)
            act("scalar", SMR[0:1, 0:8], SMR[0:1, 0:8], AF.Sqrt, reads=[bSMR],
                writes=[bSMR])
            tt("vector", SMR[0:1, 8:8 + c.NP], bass.AP(SMR, 0, [[2048, 1], [2, c.NP]]),
               bass.AP(SMR, 1, [[2048, 1], [2, c.NP]]), ALU.max, reads=[bSMR],
               writes=[bSMR])
            ts("vector", SMR[0:1, 8:8 + c.NP], SMR[0:1, 8:8 + c.NP], -1.0, None,
               ALU.mult, reads=[bSMR], writes=[bSMR])
            mm(P3[:, 0:c.NP], ONESF[0:1, :], SMR[0:1, 8:8 + c.NP], start=True,
               stop=True, reads=[bSMR, B("ONESF")], writes=[bP3])
            cp("vector", NNB[:, 0:c.NP], P3[:, 0:c.NP], reads=[bP3], writes=[bNNB])

            for g in range(c.NG):
                gbase = s * S + g * c.AG
                pend = []

                def _emit_norm(g, p, slot):
                    orw = ORW[:, slot * c.AG:(slot + 1) * c.AG]
                    em.emit("vector", lambda e: e.reciprocal(
                        SMR[64:65, slot * 1024:slot * 1024 + 2 * c.AG],
                        SMR[64:65, slot * 1024:slot * 1024 + 2 * c.AG]),
                        reads=[bSMR], writes=[bSMR])
                    mm(P2A[:, 0:c.AG], IND64[64:65, 0:128],
                       SMR[64:65, slot * 1024:slot * 1024 + c.AG],
                       start=True, stop=False,
                       reads=[bSMR, B("IND64")], writes=[bP2])
                    mm(P2A[:, 0:c.AG], IND64[64:65, 128:256],
                       SMR[64:65, slot * 1024 + c.AG:slot * 1024 + 2 * c.AG],
                       start=False, stop=True,
                       reads=[bSMR, B("IND64")], writes=[bP2])
                    cp("vector", RRC[0:64, 0:c.AG], P2A[0:64, 0:c.AG], reads=[bP2],
                       writes=[bRRC])
                    tt("vector", OBF[0:64, p * c.AG:(p + 1) * c.AG],
                       orw[0:64, :], RRC[0:64, 0:c.AG], ALU.mult,
                       reads=[bORW, bRRC], writes=[bOBF])
                    tt("vector", OST[0:64, p * c.AG:(p + 1) * c.AG],
                       OST[0:64, p * c.AG:(p + 1) * c.AG],
                       P2A[64:128, 0:c.AG], ALU.mult,
                       reads=[bP2], writes=[bOST])

                for p in range(c.NP):
                    hA, hB = 2 * p, 2 * p + 1
                    for j in range(KC):
                        psc, bsc = (P0, bP0) if j % 2 == 0 else (P1, bP1)
                        pbh = j % 2
                        mm(psc[:, 0:c.AG],
                           KBF[0:64, p * S + j * 128:p * S + (j + 1) * 128],
                           QBF[0:64, p * S + g * c.AG:p * S + (g + 1) * c.AG],
                           start=True, stop=True, tp=(0, 0),
                           reads=[bKBF, bQBF], writes=[bsc])
                        mm(psc[:, 512:512 + c.AG],
                           KBF[64:128, p * S + j * 128:p * S + (j + 1) * 128],
                           QBF[64:128, p * S + g * c.AG:p * S + (g + 1) * c.AG],
                           start=True, stop=True, tp=(64, 0),
                           reads=[bKBF, bQBF], writes=[bsc])
                        if j == 2 and pend:
                            pp, ss = pend.pop(0)
                            _emit_norm(g, pp, ss)
                        if c.AG == 512:
                            act("scalar", PBF[:, pbh * 1024:pbh * 1024 + 1024],
                                psc[:, 0:1024], AF.Exp, bias=NNB[:, p:p + 1],
                                reads=[bsc, bNNB], writes=[bPBF[pbh]])
                        else:
                            act("scalar", PBF[:, pbh * 1024:pbh * 1024 + c.AG],
                                psc[:, 0:c.AG], AF.Exp, bias=NNB[:, p:p + 1],
                                reads=[bsc, bNNB], writes=[bPBF[pbh]])
                            act("scalar",
                                PBF[:, pbh * 1024 + 512:pbh * 1024 + 512 + c.AG],
                                psc[:, 512:512 + c.AG], AF.Exp, bias=NNB[:, p:p + 1],
                                reads=[bsc, bNNB], writes=[bPBF[pbh]])
                        if j >= 1:
                            jm = j - 1
                            pbm = jm % 2
                            mm(P3[0:65, 0:c.AG],
                               VTM[:, jm * VW + 65 * hA:jm * VW + 65 * hA + 65],
                               PBF[:, pbm * 1024:pbm * 1024 + c.AG],
                               start=(jm == 0), stop=False,
                               reads=[bVTM, bPBF[pbm]], writes=[bP3])
                            mm(P4[0:65, 0:c.AG],
                               VTM[:, jm * VW + 65 * hB:jm * VW + 65 * hB + 65],
                               PBF[:, pbm * 1024 + 512:pbm * 1024 + 512 + c.AG],
                               start=(jm == 0), stop=False,
                               reads=[bVTM, bPBF[pbm]], writes=[bP4])
                    jm = KC - 1
                    pbm = jm % 2
                    mm(P3[0:65, 0:c.AG],
                       VTM[:, jm * VW + 65 * hA:jm * VW + 65 * hA + 65],
                       PBF[:, pbm * 1024:pbm * 1024 + c.AG],
                       start=False, stop=True,
                       reads=[bVTM, bPBF[pbm]], writes=[bP3])
                    mm(P4[0:65, 0:c.AG],
                       VTM[:, jm * VW + 65 * hB:jm * VW + 65 * hB + 65],
                       PBF[:, pbm * 1024 + 512:pbm * 1024 + 512 + c.AG],
                       start=False, stop=True,
                       reads=[bVTM, bPBF[pbm]], writes=[bP4])
                    slot = p % 2
                    orw = ORW[:, slot * c.AG:(slot + 1) * c.AG]
                    cp("scalar", orw[0:65, :], P3[0:65, 0:c.AG], reads=[bP3],
                       writes=[bORW])
                    cp("scalar", SMR[64:65, slot * 1024:slot * 1024 + c.AG],
                       P3[64:65, 0:c.AG], reads=[bP3], writes=[bSMR])
                    cp("scalar", OST[0:65, p * c.AG:(p + 1) * c.AG],
                       P4[0:65, 0:c.AG], reads=[bP4], writes=[bOST])
                    cp("scalar", SMR[64:65, slot * 1024 + c.AG:slot * 1024 + 2 * c.AG],
                       P4[64:65, 0:c.AG], reads=[bP4], writes=[bSMR])
                    pend.append((p, slot))
                while pend:
                    pp, ss = pend.pop(0)
                    _emit_norm(g, pp, ss)
                em.dma("scalar",
                       bass.AP(OBF, 64 * QC * c.AG,
                               [[QC * c.AG, 64], [c.AG, c.NP], [1, c.AG]]),
                       bass.AP(OST, 0, [[c.NP * c.AG, 64], [c.AG, c.NP], [1, c.AG]]),
                       reads=[bOST], writes=[bOBF])
                if c.DBG and s == 0:
                    em.dma("gpsimd",
                           bass.AP(dbg_scr.tensor, g * 128 * QC * c.AG,
                                   [[QC * c.AG, 128], [1, QC * c.AG]]),
                           OBF[:, :], reads=[bOBF], writes=[B("dbg")])
                for tc in range(NCH):
                    py, bpy = (P0, bP0) if tc % 2 == 0 else (P1, bP1)
                    nh = max(1, D // 512)
                    dw = D // nh
                    for ci in range(QC):
                        for half in range(nh):
                            mm(py[:, half * dw:(half + 1) * dw],
                               OBF[:, ci * c.AG + tc * 128:ci * c.AG + (tc + 1) * 128],
                               WOUT[:, ci * D + half * dw:ci * D + (half + 1) * dw],
                               start=(ci == 0), stop=(ci == QC - 1),
                               reads=[bOBF, B("WOUT")], writes=[bpy])
                    k4 = tc % 4
                    o = 32 * k4
                    bSk = bSTs[k4]
                    bEP1c = B(f"EP1{s}_{tc}")
                    bEP2c = B(f"EP2{s}_{tc}")
                    e1 = EP1[:, tc * D:(tc + 1) * D]
                    e2 = EP2[:, tc * D:(tc + 1) * D]
                    xa, bxa = xchunk(gbase, tc)
                    tt("vector", e1, py[:, 0:D], BOREP[:], ALU.add,
                       reads=[bpy, B("BOREP")], writes=[bEP1c])
                    if c.DBG and s == 0 and g == 0:
                        em.dma("gpsimd",
                               bass.AP(dbg2_scr.tensor, tc * 128 * D, [[D, 128], [1, D]]),
                               e1, reads=[bEP1c], writes=[B("dbg2")])
                    ln_bn(e1, D, bEP1c, k4)
                    ln_sqrt(k4)
                    ln_recip(k4)
                    ln_apply(e2, e1, k4, [bEP1c], [bEP2c])
                    tt("vector", e1, e2, GREP[:], ALU.mult, reads=[bEP2c, B("GREP")],
                       writes=[bEP1c])
                    tt("vector", e2, e1, BREP[:], ALU.add, reads=[bEP1c, B("BREP")],
                       writes=[bEP2c])
                    for j in range(DC):
                        transp(PT[:, (j % 8) * 128:(j % 8) * 128 + 128],
                               HBF[:, j * S + g * c.AG + tc * 128:
                                   j * S + g * c.AG + (tc + 1) * 128],
                               reads=[bHBF], writes=[bPT])
                    tt("vector", e1, e2, PT[:, 0:D], ALU.add, reads=[bEP2c, bPT],
                       writes=[bEP1c])
                    tt("vector", e2, e1, GAR[:], ALU.mult, reads=[bEP1c, bGAR],
                       writes=[bEP2c])
                    tt("gpsimd", xa, xa, e2, ALU.add, reads=[bEP2c], writes=[bxa])
                    em.dma("scalar",
                           bass.AP(xmid_scr.tensor, (gbase + tc * 128) * D,
                                   [[D, 128], [1, D]]),
                           xa, reads=[bxa], writes=[bXMS])

            em.barrier()
            ar.close()

            br = contextlib.ExitStack()
            W1S = br.enter_context(nc.sbuf_tensor(f"W1S{s}", [128, DC * HID], BF16))
            CB1C1 = br.enter_context(nc.sbuf_tensor(f"CB1{s}", [128, D], BF16))
            SBC1 = br.enter_context(nc.sbuf_tensor(f"SBC1{s}", [128, D], BF16))
            ZC = br.enter_context(nc.sbuf_tensor(f"ZC{s}", [128, (c.T1 // 128) * D], F32))
            DS1 = br.enter_context(nc.sbuf_tensor(f"DS1{s}", [128, D], F32))
            DS2 = br.enter_context(nc.sbuf_tensor(f"DS2{s}", [128, D], BF16))
            DI8 = br.enter_context(nc.sbuf_tensor(f"DI8{s}", [128, D], I8))
            XQ1 = br.enter_context(nc.sbuf_tensor(f"XQ1{s}", [128, 3 * DC * c.T1], BF16))
            XMB = br.enter_context(nc.sbuf_tensor(f"XMB{s}", [128, 2 * (c.T1 // 128) * D], F32))
            HA = br.enter_context(nc.sbuf_tensor(f"HA{s}", [128, (c.T1 // 128) * HID], BF16))
            HBt = br.enter_context(nc.sbuf_tensor(f"HBt{s}", [128, (c.T1 // 128) * HID], BF16))
            HI8 = br.enter_context(nc.sbuf_tensor(f"HI8{s}", [128, HID], I8))
            XQ2S = br.enter_context(nc.sbuf_tensor(f"XQ2S{s}", [128, 2048], BF16))

            bW1, bCB, bSB = B(f"W1S{s}"), B(f"CB1{s}"), B(f"SBC1{s}")
            bXQ1, bXMB = B(f"XQ1{s}"), B(f"XMB{s}")
            bXQscr, bS2 = B("xq2_scr"), B("S2COL")

            bcast("gpsimd", CB1C1[:], mod_scr.tensor, (s * 6 + 4) * D, D,
                  reads=[B("mod_scr")], writes=[bCB])
            ts("vector", CB1C1[:], CB1C1[:], 1.0, None, ALU.add, reads=[bCB],
               writes=[bCB])
            tt("vector", CB1C1[:], CB1C1[:], C1REP[:], ALU.mult, reads=[B("C1REP")],
               writes=[bCB])
            bcast("sync", DS1[:, 0:D], mod_scr.tensor, (s * 6 + 3) * D, D,
                  reads=[B("mod_scr")], writes=[B(f"DS1{s}_0")])
            tt("vector", SBC1[:], DS1[:, 0:D], C1REP[:], ALU.mult,
               reads=[B(f"DS1{s}_0"), B("C1REP")], writes=[bSB])
            for hh, q_ in ((0, "sync"), (1, "gpsimd")):
                em.dma(q_,
                       bass.AP(W1S, hh * (DC // 2) * HID,
                               [[DC * HID, 128], [HID, DC // 2], [1, HID]]),
                       bass.AP(w1q_scr.tensor, hh * (DC // 2) * 128 * HID,
                               [[HID, 128], [128 * HID, DC // 2], [1, HID]]),
                       reads=[B("qS0")], writes=[bW1])

            def xq2_finish(t1, tc):
                bHBc = B(f"HBt{s}_{tc}")
                ua = HBt[:, tc * HID:(tc + 1) * HID]
                g8 = min(8, HC)
                for quad in range(HC // g8):
                    for j in range(g8):
                        transp(PT[:, j * 128:(j + 1) * 128],
                               ua[:, (quad * g8 + j) * 128:(quad * g8 + j + 1) * 128],
                               reads=[bHBc], writes=[bPT])
                    qs = quad % 2
                    bXQ2c = B(f"XQ2S{s}_{qs}")
                    cp("scalar",
                       bass.AP(XQ2S, qs * 1024, [[2048, 128], [128, g8], [1, 128]]),
                       bass.AP(PT, 0, [[1024, 128], [128, g8], [1, 128]]),
                       reads=[bPT], writes=[bXQ2c])
                    em.dma("scalar",
                           bass.AP(xq2_scr.tensor,
                                   (quad * g8 * 128) * S + t1 * c.T1 + tc * 128,
                                   [[S, 128], [128 * S, g8], [1, 128]]),
                           bass.AP(XQ2S, qs * 1024, [[2048, 128], [128, g8], [1, 128]]),
                           reads=[bXQ2c], writes=[bXQscr])

            NC1 = c.T1 // 128

            def b1_dstage(t1):
                ts_ = t1 % 2
                sl = t1 % 3
                tb = s * S + t1 * c.T1
                em.dma("sync",
                       bass.AP(XMB, ts_ * NC1 * D,
                               [[2 * NC1 * D, 128], [D, NC1], [1, D]]),
                       bass.AP(xmid_scr.tensor, tb * D,
                               [[D, 128], [128 * D, NC1], [1, D]]),
                       reads=[bXMS], writes=[B(f"XMB{s}_{ts_}")])
                for tc in range(NC1):
                    k4 = (t1 * NC1 + tc) % 4
                    ln_bn(XMB[:, (ts_ * NC1 + tc) * D:(ts_ * NC1 + tc + 1) * D], D,
                          B(f"XMB{s}_{ts_}"), k4)
                for tc in range(NC1):
                    ln_sqrt((t1 * NC1 + tc) % 4)
                for tc in range(NC1):
                    ln_recip((t1 * NC1 + tc) % 4)
                for tc in range(NC1):
                    k4 = (t1 * NC1 + tc) % 4
                    o = 32 * k4
                    bSk = bSTs[k4]
                    tcol = t1 * NC1 + tc
                    bZCc = B(f"ZC{s}_{tc}")
                    bD1c, bD2c = B(f"DS1{s}"), B(f"DS2{s}")
                    bDIc = B(f"DI8{s}")
                    xa = XMB[:, (ts_ * NC1 + tc) * D:(ts_ * NC1 + tc + 1) * D]
                    za = ZC[:, tc * D:(tc + 1) * D]
                    d1 = DS1[:, 0:D]
                    d2 = DS2[:, 0:D]
                    di = DI8[:, 0:D]
                    xqt = d2
                    ak = 22 + (tcol // 4) % 2
                    ln_apply(za, xa, k4, [B(f"XMB{s}_{ts_}")], [bZCc])
                    tt("vector", za, za, CB1C1[:], ALU.mult, reads=[bCB], writes=[bZCc])
                    tt("vector", za, za, SBC1[:], ALU.add, reads=[bSB], writes=[bZCc])
                    tt("vector", d1, za, C1IREP[:], ALU.mult,
                       reads=[bZCc, B("C1IREP")], writes=[bD1c])
                    act("scalar", d2, d1, AF.Square, accum=STATC[:, o + 15:o + 16],
                        reads=[bD1c], writes=[bD2c, bSk])
                    ts("vector", STATC[:, o + 15:o + 16], STATC[:, o + 15:o + 16],
                       1.0 / D, EPS_NORM, ALU.mult, ALU.add, reads=[bSk], writes=[bSk])
                    em.emit("vector", lambda e, o=o: e.reciprocal(
                        STATC[:, o + 16:o + 17], STATC[:, o + 15:o + 16]),
                        reads=[bSk], writes=[bSk])
                    act("scalar", STATC[:, o + 16:o + 17], STATC[:, o + 16:o + 17],
                        AF.Sqrt, reads=[bSk], writes=[bSk])
                    red("vector", STATC[:, o + 17:o + 18], za, ALU.max, absval=True,
                        reads=[bZCc], writes=[bSk])
                    tt("vector", STATC[:, o + 17:o + 18], STATC[:, o + 17:o + 18],
                       STATC[:, o + 16:o + 17], ALU.mult, reads=[bSk], writes=[bSk])
                    ts("vector", STATC[:, o + 17:o + 18], STATC[:, o + 17:o + 18],
                       EPS_Q, None, ALU.max, reads=[bSk], writes=[bSk])
                    em.emit("vector", lambda e, o=o: e.reciprocal(
                        STATC[:, o + 18:o + 19], STATC[:, o + 17:o + 18]),
                        reads=[bSk], writes=[bSk])
                    tt("vector", STATC[:, o + 19:o + 20], STATC[:, o + 18:o + 19],
                       STATC[:, o + 16:o + 17], ALU.mult, reads=[bSk], writes=[bSk])
                    ts("vector", STATC[:, o + 19:o + 20], STATC[:, o + 19:o + 20],
                       127.0, None, ALU.mult, reads=[bSk], writes=[bSk])
                    act("scalar", za, za, AF.Identity, bias=ZCOL[:, 0:1],
                        scale=STATC[:, o + 19:o + 20],
                        reads=[bSk, bZCc, B("ZCOL")], writes=[bZCc])
                    ts("vector", STATC[:, o + 20:o + 21], STATC[:, o + 17:o + 18],
                       1.0 / 127.0, None, ALU.mult, reads=[bSk], writes=[bSk])
                    tt("vector", STATC[:, o + 20:o + 21], STATC[:, o + 20:o + 21],
                       SWCOL[:, 0:1], ALU.mult, reads=[B("SWCOL")], writes=[bSk])
                    tt("vector", STATC[:, o + ak:o + ak + 1], STATC[:, o + 20:o + 21],
                       STATC[:, o + 20:o + 21], ALU.mult, reads=[bSk], writes=[bAK])
                    tt("vector", STATC[:, o + ak:o + ak + 1], STATC[:, o + ak:o + ak + 1],
                       ABCOL[:, 0:1], ALU.mult, reads=[B("ABCOL")], writes=[bAK])
                    cp("scalar", di, za, reads=[bZCc], writes=[bDIc])
                    tt("vector", d1, za, di, ALU.subtract, reads=[bZCc, bDIc],
                       writes=[bD1c])
                    stt("vector", d2, d1, 0.5, di, ALU.is_ge, ALU.add,
                        reads=[bD1c, bDIc], writes=[bD2c])
                    stt("vector", xqt, d1, -0.5, d2, ALU.is_lt, ALU.subtract,
                        reads=[bD1c, bD2c], writes=[bD2c])
                    for j in range(DC):
                        transp(PT[:, (j % 8) * 128:(j % 8) * 128 + 128],
                               xqt[:, j * 128:(j + 1) * 128],
                               reads=[bD2c], writes=[bPT])
                    g4 = min(4, DC)
                    for half in range(DC // g4):
                        cp("scalar",
                           bass.AP(XQ1, sl * DC * c.T1 + half * g4 * c.T1 + tc * 128,
                                   [[3 * DC * c.T1, 128], [c.T1, g4], [1, 128]]),
                           bass.AP(PT, half * g4 * 128, [[1024, 128], [128, g4], [1, 128]]),
                           reads=[bPT], writes=[B(f"XQ1{s}_{sl}")])

            def b1_mstage(t1):
                sl = t1 % 3
                for tc in range(NC1):
                    bHAc, bHBc = B(f"HA{s}_{tc}"), B(f"HBt{s}_{tc}")
                    for hb2 in range(0, HID // 512, 2):
                        pairs = [(0, hb2)]
                        if hb2 + 1 < HID // 512:
                            pairs.append((1, hb2 + 1))
                        for dc in range(DC):
                            for pi, hb in pairs:
                                ph, bph = (P0, bP0) if pi == 0 else (P1, bP1)
                                mm(ph[:, 0:512],
                                   XQ1[:, sl * DC * c.T1 + dc * c.T1 + tc * 128:
                                       sl * DC * c.T1 + dc * c.T1 + (tc + 1) * 128],
                                   W1S[:, dc * HID + hb * 512:dc * HID + (hb + 1) * 512],
                                   start=(dc == 0), stop=(dc == DC - 1),
                                   reads=[B(f"XQ1{s}_{sl}"), bW1], writes=[bph])
                        for pi, hb in pairs:
                            ph, bph = (P0, bP0) if pi == 0 else (P1, bP1)
                            hs = HA[:, tc * HID + hb * 512:tc * HID + (hb + 1) * 512]
                            act("scalar", hs, ph[:, 0:512], AF.Relu, reads=[bph],
                                writes=[bHAc])
                            us = HBt[:, tc * HID + hb * 512:tc * HID + (hb + 1) * 512]
                            act("scalar", us, hs, AF.Square, reads=[bHAc],
                                writes=[bHBc])
                for tc in range(NC1):
                    k4 = (t1 * NC1 + tc) % 4
                    o = 32 * k4
                    tcol = t1 * NC1 + tc
                    ak = 22 + (tcol // 4) % 2
                    bSm = bSTm[k4]
                    bHAc, bHBc, bHIc = B(f"HA{s}_{tc}"), B(f"HBt{s}_{tc}"), B(f"HI8{s}")
                    ua = HBt[:, tc * HID:(tc + 1) * HID]
                    aa = HA[:, tc * HID:(tc + 1) * HID]
                    ia = HI8[:, 0:HID]
                    act("scalar", aa, ua, AF.Identity, bias=ABCOL[:, 1:2],
                        scale=STATC[:, o + ak:o + ak + 1],
                        reads=[bHBc, bAK, B("ABCOL")], writes=[bHAc])
                    act("scalar", ua, aa, AF.Square, accum=STATC[:, o + 24:o + 25],
                        reads=[bHAc], writes=[bHBc, bSm])
                    ts("vector", STATC[:, o + 24:o + 25], STATC[:, o + 24:o + 25],
                       1.0 / HID, EPS_NORM, ALU.mult, ALU.add, reads=[bSm], writes=[bSm])
                    em.emit("vector", lambda e, o=o: e.reciprocal(
                        STATC[:, o + 25:o + 26], STATC[:, o + 24:o + 25]),
                        reads=[bSm], writes=[bSm])
                    act("scalar", STATC[:, o + 25:o + 26], STATC[:, o + 25:o + 26],
                        AF.Sqrt, reads=[bSm], writes=[bSm])
                    tt("vector", ua, aa, C2REP[:], ALU.mult, reads=[bHAc, B("C2REP")],
                       writes=[bHBc])
                    red("vector", STATC[:, o + 26:o + 27], ua, ALU.max, absval=True,
                        reads=[bHBc], writes=[bSm])
                    tt("vector", STATC[:, o + 26:o + 27], STATC[:, o + 26:o + 27],
                       STATC[:, o + 25:o + 26], ALU.mult, reads=[bSm], writes=[bSm])
                    ts("vector", STATC[:, o + 26:o + 27], STATC[:, o + 26:o + 27],
                       EPS_Q, None, ALU.max, reads=[bSm], writes=[bSm])
                    em.emit("vector", lambda e, o=o: e.reciprocal(
                        STATC[:, o + 27:o + 28], STATC[:, o + 26:o + 27]),
                        reads=[bSm], writes=[bSm])
                    tt("vector", STATC[:, o + 28:o + 29], STATC[:, o + 27:o + 28],
                       STATC[:, o + 25:o + 26], ALU.mult, reads=[bSm], writes=[bSm])
                    ts("vector", STATC[:, o + 28:o + 29], STATC[:, o + 28:o + 29],
                       127.0, None, ALU.mult, reads=[bSm], writes=[bSm])
                    act("scalar", aa, ua, AF.Identity, bias=ZCOL[:, 0:1],
                        scale=STATC[:, o + 28:o + 29],
                        reads=[bHBc, bSm, B("ZCOL")], writes=[bHAc])
                    ts("vector", S2COL[:, tcol:tcol + 1], STATC[:, o + 26:o + 27],
                       1.0 / 127.0, None, ALU.mult, reads=[bSm], writes=[bS2])
                    tt("vector", S2COL[:, tcol:tcol + 1], S2COL[:, tcol:tcol + 1],
                       SWCOL[:, 1:2], ALU.mult, reads=[B("SWCOL")], writes=[bS2])
                    cp("scalar", ia, aa, reads=[bHAc], writes=[bHIc])
                    tt("vector", ua, aa, ia, ALU.subtract, reads=[bHAc, bHIc],
                       writes=[bHBc])
                    stt("vector", aa, ua, 0.5, ia, ALU.is_ge, ALU.add,
                        reads=[bHBc, bHIc], writes=[bHAc])
                    stt("vector", ua, ua, -0.5, aa, ALU.is_lt, ALU.subtract,
                        reads=[bHAc], writes=[bHBc])

            b1_dstage(0)
            if c.NT1 > 1:
                b1_dstage(1)
            for t1 in range(c.NT1):
                if t1 >= 1:
                    for tc in range(NC1):
                        xq2_finish(t1 - 1, tc)
                if t1 + 2 < c.NT1:
                    b1_dstage(t1 + 2)
                b1_mstage(t1)
            for tc in range(c.T1 // 128):
                xq2_finish(c.NT1 - 1, tc)

            em.barrier()
            br.close()

            b2 = contextlib.ExitStack()
            W2S = b2.enter_context(nc.sbuf_tensor(f"W2S{s}", [128, HC * D], BF16))
            GBR = b2.enter_context(nc.sbuf_tensor(f"GBR{s}", [128, D], BF16))
            GB2 = b2.enter_context(nc.sbuf_tensor(f"GB2{s}", [128, D], F32))
            XQ2F = b2.enter_context(nc.sbuf_tensor(f"XQ2F{s}", [128, HC * c.T2], BF16))
            XMT2 = b2.enter_context(nc.sbuf_tensor(f"XMT2{s}", [128, (c.T2 // 128) * D], F32))
            OUTS = b2.enter_context(nc.sbuf_tensor(f"OUTS{s}", [128, (c.T2 // 128) * D], F32))
            bW2, bGBR, bGB2 = B(f"W2S{s}"), B(f"GBR{s}"), B(f"GB2{s}")
            bXM2 = B(f"XMT2{s}")
            bXQFh = [B(f"XQ2F{s}_0"), B(f"XQ2F{s}_1")]

            bcast("sync", GB2[:], mod_scr.tensor, (s * 6 + 5) * D, D,
                  reads=[B("mod_scr")], writes=[bGB2])
            ts("vector", GBR[:], GB2[:], 0.0, 2.0, ALU.is_ge, ALU.mult, reads=[bGB2],
               writes=[bGBR])
            ts("vector", GBR[:], GBR[:], -1.0, None, ALU.add, reads=[bGBR],
               writes=[bGBR])
            act("scalar", GB2[:], GB2[:], AF.Abs, reads=[bGB2], writes=[bGB2])
            ts("vector", GB2[:], GB2[:], 0.25, None, ALU.max, reads=[bGB2],
               writes=[bGB2])
            tt("vector", GB2[:], GB2[:], GBR[:], ALU.mult, reads=[bGBR], writes=[bGB2])
            cp("vector", GBR[:], GB2[:], reads=[bGB2], writes=[bGBR])
            for hh, q_ in ((0, "sync"), (1, "gpsimd")):
                em.dma(q_,
                       bass.AP(W2S, hh * (HC // 2) * D,
                               [[HC * D, 128], [D, HC // 2], [1, D]]),
                       bass.AP(w2q_scr.tensor, hh * (HC // 2) * 128 * D,
                               [[D, 128], [128 * D, HC // 2], [1, D]]),
                       reads=[B("qS1")], writes=[bW2])

            HC2 = HC // 2
            for t2 in range(c.NT2):
                tb = s * S + t2 * c.T2
                for hh in range(2):
                    em.dma("sync" if hh == 0 else "gpsimd",
                           bass.AP(XQ2F, hh * HC2 * c.T2,
                                   [[HC * c.T2, 128], [c.T2, HC2], [1, c.T2]]),
                           bass.AP(xq2_scr.tensor, hh * HC2 * 128 * S + t2 * c.T2,
                                   [[S, 128], [128 * S, HC2], [1, c.T2]]),
                           reads=[bXQscr], writes=[bXQFh[hh]])
                em.dma("scalar",
                       bass.AP(XMT2, 0, [[(c.T2 // 128) * D, 128], [D, c.T2 // 128], [1, D]]),
                       bass.AP(xmid_scr.tensor, tb * D,
                               [[D, 128], [128 * D, c.T2 // 128], [1, D]]),
                       reads=[bXMS], writes=[bXM2])
                for tc in range(c.T2 // 128):
                    py, bpy = (P0, bP0) if tc % 2 == 0 else (P1, bP1)
                    nh = max(1, D // 512)
                    dw = D // nh
                    for hc in range(HC):
                        for half in range(nh):
                            mm(py[:, half * dw:(half + 1) * dw],
                               XQ2F[:, hc * c.T2 + tc * 128:hc * c.T2 + (tc + 1) * 128],
                               W2S[:, hc * D + half * dw:hc * D + (half + 1) * dw],
                               start=(hc == 0), stop=(hc == HC - 1),
                               reads=[bXQFh[hc // HC2], bW2], writes=[bpy])
                    oa = OUTS[:, tc * D:(tc + 1) * D]
                    bOUc = B(f"OUTS{s}_{tc % 2}")
                    tcol = (t2 * c.T2) // 128 + tc
                    ts("vector", oa, py[:, 0:D], S2COL[:, tcol:tcol + 1], None,
                       ALU.mult, reads=[bpy, bS2], writes=[bOUc])
                    tt("vector", oa, oa, GBR[:], ALU.mult, reads=[bGBR], writes=[bOUc])
                    tt("vector", oa, oa, XMT2[:, tc * D:(tc + 1) * D], ALU.add,
                       reads=[bXM2], writes=[bOUc])
                    em.dma("scalar",
                           bass.AP(out_t.tensor, (tb + tc * 128) * D,
                                   [[D, 128], [1, D]]),
                           oa, reads=[bOUc], writes=[B("out")])

            em.barrier()
            b2.close()

        em.finish()

    return nc


_NC_CACHE = {}


def _get_nc():
    if "full" not in _NC_CACHE:
        _NC_CACHE["full"] = build_program(CFG)
    return _NC_CACHE["full"]


def make_inputs(core, cfg, L0_attn_x, L0_attn_W_qk, L0_attn_b_qk, L0_attn_W_v,
                L0_attn_b_v, L0_attn_W_out, L0_attn_b_out, L0_attn_ln_gamma,
                L0_attn_ln_beta, L0_ffn_norm1_w, L0_ffn_w1, L0_ffn_act_scale,
                L0_ffn_act_bias, L0_ffn_norm2_w, L0_ffn_w2, L0_c, L0_wmod):
    c = cfg
    D, QK = c.D, c.QK
    f32 = np.float32
    perm = np.arange(QK)
    h_idx, d_idx = perm // c.DH, perm % c.DH
    qcols = h_idx * 2 * c.DH + 2 * d_idx
    kcols = qcols + 1
    W_qk = np.asarray(L0_attn_W_qk, f32)
    b_qk = np.asarray(L0_attn_b_qk, f32)
    wqk_p = np.concatenate([W_qk[:, qcols], W_qk[:, kcols]], axis=1)
    bqk2 = np.stack([b_qk[qcols], b_qk[kcols]], axis=0)
    b0 = core * c.NSEQ
    x = np.asarray(L0_attn_x, f32)
    return {
        "x": np.ascontiguousarray(x[b0:b0 + c.NSEQ].reshape(c.NSEQ * c.S, D)),
        "wqk": np.ascontiguousarray(wqk_p),
        "bqk2": np.ascontiguousarray(bqk2),
        "wv": np.ascontiguousarray(np.asarray(L0_attn_W_v, f32)),
        "bv": np.ascontiguousarray(np.asarray(L0_attn_b_v, f32).reshape(1, QK)),
        "wout": np.ascontiguousarray(np.asarray(L0_attn_W_out, f32)),
        "bout": np.ascontiguousarray(np.asarray(L0_attn_b_out, f32).reshape(1, D)),
        "ln_g": np.ascontiguousarray(np.asarray(L0_attn_ln_gamma, f32).reshape(1, D)),
        "ln_b": np.ascontiguousarray(np.asarray(L0_attn_ln_beta, f32).reshape(1, D)),
        "n1w": np.ascontiguousarray(np.asarray(L0_ffn_norm1_w, f32).reshape(1, D)),
        "w1t": np.ascontiguousarray(np.asarray(L0_ffn_w1, f32).T),
        "actsc": np.ascontiguousarray(np.array(
            [[np.asarray(L0_ffn_act_scale, f32).reshape(()),
              np.asarray(L0_ffn_act_bias, f32).reshape(())]], f32)),
        "n2w": np.ascontiguousarray(np.asarray(L0_ffn_norm2_w, f32).reshape(1, c.HID)),
        "w2t": np.ascontiguousarray(np.asarray(L0_ffn_w2, f32).T),
        "c_t": np.ascontiguousarray(np.asarray(L0_c, f32)[b0:b0 + c.NSEQ].T),
        "wmod": np.ascontiguousarray(np.asarray(L0_wmod, f32)),
    }


def kernel(**inputs):
    c = CFG
    nc = _get_nc()
    in_maps = [make_inputs(core, c, **inputs) for core in range(N_CORES)]
    res = run_bass_kernel_spmd(nc, in_maps, core_ids=list(range(N_CORES)))
    outs = [np.asarray(r["out"]).reshape(c.NSEQ, c.S, c.D) for r in res.results]
    return np.concatenate(outs, axis=0)


# version 18
# speedup vs baseline: 1.0533x; 1.0185x over previous
import contextlib
import numpy as np
import concourse.bass as bass
import concourse.mybir as mybir
from concourse.bass_utils import run_bass_kernel_spmd

F32 = mybir.dt.float32
BF16 = mybir.dt.bfloat16
I8 = mybir.dt.int8
AF = mybir.ActivationFunctionType
ALU = mybir.AluOpType
AX = mybir.AxisListType

RSQRT2 = float(2.0 ** (-0.5))
LN_EPS = 1e-5
EPS_NORM = 1e-8
EPS_Q = 1e-5

N_CORES = 8


class Cfg:
    def __init__(self, S=2048, D=1024, HID=4096, NSEQ=4, H=8, AG=512, T1=256, T2=512):
        self.S, self.D, self.HID, self.NSEQ, self.H = S, D, HID, NSEQ, H
        self.DH = 64
        self.AG, self.T1, self.T2 = AG, T1, T2
        self.DC = D // 128
        self.QK = self.H * self.DH
        self.QC = self.QK // 128
        self.HC = HID // 128
        self.KC = S // 128
        self.NG = S // AG
        self.NT1 = S // T1
        self.NT2 = S // T2
        self.NP = self.H // 2
        self.VW = self.H * 65 + 8
        self.DBG = False


CFG = Cfg()


class Buf:
    __slots__ = ("name", "writer", "readers")

    def __init__(self, name):
        self.name = name
        self.writer = None
        self.readers = []


class Em:
    ENGINES = ("tensor", "vector", "scalar", "gpsimd", "sync")
    NSLOT = {"sync": 12, "scalar": 8, "gpsimd": 4, "vector": 2}

    def __init__(self, nc):
        self.nc = nc
        self.TRACKS = list(self.ENGINES)
        self.dma_slots = {}
        for q, n in self.NSLOT.items():
            self.dma_slots[q] = [f"{q}_dma{i}" for i in range(n)]
            self.TRACKS += self.dma_slots[q]
        self.sems = {}
        self.cnt = {t: 0 for t in self.TRACKS}
        self.waited = {e: {t: 0 for t in self.TRACKS} for e in self.ENGINES}
        self.eng = {"tensor": nc.tensor, "vector": nc.vector, "scalar": nc.scalar,
                    "gpsimd": nc.gpsimd, "sync": nc.sync}
        self.dma_idx = {q: 0 for q in self.NSLOT}
        self.bufs = {}

    def B(self, name):
        if name not in self.bufs:
            self.bufs[name] = Buf(name)
        return self.bufs[name]

    def init_sems(self, ctx):
        for t in self.TRACKS:
            self.sems[t] = ctx.enter_context(self.nc.semaphore(f"sem_{t}"))

    def _wait(self, eng, dep_eng, dep_cnt):
        if dep_cnt <= 0 or self.waited[eng][dep_eng] >= dep_cnt:
            return
        self.eng[eng].wait_ge(self.sems[dep_eng], dep_cnt)
        self.waited[eng][dep_eng] = dep_cnt

    def emit(self, eng, fn, reads=(), writes=(), inc=1, track=None):
        if track is None:
            track = eng
        for b in reads:
            if b.writer is not None:
                self._wait(eng, b.writer[0], b.writer[1])
        for b in writes:
            if b.writer is not None:
                self._wait(eng, b.writer[0], b.writer[1])
            for tok in b.readers:
                self._wait(eng, tok[0], tok[1])
        inst = fn(self.eng[eng])
        self.cnt[track] += inc
        inst.then_inc(self.sems[track], inc)
        tok = (track, self.cnt[track])
        for b in reads:
            b.readers.append(tok)
        for b in writes:
            b.writer = tok
            b.readers = []
        return tok

    def dma(self, queue, out_ap, in_ap, reads=(), writes=(), slow=False):
        slot = self.dma_slots[queue][self.dma_idx[queue] % self.NSLOT[queue]]
        self.dma_idx[queue] += 1
        self._wait(queue, slot, self.cnt[slot])
        if slow:
            return self.emit(queue, lambda e: e.dma_start(
                out=out_ap, in_=in_ap, allow_slow_non_contiguous=True),
                reads=reads, writes=writes, inc=16, track=slot)
        return self.emit(queue, lambda e: e.dma_start(out=out_ap, in_=in_ap),
                         reads=reads, writes=writes, inc=16, track=slot)

    def barrier(self):
        snap = dict(self.cnt)
        for e in self.ENGINES:
            for t in self.TRACKS:
                if t != e:
                    self._wait(e, t, snap[t])

    def finish(self):
        snap = dict(self.cnt)
        for t in self.TRACKS:
            if t != "gpsimd":
                self._wait("gpsimd", t, snap[t])


def build_program(c: Cfg):
    nc = bass.Bass("TRN2", target_bir_lowering=False)
    S, D, HID, NSEQ, H = c.S, c.D, c.HID, c.NSEQ, c.H
    QK, DC, QC, HC, KC, VW = c.QK, c.DC, c.QC, c.HC, c.KC, c.VW

    def din(name, shape, dt=F32):
        return nc.dram_tensor(name, list(shape), dt, kind="ExternalInput").ap()

    x_in = din("x", (NSEQ * S, D))
    wqk = din("wqk", (D, 2 * QK))
    bqk2 = din("bqk2", (2, QK))
    wv = din("wv", (D, QK))
    bv = din("bv", (1, QK))
    wout = din("wout", (QK, D))
    bout = din("bout", (1, D))
    ln_g = din("ln_g", (1, D))
    ln_b = din("ln_b", (1, D))
    n1w = din("n1w", (1, D))
    w1t = din("w1t", (D, HID))
    actsc = din("actsc", (1, 2))
    n2w = din("n2w", (1, HID))
    w2t = din("w2t", (HID, D))
    c_t = din("c_t", (D, NSEQ))
    wmod = din("wmod", (D, 6 * D))

    out_t = nc.dram_tensor("out", [NSEQ * S, D], F32, kind="ExternalOutput").ap()
    xmid_scr = nc.dram_tensor("xmid_scr", [NSEQ * S, D], F32).ap()
    mod_scr = nc.dram_tensor("mod_scr", [NSEQ * 6, D], F32).ap()
    w1q_scr = nc.dram_tensor("w1q_scr", [D, HID], BF16).ap()
    w2q_scr = nc.dram_tensor("w2q_scr", [HID, D], BF16).ap()
    xq2_scr = nc.dram_tensor("xq2_scr", [HID, S], BF16).ap()
    tiny_scr = nc.dram_tensor("tiny_scr", [64], F32).ap()
    dbg_scr = nc.dram_tensor("dbg_scr", [c.NG * 128, QC * c.AG], F32).ap() if c.DBG else None
    dbg2_scr = nc.dram_tensor("dbg2_scr", [(c.AG // 128) * 128, D], F32).ap() if c.DBG else None

    ctx = contextlib.ExitStack()
    em = Em(nc)

    with ctx:
        em.init_sems(ctx)
        B = em.B

        WQK = ctx.enter_context(nc.sbuf_tensor("WQK", [128, DC * 2 * QK], BF16))
        WV = ctx.enter_context(nc.sbuf_tensor("WV", [128, DC * QK], BF16))
        WOUT = ctx.enter_context(nc.sbuf_tensor("WOUT", [128, QC * D], BF16))
        IDN = ctx.enter_context(nc.sbuf_tensor("IDN", [128, 128], BF16))
        ONESF = ctx.enter_context(nc.sbuf_tensor("ONESF", [1, 128], F32))
        ONESC = ctx.enter_context(nc.sbuf_tensor("ONESC", [128, 1], F32))
        EPSCOL = ctx.enter_context(nc.sbuf_tensor("EPSCOL", [128, 1], F32))
        ZCOL = ctx.enter_context(nc.sbuf_tensor("ZCOL", [128, 1], F32))
        IND64 = ctx.enter_context(nc.sbuf_tensor("IND64", [128, 256], F32))
        BQS = ctx.enter_context(nc.sbuf_tensor("BQS", [128, QC], F32))
        BKS = ctx.enter_context(nc.sbuf_tensor("BKS", [128, QC], F32))
        BVC = ctx.enter_context(nc.sbuf_tensor("BVC", [128, QC], BF16))
        BOREP = ctx.enter_context(nc.sbuf_tensor("BOREP", [128, D], BF16))
        GREP = ctx.enter_context(nc.sbuf_tensor("GREP", [128, D], BF16))
        BREP = ctx.enter_context(nc.sbuf_tensor("BREP", [128, D], BF16))
        C1REP = ctx.enter_context(nc.sbuf_tensor("C1REP", [128, D], F32))
        C1IREP = ctx.enter_context(nc.sbuf_tensor("C1IREP", [128, D], F32))
        C2REP = ctx.enter_context(nc.sbuf_tensor("C2REP", [128, HID], BF16))
        ABCOL = ctx.enter_context(nc.sbuf_tensor("ABCOL", [128, 2], F32))
        SWCOL = ctx.enter_context(nc.sbuf_tensor("SWCOL", [128, 2], F32))
        IND8S = ctx.enter_context(nc.sbuf_tensor("IND8S", [128, QC * 8], BF16))
        NNB = ctx.enter_context(nc.sbuf_tensor("NNB", [128, c.NP], F32))
        S2COL = ctx.enter_context(nc.sbuf_tensor("S2COL", [128, S // 128], F32))
        STATC = ctx.enter_context(nc.sbuf_tensor("STATC", [128, 160], F32))

        P0 = ctx.enter_context(nc.psum_tensor("P0", [128, 1024], F32))
        P1 = ctx.enter_context(nc.psum_tensor("P1", [128, 1024], F32))
        P2A = ctx.enter_context(nc.psum_tensor("P2A", [128, 512], F32))
        P3 = ctx.enter_context(nc.psum_tensor("P3", [128, 512], F32))
        P4 = ctx.enter_context(nc.psum_tensor("P4", [128, 512], F32))
        PT = ctx.enter_context(nc.psum_tensor("PT", [128, 1024], BF16))
        bP0, bP1, bP2, bP3, bP4 = B("P0"), B("P1"), B("P2A"), B("P3"), B("P4")
        bPT = B("PT")
        bSTs = [B(f"STATC{k}") for k in range(4)]
        bSTm = [B(f"STATM{k}") for k in range(4)]
        bAK = B("AKCOLS")
        bST = B("STATC")

        def ts(eng, out, in0, s1, s2, op0, op1=None, reads=(), writes=()):
            if op1 is None:
                em.emit(eng, lambda e: e.tensor_scalar(out, in0, s1, None, op0=op0),
                        reads=reads, writes=writes)
            else:
                em.emit(eng, lambda e: e.tensor_scalar(out, in0, s1, s2, op0=op0, op1=op1),
                        reads=reads, writes=writes)

        def tt(eng, out, in0, in1, op, reads=(), writes=()):
            em.emit(eng, lambda e: e.tensor_tensor(out, in0, in1, op=op),
                    reads=reads, writes=writes)

        def stt(eng, out, in0, s, in1, op0, op1, reads=(), writes=()):
            em.emit(eng, lambda e: e.scalar_tensor_tensor(out, in0, s, in1, op0=op0,
                                                          op1=op1),
                    reads=reads, writes=writes)

        def act(eng, out, in0, func, bias=0.0, scale=1.0, accum=None, reads=(), writes=()):
            if accum is None:
                em.emit(eng, lambda e: e.activation(out, in0, func, bias=bias,
                                                    scale=scale),
                        reads=reads, writes=writes)
            else:
                em.emit(eng, lambda e: e.activation(out, in0, func, bias=bias,
                                                    scale=scale, accum_out=accum),
                        reads=reads, writes=writes)

        def cp(eng, out, in0, reads=(), writes=()):
            if eng == "scalar":
                em.emit(eng, lambda e: e.copy(out, in0), reads=reads, writes=writes)
            else:
                em.emit(eng, lambda e: e.tensor_copy(out, in0), reads=reads,
                        writes=writes)

        def red(eng, out, in0, op, absval=False, reads=(), writes=()):
            em.emit(eng, lambda e: e.tensor_reduce(out, in0, axis=AX.X, op=op,
                                                   apply_absolute_value=absval),
                    reads=reads, writes=writes)

        def mm(out, lhsT, rhs, start, stop, reads=(), writes=(), tp=None):
            if tp is None:
                em.emit("tensor", lambda e: e.matmul(out, lhsT, rhs, start=start,
                                                     stop=stop),
                        reads=reads, writes=writes)
            else:
                em.emit("tensor", lambda e: e.matmul(out, lhsT, rhs, start=start,
                                                     stop=stop, tile_position=tp),
                        reads=reads, writes=writes)

        def transp(out, in0, reads=(), writes=()):
            em.emit("tensor", lambda e: e.transpose(out, in0, IDN[:]),
                    reads=list(reads) + [B("IDN")], writes=writes)

        def bcast(queue, out_tile, src_t, src_off, n, reads=(), writes=()):
            src = bass.AP(src_t, src_off, [[0, 128], [1, n]])
            em.dma(queue, out_tile, src, reads=reads, writes=writes)

        def ln_bn(xa, width, rbuf, k):
            o = 32 * k
            bSk = bSTs[k]
            nparts = width // 512 if width > 512 else 1
            wper = width // nparts
            assert wper <= 512 and nparts <= 4
            for i in range(nparts):
                em.emit("vector", lambda e, i=i: e.bn_stats(
                    STATC[:, o + 6 * i:o + 6 * i + 6], xa[:, i * wper:(i + 1) * wper]),
                    reads=[rbuf], writes=[bSk])
            em.emit("vector", lambda e: e.bn_aggr(STATC[:, o + 12:o + 14],
                                                  STATC[:, o:o + 6 * nparts]),
                    reads=[bSk], writes=[bSk])

        def ln_sqrt(k):
            o = 32 * k
            act("scalar", STATC[:, o + 13:o + 14], STATC[:, o + 13:o + 14], AF.Sqrt,
                bias=EPSCOL[:, 0:1], reads=[bSTs[k], B("EPSCOL")], writes=[bSTs[k]])

        def ln_recip(k):
            o = 32 * k
            em.emit("vector", lambda e: e.reciprocal(STATC[:, o + 13:o + 14],
                                                     STATC[:, o + 13:o + 14]),
                    reads=[bSTs[k]], writes=[bSTs[k]])
            stt("vector", STATC[:, o + 14:o + 15], STATC[:, o + 12:o + 13], -1.0,
                STATC[:, o + 13:o + 14], ALU.mult, ALU.mult,
                reads=[bSTs[k]], writes=[bSTs[k]])

        def ln_apply(eng_out, in_ap, k, rbufs, wbufs):
            o = 32 * k
            act("scalar", eng_out, in_ap, AF.Identity, bias=STATC[:, o + 14:o + 15],
                scale=STATC[:, o + 13:o + 14], reads=list(rbufs) + [bSTs[k]],
                writes=wbufs)

        def ln_stats(xa, width, rbuf, k):
            ln_bn(xa, width, rbuf, k)
            ln_sqrt(k)
            ln_recip(k)

        em.emit("gpsimd", lambda e: e.iota(
            bass.AP(IDN, 0, [[128, 128], [1, 128]]), [[1, 128]], channel_multiplier=-1,
            allow_small_or_imprecise_dtypes=True), writes=[B("IDN")])
        ts("vector", IDN[:], IDN[:], 0.0, None, ALU.is_equal, reads=[B("IDN")],
           writes=[B("IDN")])
        em.emit("vector", lambda e: e.memset(ONESF[:], 1.0), writes=[B("ONESF")])
        em.emit("vector", lambda e: e.memset(ONESC[:], 1.0), writes=[B("ONESC")])
        em.emit("vector", lambda e: e.memset(EPSCOL[:], LN_EPS), writes=[B("EPSCOL")])
        em.emit("vector", lambda e: e.memset(ZCOL[:], 0.0), writes=[B("ZCOL")])
        em.emit("vector", lambda e: e.memset(IND64[:], 0.0), writes=[B("IND64")])
        em.emit("vector", lambda e: e.memset(IND64[64:65, 0:64], 1.0), writes=[B("IND64")])
        em.emit("vector", lambda e: e.memset(IND64[64:65, 192:256], 1.0), writes=[B("IND64")])
        em.emit("vector", lambda e: e.memset(IND8S[:], 0.0), writes=[B("IND8S")])
        for fc in range(QC):
            em.emit("vector", lambda e, fc=fc: e.memset(
                IND8S[0:64, fc * 8 + 2 * fc:fc * 8 + 2 * fc + 1], 1.0), writes=[B("IND8S")])
            em.emit("vector", lambda e, fc=fc: e.memset(
                IND8S[64:128, fc * 8 + 2 * fc + 1:fc * 8 + 2 * fc + 2], 1.0),
                writes=[B("IND8S")])

        for tile, tname, srcx, n in ((GREP, "GREP", ln_g, D), (BREP, "BREP", ln_b, D),
                                     (C2REP, "C2REP", n2w, HID),
                                     (BOREP, "BOREP", bout, D)):
            em.dma("gpsimd", tile[:], bass.AP(srcx.tensor, 0, [[0, 128], [1, n]]),
                   writes=[B(tname)])
        bcast("sync", C1REP[:], n1w.tensor, 0, D, writes=[B("C1REP")])
        bcast("sync", ABCOL[:, 0:2], actsc.tensor, 0, 2, writes=[B("ABCOL")])
        em.emit("vector", lambda e: e.reciprocal(C1IREP[:], C1REP[:]),
                reads=[B("C1REP")], writes=[B("C1IREP")])

        em.dma("gpsimd", bass.AP(BVC, 0, [[QC, 128], [1, QC]]),
               bass.AP(bv.tensor, 0, [[1, 128], [128, QC]]), writes=[B("BVC")],
               slow=True)
        em.dma("sync", bass.AP(BQS, 0, [[QC, 128], [1, QC]]),
               bass.AP(bqk2.tensor, 0, [[1, 128], [128, QC]]), writes=[B("BQS")],
               slow=True)
        em.dma("sync", bass.AP(BKS, 0, [[QC, 128], [1, QC]]),
               bass.AP(bqk2.tensor, QK, [[1, 128], [128, QC]]), writes=[B("BKS")],
               slow=True)
        ts("vector", BQS[:], BQS[:], 0.125, None, ALU.mult, reads=[B("BQS")],
           writes=[B("BQS")])

        em.dma("gpsimd",
               bass.AP(WQK, 0, [[DC * 2 * QK, 128], [2 * QK, DC], [1, 2 * QK]]),
               bass.AP(wqk.tensor, 0, [[2 * QK, 128], [128 * 2 * QK, DC], [1, 2 * QK]]),
               writes=[B("WQK")])
        em.dma("gpsimd",
               bass.AP(WV, 0, [[DC * QK, 128], [QK, DC], [1, QK]]),
               bass.AP(wv.tensor, 0, [[QK, 128], [128 * QK, DC], [1, QK]]),
               writes=[B("WV")])
        em.dma("gpsimd",
               bass.AP(WOUT, 0, [[QC * D, 128], [D, QC], [1, D]]),
               bass.AP(wout.tensor, 0, [[D, 128], [128 * D, QC], [1, D]]),
               writes=[B("WOUT")])

        arena = contextlib.ExitStack()
        BVTMP = arena.enter_context(nc.sbuf_tensor("BVTMP", [1, D], F32))
        for half in range(max(1, D // 512)):
            dw2 = min(512, D)
            for ci in range(QC):
                mm(P0[0:1, half * dw2:half * dw2 + dw2], BVC[:, ci:ci + 1],
                   WOUT[:, ci * D + half * dw2:ci * D + (half + 1) * dw2],
                   start=(ci == 0), stop=(ci == QC - 1),
                   reads=[B("BVC"), B("WOUT")], writes=[bP0])
        cp("vector", BVTMP[0:1, :], P0[0:1, 0:D], reads=[bP0], writes=[B("BVTMP")])
        for half in range(max(1, D // 512)):
            dw2 = min(512, D)
            mm(P0[:, half * dw2:half * dw2 + dw2], ONESF[0:1, :],
               BVTMP[0:1, half * dw2:half * dw2 + dw2],
               start=True, stop=True, reads=[B("BVTMP"), B("ONESF")], writes=[bP0])
            tt("vector", BOREP[:, half * dw2:half * dw2 + dw2],
               BOREP[:, half * dw2:half * dw2 + dw2],
               P0[:, half * dw2:half * dw2 + dw2], ALU.add,
               reads=[bP0], writes=[B("BOREP")])
        SILC = arena.enter_context(nc.sbuf_tensor("SILC", [128, DC * NSEQ], F32))
        MODT = arena.enter_context(nc.sbuf_tensor("MODT", [NSEQ, 6 * D], F32))
        WMS = arena.enter_context(nc.sbuf_tensor("WMS", [128, 2 * 512], F32))
        em.dma("sync", bass.AP(SILC, 0, [[DC * NSEQ, 128], [NSEQ, DC], [1, NSEQ]]),
               bass.AP(c_t.tensor, 0, [[NSEQ, 128], [128 * NSEQ, DC], [1, NSEQ]]),
               writes=[B("SILC")], slow=True)
        act("scalar", SILC[:], SILC[:], AF.Silu, reads=[B("SILC")], writes=[B("SILC")])
        for nt in range(6 * D // 512):
            for dc in range(DC):
                wslot = (nt * DC + dc) % 2
                bWM = B(f"WMS{wslot}")
                em.dma("sync", WMS[:, wslot * 512:wslot * 512 + 512],
                       bass.AP(wmod.tensor, dc * 128 * 6 * D + nt * 512,
                               [[6 * D, 128], [1, 512]]),
                       writes=[bWM])
                mm(P3[0:NSEQ, 0:512], SILC[:, dc * NSEQ:(dc + 1) * NSEQ],
                   WMS[:, wslot * 512:wslot * 512 + 512],
                   start=(dc == 0), stop=(dc == DC - 1),
                   reads=[B("SILC"), bWM], writes=[bP3])
            cp("vector", MODT[0:NSEQ, nt * 512:(nt + 1) * 512], P3[0:NSEQ, 0:512],
               reads=[bP3], writes=[B("MODT")])
        em.dma("scalar", bass.AP(mod_scr.tensor, 0, [[6 * D, NSEQ], [1, 6 * D]]),
               MODT[0:NSEQ, :], reads=[B("MODT")], writes=[B("mod_scr")])

        WQT = arena.enter_context(nc.sbuf_tensor("WQT", [128, HID], F32))
        WQA = arena.enter_context(nc.sbuf_tensor("WQA", [128, HID], F32))
        WQB = arena.enter_context(nc.sbuf_tensor("WQB", [128, HID], BF16))
        WQI = arena.enter_context(nc.sbuf_tensor("WQI", [128, HID], I8))
        WQC = arena.enter_context(nc.sbuf_tensor("WQC", [128, 4], F32))

        def wquant(w_ap, rows, cols, scr_t, idx):
            nch = rows // 128
            bW, bA, bC = B("WQT"), B("WQAa"), B("WQC")
            bI, bBf, bScr = B("WQI"), B("WQB"), B(f"qS{idx}")
            for ch in range(nch):
                em.dma("sync", WQT[:, 0:cols],
                       bass.AP(w_ap.tensor, ch * 128 * cols, [[cols, 128], [1, cols]]),
                       writes=[bW])
                red("vector", WQC[:, 1:2], WQT[:, 0:cols], ALU.add, absval=True,
                    reads=[bW], writes=[bC])
                if ch == 0:
                    cp("vector", WQC[:, 0:1], WQC[:, 1:2], reads=[bC], writes=[bA])
                else:
                    tt("vector", WQC[:, 0:1], WQC[:, 0:1], WQC[:, 1:2], ALU.add,
                       reads=[bC], writes=[bA])
            mm(P3[0:1, 0:1], WQC[:, 0:1], ONESC[:, 0:1], start=True, stop=True,
               reads=[bA, B("ONESC")], writes=[bP3])
            cp("vector", WQC[0:1, 2:3], P3[0:1, 0:1], reads=[bP3], writes=[bC])
            ts("vector", WQC[0:1, 2:3], WQC[0:1, 2:3], 1.0 / (rows * cols), EPS_Q,
               ALU.mult, ALU.max, reads=[bC], writes=[bC])
            mm(P3[:, 0:1], ONESF[0:1, :], WQC[0:1, 2:3], start=True, stop=True,
               reads=[bC, B("ONESF")], writes=[bP3])
            cp("vector", SWCOL[:, idx:idx + 1], P3[:, 0:1], reads=[bP3],
               writes=[B("SWCOL")])
            em.emit("vector", lambda e: e.reciprocal(WQC[:, 3:4], SWCOL[:, idx:idx + 1]),
                    reads=[B("SWCOL")], writes=[bC])
            for ch in range(nch):
                em.dma("sync", WQT[:, 0:cols],
                       bass.AP(w_ap.tensor, ch * 128 * cols, [[cols, 128], [1, cols]]),
                       writes=[bW])
                act("scalar", WQA[:, 0:cols], WQT[:, 0:cols], AF.Identity,
                    bias=ZCOL[:, 0:1], scale=WQC[:, 3:4],
                    reads=[bW, bC, B("ZCOL")], writes=[B("WQAa")])
                ts("vector", WQT[:, 0:cols], WQA[:, 0:cols], 0.5, None, ALU.is_gt,
                   reads=[B("WQAa")], writes=[bW])
                stt("vector", WQB[:, 0:cols], WQA[:, 0:cols], -0.5, WQT[:, 0:cols],
                    ALU.is_lt, ALU.subtract, reads=[B("WQAa"), bW], writes=[bBf])
                em.dma("scalar",
                       bass.AP(scr_t, ch * 128 * cols, [[cols, 128], [1, cols]]),
                       WQB[:, 0:cols], reads=[bBf], writes=[bScr])

        wquant(w1t, D, HID, w1q_scr.tensor, 0)
        wquant(w2t, HID, D, w2q_scr.tensor, 1)
        em.barrier()
        arena.close()

        for s in range(NSEQ):
            ar = contextlib.ExitStack()
            HBF = ar.enter_context(nc.sbuf_tensor(f"HBF{s}", [128, DC * S], BF16))
            KBF = ar.enter_context(nc.sbuf_tensor(f"KBF{s}", [128, QC * S], BF16))
            QBF = ar.enter_context(nc.sbuf_tensor(f"QBF{s}", [128, QC * S], BF16))
            VTM = ar.enter_context(nc.sbuf_tensor(f"VTM{s}", [128, KC * VW], BF16))
            XT = ar.enter_context(nc.sbuf_tensor(f"XT{s}", [128, 4 * D], F32))
            PBF = ar.enter_context(nc.sbuf_tensor(f"PBF{s}", [128, 2 * 1024], BF16))
            OBF = ar.enter_context(nc.sbuf_tensor(f"OBF{s}", [128, QC * c.AG], BF16))
            OST = ar.enter_context(nc.sbuf_tensor(f"OST{s}", [128, c.NP * c.AG], BF16))
            ORW = ar.enter_context(nc.sbuf_tensor(f"ORW{s}", [128, 2 * c.AG], BF16))
            RRC = ar.enter_context(nc.sbuf_tensor(f"RRC{s}", [128, 512], BF16))
            SMR = ar.enter_context(nc.sbuf_tensor(f"SMR{s}", [128, 2048], F32))
            EP1 = ar.enter_context(nc.sbuf_tensor(f"EP1{s}", [128, (c.AG // 128) * D], BF16))
            EP2 = ar.enter_context(nc.sbuf_tensor(f"EP2{s}", [128, (c.AG // 128) * D], BF16))
            CA1 = ar.enter_context(nc.sbuf_tensor(f"CA1{s}", [128, D], F32))
            SAR = ar.enter_context(nc.sbuf_tensor(f"SAR{s}", [128, D], BF16))
            GAR = ar.enter_context(nc.sbuf_tensor(f"GAR{s}", [128, D], BF16))
            GSC = ar.enter_context(nc.sbuf_tensor(f"GSC{s}", [128, D], F32))

            bHBF, bKBF, bQBF, bVTM = B(f"HBF{s}"), B(f"KBF{s}"), B(f"QBF{s}"), B(f"VTM{s}")
            bOBF, bOST, bORW = B(f"OBF{s}"), B(f"OST{s}"), B(f"ORW{s}")
            bXTs = [B(f"XT{s}_{i}") for i in range(4)]
            bPBF = [B(f"PBF{s}_0"), B(f"PBF{s}_1")]
            bRRC, bSMR = B(f"RRC{s}"), B(f"SMR{s}")
            bEP1, bEP2 = B(f"EP1{s}"), B(f"EP2{s}")
            bCA1, bSAR, bGAR, bGSC = B(f"CA1{s}"), B(f"SAR{s}"), B(f"GAR{s}"), B(f"GSC{s}")
            bNNB, bXMS = B("NNB"), B("xmid_scr")

            def xchunk(gbase_tok, tc, nsl=4):
                sl = tc % nsl
                em.dma("sync", XT[:, sl * D:(sl + 1) * D],
                       bass.AP(x_in.tensor, (gbase_tok + tc * 128) * D,
                               [[D, 128], [1, D]]),
                       writes=[bXTs[sl]])
                return XT[:, sl * D:(sl + 1) * D], bXTs[sl]

            bcast("sync", CA1[:], mod_scr.tensor, (s * 6 + 1) * D, D,
                  reads=[B("mod_scr")], writes=[bCA1])
            ts("vector", CA1[:], CA1[:], 1.0, None, ALU.add, reads=[bCA1], writes=[bCA1])
            em.dma("gpsimd", SAR[:],
                   bass.AP(mod_scr.tensor, (s * 6 + 0) * D, [[0, 128], [1, D]]),
                   reads=[B("mod_scr")], writes=[bSAR])
            bcast("sync", GSC[:], mod_scr.tensor, (s * 6 + 2) * D, D,
                  reads=[B("mod_scr")], writes=[bGSC])
            ts("vector", GAR[:], GSC[:], 0.0, 2.0, ALU.is_ge, ALU.mult, reads=[bGSC],
               writes=[bGAR])
            ts("vector", GAR[:], GAR[:], -1.0, None, ALU.add, reads=[bGAR], writes=[bGAR])
            act("scalar", GSC[:], GSC[:], AF.Abs, reads=[bGSC], writes=[bGSC])
            ts("vector", GSC[:], GSC[:], 0.25, None, ALU.max, reads=[bGSC],
               writes=[bGSC])
            tt("vector", GSC[:], GSC[:], GAR[:], ALU.mult, reads=[bGAR], writes=[bGSC])
            ts("vector", GAR[:], GSC[:], RSQRT2, None, ALU.mult, reads=[bGSC],
               writes=[bGAR])

            em.emit("vector", lambda e: e.memset(
                bass.AP(VTM, 64, [[KC * VW, 128], [VW, KC], [65, H], [1, 1]]), 1.0),
                writes=[bVTM])

            NCH = c.AG // 128
            for g in range(c.NG):
                gbase = s * S + g * c.AG
                xas = []
                for tc in range(NCH):
                    xa, bxa = xchunk(gbase, tc)
                    xas.append((xa, bxa))
                    k4 = tc % 4
                    o = 32 * k4
                    ln_bn(xa, D, bxa, k4)
                for tc in range(NCH):
                    ln_sqrt(tc % 4)
                for tc in range(NCH):
                    ln_recip(tc % 4)
                for tc in range(NCH):
                    k4 = tc % 4
                    o = 32 * k4
                    xa, bxa = xas[tc]
                    bEP1c = B(f"EP1{s}_{tc}")
                    bEP2c = B(f"EP2{s}_{tc}")
                    ha = EP2[:, tc * D:(tc + 1) * D]
                    ln_apply(ha, xa, k4, [bxa], [bEP2c])
                    ea = EP1[:, tc * D:(tc + 1) * D]
                    tt("vector", ea, ha, CA1[:], ALU.mult, reads=[bEP2c, bCA1],
                       writes=[bEP1c])
                    tt("vector", ha, ea, SAR[:], ALU.add, reads=[bEP1c, bSAR],
                       writes=[bEP2c])
                for tc in range(NCH):
                    bEP2c = B(f"EP2{s}_{tc}")
                    ha = EP2[:, tc * D:(tc + 1) * D]
                    for j in range(DC):
                        transp(PT[:, (j % 8) * 128:(j % 8) * 128 + 128],
                               ha[:, j * 128:(j + 1) * 128],
                               reads=[bEP2c], writes=[bPT])
                    g4 = min(4, DC)
                    for half in range(DC // g4):
                        cp("scalar",
                           bass.AP(HBF, half * g4 * S + g * c.AG + tc * 128,
                                   [[DC * S, 128], [S, g4], [1, 128]]),
                           bass.AP(PT, half * g4 * 128, [[1024, 128], [128, g4], [1, 128]]),
                           reads=[bPT], writes=[bHBF])
                for fc2 in range(0, QC, 2):
                    for dc in range(DC):
                        for pi in range(2):
                            fc = fc2 + pi
                            pk, bpk = (P0, bP0) if pi == 0 else (P1, bP1)
                            mm(pk[:, 0:c.AG],
                               WQK[:, dc * 2 * QK + QK + fc * 128:
                                   dc * 2 * QK + QK + (fc + 1) * 128],
                               HBF[:, dc * S + g * c.AG:dc * S + (g + 1) * c.AG],
                               start=(dc == 0), stop=(dc == DC - 1),
                               reads=[B("WQK"), bHBF], writes=[bpk])
                    for pi in range(2):
                        fc = fc2 + pi
                        pk, bpk = (P0, bP0) if pi == 0 else (P1, bP1)
                        act("scalar", KBF[:, fc * S + g * c.AG:fc * S + (g + 1) * c.AG],
                            pk[:, 0:c.AG], AF.Identity, bias=BKS[:, fc:fc + 1],
                            reads=[bpk, B("BKS")], writes=[bKBF])
                for tc2 in range(0, NCH, 2):
                    npi = 2 if tc2 + 1 < NCH else 1
                    for dc in range(DC):
                        for pi in range(npi):
                            tc = tc2 + pi
                            pv, bpv = (P0, bP0) if pi == 0 else (P1, bP1)
                            mm(pv[:, 0:QK],
                               HBF[:, dc * S + g * c.AG + tc * 128:
                                   dc * S + g * c.AG + (tc + 1) * 128],
                               WV[:, dc * QK:(dc + 1) * QK],
                               start=(dc == 0), stop=(dc == DC - 1),
                               reads=[bHBF, B("WV")], writes=[bpv])
                    for pi in range(npi):
                        tc = tc2 + pi
                        pv, bpv = (P0, bP0) if pi == 0 else (P1, bP1)
                        kc = g * NCH + tc
                        cp("scalar",
                           bass.AP(VTM, kc * VW, [[KC * VW, 128], [65, H], [1, 64]]),
                           bass.AP(pv, 0, [[1024, 128], [64, H], [1, 64]]),
                           reads=[bpv], writes=[bVTM])

            for g in range(c.NG):
                for fc2 in range(0, QC, 2):
                    for dc in range(DC):
                        for pi in range(2):
                            fc = fc2 + pi
                            pq, bpq = (P0, bP0) if pi == 0 else (P1, bP1)
                            mm(pq[:, 0:c.AG],
                               WQK[:, dc * 2 * QK + fc * 128:dc * 2 * QK + (fc + 1) * 128],
                               HBF[:, dc * S + g * c.AG:dc * S + (g + 1) * c.AG],
                               start=(dc == 0), stop=(dc == DC - 1),
                               reads=[B("WQK"), bHBF], writes=[bpq])
                    for pi in range(2):
                        fc = fc2 + pi
                        pq, bpq = (P0, bP0) if pi == 0 else (P1, bP1)
                        act("scalar", QBF[:, fc * S + g * c.AG:fc * S + (g + 1) * c.AG],
                            pq[:, 0:c.AG], AF.Identity, bias=BQS[:, fc:fc + 1],
                            scale=0.125, reads=[bpq, B("BQS")], writes=[bQBF])
            for qsrc, bsrc, coli in ((QBF, bQBF, 22), (KBF, bKBF, 24)):
                first = True
                for fc in range(QC):
                    act("scalar", EP1[:, 0:S], qsrc[:, fc * S:(fc + 1) * S], AF.Square,
                        reads=[bsrc], writes=[B(f"EP1{s}_0"), B(f"EP1{s}_1")])
                    for ct in range((S + 511) // 512):
                        ctw = min(512, S - ct * 512)
                        mm(P3[0:8, 0:ctw], IND8S[:, fc * 8:(fc + 1) * 8],
                           EP1[:, ct * 512:ct * 512 + ctw],
                           start=True, stop=True,
                           reads=[B(f"EP1{s}_0"), B(f"EP1{s}_1"), B("IND8S")],
                           writes=[bP3])
                        red("vector", STATC[0:8, coli + 1:coli + 2], P3[0:8, 0:ctw],
                            ALU.max, reads=[bP3], writes=[bSTs[0]])
                        if first:
                            cp("vector", STATC[0:8, coli:coli + 1],
                               STATC[0:8, coli + 1:coli + 2], reads=[bSTs[0]],
                               writes=[bSTs[0]])
                            first = False
                        else:
                            tt("vector", STATC[0:8, coli:coli + 1],
                               STATC[0:8, coli:coli + 1], STATC[0:8, coli + 1:coli + 2],
                               ALU.max, reads=[bSTs[0]], writes=[bSTs[0]])
            tt("vector", STATC[0:8, 26:27], STATC[0:8, 22:23], STATC[0:8, 24:25],
               ALU.mult, reads=[bSTs[0]], writes=[bSTs[0]])
            em.dma("scalar", bass.AP(tiny_scr.tensor, 0, [[1, 8], [1, 1]]),
                   STATC[0:8, 26:27], reads=[bSTs[0]], writes=[B("tiny")], slow=True)
            em.dma("scalar", SMR[0:1, 0:8],
                   bass.AP(tiny_scr.tensor, 0, [[8, 1], [1, 8]]),
                   reads=[B("tiny")], writes=[bSMR], slow=True)
            act("scalar", SMR[0:1, 0:8], SMR[0:1, 0:8], AF.Sqrt, reads=[bSMR],
                writes=[bSMR])
            tt("vector", SMR[0:1, 8:8 + c.NP], bass.AP(SMR, 0, [[2048, 1], [2, c.NP]]),
               bass.AP(SMR, 1, [[2048, 1], [2, c.NP]]), ALU.max, reads=[bSMR],
               writes=[bSMR])
            ts("vector", SMR[0:1, 8:8 + c.NP], SMR[0:1, 8:8 + c.NP], -1.0, None,
               ALU.mult, reads=[bSMR], writes=[bSMR])
            mm(P3[:, 0:c.NP], ONESF[0:1, :], SMR[0:1, 8:8 + c.NP], start=True,
               stop=True, reads=[bSMR, B("ONESF")], writes=[bP3])
            cp("vector", NNB[:, 0:c.NP], P3[:, 0:c.NP], reads=[bP3], writes=[bNNB])

            for g in range(c.NG):
                gbase = s * S + g * c.AG
                pend = []

                def _emit_norm(g, p, slot):
                    orw = ORW[:, slot * c.AG:(slot + 1) * c.AG]
                    em.emit("vector", lambda e: e.reciprocal(
                        SMR[64:65, slot * 1024:slot * 1024 + 2 * c.AG],
                        SMR[64:65, slot * 1024:slot * 1024 + 2 * c.AG]),
                        reads=[bSMR], writes=[bSMR])
                    mm(P2A[:, 0:c.AG], IND64[64:65, 0:128],
                       SMR[64:65, slot * 1024:slot * 1024 + c.AG],
                       start=True, stop=False,
                       reads=[bSMR, B("IND64")], writes=[bP2])
                    mm(P2A[:, 0:c.AG], IND64[64:65, 128:256],
                       SMR[64:65, slot * 1024 + c.AG:slot * 1024 + 2 * c.AG],
                       start=False, stop=True,
                       reads=[bSMR, B("IND64")], writes=[bP2])
                    cp("vector", RRC[0:64, 0:c.AG], P2A[0:64, 0:c.AG], reads=[bP2],
                       writes=[bRRC])
                    tt("vector", OBF[0:64, p * c.AG:(p + 1) * c.AG],
                       orw[0:64, :], RRC[0:64, 0:c.AG], ALU.mult,
                       reads=[bORW, bRRC], writes=[bOBF])
                    tt("vector", OST[0:64, p * c.AG:(p + 1) * c.AG],
                       OST[0:64, p * c.AG:(p + 1) * c.AG],
                       P2A[64:128, 0:c.AG], ALU.mult,
                       reads=[bP2], writes=[bOST])

                for p in range(c.NP):
                    hA, hB = 2 * p, 2 * p + 1
                    for j in range(KC):
                        psc, bsc = (P0, bP0) if j % 2 == 0 else (P1, bP1)
                        pbh = j % 2
                        mm(psc[:, 0:c.AG],
                           KBF[0:64, p * S + j * 128:p * S + (j + 1) * 128],
                           QBF[0:64, p * S + g * c.AG:p * S + (g + 1) * c.AG],
                           start=True, stop=True, tp=(0, 0),
                           reads=[bKBF, bQBF], writes=[bsc])
                        mm(psc[:, 512:512 + c.AG],
                           KBF[64:128, p * S + j * 128:p * S + (j + 1) * 128],
                           QBF[64:128, p * S + g * c.AG:p * S + (g + 1) * c.AG],
                           start=True, stop=True, tp=(64, 0),
                           reads=[bKBF, bQBF], writes=[bsc])
                        if j == 2 and pend:
                            pp, ss = pend.pop(0)
                            _emit_norm(g, pp, ss)
                        if c.AG == 512:
                            act("scalar", PBF[:, pbh * 1024:pbh * 1024 + 1024],
                                psc[:, 0:1024], AF.Exp, bias=NNB[:, p:p + 1],
                                reads=[bsc, bNNB], writes=[bPBF[pbh]])
                        else:
                            act("scalar", PBF[:, pbh * 1024:pbh * 1024 + c.AG],
                                psc[:, 0:c.AG], AF.Exp, bias=NNB[:, p:p + 1],
                                reads=[bsc, bNNB], writes=[bPBF[pbh]])
                            act("scalar",
                                PBF[:, pbh * 1024 + 512:pbh * 1024 + 512 + c.AG],
                                psc[:, 512:512 + c.AG], AF.Exp, bias=NNB[:, p:p + 1],
                                reads=[bsc, bNNB], writes=[bPBF[pbh]])
                        if j >= 1:
                            jm = j - 1
                            pbm = jm % 2
                            mm(P3[0:65, 0:c.AG],
                               VTM[:, jm * VW + 65 * hA:jm * VW + 65 * hA + 65],
                               PBF[:, pbm * 1024:pbm * 1024 + c.AG],
                               start=(jm == 0), stop=False,
                               reads=[bVTM, bPBF[pbm]], writes=[bP3])
                            mm(P4[0:65, 0:c.AG],
                               VTM[:, jm * VW + 65 * hB:jm * VW + 65 * hB + 65],
                               PBF[:, pbm * 1024 + 512:pbm * 1024 + 512 + c.AG],
                               start=(jm == 0), stop=False,
                               reads=[bVTM, bPBF[pbm]], writes=[bP4])
                    jm = KC - 1
                    pbm = jm % 2
                    mm(P3[0:65, 0:c.AG],
                       VTM[:, jm * VW + 65 * hA:jm * VW + 65 * hA + 65],
                       PBF[:, pbm * 1024:pbm * 1024 + c.AG],
                       start=False, stop=True,
                       reads=[bVTM, bPBF[pbm]], writes=[bP3])
                    mm(P4[0:65, 0:c.AG],
                       VTM[:, jm * VW + 65 * hB:jm * VW + 65 * hB + 65],
                       PBF[:, pbm * 1024 + 512:pbm * 1024 + 512 + c.AG],
                       start=False, stop=True,
                       reads=[bVTM, bPBF[pbm]], writes=[bP4])
                    slot = p % 2
                    orw = ORW[:, slot * c.AG:(slot + 1) * c.AG]
                    cp("scalar", orw[0:65, :], P3[0:65, 0:c.AG], reads=[bP3],
                       writes=[bORW])
                    cp("scalar", SMR[64:65, slot * 1024:slot * 1024 + c.AG],
                       P3[64:65, 0:c.AG], reads=[bP3], writes=[bSMR])
                    cp("scalar", OST[0:65, p * c.AG:(p + 1) * c.AG],
                       P4[0:65, 0:c.AG], reads=[bP4], writes=[bOST])
                    cp("scalar", SMR[64:65, slot * 1024 + c.AG:slot * 1024 + 2 * c.AG],
                       P4[64:65, 0:c.AG], reads=[bP4], writes=[bSMR])
                    pend.append((p, slot))
                while pend:
                    pp, ss = pend.pop(0)
                    _emit_norm(g, pp, ss)
                em.dma("scalar",
                       bass.AP(OBF, 64 * QC * c.AG,
                               [[QC * c.AG, 64], [c.AG, c.NP], [1, c.AG]]),
                       bass.AP(OST, 0, [[c.NP * c.AG, 64], [c.AG, c.NP], [1, c.AG]]),
                       reads=[bOST], writes=[bOBF])
                if c.DBG and s == 0:
                    em.dma("gpsimd",
                           bass.AP(dbg_scr.tensor, g * 128 * QC * c.AG,
                                   [[QC * c.AG, 128], [1, QC * c.AG]]),
                           OBF[:, :], reads=[bOBF], writes=[B("dbg")])
                for tc in range(NCH):
                    py, bpy = (P0, bP0) if tc % 2 == 0 else (P1, bP1)
                    nh = max(1, D // 512)
                    dw = D // nh
                    for ci in range(QC):
                        for half in range(nh):
                            mm(py[:, half * dw:(half + 1) * dw],
                               OBF[:, ci * c.AG + tc * 128:ci * c.AG + (tc + 1) * 128],
                               WOUT[:, ci * D + half * dw:ci * D + (half + 1) * dw],
                               start=(ci == 0), stop=(ci == QC - 1),
                               reads=[bOBF, B("WOUT")], writes=[bpy])
                    k4 = tc % 4
                    o = 32 * k4
                    bSk = bSTs[k4]
                    bEP1c = B(f"EP1{s}_{tc}")
                    bEP2c = B(f"EP2{s}_{tc}")
                    e1 = EP1[:, tc * D:(tc + 1) * D]
                    e2 = EP2[:, tc * D:(tc + 1) * D]
                    xa, bxa = xchunk(gbase, tc)
                    tt("vector", e1, py[:, 0:D], BOREP[:], ALU.add,
                       reads=[bpy, B("BOREP")], writes=[bEP1c])
                    if c.DBG and s == 0 and g == 0:
                        em.dma("gpsimd",
                               bass.AP(dbg2_scr.tensor, tc * 128 * D, [[D, 128], [1, D]]),
                               e1, reads=[bEP1c], writes=[B("dbg2")])
                    ln_bn(e1, D, bEP1c, k4)
                    ln_sqrt(k4)
                    ln_recip(k4)
                    ln_apply(e2, e1, k4, [bEP1c], [bEP2c])
                    tt("vector", e1, e2, GREP[:], ALU.mult, reads=[bEP2c, B("GREP")],
                       writes=[bEP1c])
                    tt("vector", e2, e1, BREP[:], ALU.add, reads=[bEP1c, B("BREP")],
                       writes=[bEP2c])
                    for j in range(DC):
                        transp(PT[:, (j % 8) * 128:(j % 8) * 128 + 128],
                               HBF[:, j * S + g * c.AG + tc * 128:
                                   j * S + g * c.AG + (tc + 1) * 128],
                               reads=[bHBF], writes=[bPT])
                    tt("vector", e1, e2, PT[:, 0:D], ALU.add, reads=[bEP2c, bPT],
                       writes=[bEP1c])
                    tt("vector", e2, e1, GAR[:], ALU.mult, reads=[bEP1c, bGAR],
                       writes=[bEP2c])
                    tt("gpsimd", xa, xa, e2, ALU.add, reads=[bEP2c], writes=[bxa])
                    em.dma("scalar",
                           bass.AP(xmid_scr.tensor, (gbase + tc * 128) * D,
                                   [[D, 128], [1, D]]),
                           xa, reads=[bxa], writes=[bXMS])

            em.barrier()
            ar.close()

            br = contextlib.ExitStack()
            W1S = br.enter_context(nc.sbuf_tensor(f"W1S{s}", [128, DC * HID], BF16))
            CB1C1 = br.enter_context(nc.sbuf_tensor(f"CB1{s}", [128, D], BF16))
            SBC1 = br.enter_context(nc.sbuf_tensor(f"SBC1{s}", [128, D], BF16))
            ZC = br.enter_context(nc.sbuf_tensor(f"ZC{s}", [128, (c.T1 // 128) * D], F32))
            DS1 = br.enter_context(nc.sbuf_tensor(f"DS1{s}", [128, D], F32))
            DS2 = br.enter_context(nc.sbuf_tensor(f"DS2{s}", [128, D], BF16))
            DI8 = br.enter_context(nc.sbuf_tensor(f"DI8{s}", [128, D], I8))
            XQ1 = br.enter_context(nc.sbuf_tensor(f"XQ1{s}", [128, 3 * DC * c.T1], BF16))
            XMB = br.enter_context(nc.sbuf_tensor(f"XMB{s}", [128, 2 * (c.T1 // 128) * D], F32))
            HA = br.enter_context(nc.sbuf_tensor(f"HA{s}", [128, (c.T1 // 128) * HID], BF16))
            HBt = br.enter_context(nc.sbuf_tensor(f"HBt{s}", [128, (c.T1 // 128) * HID], BF16))
            HI8 = br.enter_context(nc.sbuf_tensor(f"HI8{s}", [128, HID], I8))
            XQ2S = br.enter_context(nc.sbuf_tensor(f"XQ2S{s}", [128, 2048], BF16))

            bW1, bCB, bSB = B(f"W1S{s}"), B(f"CB1{s}"), B(f"SBC1{s}")
            bXQ1, bXMB = B(f"XQ1{s}"), B(f"XMB{s}")
            bXQscr, bS2 = B("xq2_scr"), B("S2COL")

            bcast("gpsimd", CB1C1[:], mod_scr.tensor, (s * 6 + 4) * D, D,
                  reads=[B("mod_scr")], writes=[bCB])
            ts("vector", CB1C1[:], CB1C1[:], 1.0, None, ALU.add, reads=[bCB],
               writes=[bCB])
            tt("vector", CB1C1[:], CB1C1[:], C1REP[:], ALU.mult, reads=[B("C1REP")],
               writes=[bCB])
            bcast("sync", DS1[:, 0:D], mod_scr.tensor, (s * 6 + 3) * D, D,
                  reads=[B("mod_scr")], writes=[B(f"DS1{s}_0")])
            tt("vector", SBC1[:], DS1[:, 0:D], C1REP[:], ALU.mult,
               reads=[B(f"DS1{s}_0"), B("C1REP")], writes=[bSB])
            for hh, q_ in ((0, "sync"), (1, "gpsimd")):
                em.dma(q_,
                       bass.AP(W1S, hh * (DC // 2) * HID,
                               [[DC * HID, 128], [HID, DC // 2], [1, HID]]),
                       bass.AP(w1q_scr.tensor, hh * (DC // 2) * 128 * HID,
                               [[HID, 128], [128 * HID, DC // 2], [1, HID]]),
                       reads=[B("qS0")], writes=[bW1])

            def xq2_finish(t1, tc):
                bHBc = B(f"HBt{s}_{tc}")
                ua = HBt[:, tc * HID:(tc + 1) * HID]
                g8 = min(8, HC)
                for quad in range(HC // g8):
                    for j in range(g8):
                        transp(PT[:, j * 128:(j + 1) * 128],
                               ua[:, (quad * g8 + j) * 128:(quad * g8 + j + 1) * 128],
                               reads=[bHBc], writes=[bPT])
                    qs = quad % 2
                    bXQ2c = B(f"XQ2S{s}_{qs}")
                    cp("scalar",
                       bass.AP(XQ2S, qs * 1024, [[2048, 128], [128, g8], [1, 128]]),
                       bass.AP(PT, 0, [[1024, 128], [128, g8], [1, 128]]),
                       reads=[bPT], writes=[bXQ2c])
                    em.dma("scalar",
                           bass.AP(xq2_scr.tensor,
                                   (quad * g8 * 128) * S + t1 * c.T1 + tc * 128,
                                   [[S, 128], [128 * S, g8], [1, 128]]),
                           bass.AP(XQ2S, qs * 1024, [[2048, 128], [128, g8], [1, 128]]),
                           reads=[bXQ2c], writes=[bXQscr])

            NC1 = c.T1 // 128

            def b1_dstage(t1):
                ts_ = t1 % 2
                sl = t1 % 3
                tb = s * S + t1 * c.T1
                em.dma("sync",
                       bass.AP(XMB, ts_ * NC1 * D,
                               [[2 * NC1 * D, 128], [D, NC1], [1, D]]),
                       bass.AP(xmid_scr.tensor, tb * D,
                               [[D, 128], [128 * D, NC1], [1, D]]),
                       reads=[bXMS], writes=[B(f"XMB{s}_{ts_}")])
                for tc in range(NC1):
                    k4 = (t1 * NC1 + tc) % 4
                    ln_bn(XMB[:, (ts_ * NC1 + tc) * D:(ts_ * NC1 + tc + 1) * D], D,
                          B(f"XMB{s}_{ts_}"), k4)
                for tc in range(NC1):
                    ln_sqrt((t1 * NC1 + tc) % 4)
                for tc in range(NC1):
                    ln_recip((t1 * NC1 + tc) % 4)
                for tc in range(NC1):
                    k4 = (t1 * NC1 + tc) % 4
                    o = 32 * k4
                    bSk = bSTs[k4]
                    tcol = t1 * NC1 + tc
                    bZCc = B(f"ZC{s}_{tc}")
                    bD1c, bD2c = B(f"DS1{s}"), B(f"DS2{s}")
                    bDIc = B(f"DI8{s}")
                    xa = XMB[:, (ts_ * NC1 + tc) * D:(ts_ * NC1 + tc + 1) * D]
                    za = ZC[:, tc * D:(tc + 1) * D]
                    d1 = DS1[:, 0:D]
                    d2 = DS2[:, 0:D]
                    di = DI8[:, 0:D]
                    xqt = d2
                    ak = 22 + (tcol // 4) % 2
                    ln_apply(za, xa, k4, [B(f"XMB{s}_{ts_}")], [bZCc])
                    tt("vector", za, za, CB1C1[:], ALU.mult, reads=[bCB], writes=[bZCc])
                    tt("vector", za, za, SBC1[:], ALU.add, reads=[bSB], writes=[bZCc])
                    tt("vector", d1, za, C1IREP[:], ALU.mult,
                       reads=[bZCc, B("C1IREP")], writes=[bD1c])
                    act("scalar", d2, d1, AF.Square, accum=STATC[:, o + 15:o + 16],
                        reads=[bD1c], writes=[bD2c, bSk])
                    ts("vector", STATC[:, o + 15:o + 16], STATC[:, o + 15:o + 16],
                       1.0 / D, EPS_NORM, ALU.mult, ALU.add, reads=[bSk], writes=[bSk])
                    em.emit("vector", lambda e, o=o: e.reciprocal(
                        STATC[:, o + 16:o + 17], STATC[:, o + 15:o + 16]),
                        reads=[bSk], writes=[bSk])
                    act("scalar", STATC[:, o + 16:o + 17], STATC[:, o + 16:o + 17],
                        AF.Sqrt, reads=[bSk], writes=[bSk])
                    red("vector", STATC[:, o + 17:o + 18], za, ALU.max, absval=True,
                        reads=[bZCc], writes=[bSk])
                    tt("vector", STATC[:, o + 17:o + 18], STATC[:, o + 17:o + 18],
                       STATC[:, o + 16:o + 17], ALU.mult, reads=[bSk], writes=[bSk])
                    ts("vector", STATC[:, o + 17:o + 18], STATC[:, o + 17:o + 18],
                       EPS_Q, None, ALU.max, reads=[bSk], writes=[bSk])
                    em.emit("vector", lambda e, o=o: e.reciprocal(
                        STATC[:, o + 18:o + 19], STATC[:, o + 17:o + 18]),
                        reads=[bSk], writes=[bSk])
                    tt("vector", STATC[:, o + 19:o + 20], STATC[:, o + 18:o + 19],
                       STATC[:, o + 16:o + 17], ALU.mult, reads=[bSk], writes=[bSk])
                    ts("vector", STATC[:, o + 19:o + 20], STATC[:, o + 19:o + 20],
                       127.0, None, ALU.mult, reads=[bSk], writes=[bSk])
                    act("scalar", za, za, AF.Identity, bias=ZCOL[:, 0:1],
                        scale=STATC[:, o + 19:o + 20],
                        reads=[bSk, bZCc, B("ZCOL")], writes=[bZCc])
                    ts("vector", STATC[:, o + 20:o + 21], STATC[:, o + 17:o + 18],
                       1.0 / 127.0, None, ALU.mult, reads=[bSk], writes=[bSk])
                    tt("vector", STATC[:, o + 20:o + 21], STATC[:, o + 20:o + 21],
                       SWCOL[:, 0:1], ALU.mult, reads=[B("SWCOL")], writes=[bSk])
                    tt("vector", STATC[:, o + ak:o + ak + 1], STATC[:, o + 20:o + 21],
                       STATC[:, o + 20:o + 21], ALU.mult, reads=[bSk], writes=[bAK])
                    tt("vector", STATC[:, o + ak:o + ak + 1], STATC[:, o + ak:o + ak + 1],
                       ABCOL[:, 0:1], ALU.mult, reads=[B("ABCOL")], writes=[bAK])
                    cp("scalar", di, za, reads=[bZCc], writes=[bDIc])
                    tt("vector", d1, za, di, ALU.subtract, reads=[bZCc, bDIc],
                       writes=[bD1c])
                    stt("vector", d2, d1, 0.5, di, ALU.is_ge, ALU.add,
                        reads=[bD1c, bDIc], writes=[bD2c])
                    stt("vector", xqt, d1, -0.5, d2, ALU.is_lt, ALU.subtract,
                        reads=[bD1c, bD2c], writes=[bD2c])
                    for j in range(DC):
                        transp(PT[:, (j % 8) * 128:(j % 8) * 128 + 128],
                               xqt[:, j * 128:(j + 1) * 128],
                               reads=[bD2c], writes=[bPT])
                    g4 = min(4, DC)
                    for half in range(DC // g4):
                        cp("scalar",
                           bass.AP(XQ1, sl * DC * c.T1 + half * g4 * c.T1 + tc * 128,
                                   [[3 * DC * c.T1, 128], [c.T1, g4], [1, 128]]),
                           bass.AP(PT, half * g4 * 128, [[1024, 128], [128, g4], [1, 128]]),
                           reads=[bPT], writes=[B(f"XQ1{s}_{sl}")])

            def b1_mstage(t1):
                sl = t1 % 3
                for tc in range(NC1):
                    bHAc, bHBc = B(f"HA{s}_{tc}"), B(f"HBt{s}_{tc}")
                    for hb2 in range(0, HID // 512, 2):
                        pairs = [(0, hb2)]
                        if hb2 + 1 < HID // 512:
                            pairs.append((1, hb2 + 1))
                        for dc in range(DC):
                            for pi, hb in pairs:
                                ph, bph = (P0, bP0) if pi == 0 else (P1, bP1)
                                mm(ph[:, 0:512],
                                   XQ1[:, sl * DC * c.T1 + dc * c.T1 + tc * 128:
                                       sl * DC * c.T1 + dc * c.T1 + (tc + 1) * 128],
                                   W1S[:, dc * HID + hb * 512:dc * HID + (hb + 1) * 512],
                                   start=(dc == 0), stop=(dc == DC - 1),
                                   reads=[B(f"XQ1{s}_{sl}"), bW1], writes=[bph])
                        for pi, hb in pairs:
                            ph, bph = (P0, bP0) if pi == 0 else (P1, bP1)
                            hs = HA[:, tc * HID + hb * 512:tc * HID + (hb + 1) * 512]
                            act("scalar", hs, ph[:, 0:512], AF.Relu, reads=[bph],
                                writes=[bHAc])
                            us = HBt[:, tc * HID + hb * 512:tc * HID + (hb + 1) * 512]
                            act("scalar", us, hs, AF.Square, reads=[bHAc],
                                writes=[bHBc])
                    k4 = (t1 * NC1 + tc) % 4
                    o = 32 * k4
                    tcol = t1 * NC1 + tc
                    ak = 22 + (tcol // 4) % 2
                    bSm = bSTm[k4]
                    bHAc, bHBc, bHIc = B(f"HA{s}_{tc}"), B(f"HBt{s}_{tc}"), B(f"HI8{s}")
                    ua = HBt[:, tc * HID:(tc + 1) * HID]
                    aa = HA[:, tc * HID:(tc + 1) * HID]
                    ia = HI8[:, 0:HID]
                    act("scalar", aa, ua, AF.Identity, bias=ABCOL[:, 1:2],
                        scale=STATC[:, o + ak:o + ak + 1],
                        reads=[bHBc, bAK, B("ABCOL")], writes=[bHAc])
                    act("scalar", ua, aa, AF.Square, accum=STATC[:, o + 24:o + 25],
                        reads=[bHAc], writes=[bHBc, bSm])
                    ts("vector", STATC[:, o + 24:o + 25], STATC[:, o + 24:o + 25],
                       1.0 / HID, EPS_NORM, ALU.mult, ALU.add, reads=[bSm], writes=[bSm])
                    em.emit("vector", lambda e, o=o: e.reciprocal(
                        STATC[:, o + 25:o + 26], STATC[:, o + 24:o + 25]),
                        reads=[bSm], writes=[bSm])
                    act("scalar", STATC[:, o + 25:o + 26], STATC[:, o + 25:o + 26],
                        AF.Sqrt, reads=[bSm], writes=[bSm])
                    tt("vector", ua, aa, C2REP[:], ALU.mult, reads=[bHAc, B("C2REP")],
                       writes=[bHBc])
                    red("vector", STATC[:, o + 26:o + 27], ua, ALU.max, absval=True,
                        reads=[bHBc], writes=[bSm])
                    tt("vector", STATC[:, o + 26:o + 27], STATC[:, o + 26:o + 27],
                       STATC[:, o + 25:o + 26], ALU.mult, reads=[bSm], writes=[bSm])
                    ts("vector", STATC[:, o + 26:o + 27], STATC[:, o + 26:o + 27],
                       EPS_Q, None, ALU.max, reads=[bSm], writes=[bSm])
                    em.emit("vector", lambda e, o=o: e.reciprocal(
                        STATC[:, o + 27:o + 28], STATC[:, o + 26:o + 27]),
                        reads=[bSm], writes=[bSm])
                    tt("vector", STATC[:, o + 28:o + 29], STATC[:, o + 27:o + 28],
                       STATC[:, o + 25:o + 26], ALU.mult, reads=[bSm], writes=[bSm])
                    ts("vector", STATC[:, o + 28:o + 29], STATC[:, o + 28:o + 29],
                       127.0, None, ALU.mult, reads=[bSm], writes=[bSm])
                    act("scalar", aa, ua, AF.Identity, bias=ZCOL[:, 0:1],
                        scale=STATC[:, o + 28:o + 29],
                        reads=[bHBc, bSm, B("ZCOL")], writes=[bHAc])
                    ts("vector", S2COL[:, tcol:tcol + 1], STATC[:, o + 26:o + 27],
                       1.0 / 127.0, None, ALU.mult, reads=[bSm], writes=[bS2])
                    tt("vector", S2COL[:, tcol:tcol + 1], S2COL[:, tcol:tcol + 1],
                       SWCOL[:, 1:2], ALU.mult, reads=[B("SWCOL")], writes=[bS2])
                    cp("scalar", ia, aa, reads=[bHAc], writes=[bHIc])
                    tt("vector", ua, aa, ia, ALU.subtract, reads=[bHAc, bHIc],
                       writes=[bHBc])
                    stt("vector", aa, ua, 0.5, ia, ALU.is_ge, ALU.add,
                        reads=[bHBc, bHIc], writes=[bHAc])
                    stt("vector", ua, ua, -0.5, aa, ALU.is_lt, ALU.subtract,
                        reads=[bHAc], writes=[bHBc])

            b1_dstage(0)
            if c.NT1 > 1:
                b1_dstage(1)
            for t1 in range(c.NT1):
                if t1 >= 1:
                    for tc in range(NC1):
                        xq2_finish(t1 - 1, tc)
                if t1 + 2 < c.NT1:
                    b1_dstage(t1 + 2)
                b1_mstage(t1)
            for tc in range(c.T1 // 128):
                xq2_finish(c.NT1 - 1, tc)

            em.barrier()
            br.close()

            b2 = contextlib.ExitStack()
            W2S = b2.enter_context(nc.sbuf_tensor(f"W2S{s}", [128, HC * D], BF16))
            GBR = b2.enter_context(nc.sbuf_tensor(f"GBR{s}", [128, D], BF16))
            GB2 = b2.enter_context(nc.sbuf_tensor(f"GB2{s}", [128, D], F32))
            XQ2F = b2.enter_context(nc.sbuf_tensor(f"XQ2F{s}", [128, HC * c.T2], BF16))
            XMT2 = b2.enter_context(nc.sbuf_tensor(f"XMT2{s}", [128, (c.T2 // 128) * D], F32))
            OUTS = b2.enter_context(nc.sbuf_tensor(f"OUTS{s}", [128, (c.T2 // 128) * D], F32))
            bW2, bGBR, bGB2 = B(f"W2S{s}"), B(f"GBR{s}"), B(f"GB2{s}")
            bXM2 = B(f"XMT2{s}")
            bXQFh = [B(f"XQ2F{s}_0"), B(f"XQ2F{s}_1")]

            bcast("sync", GB2[:], mod_scr.tensor, (s * 6 + 5) * D, D,
                  reads=[B("mod_scr")], writes=[bGB2])
            ts("vector", GBR[:], GB2[:], 0.0, 2.0, ALU.is_ge, ALU.mult, reads=[bGB2],
               writes=[bGBR])
            ts("vector", GBR[:], GBR[:], -1.0, None, ALU.add, reads=[bGBR],
               writes=[bGBR])
            act("scalar", GB2[:], GB2[:], AF.Abs, reads=[bGB2], writes=[bGB2])
            ts("vector", GB2[:], GB2[:], 0.25, None, ALU.max, reads=[bGB2],
               writes=[bGB2])
            tt("vector", GB2[:], GB2[:], GBR[:], ALU.mult, reads=[bGBR], writes=[bGB2])
            cp("vector", GBR[:], GB2[:], reads=[bGB2], writes=[bGBR])
            for hh, q_ in ((0, "sync"), (1, "gpsimd")):
                em.dma(q_,
                       bass.AP(W2S, hh * (HC // 2) * D,
                               [[HC * D, 128], [D, HC // 2], [1, D]]),
                       bass.AP(w2q_scr.tensor, hh * (HC // 2) * 128 * D,
                               [[D, 128], [128 * D, HC // 2], [1, D]]),
                       reads=[B("qS1")], writes=[bW2])

            HC2 = HC // 2
            for t2 in range(c.NT2):
                tb = s * S + t2 * c.T2
                for hh in range(2):
                    em.dma("sync" if hh == 0 else "gpsimd",
                           bass.AP(XQ2F, hh * HC2 * c.T2,
                                   [[HC * c.T2, 128], [c.T2, HC2], [1, c.T2]]),
                           bass.AP(xq2_scr.tensor, hh * HC2 * 128 * S + t2 * c.T2,
                                   [[S, 128], [128 * S, HC2], [1, c.T2]]),
                           reads=[bXQscr], writes=[bXQFh[hh]])
                em.dma("scalar",
                       bass.AP(XMT2, 0, [[(c.T2 // 128) * D, 128], [D, c.T2 // 128], [1, D]]),
                       bass.AP(xmid_scr.tensor, tb * D,
                               [[D, 128], [128 * D, c.T2 // 128], [1, D]]),
                       reads=[bXMS], writes=[bXM2])
                for tc in range(c.T2 // 128):
                    py, bpy = (P0, bP0) if tc % 2 == 0 else (P1, bP1)
                    nh = max(1, D // 512)
                    dw = D // nh
                    for hc in range(HC):
                        for half in range(nh):
                            mm(py[:, half * dw:(half + 1) * dw],
                               XQ2F[:, hc * c.T2 + tc * 128:hc * c.T2 + (tc + 1) * 128],
                               W2S[:, hc * D + half * dw:hc * D + (half + 1) * dw],
                               start=(hc == 0), stop=(hc == HC - 1),
                               reads=[bXQFh[hc // HC2], bW2], writes=[bpy])
                    oa = OUTS[:, tc * D:(tc + 1) * D]
                    bOUc = B(f"OUTS{s}_{tc % 2}")
                    tcol = (t2 * c.T2) // 128 + tc
                    ts("vector", oa, py[:, 0:D], S2COL[:, tcol:tcol + 1], None,
                       ALU.mult, reads=[bpy, bS2], writes=[bOUc])
                    tt("vector", oa, oa, GBR[:], ALU.mult, reads=[bGBR], writes=[bOUc])
                    tt("vector", oa, oa, XMT2[:, tc * D:(tc + 1) * D], ALU.add,
                       reads=[bXM2], writes=[bOUc])
                    em.dma("scalar",
                           bass.AP(out_t.tensor, (tb + tc * 128) * D,
                                   [[D, 128], [1, D]]),
                           oa, reads=[bOUc], writes=[B("out")])

            em.barrier()
            b2.close()

        em.finish()

    return nc


_NC_CACHE = {}


def _get_nc():
    if "full" not in _NC_CACHE:
        _NC_CACHE["full"] = build_program(CFG)
    return _NC_CACHE["full"]


def make_inputs(core, cfg, L0_attn_x, L0_attn_W_qk, L0_attn_b_qk, L0_attn_W_v,
                L0_attn_b_v, L0_attn_W_out, L0_attn_b_out, L0_attn_ln_gamma,
                L0_attn_ln_beta, L0_ffn_norm1_w, L0_ffn_w1, L0_ffn_act_scale,
                L0_ffn_act_bias, L0_ffn_norm2_w, L0_ffn_w2, L0_c, L0_wmod):
    c = cfg
    D, QK = c.D, c.QK
    f32 = np.float32
    perm = np.arange(QK)
    h_idx, d_idx = perm // c.DH, perm % c.DH
    qcols = h_idx * 2 * c.DH + 2 * d_idx
    kcols = qcols + 1
    W_qk = np.asarray(L0_attn_W_qk, f32)
    b_qk = np.asarray(L0_attn_b_qk, f32)
    wqk_p = np.concatenate([W_qk[:, qcols], W_qk[:, kcols]], axis=1)
    bqk2 = np.stack([b_qk[qcols], b_qk[kcols]], axis=0)
    b0 = core * c.NSEQ
    x = np.asarray(L0_attn_x, f32)
    return {
        "x": np.ascontiguousarray(x[b0:b0 + c.NSEQ].reshape(c.NSEQ * c.S, D)),
        "wqk": np.ascontiguousarray(wqk_p),
        "bqk2": np.ascontiguousarray(bqk2),
        "wv": np.ascontiguousarray(np.asarray(L0_attn_W_v, f32)),
        "bv": np.ascontiguousarray(np.asarray(L0_attn_b_v, f32).reshape(1, QK)),
        "wout": np.ascontiguousarray(np.asarray(L0_attn_W_out, f32)),
        "bout": np.ascontiguousarray(np.asarray(L0_attn_b_out, f32).reshape(1, D)),
        "ln_g": np.ascontiguousarray(np.asarray(L0_attn_ln_gamma, f32).reshape(1, D)),
        "ln_b": np.ascontiguousarray(np.asarray(L0_attn_ln_beta, f32).reshape(1, D)),
        "n1w": np.ascontiguousarray(np.asarray(L0_ffn_norm1_w, f32).reshape(1, D)),
        "w1t": np.ascontiguousarray(np.asarray(L0_ffn_w1, f32).T),
        "actsc": np.ascontiguousarray(np.array(
            [[np.asarray(L0_ffn_act_scale, f32).reshape(()),
              np.asarray(L0_ffn_act_bias, f32).reshape(())]], f32)),
        "n2w": np.ascontiguousarray(np.asarray(L0_ffn_norm2_w, f32).reshape(1, c.HID)),
        "w2t": np.ascontiguousarray(np.asarray(L0_ffn_w2, f32).T),
        "c_t": np.ascontiguousarray(np.asarray(L0_c, f32)[b0:b0 + c.NSEQ].T),
        "wmod": np.ascontiguousarray(np.asarray(L0_wmod, f32)),
    }


def kernel(**inputs):
    c = CFG
    nc = _get_nc()
    in_maps = [make_inputs(core, c, **inputs) for core in range(N_CORES)]
    res = run_bass_kernel_spmd(nc, in_maps, core_ids=list(range(N_CORES)))
    outs = [np.asarray(r["out"]).reshape(c.NSEQ, c.S, c.D) for r in res.results]
    return np.concatenate(outs, axis=0)


# version 20
# speedup vs baseline: 1.0554x; 1.0020x over previous
import contextlib
import numpy as np
import concourse.bass as bass
import concourse.mybir as mybir
from concourse.bass_utils import run_bass_kernel_spmd

F32 = mybir.dt.float32
BF16 = mybir.dt.bfloat16
I8 = mybir.dt.int8
AF = mybir.ActivationFunctionType
ALU = mybir.AluOpType
AX = mybir.AxisListType

RSQRT2 = float(2.0 ** (-0.5))
LN_EPS = 1e-5
EPS_NORM = 1e-8
EPS_Q = 1e-5

N_CORES = 8


class Cfg:
    def __init__(self, S=2048, D=1024, HID=4096, NSEQ=4, H=8, AG=512, T1=256, T2=256):
        self.S, self.D, self.HID, self.NSEQ, self.H = S, D, HID, NSEQ, H
        self.DH = 64
        self.AG, self.T1, self.T2 = AG, T1, T2
        self.DC = D // 128
        self.QK = self.H * self.DH
        self.QC = self.QK // 128
        self.HC = HID // 128
        self.KC = S // 128
        self.NG = S // AG
        self.NT1 = S // T1
        self.NT2 = S // T2
        self.NP = self.H // 2
        self.VW = self.H * 65 + 8
        self.DBG = False


CFG = Cfg()


class Buf:
    __slots__ = ("name", "writer", "readers")

    def __init__(self, name):
        self.name = name
        self.writer = None
        self.readers = []


class Em:
    ENGINES = ("tensor", "vector", "scalar", "gpsimd", "sync")
    NSLOT = {"sync": 12, "scalar": 8, "gpsimd": 4, "vector": 2}

    def __init__(self, nc):
        self.nc = nc
        self.TRACKS = list(self.ENGINES)
        self.dma_slots = {}
        for q, n in self.NSLOT.items():
            self.dma_slots[q] = [f"{q}_dma{i}" for i in range(n)]
            self.TRACKS += self.dma_slots[q]
        self.sems = {}
        self.cnt = {t: 0 for t in self.TRACKS}
        self.waited = {e: {t: 0 for t in self.TRACKS} for e in self.ENGINES}
        self.eng = {"tensor": nc.tensor, "vector": nc.vector, "scalar": nc.scalar,
                    "gpsimd": nc.gpsimd, "sync": nc.sync}
        self.dma_idx = {q: 0 for q in self.NSLOT}
        self.bufs = {}

    def B(self, name):
        if name not in self.bufs:
            self.bufs[name] = Buf(name)
        return self.bufs[name]

    def init_sems(self, ctx):
        for t in self.TRACKS:
            self.sems[t] = ctx.enter_context(self.nc.semaphore(f"sem_{t}"))

    def _wait(self, eng, dep_eng, dep_cnt):
        if dep_cnt <= 0 or self.waited[eng][dep_eng] >= dep_cnt:
            return
        self.eng[eng].wait_ge(self.sems[dep_eng], dep_cnt)
        self.waited[eng][dep_eng] = dep_cnt

    def emit(self, eng, fn, reads=(), writes=(), inc=1, track=None):
        if track is None:
            track = eng
        for b in reads:
            if b.writer is not None:
                self._wait(eng, b.writer[0], b.writer[1])
        for b in writes:
            if b.writer is not None:
                self._wait(eng, b.writer[0], b.writer[1])
            for tok in b.readers:
                self._wait(eng, tok[0], tok[1])
        inst = fn(self.eng[eng])
        self.cnt[track] += inc
        inst.then_inc(self.sems[track], inc)
        tok = (track, self.cnt[track])
        for b in reads:
            b.readers.append(tok)
        for b in writes:
            b.writer = tok
            b.readers = []
        return tok

    def dma(self, queue, out_ap, in_ap, reads=(), writes=(), slow=False):
        slot = self.dma_slots[queue][self.dma_idx[queue] % self.NSLOT[queue]]
        self.dma_idx[queue] += 1
        self._wait(queue, slot, self.cnt[slot])
        if slow:
            return self.emit(queue, lambda e: e.dma_start(
                out=out_ap, in_=in_ap, allow_slow_non_contiguous=True),
                reads=reads, writes=writes, inc=16, track=slot)
        return self.emit(queue, lambda e: e.dma_start(out=out_ap, in_=in_ap),
                         reads=reads, writes=writes, inc=16, track=slot)

    def barrier(self):
        snap = dict(self.cnt)
        for e in self.ENGINES:
            for t in self.TRACKS:
                if t != e:
                    self._wait(e, t, snap[t])

    def finish(self):
        snap = dict(self.cnt)
        for t in self.TRACKS:
            if t != "gpsimd":
                self._wait("gpsimd", t, snap[t])


def build_program(c: Cfg):
    nc = bass.Bass("TRN2", target_bir_lowering=False)
    S, D, HID, NSEQ, H = c.S, c.D, c.HID, c.NSEQ, c.H
    QK, DC, QC, HC, KC, VW = c.QK, c.DC, c.QC, c.HC, c.KC, c.VW

    def din(name, shape, dt=F32):
        return nc.dram_tensor(name, list(shape), dt, kind="ExternalInput").ap()

    x_in = din("x", (NSEQ * S, D))
    wqk = din("wqk", (D, 2 * QK))
    bqk2 = din("bqk2", (2, QK))
    wv = din("wv", (D, QK))
    bv = din("bv", (1, QK))
    wout = din("wout", (QK, D))
    bout = din("bout", (1, D))
    ln_g = din("ln_g", (1, D))
    ln_b = din("ln_b", (1, D))
    n1w = din("n1w", (1, D))
    w1t = din("w1t", (D, HID))
    actsc = din("actsc", (1, 2))
    n2w = din("n2w", (1, HID))
    w2t = din("w2t", (HID, D))
    c_t = din("c_t", (D, NSEQ))
    wmod = din("wmod", (D, 6 * D))

    out_t = nc.dram_tensor("out", [NSEQ * S, D], F32, kind="ExternalOutput").ap()
    xmid_scr = nc.dram_tensor("xmid_scr", [NSEQ * S, D], F32).ap()
    mod_scr = nc.dram_tensor("mod_scr", [NSEQ * 6, D], F32).ap()
    w1q_scr = nc.dram_tensor("w1q_scr", [D, HID], BF16).ap()
    w2q_scr = nc.dram_tensor("w2q_scr", [HID, D], BF16).ap()
    xq2_scr = nc.dram_tensor("xq2_scr", [HID, S], BF16).ap()
    tiny_scr = nc.dram_tensor("tiny_scr", [64], F32).ap()
    dbg_scr = nc.dram_tensor("dbg_scr", [c.NG * 128, QC * c.AG], F32).ap() if c.DBG else None
    dbg2_scr = nc.dram_tensor("dbg2_scr", [(c.AG // 128) * 128, D], F32).ap() if c.DBG else None

    ctx = contextlib.ExitStack()
    em = Em(nc)

    with ctx:
        em.init_sems(ctx)
        B = em.B

        WQK = ctx.enter_context(nc.sbuf_tensor("WQK", [128, DC * 2 * QK], BF16))
        WV = ctx.enter_context(nc.sbuf_tensor("WV", [128, DC * QK], BF16))
        WOUT = ctx.enter_context(nc.sbuf_tensor("WOUT", [128, QC * D], BF16))
        IDN = ctx.enter_context(nc.sbuf_tensor("IDN", [128, 128], BF16))
        ONESF = ctx.enter_context(nc.sbuf_tensor("ONESF", [1, 128], F32))
        ONESC = ctx.enter_context(nc.sbuf_tensor("ONESC", [128, 1], F32))
        EPSCOL = ctx.enter_context(nc.sbuf_tensor("EPSCOL", [128, 1], F32))
        ZCOL = ctx.enter_context(nc.sbuf_tensor("ZCOL", [128, 1], F32))
        IND64 = ctx.enter_context(nc.sbuf_tensor("IND64", [128, 256], F32))
        BQS = ctx.enter_context(nc.sbuf_tensor("BQS", [128, QC], F32))
        BKS = ctx.enter_context(nc.sbuf_tensor("BKS", [128, QC], F32))
        BVC = ctx.enter_context(nc.sbuf_tensor("BVC", [128, QC], BF16))
        BOREP = ctx.enter_context(nc.sbuf_tensor("BOREP", [128, D], BF16))
        GREP = ctx.enter_context(nc.sbuf_tensor("GREP", [128, D], BF16))
        BREP = ctx.enter_context(nc.sbuf_tensor("BREP", [128, D], BF16))
        C1REP = ctx.enter_context(nc.sbuf_tensor("C1REP", [128, D], F32))
        C1IREP = ctx.enter_context(nc.sbuf_tensor("C1IREP", [128, D], F32))
        C2REP = ctx.enter_context(nc.sbuf_tensor("C2REP", [128, HID], BF16))
        ABCOL = ctx.enter_context(nc.sbuf_tensor("ABCOL", [128, 2], F32))
        SWCOL = ctx.enter_context(nc.sbuf_tensor("SWCOL", [128, 2], F32))
        IND8S = ctx.enter_context(nc.sbuf_tensor("IND8S", [128, QC * 8], BF16))
        NNB = ctx.enter_context(nc.sbuf_tensor("NNB", [128, c.NP], F32))
        S2COL = ctx.enter_context(nc.sbuf_tensor("S2COL", [128, S // 128], F32))
        STATC = ctx.enter_context(nc.sbuf_tensor("STATC", [128, 160], F32))

        P0 = ctx.enter_context(nc.psum_tensor("P0", [128, 1024], F32))
        P1 = ctx.enter_context(nc.psum_tensor("P1", [128, 1024], F32))
        P2A = ctx.enter_context(nc.psum_tensor("P2A", [128, 512], F32))
        P3 = ctx.enter_context(nc.psum_tensor("P3", [128, 512], F32))
        P4 = ctx.enter_context(nc.psum_tensor("P4", [128, 512], F32))
        PT = ctx.enter_context(nc.psum_tensor("PT", [128, 1024], BF16))
        bP0, bP1, bP2, bP3, bP4 = B("P0"), B("P1"), B("P2A"), B("P3"), B("P4")
        bPT = B("PT")
        bSTs = [B(f"STATC{k}") for k in range(4)]
        bSTm = [B(f"STATM{k}") for k in range(4)]
        bAK = B("AKCOLS")
        bST = B("STATC")

        def ts(eng, out, in0, s1, s2, op0, op1=None, reads=(), writes=()):
            if op1 is None:
                em.emit(eng, lambda e: e.tensor_scalar(out, in0, s1, None, op0=op0),
                        reads=reads, writes=writes)
            else:
                em.emit(eng, lambda e: e.tensor_scalar(out, in0, s1, s2, op0=op0, op1=op1),
                        reads=reads, writes=writes)

        def tt(eng, out, in0, in1, op, reads=(), writes=()):
            em.emit(eng, lambda e: e.tensor_tensor(out, in0, in1, op=op),
                    reads=reads, writes=writes)

        def stt(eng, out, in0, s, in1, op0, op1, reads=(), writes=()):
            em.emit(eng, lambda e: e.scalar_tensor_tensor(out, in0, s, in1, op0=op0,
                                                          op1=op1),
                    reads=reads, writes=writes)

        def act(eng, out, in0, func, bias=0.0, scale=1.0, accum=None, reads=(), writes=()):
            if accum is None:
                em.emit(eng, lambda e: e.activation(out, in0, func, bias=bias,
                                                    scale=scale),
                        reads=reads, writes=writes)
            else:
                em.emit(eng, lambda e: e.activation(out, in0, func, bias=bias,
                                                    scale=scale, accum_out=accum),
                        reads=reads, writes=writes)

        def cp(eng, out, in0, reads=(), writes=()):
            if eng == "scalar":
                em.emit(eng, lambda e: e.copy(out, in0), reads=reads, writes=writes)
            else:
                em.emit(eng, lambda e: e.tensor_copy(out, in0), reads=reads,
                        writes=writes)

        def red(eng, out, in0, op, absval=False, reads=(), writes=()):
            em.emit(eng, lambda e: e.tensor_reduce(out, in0, axis=AX.X, op=op,
                                                   apply_absolute_value=absval),
                    reads=reads, writes=writes)

        def mm(out, lhsT, rhs, start, stop, reads=(), writes=(), tp=None):
            if tp is None:
                em.emit("tensor", lambda e: e.matmul(out, lhsT, rhs, start=start,
                                                     stop=stop),
                        reads=reads, writes=writes)
            else:
                em.emit("tensor", lambda e: e.matmul(out, lhsT, rhs, start=start,
                                                     stop=stop, tile_position=tp),
                        reads=reads, writes=writes)

        def transp(out, in0, reads=(), writes=()):
            em.emit("tensor", lambda e: e.transpose(out, in0, IDN[:]),
                    reads=list(reads) + [B("IDN")], writes=writes)

        def bcast(queue, out_tile, src_t, src_off, n, reads=(), writes=()):
            src = bass.AP(src_t, src_off, [[0, 128], [1, n]])
            em.dma(queue, out_tile, src, reads=reads, writes=writes)

        def ln_bn(xa, width, rbuf, k):
            o = 32 * k
            bSk = bSTs[k]
            nparts = width // 512 if width > 512 else 1
            wper = width // nparts
            assert wper <= 512 and nparts <= 4
            for i in range(nparts):
                em.emit("vector", lambda e, i=i: e.bn_stats(
                    STATC[:, o + 6 * i:o + 6 * i + 6], xa[:, i * wper:(i + 1) * wper]),
                    reads=[rbuf], writes=[bSk])
            em.emit("vector", lambda e: e.bn_aggr(STATC[:, o + 12:o + 14],
                                                  STATC[:, o:o + 6 * nparts]),
                    reads=[bSk], writes=[bSk])

        def ln_sqrt(k):
            o = 32 * k
            act("scalar", STATC[:, o + 13:o + 14], STATC[:, o + 13:o + 14], AF.Sqrt,
                bias=EPSCOL[:, 0:1], reads=[bSTs[k], B("EPSCOL")], writes=[bSTs[k]])

        def ln_recip(k):
            o = 32 * k
            em.emit("vector", lambda e: e.reciprocal(STATC[:, o + 13:o + 14],
                                                     STATC[:, o + 13:o + 14]),
                    reads=[bSTs[k]], writes=[bSTs[k]])
            stt("vector", STATC[:, o + 14:o + 15], STATC[:, o + 12:o + 13], -1.0,
                STATC[:, o + 13:o + 14], ALU.mult, ALU.mult,
                reads=[bSTs[k]], writes=[bSTs[k]])

        def ln_apply(eng_out, in_ap, k, rbufs, wbufs):
            o = 32 * k
            act("scalar", eng_out, in_ap, AF.Identity, bias=STATC[:, o + 14:o + 15],
                scale=STATC[:, o + 13:o + 14], reads=list(rbufs) + [bSTs[k]],
                writes=wbufs)

        def ln_stats(xa, width, rbuf, k):
            ln_bn(xa, width, rbuf, k)
            ln_sqrt(k)
            ln_recip(k)

        em.emit("gpsimd", lambda e: e.iota(
            bass.AP(IDN, 0, [[128, 128], [1, 128]]), [[1, 128]], channel_multiplier=-1,
            allow_small_or_imprecise_dtypes=True), writes=[B("IDN")])
        ts("vector", IDN[:], IDN[:], 0.0, None, ALU.is_equal, reads=[B("IDN")],
           writes=[B("IDN")])
        em.emit("vector", lambda e: e.memset(ONESF[:], 1.0), writes=[B("ONESF")])
        em.emit("vector", lambda e: e.memset(ONESC[:], 1.0), writes=[B("ONESC")])
        em.emit("vector", lambda e: e.memset(EPSCOL[:], LN_EPS), writes=[B("EPSCOL")])
        em.emit("vector", lambda e: e.memset(ZCOL[:], 0.0), writes=[B("ZCOL")])
        em.emit("vector", lambda e: e.memset(IND64[:], 0.0), writes=[B("IND64")])
        em.emit("vector", lambda e: e.memset(IND64[64:65, 0:64], 1.0), writes=[B("IND64")])
        em.emit("vector", lambda e: e.memset(IND64[64:65, 192:256], 1.0), writes=[B("IND64")])
        em.emit("vector", lambda e: e.memset(IND8S[:], 0.0), writes=[B("IND8S")])
        for fc in range(QC):
            em.emit("vector", lambda e, fc=fc: e.memset(
                IND8S[0:64, fc * 8 + 2 * fc:fc * 8 + 2 * fc + 1], 1.0), writes=[B("IND8S")])
            em.emit("vector", lambda e, fc=fc: e.memset(
                IND8S[64:128, fc * 8 + 2 * fc + 1:fc * 8 + 2 * fc + 2], 1.0),
                writes=[B("IND8S")])

        for tile, tname, srcx, n in ((GREP, "GREP", ln_g, D), (BREP, "BREP", ln_b, D),
                                     (C2REP, "C2REP", n2w, HID),
                                     (BOREP, "BOREP", bout, D)):
            em.dma("gpsimd", tile[:], bass.AP(srcx.tensor, 0, [[0, 128], [1, n]]),
                   writes=[B(tname)])
        bcast("sync", C1REP[:], n1w.tensor, 0, D, writes=[B("C1REP")])
        bcast("sync", ABCOL[:, 0:2], actsc.tensor, 0, 2, writes=[B("ABCOL")])
        em.emit("vector", lambda e: e.reciprocal(C1IREP[:], C1REP[:]),
                reads=[B("C1REP")], writes=[B("C1IREP")])

        em.dma("gpsimd", bass.AP(BVC, 0, [[QC, 128], [1, QC]]),
               bass.AP(bv.tensor, 0, [[1, 128], [128, QC]]), writes=[B("BVC")],
               slow=True)
        em.dma("sync", bass.AP(BQS, 0, [[QC, 128], [1, QC]]),
               bass.AP(bqk2.tensor, 0, [[1, 128], [128, QC]]), writes=[B("BQS")],
               slow=True)
        em.dma("sync", bass.AP(BKS, 0, [[QC, 128], [1, QC]]),
               bass.AP(bqk2.tensor, QK, [[1, 128], [128, QC]]), writes=[B("BKS")],
               slow=True)
        ts("vector", BQS[:], BQS[:], 0.125, None, ALU.mult, reads=[B("BQS")],
           writes=[B("BQS")])

        em.dma("gpsimd",
               bass.AP(WQK, 0, [[DC * 2 * QK, 128], [2 * QK, DC], [1, 2 * QK]]),
               bass.AP(wqk.tensor, 0, [[2 * QK, 128], [128 * 2 * QK, DC], [1, 2 * QK]]),
               writes=[B("WQK")])
        em.dma("gpsimd",
               bass.AP(WV, 0, [[DC * QK, 128], [QK, DC], [1, QK]]),
               bass.AP(wv.tensor, 0, [[QK, 128], [128 * QK, DC], [1, QK]]),
               writes=[B("WV")])
        em.dma("gpsimd",
               bass.AP(WOUT, 0, [[QC * D, 128], [D, QC], [1, D]]),
               bass.AP(wout.tensor, 0, [[D, 128], [128 * D, QC], [1, D]]),
               writes=[B("WOUT")])

        arena = contextlib.ExitStack()
        BVTMP = arena.enter_context(nc.sbuf_tensor("BVTMP", [1, D], F32))
        for half in range(max(1, D // 512)):
            dw2 = min(512, D)
            for ci in range(QC):
                mm(P0[0:1, half * dw2:half * dw2 + dw2], BVC[:, ci:ci + 1],
                   WOUT[:, ci * D + half * dw2:ci * D + (half + 1) * dw2],
                   start=(ci == 0), stop=(ci == QC - 1),
                   reads=[B("BVC"), B("WOUT")], writes=[bP0])
        cp("vector", BVTMP[0:1, :], P0[0:1, 0:D], reads=[bP0], writes=[B("BVTMP")])
        for half in range(max(1, D // 512)):
            dw2 = min(512, D)
            mm(P0[:, half * dw2:half * dw2 + dw2], ONESF[0:1, :],
               BVTMP[0:1, half * dw2:half * dw2 + dw2],
               start=True, stop=True, reads=[B("BVTMP"), B("ONESF")], writes=[bP0])
            tt("vector", BOREP[:, half * dw2:half * dw2 + dw2],
               BOREP[:, half * dw2:half * dw2 + dw2],
               P0[:, half * dw2:half * dw2 + dw2], ALU.add,
               reads=[bP0], writes=[B("BOREP")])
        SILC = arena.enter_context(nc.sbuf_tensor("SILC", [128, DC * NSEQ], F32))
        MODT = arena.enter_context(nc.sbuf_tensor("MODT", [NSEQ, 6 * D], F32))
        WMS = arena.enter_context(nc.sbuf_tensor("WMS", [128, 2 * 512], F32))
        em.dma("sync", bass.AP(SILC, 0, [[DC * NSEQ, 128], [NSEQ, DC], [1, NSEQ]]),
               bass.AP(c_t.tensor, 0, [[NSEQ, 128], [128 * NSEQ, DC], [1, NSEQ]]),
               writes=[B("SILC")], slow=True)
        act("scalar", SILC[:], SILC[:], AF.Silu, reads=[B("SILC")], writes=[B("SILC")])
        for nt in range(6 * D // 512):
            for dc in range(DC):
                wslot = (nt * DC + dc) % 2
                bWM = B(f"WMS{wslot}")
                em.dma("sync", WMS[:, wslot * 512:wslot * 512 + 512],
                       bass.AP(wmod.tensor, dc * 128 * 6 * D + nt * 512,
                               [[6 * D, 128], [1, 512]]),
                       writes=[bWM])
                mm(P3[0:NSEQ, 0:512], SILC[:, dc * NSEQ:(dc + 1) * NSEQ],
                   WMS[:, wslot * 512:wslot * 512 + 512],
                   start=(dc == 0), stop=(dc == DC - 1),
                   reads=[B("SILC"), bWM], writes=[bP3])
            cp("vector", MODT[0:NSEQ, nt * 512:(nt + 1) * 512], P3[0:NSEQ, 0:512],
               reads=[bP3], writes=[B("MODT")])
        em.dma("scalar", bass.AP(mod_scr.tensor, 0, [[6 * D, NSEQ], [1, 6 * D]]),
               MODT[0:NSEQ, :], reads=[B("MODT")], writes=[B("mod_scr")])

        WQT = arena.enter_context(nc.sbuf_tensor("WQT", [128, HID], F32))
        WQA = arena.enter_context(nc.sbuf_tensor("WQA", [128, HID], F32))
        WQB = arena.enter_context(nc.sbuf_tensor("WQB", [128, HID], BF16))
        WQI = arena.enter_context(nc.sbuf_tensor("WQI", [128, HID], I8))
        WQC = arena.enter_context(nc.sbuf_tensor("WQC", [128, 4], F32))

        def wquant(w_ap, rows, cols, scr_t, idx):
            nch = rows // 128
            bW, bA, bC = B("WQT"), B("WQAa"), B("WQC")
            bI, bBf, bScr = B("WQI"), B("WQB"), B(f"qS{idx}")
            for ch in range(nch):
                em.dma("sync", WQT[:, 0:cols],
                       bass.AP(w_ap.tensor, ch * 128 * cols, [[cols, 128], [1, cols]]),
                       writes=[bW])
                red("vector", WQC[:, 1:2], WQT[:, 0:cols], ALU.add, absval=True,
                    reads=[bW], writes=[bC])
                if ch == 0:
                    cp("vector", WQC[:, 0:1], WQC[:, 1:2], reads=[bC], writes=[bA])
                else:
                    tt("vector", WQC[:, 0:1], WQC[:, 0:1], WQC[:, 1:2], ALU.add,
                       reads=[bC], writes=[bA])
            mm(P3[0:1, 0:1], WQC[:, 0:1], ONESC[:, 0:1], start=True, stop=True,
               reads=[bA, B("ONESC")], writes=[bP3])
            cp("vector", WQC[0:1, 2:3], P3[0:1, 0:1], reads=[bP3], writes=[bC])
            ts("vector", WQC[0:1, 2:3], WQC[0:1, 2:3], 1.0 / (rows * cols), EPS_Q,
               ALU.mult, ALU.max, reads=[bC], writes=[bC])
            mm(P3[:, 0:1], ONESF[0:1, :], WQC[0:1, 2:3], start=True, stop=True,
               reads=[bC, B("ONESF")], writes=[bP3])
            cp("vector", SWCOL[:, idx:idx + 1], P3[:, 0:1], reads=[bP3],
               writes=[B("SWCOL")])
            em.emit("vector", lambda e: e.reciprocal(WQC[:, 3:4], SWCOL[:, idx:idx + 1]),
                    reads=[B("SWCOL")], writes=[bC])
            for ch in range(nch):
                em.dma("sync", WQT[:, 0:cols],
                       bass.AP(w_ap.tensor, ch * 128 * cols, [[cols, 128], [1, cols]]),
                       writes=[bW])
                act("scalar", WQA[:, 0:cols], WQT[:, 0:cols], AF.Identity,
                    bias=ZCOL[:, 0:1], scale=WQC[:, 3:4],
                    reads=[bW, bC, B("ZCOL")], writes=[B("WQAa")])
                ts("vector", WQT[:, 0:cols], WQA[:, 0:cols], 0.5, None, ALU.is_gt,
                   reads=[B("WQAa")], writes=[bW])
                stt("vector", WQB[:, 0:cols], WQA[:, 0:cols], -0.5, WQT[:, 0:cols],
                    ALU.is_lt, ALU.subtract, reads=[B("WQAa"), bW], writes=[bBf])
                em.dma("scalar",
                       bass.AP(scr_t, ch * 128 * cols, [[cols, 128], [1, cols]]),
                       WQB[:, 0:cols], reads=[bBf], writes=[bScr])

        wquant(w1t, D, HID, w1q_scr.tensor, 0)
        wquant(w2t, HID, D, w2q_scr.tensor, 1)
        em.barrier()
        arena.close()

        for s in range(NSEQ):
            ar = contextlib.ExitStack()
            HBF = ar.enter_context(nc.sbuf_tensor(f"HBF{s}", [128, DC * S], BF16))
            KBF = ar.enter_context(nc.sbuf_tensor(f"KBF{s}", [128, QC * S], BF16))
            QBF = ar.enter_context(nc.sbuf_tensor(f"QBF{s}", [128, QC * S], BF16))
            VTM = ar.enter_context(nc.sbuf_tensor(f"VTM{s}", [128, KC * VW], BF16))
            XT = ar.enter_context(nc.sbuf_tensor(f"XT{s}", [128, 4 * D], F32))
            PBF = ar.enter_context(nc.sbuf_tensor(f"PBF{s}", [128, 2 * 1024], BF16))
            OBF = ar.enter_context(nc.sbuf_tensor(f"OBF{s}", [128, QC * c.AG], BF16))
            OST = ar.enter_context(nc.sbuf_tensor(f"OST{s}", [128, c.NP * c.AG], BF16))
            ORW = ar.enter_context(nc.sbuf_tensor(f"ORW{s}", [128, 2 * c.AG], BF16))
            RRC = ar.enter_context(nc.sbuf_tensor(f"RRC{s}", [128, 512], BF16))
            SMR = ar.enter_context(nc.sbuf_tensor(f"SMR{s}", [128, 2048], F32))
            EP1 = ar.enter_context(nc.sbuf_tensor(f"EP1{s}", [128, (c.AG // 128) * D], BF16))
            EP2 = ar.enter_context(nc.sbuf_tensor(f"EP2{s}", [128, (c.AG // 128) * D], BF16))
            CA1 = ar.enter_context(nc.sbuf_tensor(f"CA1{s}", [128, D], F32))
            SAR = ar.enter_context(nc.sbuf_tensor(f"SAR{s}", [128, D], BF16))
            GAR = ar.enter_context(nc.sbuf_tensor(f"GAR{s}", [128, D], BF16))
            GSC = ar.enter_context(nc.sbuf_tensor(f"GSC{s}", [128, D], F32))

            bHBF, bKBF, bQBF, bVTM = B(f"HBF{s}"), B(f"KBF{s}"), B(f"QBF{s}"), B(f"VTM{s}")
            bOBF, bOST, bORW = B(f"OBF{s}"), B(f"OST{s}"), B(f"ORW{s}")
            bXTs = [B(f"XT{s}_{i}") for i in range(4)]
            bPBF = [B(f"PBF{s}_0"), B(f"PBF{s}_1")]
            bRRC, bSMR = B(f"RRC{s}"), B(f"SMR{s}")
            bEP1, bEP2 = B(f"EP1{s}"), B(f"EP2{s}")
            bCA1, bSAR, bGAR, bGSC = B(f"CA1{s}"), B(f"SAR{s}"), B(f"GAR{s}"), B(f"GSC{s}")
            bNNB, bXMS = B("NNB"), B("xmid_scr")

            def xchunk(gbase_tok, tc, nsl=4):
                sl = tc % nsl
                em.dma("sync", XT[:, sl * D:(sl + 1) * D],
                       bass.AP(x_in.tensor, (gbase_tok + tc * 128) * D,
                               [[D, 128], [1, D]]),
                       writes=[bXTs[sl]])
                return XT[:, sl * D:(sl + 1) * D], bXTs[sl]

            bcast("sync", CA1[:], mod_scr.tensor, (s * 6 + 1) * D, D,
                  reads=[B("mod_scr")], writes=[bCA1])
            ts("vector", CA1[:], CA1[:], 1.0, None, ALU.add, reads=[bCA1], writes=[bCA1])
            em.dma("gpsimd", SAR[:],
                   bass.AP(mod_scr.tensor, (s * 6 + 0) * D, [[0, 128], [1, D]]),
                   reads=[B("mod_scr")], writes=[bSAR])
            bcast("sync", GSC[:], mod_scr.tensor, (s * 6 + 2) * D, D,
                  reads=[B("mod_scr")], writes=[bGSC])
            ts("vector", GAR[:], GSC[:], 0.0, 2.0, ALU.is_ge, ALU.mult, reads=[bGSC],
               writes=[bGAR])
            ts("vector", GAR[:], GAR[:], -1.0, None, ALU.add, reads=[bGAR], writes=[bGAR])
            act("scalar", GSC[:], GSC[:], AF.Abs, reads=[bGSC], writes=[bGSC])
            ts("vector", GSC[:], GSC[:], 0.25, None, ALU.max, reads=[bGSC],
               writes=[bGSC])
            tt("vector", GSC[:], GSC[:], GAR[:], ALU.mult, reads=[bGAR], writes=[bGSC])
            ts("vector", GAR[:], GSC[:], RSQRT2, None, ALU.mult, reads=[bGSC],
               writes=[bGAR])

            em.emit("vector", lambda e: e.memset(
                bass.AP(VTM, 64, [[KC * VW, 128], [VW, KC], [65, H], [1, 1]]), 1.0),
                writes=[bVTM])

            NCH = c.AG // 128
            for g in range(c.NG):
                gbase = s * S + g * c.AG
                xas = []
                for tc in range(NCH):
                    xa, bxa = xchunk(gbase, tc)
                    xas.append((xa, bxa))
                    k4 = tc % 4
                    o = 32 * k4
                    ln_bn(xa, D, bxa, k4)
                for tc in range(NCH):
                    ln_sqrt(tc % 4)
                for tc in range(NCH):
                    ln_recip(tc % 4)
                for tc in range(NCH):
                    k4 = tc % 4
                    o = 32 * k4
                    xa, bxa = xas[tc]
                    bEP1c = B(f"EP1{s}_{tc}")
                    bEP2c = B(f"EP2{s}_{tc}")
                    ha = EP2[:, tc * D:(tc + 1) * D]
                    ln_apply(ha, xa, k4, [bxa], [bEP2c])
                    ea = EP1[:, tc * D:(tc + 1) * D]
                    tt("vector", ea, ha, CA1[:], ALU.mult, reads=[bEP2c, bCA1],
                       writes=[bEP1c])
                    tt("vector", ha, ea, SAR[:], ALU.add, reads=[bEP1c, bSAR],
                       writes=[bEP2c])
                for tc in range(NCH):
                    bEP2c = B(f"EP2{s}_{tc}")
                    ha = EP2[:, tc * D:(tc + 1) * D]
                    for j in range(DC):
                        transp(PT[:, (j % 8) * 128:(j % 8) * 128 + 128],
                               ha[:, j * 128:(j + 1) * 128],
                               reads=[bEP2c], writes=[bPT])
                    g4 = min(4, DC)
                    for half in range(DC // g4):
                        cp("scalar",
                           bass.AP(HBF, half * g4 * S + g * c.AG + tc * 128,
                                   [[DC * S, 128], [S, g4], [1, 128]]),
                           bass.AP(PT, half * g4 * 128, [[1024, 128], [128, g4], [1, 128]]),
                           reads=[bPT], writes=[bHBF])
                for fc2 in range(0, QC, 2):
                    for dc in range(DC):
                        for pi in range(2):
                            fc = fc2 + pi
                            pk, bpk = (P0, bP0) if pi == 0 else (P1, bP1)
                            mm(pk[:, 0:c.AG],
                               WQK[:, dc * 2 * QK + QK + fc * 128:
                                   dc * 2 * QK + QK + (fc + 1) * 128],
                               HBF[:, dc * S + g * c.AG:dc * S + (g + 1) * c.AG],
                               start=(dc == 0), stop=(dc == DC - 1),
                               reads=[B("WQK"), bHBF], writes=[bpk])
                    for pi in range(2):
                        fc = fc2 + pi
                        pk, bpk = (P0, bP0) if pi == 0 else (P1, bP1)
                        act("scalar", KBF[:, fc * S + g * c.AG:fc * S + (g + 1) * c.AG],
                            pk[:, 0:c.AG], AF.Identity, bias=BKS[:, fc:fc + 1],
                            reads=[bpk, B("BKS")], writes=[bKBF])
                for tc2 in range(0, NCH, 2):
                    npi = 2 if tc2 + 1 < NCH else 1
                    for dc in range(DC):
                        for pi in range(npi):
                            tc = tc2 + pi
                            pv, bpv = (P0, bP0) if pi == 0 else (P1, bP1)
                            mm(pv[:, 0:QK],
                               HBF[:, dc * S + g * c.AG + tc * 128:
                                   dc * S + g * c.AG + (tc + 1) * 128],
                               WV[:, dc * QK:(dc + 1) * QK],
                               start=(dc == 0), stop=(dc == DC - 1),
                               reads=[bHBF, B("WV")], writes=[bpv])
                    for pi in range(npi):
                        tc = tc2 + pi
                        pv, bpv = (P0, bP0) if pi == 0 else (P1, bP1)
                        kc = g * NCH + tc
                        cp("scalar",
                           bass.AP(VTM, kc * VW, [[KC * VW, 128], [65, H], [1, 64]]),
                           bass.AP(pv, 0, [[1024, 128], [64, H], [1, 64]]),
                           reads=[bpv], writes=[bVTM])

            for g in range(c.NG):
                for fc2 in range(0, QC, 2):
                    for dc in range(DC):
                        for pi in range(2):
                            fc = fc2 + pi
                            pq, bpq = (P0, bP0) if pi == 0 else (P1, bP1)
                            mm(pq[:, 0:c.AG],
                               WQK[:, dc * 2 * QK + fc * 128:dc * 2 * QK + (fc + 1) * 128],
                               HBF[:, dc * S + g * c.AG:dc * S + (g + 1) * c.AG],
                               start=(dc == 0), stop=(dc == DC - 1),
                               reads=[B("WQK"), bHBF], writes=[bpq])
                    for pi in range(2):
                        fc = fc2 + pi
                        pq, bpq = (P0, bP0) if pi == 0 else (P1, bP1)
                        act("scalar", QBF[:, fc * S + g * c.AG:fc * S + (g + 1) * c.AG],
                            pq[:, 0:c.AG], AF.Identity, bias=BQS[:, fc:fc + 1],
                            scale=0.125, reads=[bpq, B("BQS")], writes=[bQBF])
            for qsrc, bsrc, coli in ((QBF, bQBF, 22), (KBF, bKBF, 24)):
                first = True
                for fc in range(QC):
                    act("scalar", EP1[:, 0:S], qsrc[:, fc * S:(fc + 1) * S], AF.Square,
                        reads=[bsrc], writes=[B(f"EP1{s}_0"), B(f"EP1{s}_1")])
                    for ct in range((S + 511) // 512):
                        ctw = min(512, S - ct * 512)
                        mm(P3[0:8, 0:ctw], IND8S[:, fc * 8:(fc + 1) * 8],
                           EP1[:, ct * 512:ct * 512 + ctw],
                           start=True, stop=True,
                           reads=[B(f"EP1{s}_0"), B(f"EP1{s}_1"), B("IND8S")],
                           writes=[bP3])
                        red("vector", STATC[0:8, coli + 1:coli + 2], P3[0:8, 0:ctw],
                            ALU.max, reads=[bP3], writes=[bSTs[0]])
                        if first:
                            cp("vector", STATC[0:8, coli:coli + 1],
                               STATC[0:8, coli + 1:coli + 2], reads=[bSTs[0]],
                               writes=[bSTs[0]])
                            first = False
                        else:
                            tt("vector", STATC[0:8, coli:coli + 1],
                               STATC[0:8, coli:coli + 1], STATC[0:8, coli + 1:coli + 2],
                               ALU.max, reads=[bSTs[0]], writes=[bSTs[0]])
            tt("vector", STATC[0:8, 26:27], STATC[0:8, 22:23], STATC[0:8, 24:25],
               ALU.mult, reads=[bSTs[0]], writes=[bSTs[0]])
            em.dma("scalar", bass.AP(tiny_scr.tensor, 0, [[1, 8], [1, 1]]),
                   STATC[0:8, 26:27], reads=[bSTs[0]], writes=[B("tiny")], slow=True)
            em.dma("scalar", SMR[0:1, 0:8],
                   bass.AP(tiny_scr.tensor, 0, [[8, 1], [1, 8]]),
                   reads=[B("tiny")], writes=[bSMR], slow=True)
            act("scalar", SMR[0:1, 0:8], SMR[0:1, 0:8], AF.Sqrt, reads=[bSMR],
                writes=[bSMR])
            tt("vector", SMR[0:1, 8:8 + c.NP], bass.AP(SMR, 0, [[2048, 1], [2, c.NP]]),
               bass.AP(SMR, 1, [[2048, 1], [2, c.NP]]), ALU.max, reads=[bSMR],
               writes=[bSMR])
            ts("vector", SMR[0:1, 8:8 + c.NP], SMR[0:1, 8:8 + c.NP], -1.0, None,
               ALU.mult, reads=[bSMR], writes=[bSMR])
            mm(P3[:, 0:c.NP], ONESF[0:1, :], SMR[0:1, 8:8 + c.NP], start=True,
               stop=True, reads=[bSMR, B("ONESF")], writes=[bP3])
            cp("vector", NNB[:, 0:c.NP], P3[:, 0:c.NP], reads=[bP3], writes=[bNNB])

            for g in range(c.NG):
                gbase = s * S + g * c.AG
                pend = []

                def _emit_norm(g, p, slot):
                    orw = ORW[:, slot * c.AG:(slot + 1) * c.AG]
                    em.emit("vector", lambda e: e.reciprocal(
                        SMR[64:65, slot * 1024:slot * 1024 + 2 * c.AG],
                        SMR[64:65, slot * 1024:slot * 1024 + 2 * c.AG]),
                        reads=[bSMR], writes=[bSMR])
                    mm(P2A[:, 0:c.AG], IND64[64:65, 0:128],
                       SMR[64:65, slot * 1024:slot * 1024 + c.AG],
                       start=True, stop=False,
                       reads=[bSMR, B("IND64")], writes=[bP2])
                    mm(P2A[:, 0:c.AG], IND64[64:65, 128:256],
                       SMR[64:65, slot * 1024 + c.AG:slot * 1024 + 2 * c.AG],
                       start=False, stop=True,
                       reads=[bSMR, B("IND64")], writes=[bP2])
                    cp("vector", RRC[0:64, 0:c.AG], P2A[0:64, 0:c.AG], reads=[bP2],
                       writes=[bRRC])
                    tt("vector", OBF[0:64, p * c.AG:(p + 1) * c.AG],
                       orw[0:64, :], RRC[0:64, 0:c.AG], ALU.mult,
                       reads=[bORW, bRRC], writes=[bOBF])
                    tt("vector", OST[0:64, p * c.AG:(p + 1) * c.AG],
                       OST[0:64, p * c.AG:(p + 1) * c.AG],
                       P2A[64:128, 0:c.AG], ALU.mult,
                       reads=[bP2], writes=[bOST])

                for p in range(c.NP):
                    hA, hB = 2 * p, 2 * p + 1
                    for j in range(KC):
                        psc, bsc = (P0, bP0) if j % 2 == 0 else (P1, bP1)
                        pbh = j % 2
                        mm(psc[:, 0:c.AG],
                           KBF[0:64, p * S + j * 128:p * S + (j + 1) * 128],
                           QBF[0:64, p * S + g * c.AG:p * S + (g + 1) * c.AG],
                           start=True, stop=True, tp=(0, 0),
                           reads=[bKBF, bQBF], writes=[bsc])
                        mm(psc[:, 512:512 + c.AG],
                           KBF[64:128, p * S + j * 128:p * S + (j + 1) * 128],
                           QBF[64:128, p * S + g * c.AG:p * S + (g + 1) * c.AG],
                           start=True, stop=True, tp=(64, 0),
                           reads=[bKBF, bQBF], writes=[bsc])
                        if j == 2 and pend:
                            pp, ss = pend.pop(0)
                            _emit_norm(g, pp, ss)
                        if c.AG == 512:
                            act("scalar", PBF[:, pbh * 1024:pbh * 1024 + 1024],
                                psc[:, 0:1024], AF.Exp, bias=NNB[:, p:p + 1],
                                reads=[bsc, bNNB], writes=[bPBF[pbh]])
                        else:
                            act("scalar", PBF[:, pbh * 1024:pbh * 1024 + c.AG],
                                psc[:, 0:c.AG], AF.Exp, bias=NNB[:, p:p + 1],
                                reads=[bsc, bNNB], writes=[bPBF[pbh]])
                            act("scalar",
                                PBF[:, pbh * 1024 + 512:pbh * 1024 + 512 + c.AG],
                                psc[:, 512:512 + c.AG], AF.Exp, bias=NNB[:, p:p + 1],
                                reads=[bsc, bNNB], writes=[bPBF[pbh]])
                        if j >= 1:
                            jm = j - 1
                            pbm = jm % 2
                            mm(P3[0:65, 0:c.AG],
                               VTM[:, jm * VW + 65 * hA:jm * VW + 65 * hA + 65],
                               PBF[:, pbm * 1024:pbm * 1024 + c.AG],
                               start=(jm == 0), stop=False,
                               reads=[bVTM, bPBF[pbm]], writes=[bP3])
                            mm(P4[0:65, 0:c.AG],
                               VTM[:, jm * VW + 65 * hB:jm * VW + 65 * hB + 65],
                               PBF[:, pbm * 1024 + 512:pbm * 1024 + 512 + c.AG],
                               start=(jm == 0), stop=False,
                               reads=[bVTM, bPBF[pbm]], writes=[bP4])
                    jm = KC - 1
                    pbm = jm % 2
                    mm(P3[0:65, 0:c.AG],
                       VTM[:, jm * VW + 65 * hA:jm * VW + 65 * hA + 65],
                       PBF[:, pbm * 1024:pbm * 1024 + c.AG],
                       start=False, stop=True,
                       reads=[bVTM, bPBF[pbm]], writes=[bP3])
                    mm(P4[0:65, 0:c.AG],
                       VTM[:, jm * VW + 65 * hB:jm * VW + 65 * hB + 65],
                       PBF[:, pbm * 1024 + 512:pbm * 1024 + 512 + c.AG],
                       start=False, stop=True,
                       reads=[bVTM, bPBF[pbm]], writes=[bP4])
                    slot = p % 2
                    orw = ORW[:, slot * c.AG:(slot + 1) * c.AG]
                    cp("scalar", orw[0:65, :], P3[0:65, 0:c.AG], reads=[bP3],
                       writes=[bORW])
                    cp("scalar", SMR[64:65, slot * 1024:slot * 1024 + c.AG],
                       P3[64:65, 0:c.AG], reads=[bP3], writes=[bSMR])
                    cp("scalar", OST[0:65, p * c.AG:(p + 1) * c.AG],
                       P4[0:65, 0:c.AG], reads=[bP4], writes=[bOST])
                    cp("scalar", SMR[64:65, slot * 1024 + c.AG:slot * 1024 + 2 * c.AG],
                       P4[64:65, 0:c.AG], reads=[bP4], writes=[bSMR])
                    pend.append((p, slot))
                while pend:
                    pp, ss = pend.pop(0)
                    _emit_norm(g, pp, ss)
                em.dma("scalar",
                       bass.AP(OBF, 64 * QC * c.AG,
                               [[QC * c.AG, 64], [c.AG, c.NP], [1, c.AG]]),
                       bass.AP(OST, 0, [[c.NP * c.AG, 64], [c.AG, c.NP], [1, c.AG]]),
                       reads=[bOST], writes=[bOBF])
                if c.DBG and s == 0:
                    em.dma("gpsimd",
                           bass.AP(dbg_scr.tensor, g * 128 * QC * c.AG,
                                   [[QC * c.AG, 128], [1, QC * c.AG]]),
                           OBF[:, :], reads=[bOBF], writes=[B("dbg")])
                for tc in range(NCH):
                    py, bpy = (P0, bP0) if tc % 2 == 0 else (P1, bP1)
                    nh = max(1, D // 512)
                    dw = D // nh
                    for ci in range(QC):
                        for half in range(nh):
                            mm(py[:, half * dw:(half + 1) * dw],
                               OBF[:, ci * c.AG + tc * 128:ci * c.AG + (tc + 1) * 128],
                               WOUT[:, ci * D + half * dw:ci * D + (half + 1) * dw],
                               start=(ci == 0), stop=(ci == QC - 1),
                               reads=[bOBF, B("WOUT")], writes=[bpy])
                    k4 = tc % 4
                    o = 32 * k4
                    bSk = bSTs[k4]
                    bEP1c = B(f"EP1{s}_{tc}")
                    bEP2c = B(f"EP2{s}_{tc}")
                    e1 = EP1[:, tc * D:(tc + 1) * D]
                    e2 = EP2[:, tc * D:(tc + 1) * D]
                    xa, bxa = xchunk(gbase, tc)
                    tt("vector", e1, py[:, 0:D], BOREP[:], ALU.add,
                       reads=[bpy, B("BOREP")], writes=[bEP1c])
                    if c.DBG and s == 0 and g == 0:
                        em.dma("gpsimd",
                               bass.AP(dbg2_scr.tensor, tc * 128 * D, [[D, 128], [1, D]]),
                               e1, reads=[bEP1c], writes=[B("dbg2")])
                    ln_bn(e1, D, bEP1c, k4)
                    ln_sqrt(k4)
                    ln_recip(k4)
                    ln_apply(e2, e1, k4, [bEP1c], [bEP2c])
                    tt("vector", e1, e2, GREP[:], ALU.mult, reads=[bEP2c, B("GREP")],
                       writes=[bEP1c])
                    tt("vector", e2, e1, BREP[:], ALU.add, reads=[bEP1c, B("BREP")],
                       writes=[bEP2c])
                    for j in range(DC):
                        transp(PT[:, (j % 8) * 128:(j % 8) * 128 + 128],
                               HBF[:, j * S + g * c.AG + tc * 128:
                                   j * S + g * c.AG + (tc + 1) * 128],
                               reads=[bHBF], writes=[bPT])
                    tt("vector", e1, e2, PT[:, 0:D], ALU.add, reads=[bEP2c, bPT],
                       writes=[bEP1c])
                    tt("vector", e2, e1, GAR[:], ALU.mult, reads=[bEP1c, bGAR],
                       writes=[bEP2c])
                    tt("gpsimd", xa, xa, e2, ALU.add, reads=[bEP2c], writes=[bxa])
                    em.dma("scalar",
                           bass.AP(xmid_scr.tensor, (gbase + tc * 128) * D,
                                   [[D, 128], [1, D]]),
                           xa, reads=[bxa], writes=[bXMS])

            em.barrier()
            ar.close()

            br = contextlib.ExitStack()
            W1S = br.enter_context(nc.sbuf_tensor(f"W1S{s}", [128, DC * HID], BF16))
            CB1C1 = br.enter_context(nc.sbuf_tensor(f"CB1{s}", [128, D], BF16))
            SBC1 = br.enter_context(nc.sbuf_tensor(f"SBC1{s}", [128, D], BF16))
            ZC = br.enter_context(nc.sbuf_tensor(f"ZC{s}", [128, (c.T1 // 128) * D], F32))
            DS1 = br.enter_context(nc.sbuf_tensor(f"DS1{s}", [128, D], F32))
            DS2 = br.enter_context(nc.sbuf_tensor(f"DS2{s}", [128, D], BF16))
            DI8 = br.enter_context(nc.sbuf_tensor(f"DI8{s}", [128, D], I8))
            XQ1 = br.enter_context(nc.sbuf_tensor(f"XQ1{s}", [128, 3 * DC * c.T1], BF16))
            XMB = br.enter_context(nc.sbuf_tensor(f"XMB{s}", [128, 2 * (c.T1 // 128) * D], F32))
            HA = br.enter_context(nc.sbuf_tensor(f"HA{s}", [128, (c.T1 // 128) * HID], BF16))
            HBt = br.enter_context(nc.sbuf_tensor(f"HBt{s}", [128, (c.T1 // 128) * HID], BF16))
            HI8 = br.enter_context(nc.sbuf_tensor(f"HI8{s}", [128, HID], I8))
            XQ2S = br.enter_context(nc.sbuf_tensor(f"XQ2S{s}", [128, 2048], BF16))

            bW1, bCB, bSB = B(f"W1S{s}"), B(f"CB1{s}"), B(f"SBC1{s}")
            bXQ1, bXMB = B(f"XQ1{s}"), B(f"XMB{s}")
            bXQscr, bS2 = B("xq2_scr"), B("S2COL")

            bcast("gpsimd", CB1C1[:], mod_scr.tensor, (s * 6 + 4) * D, D,
                  reads=[B("mod_scr")], writes=[bCB])
            ts("vector", CB1C1[:], CB1C1[:], 1.0, None, ALU.add, reads=[bCB],
               writes=[bCB])
            tt("vector", CB1C1[:], CB1C1[:], C1REP[:], ALU.mult, reads=[B("C1REP")],
               writes=[bCB])
            bcast("sync", DS1[:, 0:D], mod_scr.tensor, (s * 6 + 3) * D, D,
                  reads=[B("mod_scr")], writes=[B(f"DS1{s}_0")])
            tt("vector", SBC1[:], DS1[:, 0:D], C1REP[:], ALU.mult,
               reads=[B(f"DS1{s}_0"), B("C1REP")], writes=[bSB])
            for hh, q_ in ((0, "sync"), (1, "gpsimd")):
                em.dma(q_,
                       bass.AP(W1S, hh * (DC // 2) * HID,
                               [[DC * HID, 128], [HID, DC // 2], [1, HID]]),
                       bass.AP(w1q_scr.tensor, hh * (DC // 2) * 128 * HID,
                               [[HID, 128], [128 * HID, DC // 2], [1, HID]]),
                       reads=[B("qS0")], writes=[bW1])

            def xq2_finish(t1, tc):
                bHBc = B(f"HBt{s}_{tc}")
                ua = HBt[:, tc * HID:(tc + 1) * HID]
                g8 = min(8, HC)
                for quad in range(HC // g8):
                    for j in range(g8):
                        transp(PT[:, j * 128:(j + 1) * 128],
                               ua[:, (quad * g8 + j) * 128:(quad * g8 + j + 1) * 128],
                               reads=[bHBc], writes=[bPT])
                    qs = quad % 2
                    bXQ2c = B(f"XQ2S{s}_{qs}")
                    cp("scalar",
                       bass.AP(XQ2S, qs * 1024, [[2048, 128], [128, g8], [1, 128]]),
                       bass.AP(PT, 0, [[1024, 128], [128, g8], [1, 128]]),
                       reads=[bPT], writes=[bXQ2c])
                    em.dma("scalar",
                           bass.AP(xq2_scr.tensor,
                                   (quad * g8 * 128) * S + t1 * c.T1 + tc * 128,
                                   [[S, 128], [128 * S, g8], [1, 128]]),
                           bass.AP(XQ2S, qs * 1024, [[2048, 128], [128, g8], [1, 128]]),
                           reads=[bXQ2c], writes=[bXQscr])

            NC1 = c.T1 // 128

            def b1_dstage(t1):
                ts_ = t1 % 2
                sl = t1 % 3
                tb = s * S + t1 * c.T1
                em.dma("sync",
                       bass.AP(XMB, ts_ * NC1 * D,
                               [[2 * NC1 * D, 128], [D, NC1], [1, D]]),
                       bass.AP(xmid_scr.tensor, tb * D,
                               [[D, 128], [128 * D, NC1], [1, D]]),
                       reads=[bXMS], writes=[B(f"XMB{s}_{ts_}")])
                for tc in range(NC1):
                    k4 = (t1 * NC1 + tc) % 4
                    ln_bn(XMB[:, (ts_ * NC1 + tc) * D:(ts_ * NC1 + tc + 1) * D], D,
                          B(f"XMB{s}_{ts_}"), k4)
                for tc in range(NC1):
                    ln_sqrt((t1 * NC1 + tc) % 4)
                for tc in range(NC1):
                    ln_recip((t1 * NC1 + tc) % 4)
                for tc in range(NC1):
                    k4 = (t1 * NC1 + tc) % 4
                    o = 32 * k4
                    bSk = bSTs[k4]
                    tcol = t1 * NC1 + tc
                    bZCc = B(f"ZC{s}_{tc}")
                    bD1c, bD2c = B(f"DS1{s}"), B(f"DS2{s}")
                    bDIc = B(f"DI8{s}")
                    xa = XMB[:, (ts_ * NC1 + tc) * D:(ts_ * NC1 + tc + 1) * D]
                    za = ZC[:, tc * D:(tc + 1) * D]
                    d1 = DS1[:, 0:D]
                    d2 = DS2[:, 0:D]
                    di = DI8[:, 0:D]
                    xqt = d2
                    ak = 22 + (tcol // 4) % 2
                    ln_apply(za, xa, k4, [B(f"XMB{s}_{ts_}")], [bZCc])
                    tt("vector", za, za, CB1C1[:], ALU.mult, reads=[bCB], writes=[bZCc])
                    tt("vector", za, za, SBC1[:], ALU.add, reads=[bSB], writes=[bZCc])
                    tt("vector", d1, za, C1IREP[:], ALU.mult,
                       reads=[bZCc, B("C1IREP")], writes=[bD1c])
                    act("scalar", d2, d1, AF.Square, accum=STATC[:, o + 15:o + 16],
                        reads=[bD1c], writes=[bD2c, bSk])
                    ts("vector", STATC[:, o + 15:o + 16], STATC[:, o + 15:o + 16],
                       1.0 / D, EPS_NORM, ALU.mult, ALU.add, reads=[bSk], writes=[bSk])
                    em.emit("vector", lambda e, o=o: e.reciprocal(
                        STATC[:, o + 16:o + 17], STATC[:, o + 15:o + 16]),
                        reads=[bSk], writes=[bSk])
                    act("scalar", STATC[:, o + 16:o + 17], STATC[:, o + 16:o + 17],
                        AF.Sqrt, reads=[bSk], writes=[bSk])
                    red("vector", STATC[:, o + 17:o + 18], za, ALU.max, absval=True,
                        reads=[bZCc], writes=[bSk])
                    tt("vector", STATC[:, o + 17:o + 18], STATC[:, o + 17:o + 18],
                       STATC[:, o + 16:o + 17], ALU.mult, reads=[bSk], writes=[bSk])
                    ts("vector", STATC[:, o + 17:o + 18], STATC[:, o + 17:o + 18],
                       EPS_Q, None, ALU.max, reads=[bSk], writes=[bSk])
                    em.emit("vector", lambda e, o=o: e.reciprocal(
                        STATC[:, o + 18:o + 19], STATC[:, o + 17:o + 18]),
                        reads=[bSk], writes=[bSk])
                    tt("vector", STATC[:, o + 19:o + 20], STATC[:, o + 18:o + 19],
                       STATC[:, o + 16:o + 17], ALU.mult, reads=[bSk], writes=[bSk])
                    ts("vector", STATC[:, o + 19:o + 20], STATC[:, o + 19:o + 20],
                       127.0, None, ALU.mult, reads=[bSk], writes=[bSk])
                    act("scalar", za, za, AF.Identity, bias=ZCOL[:, 0:1],
                        scale=STATC[:, o + 19:o + 20],
                        reads=[bSk, bZCc, B("ZCOL")], writes=[bZCc])
                    ts("vector", STATC[:, o + 20:o + 21], STATC[:, o + 17:o + 18],
                       1.0 / 127.0, None, ALU.mult, reads=[bSk], writes=[bSk])
                    tt("vector", STATC[:, o + 20:o + 21], STATC[:, o + 20:o + 21],
                       SWCOL[:, 0:1], ALU.mult, reads=[B("SWCOL")], writes=[bSk])
                    tt("vector", STATC[:, o + ak:o + ak + 1], STATC[:, o + 20:o + 21],
                       STATC[:, o + 20:o + 21], ALU.mult, reads=[bSk], writes=[bAK])
                    tt("vector", STATC[:, o + ak:o + ak + 1], STATC[:, o + ak:o + ak + 1],
                       ABCOL[:, 0:1], ALU.mult, reads=[B("ABCOL")], writes=[bAK])
                    cp("scalar", di, za, reads=[bZCc], writes=[bDIc])
                    tt("vector", d1, za, di, ALU.subtract, reads=[bZCc, bDIc],
                       writes=[bD1c])
                    stt("vector", d2, d1, 0.5, di, ALU.is_ge, ALU.add,
                        reads=[bD1c, bDIc], writes=[bD2c])
                    stt("vector", xqt, d1, -0.5, d2, ALU.is_lt, ALU.subtract,
                        reads=[bD1c, bD2c], writes=[bD2c])
                    for j in range(DC):
                        transp(PT[:, (j % 8) * 128:(j % 8) * 128 + 128],
                               xqt[:, j * 128:(j + 1) * 128],
                               reads=[bD2c], writes=[bPT])
                    g4 = min(4, DC)
                    for half in range(DC // g4):
                        cp("scalar",
                           bass.AP(XQ1, sl * DC * c.T1 + half * g4 * c.T1 + tc * 128,
                                   [[3 * DC * c.T1, 128], [c.T1, g4], [1, 128]]),
                           bass.AP(PT, half * g4 * 128, [[1024, 128], [128, g4], [1, 128]]),
                           reads=[bPT], writes=[B(f"XQ1{s}_{sl}")])

            def b1_mstage(t1):
                sl = t1 % 3
                for tc in range(NC1):
                    bHAc, bHBc = B(f"HA{s}_{tc}"), B(f"HBt{s}_{tc}")
                    for hb2 in range(0, HID // 512, 2):
                        pairs = [(0, hb2)]
                        if hb2 + 1 < HID // 512:
                            pairs.append((1, hb2 + 1))
                        for dc in range(DC):
                            for pi, hb in pairs:
                                ph, bph = (P0, bP0) if pi == 0 else (P1, bP1)
                                mm(ph[:, 0:512],
                                   XQ1[:, sl * DC * c.T1 + dc * c.T1 + tc * 128:
                                       sl * DC * c.T1 + dc * c.T1 + (tc + 1) * 128],
                                   W1S[:, dc * HID + hb * 512:dc * HID + (hb + 1) * 512],
                                   start=(dc == 0), stop=(dc == DC - 1),
                                   reads=[B(f"XQ1{s}_{sl}"), bW1], writes=[bph])
                        for pi, hb in pairs:
                            ph, bph = (P0, bP0) if pi == 0 else (P1, bP1)
                            hs = HA[:, tc * HID + hb * 512:tc * HID + (hb + 1) * 512]
                            act("scalar", hs, ph[:, 0:512], AF.Relu, reads=[bph],
                                writes=[bHAc])
                            us = HBt[:, tc * HID + hb * 512:tc * HID + (hb + 1) * 512]
                            act("scalar", us, hs, AF.Square, reads=[bHAc],
                                writes=[bHBc])
                    k4 = (t1 * NC1 + tc) % 4
                    o = 32 * k4
                    tcol = t1 * NC1 + tc
                    ak = 22 + (tcol // 4) % 2
                    bSm = bSTm[k4]
                    bHAc, bHBc, bHIc = B(f"HA{s}_{tc}"), B(f"HBt{s}_{tc}"), B(f"HI8{s}")
                    ua = HBt[:, tc * HID:(tc + 1) * HID]
                    aa = HA[:, tc * HID:(tc + 1) * HID]
                    ia = HI8[:, 0:HID]
                    act("scalar", aa, ua, AF.Identity, bias=ABCOL[:, 1:2],
                        scale=STATC[:, o + ak:o + ak + 1],
                        reads=[bHBc, bAK, B("ABCOL")], writes=[bHAc])
                    act("scalar", ua, aa, AF.Square, accum=STATC[:, o + 24:o + 25],
                        reads=[bHAc], writes=[bHBc, bSm])
                    ts("vector", STATC[:, o + 24:o + 25], STATC[:, o + 24:o + 25],
                       1.0 / HID, EPS_NORM, ALU.mult, ALU.add, reads=[bSm], writes=[bSm])
                    em.emit("vector", lambda e, o=o: e.reciprocal(
                        STATC[:, o + 25:o + 26], STATC[:, o + 24:o + 25]),
                        reads=[bSm], writes=[bSm])
                    act("scalar", STATC[:, o + 25:o + 26], STATC[:, o + 25:o + 26],
                        AF.Sqrt, reads=[bSm], writes=[bSm])
                    tt("vector", ua, aa, C2REP[:], ALU.mult, reads=[bHAc, B("C2REP")],
                       writes=[bHBc])
                    red("vector", STATC[:, o + 26:o + 27], ua, ALU.max, absval=True,
                        reads=[bHBc], writes=[bSm])
                    tt("vector", STATC[:, o + 26:o + 27], STATC[:, o + 26:o + 27],
                       STATC[:, o + 25:o + 26], ALU.mult, reads=[bSm], writes=[bSm])
                    ts("vector", STATC[:, o + 26:o + 27], STATC[:, o + 26:o + 27],
                       EPS_Q, None, ALU.max, reads=[bSm], writes=[bSm])
                    em.emit("vector", lambda e, o=o: e.reciprocal(
                        STATC[:, o + 27:o + 28], STATC[:, o + 26:o + 27]),
                        reads=[bSm], writes=[bSm])
                    tt("vector", STATC[:, o + 28:o + 29], STATC[:, o + 27:o + 28],
                       STATC[:, o + 25:o + 26], ALU.mult, reads=[bSm], writes=[bSm])
                    ts("vector", STATC[:, o + 28:o + 29], STATC[:, o + 28:o + 29],
                       127.0, None, ALU.mult, reads=[bSm], writes=[bSm])
                    act("scalar", aa, ua, AF.Identity, bias=ZCOL[:, 0:1],
                        scale=STATC[:, o + 28:o + 29],
                        reads=[bHBc, bSm, B("ZCOL")], writes=[bHAc])
                    ts("vector", S2COL[:, tcol:tcol + 1], STATC[:, o + 26:o + 27],
                       1.0 / 127.0, None, ALU.mult, reads=[bSm], writes=[bS2])
                    tt("vector", S2COL[:, tcol:tcol + 1], S2COL[:, tcol:tcol + 1],
                       SWCOL[:, 1:2], ALU.mult, reads=[B("SWCOL")], writes=[bS2])
                    cp("scalar", ia, aa, reads=[bHAc], writes=[bHIc])
                    tt("vector", ua, aa, ia, ALU.subtract, reads=[bHAc, bHIc],
                       writes=[bHBc])
                    stt("vector", aa, ua, 0.5, ia, ALU.is_ge, ALU.add,
                        reads=[bHBc, bHIc], writes=[bHAc])
                    stt("vector", ua, ua, -0.5, aa, ALU.is_lt, ALU.subtract,
                        reads=[bHAc], writes=[bHBc])

            b1_dstage(0)
            if c.NT1 > 1:
                b1_dstage(1)
            for t1 in range(c.NT1):
                if t1 >= 1:
                    for tc in range(NC1):
                        xq2_finish(t1 - 1, tc)
                if t1 + 2 < c.NT1:
                    b1_dstage(t1 + 2)
                b1_mstage(t1)
            for tc in range(c.T1 // 128):
                xq2_finish(c.NT1 - 1, tc)

            em.barrier()
            br.close()

            b2 = contextlib.ExitStack()
            W2S = b2.enter_context(nc.sbuf_tensor(f"W2S{s}", [128, HC * D], BF16))
            GBR = b2.enter_context(nc.sbuf_tensor(f"GBR{s}", [128, D], BF16))
            GB2 = b2.enter_context(nc.sbuf_tensor(f"GB2{s}", [128, D], F32))
            XQ2F = b2.enter_context(nc.sbuf_tensor(f"XQ2F{s}", [128, 2 * HC * c.T2], BF16))
            XMT2 = b2.enter_context(nc.sbuf_tensor(f"XMT2{s}", [128, (c.T2 // 128) * D], F32))
            OUTS = b2.enter_context(nc.sbuf_tensor(f"OUTS{s}", [128, (c.T2 // 128) * D], F32))
            bW2, bGBR, bGB2 = B(f"W2S{s}"), B(f"GBR{s}"), B(f"GB2{s}")
            bXM2 = B(f"XMT2{s}")
            bXQFh = {(t2s, hh): B(f"XQ2F{s}_{t2s}_{hh}")
                     for t2s in range(2) for hh in range(2)}

            bcast("sync", GB2[:], mod_scr.tensor, (s * 6 + 5) * D, D,
                  reads=[B("mod_scr")], writes=[bGB2])
            ts("vector", GBR[:], GB2[:], 0.0, 2.0, ALU.is_ge, ALU.mult, reads=[bGB2],
               writes=[bGBR])
            ts("vector", GBR[:], GBR[:], -1.0, None, ALU.add, reads=[bGBR],
               writes=[bGBR])
            act("scalar", GB2[:], GB2[:], AF.Abs, reads=[bGB2], writes=[bGB2])
            ts("vector", GB2[:], GB2[:], 0.25, None, ALU.max, reads=[bGB2],
               writes=[bGB2])
            tt("vector", GB2[:], GB2[:], GBR[:], ALU.mult, reads=[bGBR], writes=[bGB2])
            cp("vector", GBR[:], GB2[:], reads=[bGB2], writes=[bGBR])
            for hh, q_ in ((0, "sync"), (1, "gpsimd")):
                em.dma(q_,
                       bass.AP(W2S, hh * (HC // 2) * D,
                               [[HC * D, 128], [D, HC // 2], [1, D]]),
                       bass.AP(w2q_scr.tensor, hh * (HC // 2) * 128 * D,
                               [[D, 128], [128 * D, HC // 2], [1, D]]),
                       reads=[B("qS1")], writes=[bW2])

            HC2 = HC // 2
            def xq2f_load(t2):
                t2s = t2 % 2
                for hh in range(2):
                    em.dma("sync" if hh == 0 else "gpsimd",
                           bass.AP(XQ2F, t2s * HC * c.T2 + hh * HC2 * c.T2,
                                   [[2 * HC * c.T2, 128], [c.T2, HC2], [1, c.T2]]),
                           bass.AP(xq2_scr.tensor, hh * HC2 * 128 * S + t2 * c.T2,
                                   [[S, 128], [128 * S, HC2], [1, c.T2]]),
                           reads=[bXQscr], writes=[bXQFh[(t2s, hh)]])

            xq2f_load(0)
            for t2 in range(c.NT2):
                t2s = t2 % 2
                tb = s * S + t2 * c.T2
                if t2 + 1 < c.NT2:
                    xq2f_load(t2 + 1)
                em.dma("scalar",
                       bass.AP(XMT2, 0, [[(c.T2 // 128) * D, 128], [D, c.T2 // 128], [1, D]]),
                       bass.AP(xmid_scr.tensor, tb * D,
                               [[D, 128], [128 * D, c.T2 // 128], [1, D]]),
                       reads=[bXMS], writes=[bXM2])
                for tc in range(c.T2 // 128):
                    py, bpy = (P0, bP0) if tc % 2 == 0 else (P1, bP1)
                    nh = max(1, D // 512)
                    dw = D // nh
                    for hc in range(HC):
                        for half in range(nh):
                            mm(py[:, half * dw:(half + 1) * dw],
                               XQ2F[:, t2s * HC * c.T2 + hc * c.T2 + tc * 128:
                                   t2s * HC * c.T2 + hc * c.T2 + (tc + 1) * 128],
                               W2S[:, hc * D + half * dw:hc * D + (half + 1) * dw],
                               start=(hc == 0), stop=(hc == HC - 1),
                               reads=[bXQFh[(t2s, hc // HC2)], bW2], writes=[bpy])
                    oa = OUTS[:, tc * D:(tc + 1) * D]
                    bOUc = B(f"OUTS{s}_{tc % 2}")
                    tcol = (t2 * c.T2) // 128 + tc
                    ts("vector", oa, py[:, 0:D], S2COL[:, tcol:tcol + 1], None,
                       ALU.mult, reads=[bpy, bS2], writes=[bOUc])
                    tt("vector", oa, oa, GBR[:], ALU.mult, reads=[bGBR], writes=[bOUc])
                    tt("vector", oa, oa, XMT2[:, tc * D:(tc + 1) * D], ALU.add,
                       reads=[bXM2], writes=[bOUc])
                    em.dma("scalar",
                           bass.AP(out_t.tensor, (tb + tc * 128) * D,
                                   [[D, 128], [1, D]]),
                           oa, reads=[bOUc], writes=[B("out")])

            em.barrier()
            b2.close()

        em.finish()

    return nc


_NC_CACHE = {}


def _get_nc():
    if "full" not in _NC_CACHE:
        _NC_CACHE["full"] = build_program(CFG)
    return _NC_CACHE["full"]


def make_inputs(core, cfg, L0_attn_x, L0_attn_W_qk, L0_attn_b_qk, L0_attn_W_v,
                L0_attn_b_v, L0_attn_W_out, L0_attn_b_out, L0_attn_ln_gamma,
                L0_attn_ln_beta, L0_ffn_norm1_w, L0_ffn_w1, L0_ffn_act_scale,
                L0_ffn_act_bias, L0_ffn_norm2_w, L0_ffn_w2, L0_c, L0_wmod):
    c = cfg
    D, QK = c.D, c.QK
    f32 = np.float32
    perm = np.arange(QK)
    h_idx, d_idx = perm // c.DH, perm % c.DH
    qcols = h_idx * 2 * c.DH + 2 * d_idx
    kcols = qcols + 1
    W_qk = np.asarray(L0_attn_W_qk, f32)
    b_qk = np.asarray(L0_attn_b_qk, f32)
    wqk_p = np.concatenate([W_qk[:, qcols], W_qk[:, kcols]], axis=1)
    bqk2 = np.stack([b_qk[qcols], b_qk[kcols]], axis=0)
    b0 = core * c.NSEQ
    x = np.asarray(L0_attn_x, f32)
    return {
        "x": np.ascontiguousarray(x[b0:b0 + c.NSEQ].reshape(c.NSEQ * c.S, D)),
        "wqk": np.ascontiguousarray(wqk_p),
        "bqk2": np.ascontiguousarray(bqk2),
        "wv": np.ascontiguousarray(np.asarray(L0_attn_W_v, f32)),
        "bv": np.ascontiguousarray(np.asarray(L0_attn_b_v, f32).reshape(1, QK)),
        "wout": np.ascontiguousarray(np.asarray(L0_attn_W_out, f32)),
        "bout": np.ascontiguousarray(np.asarray(L0_attn_b_out, f32).reshape(1, D)),
        "ln_g": np.ascontiguousarray(np.asarray(L0_attn_ln_gamma, f32).reshape(1, D)),
        "ln_b": np.ascontiguousarray(np.asarray(L0_attn_ln_beta, f32).reshape(1, D)),
        "n1w": np.ascontiguousarray(np.asarray(L0_ffn_norm1_w, f32).reshape(1, D)),
        "w1t": np.ascontiguousarray(np.asarray(L0_ffn_w1, f32).T),
        "actsc": np.ascontiguousarray(np.array(
            [[np.asarray(L0_ffn_act_scale, f32).reshape(()),
              np.asarray(L0_ffn_act_bias, f32).reshape(())]], f32)),
        "n2w": np.ascontiguousarray(np.asarray(L0_ffn_norm2_w, f32).reshape(1, c.HID)),
        "w2t": np.ascontiguousarray(np.asarray(L0_ffn_w2, f32).T),
        "c_t": np.ascontiguousarray(np.asarray(L0_c, f32)[b0:b0 + c.NSEQ].T),
        "wmod": np.ascontiguousarray(np.asarray(L0_wmod, f32)),
    }


def kernel(**inputs):
    c = CFG
    nc = _get_nc()
    in_maps = [make_inputs(core, c, **inputs) for core in range(N_CORES)]
    res = run_bass_kernel_spmd(nc, in_maps, core_ids=list(range(N_CORES)))
    outs = [np.asarray(r["out"]).reshape(c.NSEQ, c.S, c.D) for r in res.results]
    return np.concatenate(outs, axis=0)


# version 21
# speedup vs baseline: 1.2487x; 1.1831x over previous
import contextlib
import numpy as np
import concourse.bass as bass
import concourse.mybir as mybir
from concourse.bass_utils import run_bass_kernel_spmd

F32 = mybir.dt.float32
BF16 = mybir.dt.bfloat16
I8 = mybir.dt.int8
AF = mybir.ActivationFunctionType
ALU = mybir.AluOpType
AX = mybir.AxisListType

RSQRT2 = float(2.0 ** (-0.5))
LN_EPS = 1e-5
EPS_NORM = 1e-8
EPS_Q = 1e-5

N_CORES = 8


class Cfg:
    def __init__(self, S=2048, D=1024, HID=4096, NSEQ=4, H=8, AG=512, T1=256, T2=256):
        self.S, self.D, self.HID, self.NSEQ, self.H = S, D, HID, NSEQ, H
        self.DH = 64
        self.AG, self.T1, self.T2 = AG, T1, T2
        self.DC = D // 128
        self.QK = self.H * self.DH
        self.QC = self.QK // 128
        self.HC = HID // 128
        self.KC = S // 128
        self.NG = S // AG
        self.NT1 = S // T1
        self.NT2 = S // T2
        self.NP = self.H // 2
        self.VW = self.H * 65 + 8
        self.DBG = False


CFG = Cfg()


class Buf:
    __slots__ = ("name", "writer", "readers")

    def __init__(self, name):
        self.name = name
        self.writer = None
        self.readers = []


class Em:
    ENGINES = ("tensor", "vector", "scalar", "gpsimd", "sync")
    NSLOT = {"sync": 12, "scalar": 8, "gpsimd": 4, "vector": 2}

    def __init__(self, nc):
        self.nc = nc
        self.TRACKS = list(self.ENGINES)
        self.dma_slots = {}
        for q, n in self.NSLOT.items():
            self.dma_slots[q] = [f"{q}_dma{i}" for i in range(n)]
            self.TRACKS += self.dma_slots[q]
        self.sems = {}
        self.cnt = {t: 0 for t in self.TRACKS}
        self.waited = {e: {t: 0 for t in self.TRACKS} for e in self.ENGINES}
        self.eng = {"tensor": nc.tensor, "vector": nc.vector, "scalar": nc.scalar,
                    "gpsimd": nc.gpsimd, "sync": nc.sync}
        self.dma_idx = {q: 0 for q in self.NSLOT}
        self.bufs = {}

    def B(self, name):
        if name not in self.bufs:
            self.bufs[name] = Buf(name)
        return self.bufs[name]

    def init_sems(self, ctx):
        for t in self.TRACKS:
            self.sems[t] = ctx.enter_context(self.nc.semaphore(f"sem_{t}"))

    def _wait(self, eng, dep_eng, dep_cnt):
        if dep_cnt <= 0 or self.waited[eng][dep_eng] >= dep_cnt:
            return
        self.eng[eng].wait_ge(self.sems[dep_eng], dep_cnt)
        self.waited[eng][dep_eng] = dep_cnt

    def emit(self, eng, fn, reads=(), writes=(), inc=1, track=None):
        if track is None:
            track = eng
        for b in reads:
            if b.writer is not None:
                self._wait(eng, b.writer[0], b.writer[1])
        for b in writes:
            if b.writer is not None:
                self._wait(eng, b.writer[0], b.writer[1])
            for tok in b.readers:
                self._wait(eng, tok[0], tok[1])
        inst = fn(self.eng[eng])
        self.cnt[track] += inc
        inst.then_inc(self.sems[track], inc)
        tok = (track, self.cnt[track])
        for b in reads:
            b.readers.append(tok)
        for b in writes:
            b.writer = tok
            b.readers = []
        return tok

    def dma(self, queue, out_ap, in_ap, reads=(), writes=(), slow=False):
        slot = self.dma_slots[queue][self.dma_idx[queue] % self.NSLOT[queue]]
        self.dma_idx[queue] += 1
        self._wait(queue, slot, self.cnt[slot])
        if slow:
            return self.emit(queue, lambda e: e.dma_start(
                out=out_ap, in_=in_ap, allow_slow_non_contiguous=True),
                reads=reads, writes=writes, inc=16, track=slot)
        return self.emit(queue, lambda e: e.dma_start(out=out_ap, in_=in_ap),
                         reads=reads, writes=writes, inc=16, track=slot)

    def barrier(self):
        snap = dict(self.cnt)
        for e in self.ENGINES:
            for t in self.TRACKS:
                if t != e:
                    self._wait(e, t, snap[t])

    def finish(self):
        snap = dict(self.cnt)
        for t in self.TRACKS:
            if t != "gpsimd":
                self._wait("gpsimd", t, snap[t])


def build_program(c: Cfg):
    nc = bass.Bass("TRN2", target_bir_lowering=False)
    S, D, HID, NSEQ, H = c.S, c.D, c.HID, c.NSEQ, c.H
    QK, DC, QC, HC, KC, VW = c.QK, c.DC, c.QC, c.HC, c.KC, c.VW

    def din(name, shape, dt=F32):
        return nc.dram_tensor(name, list(shape), dt, kind="ExternalInput").ap()

    x_in = din("x", (NSEQ * S, D))
    wqk = din("wqk", (D, 2 * QK))
    bqk2 = din("bqk2", (2, QK))
    wv = din("wv", (D, QK))
    bv = din("bv", (1, QK))
    wout = din("wout", (QK, D))
    bout = din("bout", (1, D))
    ln_g = din("ln_g", (1, D))
    ln_b = din("ln_b", (1, D))
    n1w = din("n1w", (1, D))
    w1t = din("w1t", (D, HID))
    actsc = din("actsc", (1, 2))
    n2w = din("n2w", (1, HID))
    w2t = din("w2t", (HID, D))
    c_t = din("c_t", (D, NSEQ))
    wmod = din("wmod", (D, 6 * D))

    out_t = nc.dram_tensor("out", [NSEQ * S, D], F32, kind="ExternalOutput").ap()
    xmid_scr = nc.dram_tensor("xmid_scr", [NSEQ * S, D], F32).ap()
    mod_scr = nc.dram_tensor("mod_scr", [NSEQ * 6, D], F32).ap()
    w1q_scr = nc.dram_tensor("w1q_scr", [D, HID], BF16).ap()
    w2q_scr = nc.dram_tensor("w2q_scr", [HID, D], BF16).ap()
    xq2_scr = nc.dram_tensor("xq2_scr", [HID, S], BF16).ap()
    tiny_scr = nc.dram_tensor("tiny_scr", [64], F32).ap()
    dbg_scr = nc.dram_tensor("dbg_scr", [c.NG * 128, QC * c.AG], F32).ap() if c.DBG else None
    dbg2_scr = nc.dram_tensor("dbg2_scr", [(c.AG // 128) * 128, D], F32).ap() if c.DBG else None

    ctx = contextlib.ExitStack()
    em = Em(nc)

    with ctx:
        em.init_sems(ctx)
        B = em.B

        WQK = ctx.enter_context(nc.sbuf_tensor("WQK", [128, DC * 2 * QK], BF16))
        WV = ctx.enter_context(nc.sbuf_tensor("WV", [128, DC * QK], BF16))
        WOUT = ctx.enter_context(nc.sbuf_tensor("WOUT", [128, QC * D], BF16))
        IDN = ctx.enter_context(nc.sbuf_tensor("IDN", [128, 128], BF16))
        ONESF = ctx.enter_context(nc.sbuf_tensor("ONESF", [1, 128], F32))
        ONESC = ctx.enter_context(nc.sbuf_tensor("ONESC", [128, 1], F32))
        EPSCOL = ctx.enter_context(nc.sbuf_tensor("EPSCOL", [128, 1], F32))
        ZCOL = ctx.enter_context(nc.sbuf_tensor("ZCOL", [128, 1], F32))
        IND64 = ctx.enter_context(nc.sbuf_tensor("IND64", [128, 256], F32))
        BQS = ctx.enter_context(nc.sbuf_tensor("BQS", [128, QC], F32))
        BKS = ctx.enter_context(nc.sbuf_tensor("BKS", [128, QC], F32))
        BVC = ctx.enter_context(nc.sbuf_tensor("BVC", [128, QC], BF16))
        BOREP = ctx.enter_context(nc.sbuf_tensor("BOREP", [128, D], BF16))
        GREP = ctx.enter_context(nc.sbuf_tensor("GREP", [128, D], BF16))
        BREP = ctx.enter_context(nc.sbuf_tensor("BREP", [128, D], BF16))
        C1REP = ctx.enter_context(nc.sbuf_tensor("C1REP", [128, D], F32))
        C1IREP = ctx.enter_context(nc.sbuf_tensor("C1IREP", [128, D], F32))
        C2REP = ctx.enter_context(nc.sbuf_tensor("C2REP", [128, HID], BF16))
        ABCOL = ctx.enter_context(nc.sbuf_tensor("ABCOL", [128, 2], F32))
        SWCOL = ctx.enter_context(nc.sbuf_tensor("SWCOL", [128, 2], F32))
        IND8S = ctx.enter_context(nc.sbuf_tensor("IND8S", [128, QC * 8], BF16))
        NNB = ctx.enter_context(nc.sbuf_tensor("NNB", [128, c.NP], F32))
        S2COL = ctx.enter_context(nc.sbuf_tensor("S2COL", [128, S // 128], F32))
        STATC = ctx.enter_context(nc.sbuf_tensor("STATC", [128, 160], F32))

        P0 = ctx.enter_context(nc.psum_tensor("P0", [128, 1024], F32))
        P1 = ctx.enter_context(nc.psum_tensor("P1", [128, 1024], F32))
        P2A = ctx.enter_context(nc.psum_tensor("P2A", [128, 512], F32))
        P3 = ctx.enter_context(nc.psum_tensor("P3", [128, 512], F32))
        P4 = ctx.enter_context(nc.psum_tensor("P4", [128, 512], F32))
        PT = ctx.enter_context(nc.psum_tensor("PT", [128, 1024], BF16))
        bP0, bP1, bP2, bP3, bP4 = B("P0"), B("P1"), B("P2A"), B("P3"), B("P4")
        bPT = B("PT")
        bPT2 = [B("PTa"), B("PTb")]
        bSTs = [B(f"STATC{k}") for k in range(4)]
        bSTm = [B(f"STATM{k}") for k in range(4)]
        bAK = B("AKCOLS")
        bST = B("STATC")

        def ts(eng, out, in0, s1, s2, op0, op1=None, reads=(), writes=()):
            if op1 is None:
                em.emit(eng, lambda e: e.tensor_scalar(out, in0, s1, None, op0=op0),
                        reads=reads, writes=writes)
            else:
                em.emit(eng, lambda e: e.tensor_scalar(out, in0, s1, s2, op0=op0, op1=op1),
                        reads=reads, writes=writes)

        def tt(eng, out, in0, in1, op, reads=(), writes=()):
            em.emit(eng, lambda e: e.tensor_tensor(out, in0, in1, op=op),
                    reads=reads, writes=writes)

        def stt(eng, out, in0, s, in1, op0, op1, reads=(), writes=()):
            em.emit(eng, lambda e: e.scalar_tensor_tensor(out, in0, s, in1, op0=op0,
                                                          op1=op1),
                    reads=reads, writes=writes)

        def act(eng, out, in0, func, bias=0.0, scale=1.0, accum=None, reads=(), writes=()):
            if accum is None:
                em.emit(eng, lambda e: e.activation(out, in0, func, bias=bias,
                                                    scale=scale),
                        reads=reads, writes=writes)
            else:
                em.emit(eng, lambda e: e.activation(out, in0, func, bias=bias,
                                                    scale=scale, accum_out=accum),
                        reads=reads, writes=writes)

        def cp(eng, out, in0, reads=(), writes=()):
            if eng == "scalar":
                em.emit(eng, lambda e: e.copy(out, in0), reads=reads, writes=writes)
            else:
                em.emit(eng, lambda e: e.tensor_copy(out, in0), reads=reads,
                        writes=writes)

        def red(eng, out, in0, op, absval=False, reads=(), writes=()):
            em.emit(eng, lambda e: e.tensor_reduce(out, in0, axis=AX.X, op=op,
                                                   apply_absolute_value=absval),
                    reads=reads, writes=writes)

        def mm(out, lhsT, rhs, start, stop, reads=(), writes=(), tp=None):
            if tp is None:
                em.emit("tensor", lambda e: e.matmul(out, lhsT, rhs, start=start,
                                                     stop=stop),
                        reads=reads, writes=writes)
            else:
                em.emit("tensor", lambda e: e.matmul(out, lhsT, rhs, start=start,
                                                     stop=stop, tile_position=tp),
                        reads=reads, writes=writes)

        def transp(out, in0, reads=(), writes=()):
            em.emit("tensor", lambda e: e.transpose(out, in0, IDN[:]),
                    reads=list(reads) + [B("IDN")], writes=writes)

        def bcast(queue, out_tile, src_t, src_off, n, reads=(), writes=()):
            src = bass.AP(src_t, src_off, [[0, 128], [1, n]])
            em.dma(queue, out_tile, src, reads=reads, writes=writes)

        def ln_bn(xa, width, rbuf, k):
            o = 32 * k
            bSk = bSTs[k]
            nparts = width // 512 if width > 512 else 1
            wper = width // nparts
            assert wper <= 512 and nparts <= 4
            for i in range(nparts):
                em.emit("vector", lambda e, i=i: e.bn_stats(
                    STATC[:, o + 6 * i:o + 6 * i + 6], xa[:, i * wper:(i + 1) * wper]),
                    reads=[rbuf], writes=[bSk])
            em.emit("vector", lambda e: e.bn_aggr(STATC[:, o + 12:o + 14],
                                                  STATC[:, o:o + 6 * nparts]),
                    reads=[bSk], writes=[bSk])

        def ln_sqrt(k):
            o = 32 * k
            act("scalar", STATC[:, o + 13:o + 14], STATC[:, o + 13:o + 14], AF.Sqrt,
                bias=EPSCOL[:, 0:1], reads=[bSTs[k], B("EPSCOL")], writes=[bSTs[k]])

        def ln_recip(k):
            o = 32 * k
            em.emit("vector", lambda e: e.reciprocal(STATC[:, o + 13:o + 14],
                                                     STATC[:, o + 13:o + 14]),
                    reads=[bSTs[k]], writes=[bSTs[k]])
            stt("vector", STATC[:, o + 14:o + 15], STATC[:, o + 12:o + 13], -1.0,
                STATC[:, o + 13:o + 14], ALU.mult, ALU.mult,
                reads=[bSTs[k]], writes=[bSTs[k]])

        def ln_apply(eng_out, in_ap, k, rbufs, wbufs):
            o = 32 * k
            act("scalar", eng_out, in_ap, AF.Identity, bias=STATC[:, o + 14:o + 15],
                scale=STATC[:, o + 13:o + 14], reads=list(rbufs) + [bSTs[k]],
                writes=wbufs)

        def ln_stats(xa, width, rbuf, k):
            ln_bn(xa, width, rbuf, k)
            ln_sqrt(k)
            ln_recip(k)

        em.emit("gpsimd", lambda e: e.iota(
            bass.AP(IDN, 0, [[128, 128], [1, 128]]), [[1, 128]], channel_multiplier=-1,
            allow_small_or_imprecise_dtypes=True), writes=[B("IDN")])
        ts("vector", IDN[:], IDN[:], 0.0, None, ALU.is_equal, reads=[B("IDN")],
           writes=[B("IDN")])
        em.emit("vector", lambda e: e.memset(ONESF[:], 1.0), writes=[B("ONESF")])
        em.emit("vector", lambda e: e.memset(ONESC[:], 1.0), writes=[B("ONESC")])
        em.emit("vector", lambda e: e.memset(EPSCOL[:], LN_EPS), writes=[B("EPSCOL")])
        em.emit("vector", lambda e: e.memset(ZCOL[:], 0.0), writes=[B("ZCOL")])
        em.emit("vector", lambda e: e.memset(IND64[:], 0.0), writes=[B("IND64")])
        em.emit("vector", lambda e: e.memset(IND64[64:65, 0:64], 1.0), writes=[B("IND64")])
        em.emit("vector", lambda e: e.memset(IND64[64:65, 192:256], 1.0), writes=[B("IND64")])
        em.emit("vector", lambda e: e.memset(IND8S[:], 0.0), writes=[B("IND8S")])
        for fc in range(QC):
            em.emit("vector", lambda e, fc=fc: e.memset(
                IND8S[0:64, fc * 8 + 2 * fc:fc * 8 + 2 * fc + 1], 1.0), writes=[B("IND8S")])
            em.emit("vector", lambda e, fc=fc: e.memset(
                IND8S[64:128, fc * 8 + 2 * fc + 1:fc * 8 + 2 * fc + 2], 1.0),
                writes=[B("IND8S")])

        for tile, tname, srcx, n in ((GREP, "GREP", ln_g, D), (BREP, "BREP", ln_b, D),
                                     (C2REP, "C2REP", n2w, HID),
                                     (BOREP, "BOREP", bout, D)):
            em.dma("gpsimd", tile[:], bass.AP(srcx.tensor, 0, [[0, 128], [1, n]]),
                   writes=[B(tname)])
        bcast("sync", C1REP[:], n1w.tensor, 0, D, writes=[B("C1REP")])
        bcast("sync", ABCOL[:, 0:2], actsc.tensor, 0, 2, writes=[B("ABCOL")])
        em.emit("vector", lambda e: e.reciprocal(C1IREP[:], C1REP[:]),
                reads=[B("C1REP")], writes=[B("C1IREP")])

        em.dma("gpsimd", bass.AP(BVC, 0, [[QC, 128], [1, QC]]),
               bass.AP(bv.tensor, 0, [[1, 128], [128, QC]]), writes=[B("BVC")],
               slow=True)
        em.dma("sync", bass.AP(BQS, 0, [[QC, 128], [1, QC]]),
               bass.AP(bqk2.tensor, 0, [[1, 128], [128, QC]]), writes=[B("BQS")],
               slow=True)
        em.dma("sync", bass.AP(BKS, 0, [[QC, 128], [1, QC]]),
               bass.AP(bqk2.tensor, QK, [[1, 128], [128, QC]]), writes=[B("BKS")],
               slow=True)
        ts("vector", BQS[:], BQS[:], 0.125, None, ALU.mult, reads=[B("BQS")],
           writes=[B("BQS")])

        em.dma("gpsimd",
               bass.AP(WQK, 0, [[DC * 2 * QK, 128], [2 * QK, DC], [1, 2 * QK]]),
               bass.AP(wqk.tensor, 0, [[2 * QK, 128], [128 * 2 * QK, DC], [1, 2 * QK]]),
               writes=[B("WQK")])
        em.dma("gpsimd",
               bass.AP(WV, 0, [[DC * QK, 128], [QK, DC], [1, QK]]),
               bass.AP(wv.tensor, 0, [[QK, 128], [128 * QK, DC], [1, QK]]),
               writes=[B("WV")])
        em.dma("gpsimd",
               bass.AP(WOUT, 0, [[QC * D, 128], [D, QC], [1, D]]),
               bass.AP(wout.tensor, 0, [[D, 128], [128 * D, QC], [1, D]]),
               writes=[B("WOUT")])

        arena = contextlib.ExitStack()
        BVTMP = arena.enter_context(nc.sbuf_tensor("BVTMP", [1, D], F32))
        for half in range(max(1, D // 512)):
            dw2 = min(512, D)
            for ci in range(QC):
                mm(P0[0:1, half * dw2:half * dw2 + dw2], BVC[:, ci:ci + 1],
                   WOUT[:, ci * D + half * dw2:ci * D + (half + 1) * dw2],
                   start=(ci == 0), stop=(ci == QC - 1),
                   reads=[B("BVC"), B("WOUT")], writes=[bP0])
        cp("vector", BVTMP[0:1, :], P0[0:1, 0:D], reads=[bP0], writes=[B("BVTMP")])
        for half in range(max(1, D // 512)):
            dw2 = min(512, D)
            mm(P0[:, half * dw2:half * dw2 + dw2], ONESF[0:1, :],
               BVTMP[0:1, half * dw2:half * dw2 + dw2],
               start=True, stop=True, reads=[B("BVTMP"), B("ONESF")], writes=[bP0])
            tt("vector", BOREP[:, half * dw2:half * dw2 + dw2],
               BOREP[:, half * dw2:half * dw2 + dw2],
               P0[:, half * dw2:half * dw2 + dw2], ALU.add,
               reads=[bP0], writes=[B("BOREP")])
        SILC = arena.enter_context(nc.sbuf_tensor("SILC", [128, DC * NSEQ], F32))
        MODT = arena.enter_context(nc.sbuf_tensor("MODT", [NSEQ, 6 * D], F32))
        WMS = arena.enter_context(nc.sbuf_tensor("WMS", [128, 2 * 512], F32))
        em.dma("sync", bass.AP(SILC, 0, [[DC * NSEQ, 128], [NSEQ, DC], [1, NSEQ]]),
               bass.AP(c_t.tensor, 0, [[NSEQ, 128], [128 * NSEQ, DC], [1, NSEQ]]),
               writes=[B("SILC")], slow=True)
        act("scalar", SILC[:], SILC[:], AF.Silu, reads=[B("SILC")], writes=[B("SILC")])
        for nt in range(6 * D // 512):
            for dc in range(DC):
                wslot = (nt * DC + dc) % 2
                bWM = B(f"WMS{wslot}")
                em.dma("sync", WMS[:, wslot * 512:wslot * 512 + 512],
                       bass.AP(wmod.tensor, dc * 128 * 6 * D + nt * 512,
                               [[6 * D, 128], [1, 512]]),
                       writes=[bWM])
                mm(P3[0:NSEQ, 0:512], SILC[:, dc * NSEQ:(dc + 1) * NSEQ],
                   WMS[:, wslot * 512:wslot * 512 + 512],
                   start=(dc == 0), stop=(dc == DC - 1),
                   reads=[B("SILC"), bWM], writes=[bP3])
            cp("vector", MODT[0:NSEQ, nt * 512:(nt + 1) * 512], P3[0:NSEQ, 0:512],
               reads=[bP3], writes=[B("MODT")])
        em.dma("scalar", bass.AP(mod_scr.tensor, 0, [[6 * D, NSEQ], [1, 6 * D]]),
               MODT[0:NSEQ, :], reads=[B("MODT")], writes=[B("mod_scr")])

        WQT = arena.enter_context(nc.sbuf_tensor("WQT", [128, HID], F32))
        WQA = arena.enter_context(nc.sbuf_tensor("WQA", [128, HID], F32))
        WQB = arena.enter_context(nc.sbuf_tensor("WQB", [128, HID], BF16))
        WQI = arena.enter_context(nc.sbuf_tensor("WQI", [128, HID], I8))
        WQC = arena.enter_context(nc.sbuf_tensor("WQC", [128, 4], F32))

        def wquant(w_ap, rows, cols, scr_t, idx):
            nch = rows // 128
            bW, bA, bC = B("WQT"), B("WQAa"), B("WQC")
            bI, bBf, bScr = B("WQI"), B("WQB"), B(f"qS{idx}")
            for ch in range(nch):
                em.dma("sync", WQT[:, 0:cols],
                       bass.AP(w_ap.tensor, ch * 128 * cols, [[cols, 128], [1, cols]]),
                       writes=[bW])
                red("vector", WQC[:, 1:2], WQT[:, 0:cols], ALU.add, absval=True,
                    reads=[bW], writes=[bC])
                if ch == 0:
                    cp("vector", WQC[:, 0:1], WQC[:, 1:2], reads=[bC], writes=[bA])
                else:
                    tt("vector", WQC[:, 0:1], WQC[:, 0:1], WQC[:, 1:2], ALU.add,
                       reads=[bC], writes=[bA])
            mm(P3[0:1, 0:1], WQC[:, 0:1], ONESC[:, 0:1], start=True, stop=True,
               reads=[bA, B("ONESC")], writes=[bP3])
            cp("vector", WQC[0:1, 2:3], P3[0:1, 0:1], reads=[bP3], writes=[bC])
            ts("vector", WQC[0:1, 2:3], WQC[0:1, 2:3], 1.0 / (rows * cols), EPS_Q,
               ALU.mult, ALU.max, reads=[bC], writes=[bC])
            mm(P3[:, 0:1], ONESF[0:1, :], WQC[0:1, 2:3], start=True, stop=True,
               reads=[bC, B("ONESF")], writes=[bP3])
            cp("vector", SWCOL[:, idx:idx + 1], P3[:, 0:1], reads=[bP3],
               writes=[B("SWCOL")])
            em.emit("vector", lambda e: e.reciprocal(WQC[:, 3:4], SWCOL[:, idx:idx + 1]),
                    reads=[B("SWCOL")], writes=[bC])
            for ch in range(nch):
                em.dma("sync", WQT[:, 0:cols],
                       bass.AP(w_ap.tensor, ch * 128 * cols, [[cols, 128], [1, cols]]),
                       writes=[bW])
                act("scalar", WQA[:, 0:cols], WQT[:, 0:cols], AF.Identity,
                    bias=ZCOL[:, 0:1], scale=WQC[:, 3:4],
                    reads=[bW, bC, B("ZCOL")], writes=[B("WQAa")])
                ts("vector", WQT[:, 0:cols], WQA[:, 0:cols], 0.5, None, ALU.is_gt,
                   reads=[B("WQAa")], writes=[bW])
                stt("vector", WQB[:, 0:cols], WQA[:, 0:cols], -0.5, WQT[:, 0:cols],
                    ALU.is_lt, ALU.subtract, reads=[B("WQAa"), bW], writes=[bBf])
                em.dma("scalar",
                       bass.AP(scr_t, ch * 128 * cols, [[cols, 128], [1, cols]]),
                       WQB[:, 0:cols], reads=[bBf], writes=[bScr])

        wquant(w1t, D, HID, w1q_scr.tensor, 0)
        wquant(w2t, HID, D, w2q_scr.tensor, 1)
        em.barrier()
        arena.close()

        for s in range(NSEQ):
            ar = contextlib.ExitStack()
            HBF = ar.enter_context(nc.sbuf_tensor(f"HBF{s}", [128, DC * S], BF16))
            KBF = ar.enter_context(nc.sbuf_tensor(f"KBF{s}", [128, QC * S], BF16))
            QBF = ar.enter_context(nc.sbuf_tensor(f"QBF{s}", [128, QC * S], BF16))
            VTM = ar.enter_context(nc.sbuf_tensor(f"VTM{s}", [128, KC * VW], BF16))
            XT = ar.enter_context(nc.sbuf_tensor(f"XT{s}", [128, 4 * D], F32))
            PBF = ar.enter_context(nc.sbuf_tensor(f"PBF{s}", [128, 2 * 1024], BF16))
            OBF = ar.enter_context(nc.sbuf_tensor(f"OBF{s}", [128, QC * c.AG], BF16))
            OST = ar.enter_context(nc.sbuf_tensor(f"OST{s}", [128, c.NP * c.AG], BF16))
            ORW = ar.enter_context(nc.sbuf_tensor(f"ORW{s}", [128, 2 * c.AG], BF16))
            RRC = ar.enter_context(nc.sbuf_tensor(f"RRC{s}", [128, 512], BF16))
            SMR = ar.enter_context(nc.sbuf_tensor(f"SMR{s}", [128, 2048], F32))
            EP1 = ar.enter_context(nc.sbuf_tensor(f"EP1{s}", [128, (c.AG // 128) * D], BF16))
            EP2 = ar.enter_context(nc.sbuf_tensor(f"EP2{s}", [128, (c.AG // 128) * D], BF16))
            CA1 = ar.enter_context(nc.sbuf_tensor(f"CA1{s}", [128, D], F32))
            SAR = ar.enter_context(nc.sbuf_tensor(f"SAR{s}", [128, D], BF16))
            GAR = ar.enter_context(nc.sbuf_tensor(f"GAR{s}", [128, D], BF16))
            GSC = ar.enter_context(nc.sbuf_tensor(f"GSC{s}", [128, D], F32))

            bHBF, bKBF, bQBF, bVTM = B(f"HBF{s}"), B(f"KBF{s}"), B(f"QBF{s}"), B(f"VTM{s}")
            bOBF, bOST, bORW = B(f"OBF{s}"), B(f"OST{s}"), B(f"ORW{s}")
            bXTs = [B(f"XT{s}_{i}") for i in range(4)]
            bPBF = [B(f"PBF{s}_0"), B(f"PBF{s}_1")]
            bRRC, bSMR = B(f"RRC{s}"), B(f"SMR{s}")
            bEP1, bEP2 = B(f"EP1{s}"), B(f"EP2{s}")
            bCA1, bSAR, bGAR, bGSC = B(f"CA1{s}"), B(f"SAR{s}"), B(f"GAR{s}"), B(f"GSC{s}")
            bNNB, bXMS = B("NNB"), B("xmid_scr")

            def xchunk(gbase_tok, tc, nsl=4):
                sl = tc % nsl
                em.dma("sync", XT[:, sl * D:(sl + 1) * D],
                       bass.AP(x_in.tensor, (gbase_tok + tc * 128) * D,
                               [[D, 128], [1, D]]),
                       writes=[bXTs[sl]])
                return XT[:, sl * D:(sl + 1) * D], bXTs[sl]

            bcast("sync", CA1[:], mod_scr.tensor, (s * 6 + 1) * D, D,
                  reads=[B("mod_scr")], writes=[bCA1])
            ts("vector", CA1[:], CA1[:], 1.0, None, ALU.add, reads=[bCA1], writes=[bCA1])
            em.dma("gpsimd", SAR[:],
                   bass.AP(mod_scr.tensor, (s * 6 + 0) * D, [[0, 128], [1, D]]),
                   reads=[B("mod_scr")], writes=[bSAR])
            bcast("sync", GSC[:], mod_scr.tensor, (s * 6 + 2) * D, D,
                  reads=[B("mod_scr")], writes=[bGSC])
            ts("vector", GAR[:], GSC[:], 0.0, 2.0, ALU.is_ge, ALU.mult, reads=[bGSC],
               writes=[bGAR])
            ts("vector", GAR[:], GAR[:], -1.0, None, ALU.add, reads=[bGAR], writes=[bGAR])
            act("scalar", GSC[:], GSC[:], AF.Abs, reads=[bGSC], writes=[bGSC])
            ts("vector", GSC[:], GSC[:], 0.25, None, ALU.max, reads=[bGSC],
               writes=[bGSC])
            tt("vector", GSC[:], GSC[:], GAR[:], ALU.mult, reads=[bGAR], writes=[bGSC])
            ts("vector", GAR[:], GSC[:], RSQRT2, None, ALU.mult, reads=[bGSC],
               writes=[bGAR])

            em.emit("vector", lambda e: e.memset(
                bass.AP(VTM, 64, [[KC * VW, 128], [VW, KC], [65, H], [1, 1]]), 1.0),
                writes=[bVTM])

            NCH = c.AG // 128
            for g in range(c.NG):
                gbase = s * S + g * c.AG
                xas = []
                for tc in range(NCH):
                    xa, bxa = xchunk(gbase, tc)
                    xas.append((xa, bxa))
                    k4 = tc % 4
                    o = 32 * k4
                    ln_bn(xa, D, bxa, k4)
                for tc in range(NCH):
                    ln_sqrt(tc % 4)
                for tc in range(NCH):
                    ln_recip(tc % 4)
                for tc in range(NCH):
                    k4 = tc % 4
                    o = 32 * k4
                    xa, bxa = xas[tc]
                    bEP1c = B(f"EP1{s}_{tc}")
                    bEP2c = B(f"EP2{s}_{tc}")
                    ha = EP2[:, tc * D:(tc + 1) * D]
                    ln_apply(ha, xa, k4, [bxa], [bEP2c])
                    ea = EP1[:, tc * D:(tc + 1) * D]
                    tt("vector", ea, ha, CA1[:], ALU.mult, reads=[bEP2c, bCA1],
                       writes=[bEP1c])
                    tt("vector", ha, ea, SAR[:], ALU.add, reads=[bEP1c, bSAR],
                       writes=[bEP2c])
                for tc in range(NCH):
                    bEP2c = B(f"EP2{s}_{tc}")
                    ha = EP2[:, tc * D:(tc + 1) * D]
                    for j in range(DC):
                        transp(PT[:, (j % 8) * 128:(j % 8) * 128 + 128],
                               ha[:, j * 128:(j + 1) * 128],
                               reads=[bEP2c], writes=[bPT2[j % 2]])
                    g4 = min(4, DC)
                    for half in range(DC // g4):
                        cp("scalar",
                           bass.AP(HBF, half * g4 * S + g * c.AG + tc * 128,
                                   [[DC * S, 128], [S, g4], [1, 128]]),
                           bass.AP(PT, half * g4 * 128, [[1024, 128], [128, g4], [1, 128]]),
                           reads=[bPT2[0], bPT2[1]], writes=[bHBF])
                for fc2 in range(0, QC, 2):
                    for dc in range(DC):
                        for pi in range(2):
                            fc = fc2 + pi
                            pk, bpk = (P0, bP0) if pi == 0 else (P1, bP1)
                            mm(pk[:, 0:c.AG],
                               WQK[:, dc * 2 * QK + QK + fc * 128:
                                   dc * 2 * QK + QK + (fc + 1) * 128],
                               HBF[:, dc * S + g * c.AG:dc * S + (g + 1) * c.AG],
                               start=(dc == 0), stop=(dc == DC - 1),
                               reads=[B("WQK"), bHBF], writes=[bpk])
                    for pi in range(2):
                        fc = fc2 + pi
                        pk, bpk = (P0, bP0) if pi == 0 else (P1, bP1)
                        act("scalar", KBF[:, fc * S + g * c.AG:fc * S + (g + 1) * c.AG],
                            pk[:, 0:c.AG], AF.Identity, bias=BKS[:, fc:fc + 1],
                            reads=[bpk, B("BKS")], writes=[bKBF])
                for tc2 in range(0, NCH, 2):
                    npi = 2 if tc2 + 1 < NCH else 1
                    for dc in range(DC):
                        for pi in range(npi):
                            tc = tc2 + pi
                            pv, bpv = (P0, bP0) if pi == 0 else (P1, bP1)
                            mm(pv[:, 0:QK],
                               HBF[:, dc * S + g * c.AG + tc * 128:
                                   dc * S + g * c.AG + (tc + 1) * 128],
                               WV[:, dc * QK:(dc + 1) * QK],
                               start=(dc == 0), stop=(dc == DC - 1),
                               reads=[bHBF, B("WV")], writes=[bpv])
                    for pi in range(npi):
                        tc = tc2 + pi
                        pv, bpv = (P0, bP0) if pi == 0 else (P1, bP1)
                        kc = g * NCH + tc
                        cp("scalar",
                           bass.AP(VTM, kc * VW, [[KC * VW, 128], [65, H], [1, 64]]),
                           bass.AP(pv, 0, [[1024, 128], [64, H], [1, 64]]),
                           reads=[bpv], writes=[bVTM])

            for g in range(c.NG):
                for fc2 in range(0, QC, 2):
                    for dc in range(DC):
                        for pi in range(2):
                            fc = fc2 + pi
                            pq, bpq = (P0, bP0) if pi == 0 else (P1, bP1)
                            mm(pq[:, 0:c.AG],
                               WQK[:, dc * 2 * QK + fc * 128:dc * 2 * QK + (fc + 1) * 128],
                               HBF[:, dc * S + g * c.AG:dc * S + (g + 1) * c.AG],
                               start=(dc == 0), stop=(dc == DC - 1),
                               reads=[B("WQK"), bHBF], writes=[bpq])
                    for pi in range(2):
                        fc = fc2 + pi
                        pq, bpq = (P0, bP0) if pi == 0 else (P1, bP1)
                        act("scalar", QBF[:, fc * S + g * c.AG:fc * S + (g + 1) * c.AG],
                            pq[:, 0:c.AG], AF.Identity, bias=BQS[:, fc:fc + 1],
                            scale=0.125, reads=[bpq, B("BQS")], writes=[bQBF])
            for qsrc, bsrc, coli in ((QBF, bQBF, 22), (KBF, bKBF, 24)):
                first = True
                for fc in range(QC):
                    act("scalar", EP1[:, 0:S], qsrc[:, fc * S:(fc + 1) * S], AF.Square,
                        reads=[bsrc], writes=[B(f"EP1{s}_0"), B(f"EP1{s}_1")])
                    for ct in range((S + 511) // 512):
                        ctw = min(512, S - ct * 512)
                        mm(P3[0:8, 0:ctw], IND8S[:, fc * 8:(fc + 1) * 8],
                           EP1[:, ct * 512:ct * 512 + ctw],
                           start=True, stop=True,
                           reads=[B(f"EP1{s}_0"), B(f"EP1{s}_1"), B("IND8S")],
                           writes=[bP3])
                        red("vector", STATC[0:8, coli + 1:coli + 2], P3[0:8, 0:ctw],
                            ALU.max, reads=[bP3], writes=[bSTs[0]])
                        if first:
                            cp("vector", STATC[0:8, coli:coli + 1],
                               STATC[0:8, coli + 1:coli + 2], reads=[bSTs[0]],
                               writes=[bSTs[0]])
                            first = False
                        else:
                            tt("vector", STATC[0:8, coli:coli + 1],
                               STATC[0:8, coli:coli + 1], STATC[0:8, coli + 1:coli + 2],
                               ALU.max, reads=[bSTs[0]], writes=[bSTs[0]])
            tt("vector", STATC[0:8, 26:27], STATC[0:8, 22:23], STATC[0:8, 24:25],
               ALU.mult, reads=[bSTs[0]], writes=[bSTs[0]])
            em.dma("scalar", bass.AP(tiny_scr.tensor, 0, [[1, 8], [1, 1]]),
                   STATC[0:8, 26:27], reads=[bSTs[0]], writes=[B("tiny")], slow=True)
            em.dma("scalar", SMR[0:1, 0:8],
                   bass.AP(tiny_scr.tensor, 0, [[8, 1], [1, 8]]),
                   reads=[B("tiny")], writes=[bSMR], slow=True)
            act("scalar", SMR[0:1, 0:8], SMR[0:1, 0:8], AF.Sqrt, reads=[bSMR],
                writes=[bSMR])
            tt("vector", SMR[0:1, 8:8 + c.NP], bass.AP(SMR, 0, [[2048, 1], [2, c.NP]]),
               bass.AP(SMR, 1, [[2048, 1], [2, c.NP]]), ALU.max, reads=[bSMR],
               writes=[bSMR])
            ts("vector", SMR[0:1, 8:8 + c.NP], SMR[0:1, 8:8 + c.NP], -1.0, None,
               ALU.mult, reads=[bSMR], writes=[bSMR])
            mm(P3[:, 0:c.NP], ONESF[0:1, :], SMR[0:1, 8:8 + c.NP], start=True,
               stop=True, reads=[bSMR, B("ONESF")], writes=[bP3])
            cp("vector", NNB[:, 0:c.NP], P3[:, 0:c.NP], reads=[bP3], writes=[bNNB])

            for g in range(c.NG):
                gbase = s * S + g * c.AG
                pend = []

                def _emit_norm(g, p, slot):
                    orw = ORW[:, slot * c.AG:(slot + 1) * c.AG]
                    em.emit("vector", lambda e: e.reciprocal(
                        SMR[64:65, slot * 1024:slot * 1024 + 2 * c.AG],
                        SMR[64:65, slot * 1024:slot * 1024 + 2 * c.AG]),
                        reads=[bSMR], writes=[bSMR])
                    mm(P2A[:, 0:c.AG], IND64[64:65, 0:128],
                       SMR[64:65, slot * 1024:slot * 1024 + c.AG],
                       start=True, stop=False,
                       reads=[bSMR, B("IND64")], writes=[bP2])
                    mm(P2A[:, 0:c.AG], IND64[64:65, 128:256],
                       SMR[64:65, slot * 1024 + c.AG:slot * 1024 + 2 * c.AG],
                       start=False, stop=True,
                       reads=[bSMR, B("IND64")], writes=[bP2])
                    cp("vector", RRC[0:64, 0:c.AG], P2A[0:64, 0:c.AG], reads=[bP2],
                       writes=[bRRC])
                    tt("vector", OBF[0:64, p * c.AG:(p + 1) * c.AG],
                       orw[0:64, :], RRC[0:64, 0:c.AG], ALU.mult,
                       reads=[bORW, bRRC], writes=[bOBF])
                    tt("vector", OST[0:64, p * c.AG:(p + 1) * c.AG],
                       OST[0:64, p * c.AG:(p + 1) * c.AG],
                       P2A[64:128, 0:c.AG], ALU.mult,
                       reads=[bP2], writes=[bOST])

                for p in range(c.NP):
                    hA, hB = 2 * p, 2 * p + 1
                    for j in range(KC):
                        psc, bsc = (P0, bP0) if j % 2 == 0 else (P1, bP1)
                        pbh = j % 2
                        mm(psc[:, 0:c.AG],
                           KBF[0:64, p * S + j * 128:p * S + (j + 1) * 128],
                           QBF[0:64, p * S + g * c.AG:p * S + (g + 1) * c.AG],
                           start=True, stop=True, tp=(0, 0),
                           reads=[bKBF, bQBF], writes=[bsc])
                        mm(psc[:, 512:512 + c.AG],
                           KBF[64:128, p * S + j * 128:p * S + (j + 1) * 128],
                           QBF[64:128, p * S + g * c.AG:p * S + (g + 1) * c.AG],
                           start=True, stop=True, tp=(64, 0),
                           reads=[bKBF, bQBF], writes=[bsc])
                        if j == 2 and pend:
                            pp, ss = pend.pop(0)
                            _emit_norm(g, pp, ss)
                        if c.AG == 512:
                            act("scalar", PBF[:, pbh * 1024:pbh * 1024 + 1024],
                                psc[:, 0:1024], AF.Exp, bias=NNB[:, p:p + 1],
                                reads=[bsc, bNNB], writes=[bPBF[pbh]])
                        else:
                            act("scalar", PBF[:, pbh * 1024:pbh * 1024 + c.AG],
                                psc[:, 0:c.AG], AF.Exp, bias=NNB[:, p:p + 1],
                                reads=[bsc, bNNB], writes=[bPBF[pbh]])
                            act("scalar",
                                PBF[:, pbh * 1024 + 512:pbh * 1024 + 512 + c.AG],
                                psc[:, 512:512 + c.AG], AF.Exp, bias=NNB[:, p:p + 1],
                                reads=[bsc, bNNB], writes=[bPBF[pbh]])
                        if j >= 1:
                            jm = j - 1
                            pbm = jm % 2
                            mm(P3[0:65, 0:c.AG],
                               VTM[:, jm * VW + 65 * hA:jm * VW + 65 * hA + 65],
                               PBF[:, pbm * 1024:pbm * 1024 + c.AG],
                               start=(jm == 0), stop=False,
                               reads=[bVTM, bPBF[pbm]], writes=[bP3])
                            mm(P4[0:65, 0:c.AG],
                               VTM[:, jm * VW + 65 * hB:jm * VW + 65 * hB + 65],
                               PBF[:, pbm * 1024 + 512:pbm * 1024 + 512 + c.AG],
                               start=(jm == 0), stop=False,
                               reads=[bVTM, bPBF[pbm]], writes=[bP4])
                    jm = KC - 1
                    pbm = jm % 2
                    mm(P3[0:65, 0:c.AG],
                       VTM[:, jm * VW + 65 * hA:jm * VW + 65 * hA + 65],
                       PBF[:, pbm * 1024:pbm * 1024 + c.AG],
                       start=False, stop=True,
                       reads=[bVTM, bPBF[pbm]], writes=[bP3])
                    mm(P4[0:65, 0:c.AG],
                       VTM[:, jm * VW + 65 * hB:jm * VW + 65 * hB + 65],
                       PBF[:, pbm * 1024 + 512:pbm * 1024 + 512 + c.AG],
                       start=False, stop=True,
                       reads=[bVTM, bPBF[pbm]], writes=[bP4])
                    slot = p % 2
                    orw = ORW[:, slot * c.AG:(slot + 1) * c.AG]
                    cp("scalar", orw[0:65, :], P3[0:65, 0:c.AG], reads=[bP3],
                       writes=[bORW])
                    cp("scalar", SMR[64:65, slot * 1024:slot * 1024 + c.AG],
                       P3[64:65, 0:c.AG], reads=[bP3], writes=[bSMR])
                    cp("scalar", OST[0:65, p * c.AG:(p + 1) * c.AG],
                       P4[0:65, 0:c.AG], reads=[bP4], writes=[bOST])
                    cp("scalar", SMR[64:65, slot * 1024 + c.AG:slot * 1024 + 2 * c.AG],
                       P4[64:65, 0:c.AG], reads=[bP4], writes=[bSMR])
                    pend.append((p, slot))
                while pend:
                    pp, ss = pend.pop(0)
                    _emit_norm(g, pp, ss)
                em.dma("scalar",
                       bass.AP(OBF, 64 * QC * c.AG,
                               [[QC * c.AG, 64], [c.AG, c.NP], [1, c.AG]]),
                       bass.AP(OST, 0, [[c.NP * c.AG, 64], [c.AG, c.NP], [1, c.AG]]),
                       reads=[bOST], writes=[bOBF])
                if c.DBG and s == 0:
                    em.dma("gpsimd",
                           bass.AP(dbg_scr.tensor, g * 128 * QC * c.AG,
                                   [[QC * c.AG, 128], [1, QC * c.AG]]),
                           OBF[:, :], reads=[bOBF], writes=[B("dbg")])
                for tc in range(NCH):
                    pi2 = tc % 2
                    py, bpy = (P0, bP0) if pi2 == 0 else (P1, bP1)
                    bh = [B(f"POP{s}_{pi2}_0"), B(f"POP{s}_{pi2}_1")]
                    nh = max(1, D // 512)
                    dw = D // nh
                    for ci in range(QC):
                        for half in range(nh):
                            wl = [bh[half % nh]]
                            if ci == 0:
                                wl = [bpy] + wl
                            mm(py[:, half * dw:(half + 1) * dw],
                               OBF[:, ci * c.AG + tc * 128:ci * c.AG + (tc + 1) * 128],
                               WOUT[:, ci * D + half * dw:ci * D + (half + 1) * dw],
                               start=(ci == 0), stop=(ci == QC - 1),
                               reads=[bOBF, B("WOUT")], writes=wl)
                    k4 = tc % 4
                    o = 32 * k4
                    bSk = bSTs[k4]
                    bEP1c = B(f"EP1{s}_{tc}")
                    bEP2c = B(f"EP2{s}_{tc}")
                    e1 = EP1[:, tc * D:(tc + 1) * D]
                    e2 = EP2[:, tc * D:(tc + 1) * D]
                    xa, bxa = xchunk(gbase, tc)
                    tt("vector", e1, py[:, 0:D], BOREP[:], ALU.add,
                       reads=[bh[0], bh[nh - 1], B("BOREP")], writes=[bEP1c])
                    if c.DBG and s == 0 and g == 0:
                        em.dma("gpsimd",
                               bass.AP(dbg2_scr.tensor, tc * 128 * D, [[D, 128], [1, D]]),
                               e1, reads=[bEP1c], writes=[B("dbg2")])
                    ln_bn(e1, D, bEP1c, k4)
                    ln_sqrt(k4)
                    ln_recip(k4)
                    ln_apply(e2, e1, k4, [bEP1c], [bEP2c])
                    tt("vector", e1, e2, GREP[:], ALU.mult, reads=[bEP2c, B("GREP")],
                       writes=[bEP1c])
                    tt("vector", e2, e1, BREP[:], ALU.add, reads=[bEP1c, B("BREP")],
                       writes=[bEP2c])
                    for j in range(DC):
                        transp(PT[:, (j % 8) * 128:(j % 8) * 128 + 128],
                               HBF[:, j * S + g * c.AG + tc * 128:
                                   j * S + g * c.AG + (tc + 1) * 128],
                               reads=[bHBF], writes=[bPT2[j % 2]])
                    tt("vector", e1, e2, PT[:, 0:D], ALU.add,
                       reads=[bEP2c, bPT2[0], bPT2[1]],
                       writes=[bEP1c])
                    tt("vector", e2, e1, GAR[:], ALU.mult, reads=[bEP1c, bGAR],
                       writes=[bEP2c])
                    tt("gpsimd", xa, xa, e2, ALU.add, reads=[bEP2c], writes=[bxa])
                    em.dma("scalar",
                           bass.AP(xmid_scr.tensor, (gbase + tc * 128) * D,
                                   [[D, 128], [1, D]]),
                           xa, reads=[bxa], writes=[bXMS])

            em.barrier()
            ar.close()

            br = contextlib.ExitStack()
            W1S = br.enter_context(nc.sbuf_tensor(f"W1S{s}", [128, DC * HID], BF16))
            CB1C1 = br.enter_context(nc.sbuf_tensor(f"CB1{s}", [128, D], BF16))
            SBC1 = br.enter_context(nc.sbuf_tensor(f"SBC1{s}", [128, D], BF16))
            ZC = br.enter_context(nc.sbuf_tensor(f"ZC{s}", [128, (c.T1 // 128) * D], F32))
            DS1 = br.enter_context(nc.sbuf_tensor(f"DS1{s}", [128, D], F32))
            DS2 = br.enter_context(nc.sbuf_tensor(f"DS2{s}", [128, D], BF16))
            DI8 = br.enter_context(nc.sbuf_tensor(f"DI8{s}", [128, D], I8))
            XQ1 = br.enter_context(nc.sbuf_tensor(f"XQ1{s}", [128, 3 * DC * c.T1], BF16))
            XMB = br.enter_context(nc.sbuf_tensor(f"XMB{s}", [128, 2 * (c.T1 // 128) * D], F32))
            HA = br.enter_context(nc.sbuf_tensor(f"HA{s}", [128, (c.T1 // 128) * HID], BF16))
            HBt = br.enter_context(nc.sbuf_tensor(f"HBt{s}", [128, (c.T1 // 128) * HID], BF16))
            HI8 = br.enter_context(nc.sbuf_tensor(f"HI8{s}", [128, HID], I8))
            XQ2S = br.enter_context(nc.sbuf_tensor(f"XQ2S{s}", [128, 2048], BF16))

            bW1, bCB, bSB = B(f"W1S{s}"), B(f"CB1{s}"), B(f"SBC1{s}")
            bXQ1, bXMB = B(f"XQ1{s}"), B(f"XMB{s}")
            bXQscr, bS2 = B("xq2_scr"), B("S2COL")

            bcast("gpsimd", CB1C1[:], mod_scr.tensor, (s * 6 + 4) * D, D,
                  reads=[B("mod_scr")], writes=[bCB])
            ts("vector", CB1C1[:], CB1C1[:], 1.0, None, ALU.add, reads=[bCB],
               writes=[bCB])
            tt("vector", CB1C1[:], CB1C1[:], C1REP[:], ALU.mult, reads=[B("C1REP")],
               writes=[bCB])
            bcast("sync", DS1[:, 0:D], mod_scr.tensor, (s * 6 + 3) * D, D,
                  reads=[B("mod_scr")], writes=[B(f"DS1{s}_0")])
            tt("vector", SBC1[:], DS1[:, 0:D], C1REP[:], ALU.mult,
               reads=[B(f"DS1{s}_0"), B("C1REP")], writes=[bSB])
            for hh, q_ in ((0, "sync"), (1, "gpsimd")):
                em.dma(q_,
                       bass.AP(W1S, hh * (DC // 2) * HID,
                               [[DC * HID, 128], [HID, DC // 2], [1, HID]]),
                       bass.AP(w1q_scr.tensor, hh * (DC // 2) * 128 * HID,
                               [[HID, 128], [128 * HID, DC // 2], [1, HID]]),
                       reads=[B("qS0")], writes=[bW1])

            def xq2_finish(t1, tc):
                bHBc = B(f"HBt{s}_{tc}")
                ua = HBt[:, tc * HID:(tc + 1) * HID]
                g8 = min(8, HC)
                for quad in range(HC // g8):
                    for j in range(g8):
                        transp(PT[:, j * 128:(j + 1) * 128],
                               ua[:, (quad * g8 + j) * 128:(quad * g8 + j + 1) * 128],
                               reads=[bHBc], writes=[bPT2[j % 2]])
                    qs = quad % 2
                    bXQ2c = B(f"XQ2S{s}_{qs}")
                    cp("scalar",
                       bass.AP(XQ2S, qs * 1024, [[2048, 128], [128, g8], [1, 128]]),
                       bass.AP(PT, 0, [[1024, 128], [128, g8], [1, 128]]),
                       reads=[bPT2[0], bPT2[1]], writes=[bXQ2c])
                    em.dma("scalar",
                           bass.AP(xq2_scr.tensor,
                                   (quad * g8 * 128) * S + t1 * c.T1 + tc * 128,
                                   [[S, 128], [128 * S, g8], [1, 128]]),
                           bass.AP(XQ2S, qs * 1024, [[2048, 128], [128, g8], [1, 128]]),
                           reads=[bXQ2c], writes=[bXQscr])

            NC1 = c.T1 // 128

            def b1_dstage(t1):
                ts_ = t1 % 2
                sl = t1 % 3
                tb = s * S + t1 * c.T1
                em.dma("sync",
                       bass.AP(XMB, ts_ * NC1 * D,
                               [[2 * NC1 * D, 128], [D, NC1], [1, D]]),
                       bass.AP(xmid_scr.tensor, tb * D,
                               [[D, 128], [128 * D, NC1], [1, D]]),
                       reads=[bXMS], writes=[B(f"XMB{s}_{ts_}")])
                for tc in range(NC1):
                    k4 = (t1 * NC1 + tc) % 4
                    ln_bn(XMB[:, (ts_ * NC1 + tc) * D:(ts_ * NC1 + tc + 1) * D], D,
                          B(f"XMB{s}_{ts_}"), k4)
                for tc in range(NC1):
                    ln_sqrt((t1 * NC1 + tc) % 4)
                for tc in range(NC1):
                    ln_recip((t1 * NC1 + tc) % 4)
                for tc in range(NC1):
                    k4 = (t1 * NC1 + tc) % 4
                    o = 32 * k4
                    bSk = bSTs[k4]
                    tcol = t1 * NC1 + tc
                    bZCc = B(f"ZC{s}_{tc}")
                    bD1c, bD2c = B(f"DS1{s}"), B(f"DS2{s}")
                    bDIc = B(f"DI8{s}")
                    xa = XMB[:, (ts_ * NC1 + tc) * D:(ts_ * NC1 + tc + 1) * D]
                    za = ZC[:, tc * D:(tc + 1) * D]
                    d1 = DS1[:, 0:D]
                    d2 = DS2[:, 0:D]
                    di = DI8[:, 0:D]
                    xqt = d2
                    ak = 22 + (tcol // 4) % 2
                    ln_apply(za, xa, k4, [B(f"XMB{s}_{ts_}")], [bZCc])
                    tt("vector", za, za, CB1C1[:], ALU.mult, reads=[bCB], writes=[bZCc])
                    tt("vector", za, za, SBC1[:], ALU.add, reads=[bSB], writes=[bZCc])
                    tt("vector", d1, za, C1IREP[:], ALU.mult,
                       reads=[bZCc, B("C1IREP")], writes=[bD1c])
                    act("scalar", d2, d1, AF.Square, accum=STATC[:, o + 15:o + 16],
                        reads=[bD1c], writes=[bD2c, bSk])
                    ts("vector", STATC[:, o + 15:o + 16], STATC[:, o + 15:o + 16],
                       1.0 / D, EPS_NORM, ALU.mult, ALU.add, reads=[bSk], writes=[bSk])
                    em.emit("vector", lambda e, o=o: e.reciprocal(
                        STATC[:, o + 16:o + 17], STATC[:, o + 15:o + 16]),
                        reads=[bSk], writes=[bSk])
                    act("scalar", STATC[:, o + 16:o + 17], STATC[:, o + 16:o + 17],
                        AF.Sqrt, reads=[bSk], writes=[bSk])
                    red("vector", STATC[:, o + 17:o + 18], za, ALU.max, absval=True,
                        reads=[bZCc], writes=[bSk])
                    tt("vector", STATC[:, o + 17:o + 18], STATC[:, o + 17:o + 18],
                       STATC[:, o + 16:o + 17], ALU.mult, reads=[bSk], writes=[bSk])
                    ts("vector", STATC[:, o + 17:o + 18], STATC[:, o + 17:o + 18],
                       EPS_Q, None, ALU.max, reads=[bSk], writes=[bSk])
                    em.emit("vector", lambda e, o=o: e.reciprocal(
                        STATC[:, o + 18:o + 19], STATC[:, o + 17:o + 18]),
                        reads=[bSk], writes=[bSk])
                    tt("vector", STATC[:, o + 19:o + 20], STATC[:, o + 18:o + 19],
                       STATC[:, o + 16:o + 17], ALU.mult, reads=[bSk], writes=[bSk])
                    ts("vector", STATC[:, o + 19:o + 20], STATC[:, o + 19:o + 20],
                       127.0, None, ALU.mult, reads=[bSk], writes=[bSk])
                    act("scalar", za, za, AF.Identity, bias=ZCOL[:, 0:1],
                        scale=STATC[:, o + 19:o + 20],
                        reads=[bSk, bZCc, B("ZCOL")], writes=[bZCc])
                    ts("vector", STATC[:, o + 20:o + 21], STATC[:, o + 17:o + 18],
                       1.0 / 127.0, None, ALU.mult, reads=[bSk], writes=[bSk])
                    tt("vector", STATC[:, o + 20:o + 21], STATC[:, o + 20:o + 21],
                       SWCOL[:, 0:1], ALU.mult, reads=[B("SWCOL")], writes=[bSk])
                    tt("vector", STATC[:, o + ak:o + ak + 1], STATC[:, o + 20:o + 21],
                       STATC[:, o + 20:o + 21], ALU.mult, reads=[bSk], writes=[bAK])
                    tt("vector", STATC[:, o + ak:o + ak + 1], STATC[:, o + ak:o + ak + 1],
                       ABCOL[:, 0:1], ALU.mult, reads=[B("ABCOL")], writes=[bAK])
                    cp("scalar", di, za, reads=[bZCc], writes=[bDIc])
                    tt("vector", d1, za, di, ALU.subtract, reads=[bZCc, bDIc],
                       writes=[bD1c])
                    stt("vector", d2, d1, 0.5, di, ALU.is_ge, ALU.add,
                        reads=[bD1c, bDIc], writes=[bD2c])
                    stt("vector", xqt, d1, -0.5, d2, ALU.is_lt, ALU.subtract,
                        reads=[bD1c, bD2c], writes=[bD2c])
                    for j in range(DC):
                        transp(PT[:, (j % 8) * 128:(j % 8) * 128 + 128],
                               xqt[:, j * 128:(j + 1) * 128],
                               reads=[bD2c], writes=[bPT2[j % 2]])
                    g4 = min(4, DC)
                    for half in range(DC // g4):
                        cp("scalar",
                           bass.AP(XQ1, sl * DC * c.T1 + half * g4 * c.T1 + tc * 128,
                                   [[3 * DC * c.T1, 128], [c.T1, g4], [1, 128]]),
                           bass.AP(PT, half * g4 * 128, [[1024, 128], [128, g4], [1, 128]]),
                           reads=[bPT2[0], bPT2[1]], writes=[B(f"XQ1{s}_{sl}")])

            def b1_mstage(t1):
                sl = t1 % 3
                for tc in range(NC1):
                    bHAc, bHBc = B(f"HA{s}_{tc}"), B(f"HBt{s}_{tc}")
                    for hb2 in range(0, HID // 512, 2):
                        pairs = [(0, hb2)]
                        if hb2 + 1 < HID // 512:
                            pairs.append((1, hb2 + 1))
                        for dc in range(DC):
                            for pi, hb in pairs:
                                ph, bph = (P0, bP0) if pi == 0 else (P1, bP1)
                                mm(ph[:, 0:512],
                                   XQ1[:, sl * DC * c.T1 + dc * c.T1 + tc * 128:
                                       sl * DC * c.T1 + dc * c.T1 + (tc + 1) * 128],
                                   W1S[:, dc * HID + hb * 512:dc * HID + (hb + 1) * 512],
                                   start=(dc == 0), stop=(dc == DC - 1),
                                   reads=[B(f"XQ1{s}_{sl}"), bW1], writes=[bph])
                        for pi, hb in pairs:
                            ph, bph = (P0, bP0) if pi == 0 else (P1, bP1)
                            hs = HA[:, tc * HID + hb * 512:tc * HID + (hb + 1) * 512]
                            act("scalar", hs, ph[:, 0:512], AF.Relu, reads=[bph],
                                writes=[bHAc])
                            us = HBt[:, tc * HID + hb * 512:tc * HID + (hb + 1) * 512]
                            act("scalar", us, hs, AF.Square, reads=[bHAc],
                                writes=[bHBc])
                    k4 = (t1 * NC1 + tc) % 4
                    o = 32 * k4
                    tcol = t1 * NC1 + tc
                    ak = 22 + (tcol // 4) % 2
                    bSm = bSTm[k4]
                    bHAc, bHBc, bHIc = B(f"HA{s}_{tc}"), B(f"HBt{s}_{tc}"), B(f"HI8{s}")
                    ua = HBt[:, tc * HID:(tc + 1) * HID]
                    aa = HA[:, tc * HID:(tc + 1) * HID]
                    ia = HI8[:, 0:HID]
                    act("scalar", aa, ua, AF.Identity, bias=ABCOL[:, 1:2],
                        scale=STATC[:, o + ak:o + ak + 1],
                        reads=[bHBc, bAK, B("ABCOL")], writes=[bHAc])
                    act("scalar", ua, aa, AF.Square, accum=STATC[:, o + 24:o + 25],
                        reads=[bHAc], writes=[bHBc, bSm])
                    ts("vector", STATC[:, o + 24:o + 25], STATC[:, o + 24:o + 25],
                       1.0 / HID, EPS_NORM, ALU.mult, ALU.add, reads=[bSm], writes=[bSm])
                    em.emit("vector", lambda e, o=o: e.reciprocal(
                        STATC[:, o + 25:o + 26], STATC[:, o + 24:o + 25]),
                        reads=[bSm], writes=[bSm])
                    act("scalar", STATC[:, o + 25:o + 26], STATC[:, o + 25:o + 26],
                        AF.Sqrt, reads=[bSm], writes=[bSm])
                    tt("vector", ua, aa, C2REP[:], ALU.mult, reads=[bHAc, B("C2REP")],
                       writes=[bHBc])
                    red("vector", STATC[:, o + 26:o + 27], ua, ALU.max, absval=True,
                        reads=[bHBc], writes=[bSm])
                    tt("vector", STATC[:, o + 26:o + 27], STATC[:, o + 26:o + 27],
                       STATC[:, o + 25:o + 26], ALU.mult, reads=[bSm], writes=[bSm])
                    ts("vector", STATC[:, o + 26:o + 27], STATC[:, o + 26:o + 27],
                       EPS_Q, None, ALU.max, reads=[bSm], writes=[bSm])
                    em.emit("vector", lambda e, o=o: e.reciprocal(
                        STATC[:, o + 27:o + 28], STATC[:, o + 26:o + 27]),
                        reads=[bSm], writes=[bSm])
                    tt("vector", STATC[:, o + 28:o + 29], STATC[:, o + 27:o + 28],
                       STATC[:, o + 25:o + 26], ALU.mult, reads=[bSm], writes=[bSm])
                    ts("vector", STATC[:, o + 28:o + 29], STATC[:, o + 28:o + 29],
                       127.0, None, ALU.mult, reads=[bSm], writes=[bSm])
                    act("scalar", aa, ua, AF.Identity, bias=ZCOL[:, 0:1],
                        scale=STATC[:, o + 28:o + 29],
                        reads=[bHBc, bSm, B("ZCOL")], writes=[bHAc])
                    ts("vector", S2COL[:, tcol:tcol + 1], STATC[:, o + 26:o + 27],
                       1.0 / 127.0, None, ALU.mult, reads=[bSm], writes=[bS2])
                    tt("vector", S2COL[:, tcol:tcol + 1], S2COL[:, tcol:tcol + 1],
                       SWCOL[:, 1:2], ALU.mult, reads=[B("SWCOL")], writes=[bS2])
                    cp("scalar", ia, aa, reads=[bHAc], writes=[bHIc])
                    tt("vector", ua, aa, ia, ALU.subtract, reads=[bHAc, bHIc],
                       writes=[bHBc])
                    stt("vector", aa, ua, 0.5, ia, ALU.is_ge, ALU.add,
                        reads=[bHBc, bHIc], writes=[bHAc])
                    stt("vector", ua, ua, -0.5, aa, ALU.is_lt, ALU.subtract,
                        reads=[bHAc], writes=[bHBc])

            b1_dstage(0)
            if c.NT1 > 1:
                b1_dstage(1)
            for t1 in range(c.NT1):
                if t1 >= 1:
                    for tc in range(NC1):
                        xq2_finish(t1 - 1, tc)
                if t1 + 2 < c.NT1:
                    b1_dstage(t1 + 2)
                b1_mstage(t1)
            for tc in range(c.T1 // 128):
                xq2_finish(c.NT1 - 1, tc)

            em.barrier()
            br.close()

            b2 = contextlib.ExitStack()
            W2S = b2.enter_context(nc.sbuf_tensor(f"W2S{s}", [128, HC * D], BF16))
            GBR = b2.enter_context(nc.sbuf_tensor(f"GBR{s}", [128, D], BF16))
            GB2 = b2.enter_context(nc.sbuf_tensor(f"GB2{s}", [128, D], F32))
            XQ2F = b2.enter_context(nc.sbuf_tensor(f"XQ2F{s}", [128, 2 * HC * c.T2], BF16))
            XMT2 = b2.enter_context(nc.sbuf_tensor(f"XMT2{s}", [128, (c.T2 // 128) * D], F32))
            OUTS = b2.enter_context(nc.sbuf_tensor(f"OUTS{s}", [128, (c.T2 // 128) * D], F32))
            bW2, bGBR, bGB2 = B(f"W2S{s}"), B(f"GBR{s}"), B(f"GB2{s}")
            bXM2 = B(f"XMT2{s}")
            bXQFh = {(t2s, hh): B(f"XQ2F{s}_{t2s}_{hh}")
                     for t2s in range(2) for hh in range(2)}

            bcast("sync", GB2[:], mod_scr.tensor, (s * 6 + 5) * D, D,
                  reads=[B("mod_scr")], writes=[bGB2])
            ts("vector", GBR[:], GB2[:], 0.0, 2.0, ALU.is_ge, ALU.mult, reads=[bGB2],
               writes=[bGBR])
            ts("vector", GBR[:], GBR[:], -1.0, None, ALU.add, reads=[bGBR],
               writes=[bGBR])
            act("scalar", GB2[:], GB2[:], AF.Abs, reads=[bGB2], writes=[bGB2])
            ts("vector", GB2[:], GB2[:], 0.25, None, ALU.max, reads=[bGB2],
               writes=[bGB2])
            tt("vector", GB2[:], GB2[:], GBR[:], ALU.mult, reads=[bGBR], writes=[bGB2])
            cp("vector", GBR[:], GB2[:], reads=[bGB2], writes=[bGBR])
            for hh, q_ in ((0, "sync"), (1, "gpsimd")):
                em.dma(q_,
                       bass.AP(W2S, hh * (HC // 2) * D,
                               [[HC * D, 128], [D, HC // 2], [1, D]]),
                       bass.AP(w2q_scr.tensor, hh * (HC // 2) * 128 * D,
                               [[D, 128], [128 * D, HC // 2], [1, D]]),
                       reads=[B("qS1")], writes=[bW2])

            HC2 = HC // 2
            def xq2f_load(t2):
                t2s = t2 % 2
                for hh in range(2):
                    em.dma("sync" if hh == 0 else "gpsimd",
                           bass.AP(XQ2F, t2s * HC * c.T2 + hh * HC2 * c.T2,
                                   [[2 * HC * c.T2, 128], [c.T2, HC2], [1, c.T2]]),
                           bass.AP(xq2_scr.tensor, hh * HC2 * 128 * S + t2 * c.T2,
                                   [[S, 128], [128 * S, HC2], [1, c.T2]]),
                           reads=[bXQscr], writes=[bXQFh[(t2s, hh)]])

            xq2f_load(0)
            for t2 in range(c.NT2):
                t2s = t2 % 2
                tb = s * S + t2 * c.T2
                if t2 + 1 < c.NT2:
                    xq2f_load(t2 + 1)
                em.dma("scalar",
                       bass.AP(XMT2, 0, [[(c.T2 // 128) * D, 128], [D, c.T2 // 128], [1, D]]),
                       bass.AP(xmid_scr.tensor, tb * D,
                               [[D, 128], [128 * D, c.T2 // 128], [1, D]]),
                       reads=[bXMS], writes=[bXM2])
                for tc in range(c.T2 // 128):
                    pi2 = tc % 2
                    py = P0 if pi2 == 0 else P1
                    bh = [B(f"PB2{s}_{pi2}_0"), B(f"PB2{s}_{pi2}_1")]
                    nh = max(1, D // 512)
                    dw = D // nh
                    for hc in range(HC):
                        for half in range(nh):
                            mm(py[:, half * dw:(half + 1) * dw],
                               XQ2F[:, t2s * HC * c.T2 + hc * c.T2 + tc * 128:
                                   t2s * HC * c.T2 + hc * c.T2 + (tc + 1) * 128],
                               W2S[:, hc * D + half * dw:hc * D + (half + 1) * dw],
                               start=(hc == 0), stop=(hc == HC - 1),
                               reads=[bXQFh[(t2s, hc // HC2)], bW2],
                               writes=[bh[half % nh]])
                    oa = OUTS[:, tc * D:(tc + 1) * D]
                    bOUc = B(f"OUTS{s}_{tc % 2}")
                    tcol = (t2 * c.T2) // 128 + tc
                    ts("vector", oa, py[:, 0:D], S2COL[:, tcol:tcol + 1], None,
                       ALU.mult, reads=[bh[0], bh[nh - 1], bS2], writes=[bOUc])
                    tt("vector", oa, oa, GBR[:], ALU.mult, reads=[bGBR], writes=[bOUc])
                    tt("vector", oa, oa, XMT2[:, tc * D:(tc + 1) * D], ALU.add,
                       reads=[bXM2], writes=[bOUc])
                    em.dma("scalar",
                           bass.AP(out_t.tensor, (tb + tc * 128) * D,
                                   [[D, 128], [1, D]]),
                           oa, reads=[bOUc], writes=[B("out")])

            em.barrier()
            b2.close()

        em.finish()

    return nc


_NC_CACHE = {}


def _get_nc():
    if "full" not in _NC_CACHE:
        _NC_CACHE["full"] = build_program(CFG)
    return _NC_CACHE["full"]


def make_inputs(core, cfg, L0_attn_x, L0_attn_W_qk, L0_attn_b_qk, L0_attn_W_v,
                L0_attn_b_v, L0_attn_W_out, L0_attn_b_out, L0_attn_ln_gamma,
                L0_attn_ln_beta, L0_ffn_norm1_w, L0_ffn_w1, L0_ffn_act_scale,
                L0_ffn_act_bias, L0_ffn_norm2_w, L0_ffn_w2, L0_c, L0_wmod):
    c = cfg
    D, QK = c.D, c.QK
    f32 = np.float32
    perm = np.arange(QK)
    h_idx, d_idx = perm // c.DH, perm % c.DH
    qcols = h_idx * 2 * c.DH + 2 * d_idx
    kcols = qcols + 1
    W_qk = np.asarray(L0_attn_W_qk, f32)
    b_qk = np.asarray(L0_attn_b_qk, f32)
    wqk_p = np.concatenate([W_qk[:, qcols], W_qk[:, kcols]], axis=1)
    bqk2 = np.stack([b_qk[qcols], b_qk[kcols]], axis=0)
    b0 = core * c.NSEQ
    x = np.asarray(L0_attn_x, f32)
    return {
        "x": np.ascontiguousarray(x[b0:b0 + c.NSEQ].reshape(c.NSEQ * c.S, D)),
        "wqk": np.ascontiguousarray(wqk_p),
        "bqk2": np.ascontiguousarray(bqk2),
        "wv": np.ascontiguousarray(np.asarray(L0_attn_W_v, f32)),
        "bv": np.ascontiguousarray(np.asarray(L0_attn_b_v, f32).reshape(1, QK)),
        "wout": np.ascontiguousarray(np.asarray(L0_attn_W_out, f32)),
        "bout": np.ascontiguousarray(np.asarray(L0_attn_b_out, f32).reshape(1, D)),
        "ln_g": np.ascontiguousarray(np.asarray(L0_attn_ln_gamma, f32).reshape(1, D)),
        "ln_b": np.ascontiguousarray(np.asarray(L0_attn_ln_beta, f32).reshape(1, D)),
        "n1w": np.ascontiguousarray(np.asarray(L0_ffn_norm1_w, f32).reshape(1, D)),
        "w1t": np.ascontiguousarray(np.asarray(L0_ffn_w1, f32).T),
        "actsc": np.ascontiguousarray(np.array(
            [[np.asarray(L0_ffn_act_scale, f32).reshape(()),
              np.asarray(L0_ffn_act_bias, f32).reshape(())]], f32)),
        "n2w": np.ascontiguousarray(np.asarray(L0_ffn_norm2_w, f32).reshape(1, c.HID)),
        "w2t": np.ascontiguousarray(np.asarray(L0_ffn_w2, f32).T),
        "c_t": np.ascontiguousarray(np.asarray(L0_c, f32)[b0:b0 + c.NSEQ].T),
        "wmod": np.ascontiguousarray(np.asarray(L0_wmod, f32)),
    }


def kernel(**inputs):
    c = CFG
    nc = _get_nc()
    in_maps = [make_inputs(core, c, **inputs) for core in range(N_CORES)]
    res = run_bass_kernel_spmd(nc, in_maps, core_ids=list(range(N_CORES)))
    outs = [np.asarray(r["out"]).reshape(c.NSEQ, c.S, c.D) for r in res.results]
    return np.concatenate(outs, axis=0)


# version 22
# speedup vs baseline: 1.3295x; 1.0647x over previous
import contextlib
import numpy as np
import concourse.bass as bass
import concourse.mybir as mybir
from concourse.bass_utils import run_bass_kernel_spmd

F32 = mybir.dt.float32
BF16 = mybir.dt.bfloat16
I8 = mybir.dt.int8
AF = mybir.ActivationFunctionType
ALU = mybir.AluOpType
AX = mybir.AxisListType

RSQRT2 = float(2.0 ** (-0.5))
LN_EPS = 1e-5
EPS_NORM = 1e-8
EPS_Q = 1e-5

N_CORES = 8


class Cfg:
    def __init__(self, S=2048, D=1024, HID=4096, NSEQ=4, H=8, AG=512, T1=256, T2=256):
        self.S, self.D, self.HID, self.NSEQ, self.H = S, D, HID, NSEQ, H
        self.DH = 64
        self.AG, self.T1, self.T2 = AG, T1, T2
        self.DC = D // 128
        self.QK = self.H * self.DH
        self.QC = self.QK // 128
        self.HC = HID // 128
        self.KC = S // 128
        self.NG = S // AG
        self.NT1 = S // T1
        self.NT2 = S // T2
        self.NP = self.H // 2
        self.VW = self.H * 65 + 8
        self.DBG = False


CFG = Cfg()


class Buf:
    __slots__ = ("name", "writer", "readers")

    def __init__(self, name):
        self.name = name
        self.writer = None
        self.readers = []


class Em:
    ENGINES = ("tensor", "vector", "scalar", "gpsimd", "sync")
    NSLOT = {"sync": 12, "scalar": 8, "gpsimd": 4, "vector": 2}

    def __init__(self, nc):
        self.nc = nc
        self.TRACKS = list(self.ENGINES)
        self.dma_slots = {}
        for q, n in self.NSLOT.items():
            self.dma_slots[q] = [f"{q}_dma{i}" for i in range(n)]
            self.TRACKS += self.dma_slots[q]
        self.sems = {}
        self.cnt = {t: 0 for t in self.TRACKS}
        self.waited = {e: {t: 0 for t in self.TRACKS} for e in self.ENGINES}
        self.eng = {"tensor": nc.tensor, "vector": nc.vector, "scalar": nc.scalar,
                    "gpsimd": nc.gpsimd, "sync": nc.sync}
        self.dma_idx = {q: 0 for q in self.NSLOT}
        self.bufs = {}

    def B(self, name):
        if name not in self.bufs:
            self.bufs[name] = Buf(name)
        return self.bufs[name]

    def init_sems(self, ctx):
        for t in self.TRACKS:
            self.sems[t] = ctx.enter_context(self.nc.semaphore(f"sem_{t}"))

    def _wait(self, eng, dep_eng, dep_cnt):
        if dep_cnt <= 0 or self.waited[eng][dep_eng] >= dep_cnt:
            return
        self.eng[eng].wait_ge(self.sems[dep_eng], dep_cnt)
        self.waited[eng][dep_eng] = dep_cnt

    def emit(self, eng, fn, reads=(), writes=(), inc=1, track=None):
        if track is None:
            track = eng
        for b in reads:
            if b.writer is not None:
                self._wait(eng, b.writer[0], b.writer[1])
        for b in writes:
            if b.writer is not None:
                self._wait(eng, b.writer[0], b.writer[1])
            for tok in b.readers:
                self._wait(eng, tok[0], tok[1])
        inst = fn(self.eng[eng])
        self.cnt[track] += inc
        inst.then_inc(self.sems[track], inc)
        tok = (track, self.cnt[track])
        for b in reads:
            b.readers.append(tok)
        for b in writes:
            b.writer = tok
            b.readers = []
        return tok

    def dma(self, queue, out_ap, in_ap, reads=(), writes=(), slow=False):
        slot = self.dma_slots[queue][self.dma_idx[queue] % self.NSLOT[queue]]
        self.dma_idx[queue] += 1
        self._wait(queue, slot, self.cnt[slot])
        if slow:
            return self.emit(queue, lambda e: e.dma_start(
                out=out_ap, in_=in_ap, allow_slow_non_contiguous=True),
                reads=reads, writes=writes, inc=16, track=slot)
        return self.emit(queue, lambda e: e.dma_start(out=out_ap, in_=in_ap),
                         reads=reads, writes=writes, inc=16, track=slot)

    def barrier(self):
        snap = dict(self.cnt)
        for e in self.ENGINES:
            for t in self.TRACKS:
                if t != e:
                    self._wait(e, t, snap[t])

    def finish(self):
        snap = dict(self.cnt)
        for t in self.TRACKS:
            if t != "gpsimd":
                self._wait("gpsimd", t, snap[t])


def build_program(c: Cfg):
    nc = bass.Bass("TRN2", target_bir_lowering=False)
    S, D, HID, NSEQ, H = c.S, c.D, c.HID, c.NSEQ, c.H
    QK, DC, QC, HC, KC, VW = c.QK, c.DC, c.QC, c.HC, c.KC, c.VW

    def din(name, shape, dt=F32):
        return nc.dram_tensor(name, list(shape), dt, kind="ExternalInput").ap()

    x_in = din("x", (NSEQ * S, D))
    wqk = din("wqk", (D, 2 * QK))
    bqk2 = din("bqk2", (2, QK))
    wv = din("wv", (D, QK))
    bv = din("bv", (1, QK))
    wout = din("wout", (QK, D))
    bout = din("bout", (1, D))
    ln_g = din("ln_g", (1, D))
    ln_b = din("ln_b", (1, D))
    n1w = din("n1w", (1, D))
    w1t = din("w1t", (D, HID))
    actsc = din("actsc", (1, 2))
    n2w = din("n2w", (1, HID))
    w2t = din("w2t", (HID, D))
    c_t = din("c_t", (D, NSEQ))
    wmod = din("wmod", (D, 6 * D))

    out_t = nc.dram_tensor("out", [NSEQ * S, D], F32, kind="ExternalOutput").ap()
    xmid_scr = nc.dram_tensor("xmid_scr", [NSEQ * S, D], F32).ap()
    mod_scr = nc.dram_tensor("mod_scr", [NSEQ * 6, D], F32).ap()
    w1q_scr = nc.dram_tensor("w1q_scr", [D, HID], BF16).ap()
    w2q_scr = nc.dram_tensor("w2q_scr", [HID, D], BF16).ap()
    xq2_scr = nc.dram_tensor("xq2_scr", [HID, S], BF16).ap()
    tiny_scr = nc.dram_tensor("tiny_scr", [64], F32).ap()
    dbg_scr = nc.dram_tensor("dbg_scr", [c.NG * 128, QC * c.AG], F32).ap() if c.DBG else None
    dbg2_scr = nc.dram_tensor("dbg2_scr", [(c.AG // 128) * 128, D], F32).ap() if c.DBG else None

    ctx = contextlib.ExitStack()
    em = Em(nc)

    with ctx:
        em.init_sems(ctx)
        B = em.B

        WQK = ctx.enter_context(nc.sbuf_tensor("WQK", [128, DC * 2 * QK], BF16))
        WV = ctx.enter_context(nc.sbuf_tensor("WV", [128, DC * QK], BF16))
        WOUT = ctx.enter_context(nc.sbuf_tensor("WOUT", [128, QC * D], BF16))
        IDN = ctx.enter_context(nc.sbuf_tensor("IDN", [128, 128], BF16))
        ONESF = ctx.enter_context(nc.sbuf_tensor("ONESF", [1, 128], F32))
        ONESC = ctx.enter_context(nc.sbuf_tensor("ONESC", [128, 1], F32))
        EPSCOL = ctx.enter_context(nc.sbuf_tensor("EPSCOL", [128, 1], F32))
        ZCOL = ctx.enter_context(nc.sbuf_tensor("ZCOL", [128, 1], F32))
        IND64 = ctx.enter_context(nc.sbuf_tensor("IND64", [128, 256], F32))
        BQS = ctx.enter_context(nc.sbuf_tensor("BQS", [128, QC], F32))
        BKS = ctx.enter_context(nc.sbuf_tensor("BKS", [128, QC], F32))
        BVC = ctx.enter_context(nc.sbuf_tensor("BVC", [128, QC], BF16))
        BOREP = ctx.enter_context(nc.sbuf_tensor("BOREP", [128, D], BF16))
        GREP = ctx.enter_context(nc.sbuf_tensor("GREP", [128, D], BF16))
        BREP = ctx.enter_context(nc.sbuf_tensor("BREP", [128, D], BF16))
        C1REP = ctx.enter_context(nc.sbuf_tensor("C1REP", [128, D], F32))
        C1IREP = ctx.enter_context(nc.sbuf_tensor("C1IREP", [128, D], F32))
        C2REP = ctx.enter_context(nc.sbuf_tensor("C2REP", [128, HID], BF16))
        ABCOL = ctx.enter_context(nc.sbuf_tensor("ABCOL", [128, 2], F32))
        SWCOL = ctx.enter_context(nc.sbuf_tensor("SWCOL", [128, 2], F32))
        IND8S = ctx.enter_context(nc.sbuf_tensor("IND8S", [128, QC * 8], BF16))
        NNB = ctx.enter_context(nc.sbuf_tensor("NNB", [128, c.NP], F32))
        S2COL = ctx.enter_context(nc.sbuf_tensor("S2COL", [128, S // 128], F32))
        STATC = ctx.enter_context(nc.sbuf_tensor("STATC", [128, 160], F32))

        P0 = ctx.enter_context(nc.psum_tensor("P0", [128, 1024], F32))
        P1 = ctx.enter_context(nc.psum_tensor("P1", [128, 1024], F32))
        P2A = ctx.enter_context(nc.psum_tensor("P2A", [128, 512], F32))
        P3 = ctx.enter_context(nc.psum_tensor("P3", [128, 512], F32))
        P4 = ctx.enter_context(nc.psum_tensor("P4", [128, 512], F32))
        PT = ctx.enter_context(nc.psum_tensor("PT", [128, 1024], BF16))
        bP0, bP1, bP2, bP3, bP4 = B("P0"), B("P1"), B("P2A"), B("P3"), B("P4")
        bPT = B("PT")
        bPT2 = [B("PTa"), B("PTb")]
        bSTs = [B(f"STATC{k}") for k in range(4)]
        bSTm = [B(f"STATM{k}") for k in range(4)]
        bAK = B("AKCOLS")
        bST = B("STATC")

        def ts(eng, out, in0, s1, s2, op0, op1=None, reads=(), writes=()):
            if op1 is None:
                em.emit(eng, lambda e: e.tensor_scalar(out, in0, s1, None, op0=op0),
                        reads=reads, writes=writes)
            else:
                em.emit(eng, lambda e: e.tensor_scalar(out, in0, s1, s2, op0=op0, op1=op1),
                        reads=reads, writes=writes)

        def tt(eng, out, in0, in1, op, reads=(), writes=()):
            em.emit(eng, lambda e: e.tensor_tensor(out, in0, in1, op=op),
                    reads=reads, writes=writes)

        def stt(eng, out, in0, s, in1, op0, op1, reads=(), writes=()):
            em.emit(eng, lambda e: e.scalar_tensor_tensor(out, in0, s, in1, op0=op0,
                                                          op1=op1),
                    reads=reads, writes=writes)

        def act(eng, out, in0, func, bias=0.0, scale=1.0, accum=None, reads=(), writes=()):
            if accum is None:
                em.emit(eng, lambda e: e.activation(out, in0, func, bias=bias,
                                                    scale=scale),
                        reads=reads, writes=writes)
            else:
                em.emit(eng, lambda e: e.activation(out, in0, func, bias=bias,
                                                    scale=scale, accum_out=accum),
                        reads=reads, writes=writes)

        def cp(eng, out, in0, reads=(), writes=()):
            if eng == "scalar":
                em.emit(eng, lambda e: e.copy(out, in0), reads=reads, writes=writes)
            else:
                em.emit(eng, lambda e: e.tensor_copy(out, in0), reads=reads,
                        writes=writes)

        def red(eng, out, in0, op, absval=False, reads=(), writes=()):
            em.emit(eng, lambda e: e.tensor_reduce(out, in0, axis=AX.X, op=op,
                                                   apply_absolute_value=absval),
                    reads=reads, writes=writes)

        def mm(out, lhsT, rhs, start, stop, reads=(), writes=(), tp=None):
            if tp is None:
                em.emit("tensor", lambda e: e.matmul(out, lhsT, rhs, start=start,
                                                     stop=stop),
                        reads=reads, writes=writes)
            else:
                em.emit("tensor", lambda e: e.matmul(out, lhsT, rhs, start=start,
                                                     stop=stop, tile_position=tp),
                        reads=reads, writes=writes)

        def transp(out, in0, reads=(), writes=()):
            em.emit("tensor", lambda e: e.transpose(out, in0, IDN[:]),
                    reads=list(reads) + [B("IDN")], writes=writes)

        def bcast(queue, out_tile, src_t, src_off, n, reads=(), writes=()):
            src = bass.AP(src_t, src_off, [[0, 128], [1, n]])
            em.dma(queue, out_tile, src, reads=reads, writes=writes)

        def ln_bn(xa, width, rbuf, k):
            o = 32 * k
            bSk = bSTs[k]
            nparts = width // 512 if width > 512 else 1
            wper = width // nparts
            assert wper <= 512 and nparts <= 4
            for i in range(nparts):
                em.emit("vector", lambda e, i=i: e.bn_stats(
                    STATC[:, o + 6 * i:o + 6 * i + 6], xa[:, i * wper:(i + 1) * wper]),
                    reads=[rbuf], writes=[bSk])
            em.emit("vector", lambda e: e.bn_aggr(STATC[:, o + 12:o + 14],
                                                  STATC[:, o:o + 6 * nparts]),
                    reads=[bSk], writes=[bSk])

        def ln_sqrt(k):
            o = 32 * k
            act("scalar", STATC[:, o + 13:o + 14], STATC[:, o + 13:o + 14], AF.Sqrt,
                bias=EPSCOL[:, 0:1], reads=[bSTs[k], B("EPSCOL")], writes=[bSTs[k]])

        def ln_recip(k):
            o = 32 * k
            em.emit("vector", lambda e: e.reciprocal(STATC[:, o + 13:o + 14],
                                                     STATC[:, o + 13:o + 14]),
                    reads=[bSTs[k]], writes=[bSTs[k]])
            stt("vector", STATC[:, o + 14:o + 15], STATC[:, o + 12:o + 13], -1.0,
                STATC[:, o + 13:o + 14], ALU.mult, ALU.mult,
                reads=[bSTs[k]], writes=[bSTs[k]])

        def ln_apply(eng_out, in_ap, k, rbufs, wbufs):
            o = 32 * k
            act("scalar", eng_out, in_ap, AF.Identity, bias=STATC[:, o + 14:o + 15],
                scale=STATC[:, o + 13:o + 14], reads=list(rbufs) + [bSTs[k]],
                writes=wbufs)

        def ln_stats(xa, width, rbuf, k):
            ln_bn(xa, width, rbuf, k)
            ln_sqrt(k)
            ln_recip(k)

        em.emit("gpsimd", lambda e: e.iota(
            bass.AP(IDN, 0, [[128, 128], [1, 128]]), [[1, 128]], channel_multiplier=-1,
            allow_small_or_imprecise_dtypes=True), writes=[B("IDN")])
        ts("vector", IDN[:], IDN[:], 0.0, None, ALU.is_equal, reads=[B("IDN")],
           writes=[B("IDN")])
        em.emit("vector", lambda e: e.memset(ONESF[:], 1.0), writes=[B("ONESF")])
        em.emit("vector", lambda e: e.memset(ONESC[:], 1.0), writes=[B("ONESC")])
        em.emit("vector", lambda e: e.memset(EPSCOL[:], LN_EPS), writes=[B("EPSCOL")])
        em.emit("vector", lambda e: e.memset(ZCOL[:], 0.0), writes=[B("ZCOL")])
        em.emit("vector", lambda e: e.memset(IND64[:], 0.0), writes=[B("IND64")])
        em.emit("vector", lambda e: e.memset(IND64[64:65, 0:64], 1.0), writes=[B("IND64")])
        em.emit("vector", lambda e: e.memset(IND64[64:65, 192:256], 1.0), writes=[B("IND64")])
        em.emit("vector", lambda e: e.memset(IND8S[:], 0.0), writes=[B("IND8S")])
        for fc in range(QC):
            em.emit("vector", lambda e, fc=fc: e.memset(
                IND8S[0:64, fc * 8 + 2 * fc:fc * 8 + 2 * fc + 1], 1.0), writes=[B("IND8S")])
            em.emit("vector", lambda e, fc=fc: e.memset(
                IND8S[64:128, fc * 8 + 2 * fc + 1:fc * 8 + 2 * fc + 2], 1.0),
                writes=[B("IND8S")])

        for tile, tname, srcx, n in ((GREP, "GREP", ln_g, D), (BREP, "BREP", ln_b, D),
                                     (C2REP, "C2REP", n2w, HID),
                                     (BOREP, "BOREP", bout, D)):
            em.dma("gpsimd", tile[:], bass.AP(srcx.tensor, 0, [[0, 128], [1, n]]),
                   writes=[B(tname)])
        bcast("sync", C1REP[:], n1w.tensor, 0, D, writes=[B("C1REP")])
        bcast("sync", ABCOL[:, 0:2], actsc.tensor, 0, 2, writes=[B("ABCOL")])
        em.emit("vector", lambda e: e.reciprocal(C1IREP[:], C1REP[:]),
                reads=[B("C1REP")], writes=[B("C1IREP")])

        em.dma("gpsimd", bass.AP(BVC, 0, [[QC, 128], [1, QC]]),
               bass.AP(bv.tensor, 0, [[1, 128], [128, QC]]), writes=[B("BVC")],
               slow=True)
        em.dma("sync", bass.AP(BQS, 0, [[QC, 128], [1, QC]]),
               bass.AP(bqk2.tensor, 0, [[1, 128], [128, QC]]), writes=[B("BQS")],
               slow=True)
        em.dma("sync", bass.AP(BKS, 0, [[QC, 128], [1, QC]]),
               bass.AP(bqk2.tensor, QK, [[1, 128], [128, QC]]), writes=[B("BKS")],
               slow=True)
        ts("vector", BQS[:], BQS[:], 0.125, None, ALU.mult, reads=[B("BQS")],
           writes=[B("BQS")])

        em.dma("gpsimd",
               bass.AP(WQK, 0, [[DC * 2 * QK, 128], [2 * QK, DC], [1, 2 * QK]]),
               bass.AP(wqk.tensor, 0, [[2 * QK, 128], [128 * 2 * QK, DC], [1, 2 * QK]]),
               writes=[B("WQK")])
        em.dma("gpsimd",
               bass.AP(WV, 0, [[DC * QK, 128], [QK, DC], [1, QK]]),
               bass.AP(wv.tensor, 0, [[QK, 128], [128 * QK, DC], [1, QK]]),
               writes=[B("WV")])
        em.dma("gpsimd",
               bass.AP(WOUT, 0, [[QC * D, 128], [D, QC], [1, D]]),
               bass.AP(wout.tensor, 0, [[D, 128], [128 * D, QC], [1, D]]),
               writes=[B("WOUT")])

        arena = contextlib.ExitStack()
        BVTMP = arena.enter_context(nc.sbuf_tensor("BVTMP", [1, D], F32))
        for half in range(max(1, D // 512)):
            dw2 = min(512, D)
            for ci in range(QC):
                mm(P0[0:1, half * dw2:half * dw2 + dw2], BVC[:, ci:ci + 1],
                   WOUT[:, ci * D + half * dw2:ci * D + (half + 1) * dw2],
                   start=(ci == 0), stop=(ci == QC - 1),
                   reads=[B("BVC"), B("WOUT")], writes=[bP0])
        cp("vector", BVTMP[0:1, :], P0[0:1, 0:D], reads=[bP0], writes=[B("BVTMP")])
        for half in range(max(1, D // 512)):
            dw2 = min(512, D)
            mm(P0[:, half * dw2:half * dw2 + dw2], ONESF[0:1, :],
               BVTMP[0:1, half * dw2:half * dw2 + dw2],
               start=True, stop=True, reads=[B("BVTMP"), B("ONESF")], writes=[bP0])
            tt("vector", BOREP[:, half * dw2:half * dw2 + dw2],
               BOREP[:, half * dw2:half * dw2 + dw2],
               P0[:, half * dw2:half * dw2 + dw2], ALU.add,
               reads=[bP0], writes=[B("BOREP")])
        SILC = arena.enter_context(nc.sbuf_tensor("SILC", [128, DC * NSEQ], F32))
        MODT = arena.enter_context(nc.sbuf_tensor("MODT", [NSEQ, 6 * D], F32))
        WMS = arena.enter_context(nc.sbuf_tensor("WMS", [128, 2 * 512], F32))
        em.dma("sync", bass.AP(SILC, 0, [[DC * NSEQ, 128], [NSEQ, DC], [1, NSEQ]]),
               bass.AP(c_t.tensor, 0, [[NSEQ, 128], [128 * NSEQ, DC], [1, NSEQ]]),
               writes=[B("SILC")], slow=True)
        act("scalar", SILC[:], SILC[:], AF.Silu, reads=[B("SILC")], writes=[B("SILC")])
        for nt in range(6 * D // 512):
            for dc in range(DC):
                wslot = (nt * DC + dc) % 2
                bWM = B(f"WMS{wslot}")
                em.dma("sync", WMS[:, wslot * 512:wslot * 512 + 512],
                       bass.AP(wmod.tensor, dc * 128 * 6 * D + nt * 512,
                               [[6 * D, 128], [1, 512]]),
                       writes=[bWM])
                mm(P3[0:NSEQ, 0:512], SILC[:, dc * NSEQ:(dc + 1) * NSEQ],
                   WMS[:, wslot * 512:wslot * 512 + 512],
                   start=(dc == 0), stop=(dc == DC - 1),
                   reads=[B("SILC"), bWM], writes=[bP3])
            cp("vector", MODT[0:NSEQ, nt * 512:(nt + 1) * 512], P3[0:NSEQ, 0:512],
               reads=[bP3], writes=[B("MODT")])
        em.dma("scalar", bass.AP(mod_scr.tensor, 0, [[6 * D, NSEQ], [1, 6 * D]]),
               MODT[0:NSEQ, :], reads=[B("MODT")], writes=[B("mod_scr")])

        WQT = arena.enter_context(nc.sbuf_tensor("WQT", [128, HID], F32))
        WQA = arena.enter_context(nc.sbuf_tensor("WQA", [128, HID], F32))
        WQB = arena.enter_context(nc.sbuf_tensor("WQB", [128, HID], BF16))
        WQI = arena.enter_context(nc.sbuf_tensor("WQI", [128, HID], I8))
        WQC = arena.enter_context(nc.sbuf_tensor("WQC", [128, 4], F32))

        def wquant(w_ap, rows, cols, scr_t, idx):
            nch = rows // 128
            bW, bA, bC = B("WQT"), B("WQAa"), B("WQC")
            bI, bBf, bScr = B("WQI"), B("WQB"), B(f"qS{idx}")
            for ch in range(nch):
                em.dma("sync", WQT[:, 0:cols],
                       bass.AP(w_ap.tensor, ch * 128 * cols, [[cols, 128], [1, cols]]),
                       writes=[bW])
                red("vector", WQC[:, 1:2], WQT[:, 0:cols], ALU.add, absval=True,
                    reads=[bW], writes=[bC])
                if ch == 0:
                    cp("vector", WQC[:, 0:1], WQC[:, 1:2], reads=[bC], writes=[bA])
                else:
                    tt("vector", WQC[:, 0:1], WQC[:, 0:1], WQC[:, 1:2], ALU.add,
                       reads=[bC], writes=[bA])
            mm(P3[0:1, 0:1], WQC[:, 0:1], ONESC[:, 0:1], start=True, stop=True,
               reads=[bA, B("ONESC")], writes=[bP3])
            cp("vector", WQC[0:1, 2:3], P3[0:1, 0:1], reads=[bP3], writes=[bC])
            ts("vector", WQC[0:1, 2:3], WQC[0:1, 2:3], 1.0 / (rows * cols), EPS_Q,
               ALU.mult, ALU.max, reads=[bC], writes=[bC])
            mm(P3[:, 0:1], ONESF[0:1, :], WQC[0:1, 2:3], start=True, stop=True,
               reads=[bC, B("ONESF")], writes=[bP3])
            cp("vector", SWCOL[:, idx:idx + 1], P3[:, 0:1], reads=[bP3],
               writes=[B("SWCOL")])
            em.emit("vector", lambda e: e.reciprocal(WQC[:, 3:4], SWCOL[:, idx:idx + 1]),
                    reads=[B("SWCOL")], writes=[bC])
            for ch in range(nch):
                em.dma("sync", WQT[:, 0:cols],
                       bass.AP(w_ap.tensor, ch * 128 * cols, [[cols, 128], [1, cols]]),
                       writes=[bW])
                act("scalar", WQA[:, 0:cols], WQT[:, 0:cols], AF.Identity,
                    bias=ZCOL[:, 0:1], scale=WQC[:, 3:4],
                    reads=[bW, bC, B("ZCOL")], writes=[B("WQAa")])
                ts("vector", WQT[:, 0:cols], WQA[:, 0:cols], 0.5, None, ALU.is_gt,
                   reads=[B("WQAa")], writes=[bW])
                stt("vector", WQB[:, 0:cols], WQA[:, 0:cols], -0.5, WQT[:, 0:cols],
                    ALU.is_lt, ALU.subtract, reads=[B("WQAa"), bW], writes=[bBf])
                em.dma("scalar",
                       bass.AP(scr_t, ch * 128 * cols, [[cols, 128], [1, cols]]),
                       WQB[:, 0:cols], reads=[bBf], writes=[bScr])

        wquant(w1t, D, HID, w1q_scr.tensor, 0)
        wquant(w2t, HID, D, w2q_scr.tensor, 1)
        em.barrier()
        arena.close()

        for s in range(NSEQ):
            ar = contextlib.ExitStack()
            HBF = ar.enter_context(nc.sbuf_tensor(f"HBF{s}", [128, DC * S], BF16))
            KBF = ar.enter_context(nc.sbuf_tensor(f"KBF{s}", [128, QC * S], BF16))
            QBF = ar.enter_context(nc.sbuf_tensor(f"QBF{s}", [128, QC * S], BF16))
            VTM = ar.enter_context(nc.sbuf_tensor(f"VTM{s}", [128, KC * VW], BF16))
            XT = ar.enter_context(nc.sbuf_tensor(f"XT{s}", [128, 4 * D], F32))
            PBF = ar.enter_context(nc.sbuf_tensor(f"PBF{s}", [128, 2 * 1024], BF16))
            OBF = ar.enter_context(nc.sbuf_tensor(f"OBF{s}", [128, QC * c.AG], BF16))
            OST = ar.enter_context(nc.sbuf_tensor(f"OST{s}", [128, c.NP * c.AG], BF16))
            ORW = ar.enter_context(nc.sbuf_tensor(f"ORW{s}", [128, 2 * c.AG], BF16))
            RRC = ar.enter_context(nc.sbuf_tensor(f"RRC{s}", [128, 512], BF16))
            SMR = ar.enter_context(nc.sbuf_tensor(f"SMR{s}", [128, 2048], F32))
            EP1 = ar.enter_context(nc.sbuf_tensor(f"EP1{s}", [128, (c.AG // 128) * D], BF16))
            EP2 = ar.enter_context(nc.sbuf_tensor(f"EP2{s}", [128, (c.AG // 128) * D], BF16))
            CA1 = ar.enter_context(nc.sbuf_tensor(f"CA1{s}", [128, D], F32))
            SAR = ar.enter_context(nc.sbuf_tensor(f"SAR{s}", [128, D], BF16))
            GAR = ar.enter_context(nc.sbuf_tensor(f"GAR{s}", [128, D], BF16))
            GSC = ar.enter_context(nc.sbuf_tensor(f"GSC{s}", [128, D], F32))

            bHBF, bKBF, bQBF, bVTM = B(f"HBF{s}"), B(f"KBF{s}"), B(f"QBF{s}"), B(f"VTM{s}")
            bOBF, bOST, bORW = B(f"OBF{s}"), B(f"OST{s}"), B(f"ORW{s}")
            bXTs = [B(f"XT{s}_{i}") for i in range(4)]
            bPBF = [B(f"PBF{s}_0"), B(f"PBF{s}_1")]
            bRRC, bSMR = B(f"RRC{s}"), B(f"SMR{s}")
            bEP1, bEP2 = B(f"EP1{s}"), B(f"EP2{s}")
            bCA1, bSAR, bGAR, bGSC = B(f"CA1{s}"), B(f"SAR{s}"), B(f"GAR{s}"), B(f"GSC{s}")
            bNNB, bXMS = B("NNB"), B("xmid_scr")

            def xchunk(gbase_tok, tc, nsl=4):
                sl = tc % nsl
                em.dma("sync", XT[:, sl * D:(sl + 1) * D],
                       bass.AP(x_in.tensor, (gbase_tok + tc * 128) * D,
                               [[D, 128], [1, D]]),
                       writes=[bXTs[sl]])
                return XT[:, sl * D:(sl + 1) * D], bXTs[sl]

            bcast("sync", CA1[:], mod_scr.tensor, (s * 6 + 1) * D, D,
                  reads=[B("mod_scr")], writes=[bCA1])
            ts("vector", CA1[:], CA1[:], 1.0, None, ALU.add, reads=[bCA1], writes=[bCA1])
            em.dma("gpsimd", SAR[:],
                   bass.AP(mod_scr.tensor, (s * 6 + 0) * D, [[0, 128], [1, D]]),
                   reads=[B("mod_scr")], writes=[bSAR])
            bcast("sync", GSC[:], mod_scr.tensor, (s * 6 + 2) * D, D,
                  reads=[B("mod_scr")], writes=[bGSC])
            ts("vector", GAR[:], GSC[:], 0.0, 2.0, ALU.is_ge, ALU.mult, reads=[bGSC],
               writes=[bGAR])
            ts("vector", GAR[:], GAR[:], -1.0, None, ALU.add, reads=[bGAR], writes=[bGAR])
            act("scalar", GSC[:], GSC[:], AF.Abs, reads=[bGSC], writes=[bGSC])
            ts("vector", GSC[:], GSC[:], 0.25, None, ALU.max, reads=[bGSC],
               writes=[bGSC])
            tt("vector", GSC[:], GSC[:], GAR[:], ALU.mult, reads=[bGAR], writes=[bGSC])
            ts("vector", GAR[:], GSC[:], RSQRT2, None, ALU.mult, reads=[bGSC],
               writes=[bGAR])

            em.emit("vector", lambda e: e.memset(
                bass.AP(VTM, 64, [[KC * VW, 128], [VW, KC], [65, H], [1, 1]]), 1.0),
                writes=[bVTM])

            NCH = c.AG // 128
            for g in range(c.NG):
                gbase = s * S + g * c.AG
                xas = []
                for tc in range(NCH):
                    xa, bxa = xchunk(gbase, tc)
                    xas.append((xa, bxa))
                    k4 = tc % 4
                    o = 32 * k4
                    ln_bn(xa, D, bxa, k4)
                for tc in range(NCH):
                    ln_sqrt(tc % 4)
                for tc in range(NCH):
                    ln_recip(tc % 4)
                for tc in range(NCH):
                    k4 = tc % 4
                    o = 32 * k4
                    xa, bxa = xas[tc]
                    bEP1c = B(f"EP1{s}_{tc}")
                    bEP2c = B(f"EP2{s}_{tc}")
                    ha = EP2[:, tc * D:(tc + 1) * D]
                    ln_apply(ha, xa, k4, [bxa], [bEP2c])
                    ea = EP1[:, tc * D:(tc + 1) * D]
                    tt("vector", ea, ha, CA1[:], ALU.mult, reads=[bEP2c, bCA1],
                       writes=[bEP1c])
                    tt("vector", ha, ea, SAR[:], ALU.add, reads=[bEP1c, bSAR],
                       writes=[bEP2c])
                for tc in range(NCH):
                    bEP2c = B(f"EP2{s}_{tc}")
                    ha = EP2[:, tc * D:(tc + 1) * D]
                    for j in range(DC):
                        transp(PT[:, (j % 8) * 128:(j % 8) * 128 + 128],
                               ha[:, j * 128:(j + 1) * 128],
                               reads=[bEP2c], writes=[bPT2[j % 2]])
                    g4 = min(4, DC)
                    for half in range(DC // g4):
                        cp("scalar",
                           bass.AP(HBF, half * g4 * S + g * c.AG + tc * 128,
                                   [[DC * S, 128], [S, g4], [1, 128]]),
                           bass.AP(PT, half * g4 * 128, [[1024, 128], [128, g4], [1, 128]]),
                           reads=[bPT2[0], bPT2[1]], writes=[bHBF])
                for fc2 in range(0, QC, 2):
                    for dc in range(DC):
                        for pi in range(2):
                            fc = fc2 + pi
                            pk, bpk = (P0, bP0) if pi == 0 else (P1, bP1)
                            mm(pk[:, 0:c.AG],
                               WQK[:, dc * 2 * QK + QK + fc * 128:
                                   dc * 2 * QK + QK + (fc + 1) * 128],
                               HBF[:, dc * S + g * c.AG:dc * S + (g + 1) * c.AG],
                               start=(dc == 0), stop=(dc == DC - 1),
                               reads=[B("WQK"), bHBF], writes=[bpk])
                    for pi in range(2):
                        fc = fc2 + pi
                        pk, bpk = (P0, bP0) if pi == 0 else (P1, bP1)
                        act("scalar", KBF[:, fc * S + g * c.AG:fc * S + (g + 1) * c.AG],
                            pk[:, 0:c.AG], AF.Identity, bias=BKS[:, fc:fc + 1],
                            reads=[bpk, B("BKS")], writes=[bKBF])
                for tc2 in range(0, NCH, 2):
                    npi = 2 if tc2 + 1 < NCH else 1
                    for dc in range(DC):
                        for pi in range(npi):
                            tc = tc2 + pi
                            pv, bpv = (P0, bP0) if pi == 0 else (P1, bP1)
                            mm(pv[:, 0:QK],
                               HBF[:, dc * S + g * c.AG + tc * 128:
                                   dc * S + g * c.AG + (tc + 1) * 128],
                               WV[:, dc * QK:(dc + 1) * QK],
                               start=(dc == 0), stop=(dc == DC - 1),
                               reads=[bHBF, B("WV")], writes=[bpv])
                    for pi in range(npi):
                        tc = tc2 + pi
                        pv, bpv = (P0, bP0) if pi == 0 else (P1, bP1)
                        kc = g * NCH + tc
                        cp("scalar",
                           bass.AP(VTM, kc * VW, [[KC * VW, 128], [65, H], [1, 64]]),
                           bass.AP(pv, 0, [[1024, 128], [64, H], [1, 64]]),
                           reads=[bpv], writes=[bVTM])

            for g in range(c.NG):
                for fc2 in range(0, QC, 2):
                    for dc in range(DC):
                        for pi in range(2):
                            fc = fc2 + pi
                            pq, bpq = (P0, bP0) if pi == 0 else (P1, bP1)
                            mm(pq[:, 0:c.AG],
                               WQK[:, dc * 2 * QK + fc * 128:dc * 2 * QK + (fc + 1) * 128],
                               HBF[:, dc * S + g * c.AG:dc * S + (g + 1) * c.AG],
                               start=(dc == 0), stop=(dc == DC - 1),
                               reads=[B("WQK"), bHBF], writes=[bpq])
                    for pi in range(2):
                        fc = fc2 + pi
                        pq, bpq = (P0, bP0) if pi == 0 else (P1, bP1)
                        act("scalar", QBF[:, fc * S + g * c.AG:fc * S + (g + 1) * c.AG],
                            pq[:, 0:c.AG], AF.Identity, bias=BQS[:, fc:fc + 1],
                            scale=0.125, reads=[bpq, B("BQS")], writes=[bQBF])
            for qsrc, bsrc, coli in ((QBF, bQBF, 22), (KBF, bKBF, 24)):
                first = True
                for fc in range(QC):
                    act("scalar", EP1[:, 0:S], qsrc[:, fc * S:(fc + 1) * S], AF.Square,
                        reads=[bsrc], writes=[B(f"EP1{s}_0"), B(f"EP1{s}_1")])
                    for ct in range((S + 511) // 512):
                        ctw = min(512, S - ct * 512)
                        mm(P3[0:8, 0:ctw], IND8S[:, fc * 8:(fc + 1) * 8],
                           EP1[:, ct * 512:ct * 512 + ctw],
                           start=True, stop=True,
                           reads=[B(f"EP1{s}_0"), B(f"EP1{s}_1"), B("IND8S")],
                           writes=[bP3])
                        red("vector", STATC[0:8, coli + 1:coli + 2], P3[0:8, 0:ctw],
                            ALU.max, reads=[bP3], writes=[bSTs[0]])
                        if first:
                            cp("vector", STATC[0:8, coli:coli + 1],
                               STATC[0:8, coli + 1:coli + 2], reads=[bSTs[0]],
                               writes=[bSTs[0]])
                            first = False
                        else:
                            tt("vector", STATC[0:8, coli:coli + 1],
                               STATC[0:8, coli:coli + 1], STATC[0:8, coli + 1:coli + 2],
                               ALU.max, reads=[bSTs[0]], writes=[bSTs[0]])
            tt("vector", STATC[0:8, 26:27], STATC[0:8, 22:23], STATC[0:8, 24:25],
               ALU.mult, reads=[bSTs[0]], writes=[bSTs[0]])
            em.dma("scalar", bass.AP(tiny_scr.tensor, 0, [[1, 8], [1, 1]]),
                   STATC[0:8, 26:27], reads=[bSTs[0]], writes=[B("tiny")], slow=True)
            em.dma("scalar", SMR[0:1, 0:8],
                   bass.AP(tiny_scr.tensor, 0, [[8, 1], [1, 8]]),
                   reads=[B("tiny")], writes=[bSMR], slow=True)
            act("scalar", SMR[0:1, 0:8], SMR[0:1, 0:8], AF.Sqrt, reads=[bSMR],
                writes=[bSMR])
            tt("vector", SMR[0:1, 8:8 + c.NP], bass.AP(SMR, 0, [[2048, 1], [2, c.NP]]),
               bass.AP(SMR, 1, [[2048, 1], [2, c.NP]]), ALU.max, reads=[bSMR],
               writes=[bSMR])
            ts("vector", SMR[0:1, 8:8 + c.NP], SMR[0:1, 8:8 + c.NP], -1.0, None,
               ALU.mult, reads=[bSMR], writes=[bSMR])
            mm(P3[:, 0:c.NP], ONESF[0:1, :], SMR[0:1, 8:8 + c.NP], start=True,
               stop=True, reads=[bSMR, B("ONESF")], writes=[bP3])
            cp("vector", NNB[:, 0:c.NP], P3[:, 0:c.NP], reads=[bP3], writes=[bNNB])

            for g in range(c.NG):
                gbase = s * S + g * c.AG
                pend = []

                def _emit_norm(g, p, slot):
                    orw = ORW[:, slot * c.AG:(slot + 1) * c.AG]
                    em.emit("vector", lambda e: e.reciprocal(
                        SMR[64:65, slot * 1024:slot * 1024 + 2 * c.AG],
                        SMR[64:65, slot * 1024:slot * 1024 + 2 * c.AG]),
                        reads=[bSMR], writes=[bSMR])
                    mm(P2A[:, 0:c.AG], IND64[64:65, 0:128],
                       SMR[64:65, slot * 1024:slot * 1024 + c.AG],
                       start=True, stop=False,
                       reads=[bSMR, B("IND64")], writes=[bP2])
                    mm(P2A[:, 0:c.AG], IND64[64:65, 128:256],
                       SMR[64:65, slot * 1024 + c.AG:slot * 1024 + 2 * c.AG],
                       start=False, stop=True,
                       reads=[bSMR, B("IND64")], writes=[bP2])
                    cp("vector", RRC[0:64, 0:c.AG], P2A[0:64, 0:c.AG], reads=[bP2],
                       writes=[bRRC])
                    tt("vector", OBF[0:64, p * c.AG:(p + 1) * c.AG],
                       orw[0:64, :], RRC[0:64, 0:c.AG], ALU.mult,
                       reads=[bORW, bRRC], writes=[bOBF])
                    tt("vector", OST[0:64, p * c.AG:(p + 1) * c.AG],
                       OST[0:64, p * c.AG:(p + 1) * c.AG],
                       P2A[64:128, 0:c.AG], ALU.mult,
                       reads=[bP2], writes=[bOST])

                for p in range(c.NP):
                    hA, hB = 2 * p, 2 * p + 1
                    for j in range(KC):
                        psc, bsc = (P0, bP0) if j % 2 == 0 else (P1, bP1)
                        bscH = [B(f"SC{s}_{j % 2}_0"), B(f"SC{s}_{j % 2}_1")]
                        pbh = j % 2
                        w0 = [bsc, bscH[0]] if (p == 0 and j < 2) else [bscH[0]]
                        mm(psc[:, 0:c.AG],
                           KBF[0:64, p * S + j * 128:p * S + (j + 1) * 128],
                           QBF[0:64, p * S + g * c.AG:p * S + (g + 1) * c.AG],
                           start=True, stop=True, tp=(0, 0),
                           reads=[bKBF, bQBF], writes=w0)
                        mm(psc[:, 512:512 + c.AG],
                           KBF[64:128, p * S + j * 128:p * S + (j + 1) * 128],
                           QBF[64:128, p * S + g * c.AG:p * S + (g + 1) * c.AG],
                           start=True, stop=True, tp=(64, 0),
                           reads=[bKBF, bQBF], writes=[bscH[1]])
                        if j == 2 and pend:
                            pp, ss = pend.pop(0)
                            _emit_norm(g, pp, ss)
                        if c.AG == 512:
                            act("scalar", PBF[:, pbh * 1024:pbh * 1024 + 1024],
                                psc[:, 0:1024], AF.Exp, bias=NNB[:, p:p + 1],
                                reads=[bscH[0], bscH[1], bsc, bNNB],
                                writes=[bPBF[pbh]])
                        else:
                            act("scalar", PBF[:, pbh * 1024:pbh * 1024 + c.AG],
                                psc[:, 0:c.AG], AF.Exp, bias=NNB[:, p:p + 1],
                                reads=[bscH[0], bsc, bNNB], writes=[bPBF[pbh]])
                            act("scalar",
                                PBF[:, pbh * 1024 + 512:pbh * 1024 + 512 + c.AG],
                                psc[:, 512:512 + c.AG], AF.Exp, bias=NNB[:, p:p + 1],
                                reads=[bscH[1], bsc, bNNB], writes=[bPBF[pbh]])
                        if j >= 1:
                            jm = j - 1
                            pbm = jm % 2
                            mm(P3[0:65, 0:c.AG],
                               VTM[:, jm * VW + 65 * hA:jm * VW + 65 * hA + 65],
                               PBF[:, pbm * 1024:pbm * 1024 + c.AG],
                               start=(jm == 0), stop=False,
                               reads=[bVTM, bPBF[pbm]], writes=[bP3])
                            mm(P4[0:65, 0:c.AG],
                               VTM[:, jm * VW + 65 * hB:jm * VW + 65 * hB + 65],
                               PBF[:, pbm * 1024 + 512:pbm * 1024 + 512 + c.AG],
                               start=(jm == 0), stop=False,
                               reads=[bVTM, bPBF[pbm]], writes=[bP4])
                    jm = KC - 1
                    pbm = jm % 2
                    mm(P3[0:65, 0:c.AG],
                       VTM[:, jm * VW + 65 * hA:jm * VW + 65 * hA + 65],
                       PBF[:, pbm * 1024:pbm * 1024 + c.AG],
                       start=False, stop=True,
                       reads=[bVTM, bPBF[pbm]], writes=[bP3])
                    mm(P4[0:65, 0:c.AG],
                       VTM[:, jm * VW + 65 * hB:jm * VW + 65 * hB + 65],
                       PBF[:, pbm * 1024 + 512:pbm * 1024 + 512 + c.AG],
                       start=False, stop=True,
                       reads=[bVTM, bPBF[pbm]], writes=[bP4])
                    slot = p % 2
                    orw = ORW[:, slot * c.AG:(slot + 1) * c.AG]
                    cp("scalar", orw[0:65, :], P3[0:65, 0:c.AG], reads=[bP3],
                       writes=[bORW])
                    cp("scalar", SMR[64:65, slot * 1024:slot * 1024 + c.AG],
                       P3[64:65, 0:c.AG], reads=[bP3], writes=[bSMR])
                    cp("scalar", OST[0:65, p * c.AG:(p + 1) * c.AG],
                       P4[0:65, 0:c.AG], reads=[bP4], writes=[bOST])
                    cp("scalar", SMR[64:65, slot * 1024 + c.AG:slot * 1024 + 2 * c.AG],
                       P4[64:65, 0:c.AG], reads=[bP4], writes=[bSMR])
                    pend.append((p, slot))
                while pend:
                    pp, ss = pend.pop(0)
                    _emit_norm(g, pp, ss)
                em.dma("scalar",
                       bass.AP(OBF, 64 * QC * c.AG,
                               [[QC * c.AG, 64], [c.AG, c.NP], [1, c.AG]]),
                       bass.AP(OST, 0, [[c.NP * c.AG, 64], [c.AG, c.NP], [1, c.AG]]),
                       reads=[bOST], writes=[bOBF])
                if c.DBG and s == 0:
                    em.dma("gpsimd",
                           bass.AP(dbg_scr.tensor, g * 128 * QC * c.AG,
                                   [[QC * c.AG, 128], [1, QC * c.AG]]),
                           OBF[:, :], reads=[bOBF], writes=[B("dbg")])
                for tc in range(NCH):
                    pi2 = tc % 2
                    py, bpy = (P0, bP0) if pi2 == 0 else (P1, bP1)
                    bh = [B(f"POP{s}_{pi2}_0"), B(f"POP{s}_{pi2}_1")]
                    nh = max(1, D // 512)
                    dw = D // nh
                    for ci in range(QC):
                        for half in range(nh):
                            wl = [bh[half % nh]]
                            if ci == 0:
                                wl = [bpy] + wl
                            mm(py[:, half * dw:(half + 1) * dw],
                               OBF[:, ci * c.AG + tc * 128:ci * c.AG + (tc + 1) * 128],
                               WOUT[:, ci * D + half * dw:ci * D + (half + 1) * dw],
                               start=(ci == 0), stop=(ci == QC - 1),
                               reads=[bOBF, B("WOUT")], writes=wl)
                    k4 = tc % 4
                    o = 32 * k4
                    bSk = bSTs[k4]
                    bEP1c = B(f"EP1{s}_{tc}")
                    bEP2c = B(f"EP2{s}_{tc}")
                    e1 = EP1[:, tc * D:(tc + 1) * D]
                    e2 = EP2[:, tc * D:(tc + 1) * D]
                    xa, bxa = xchunk(gbase, tc)
                    tt("vector", e1, py[:, 0:D], BOREP[:], ALU.add,
                       reads=[bh[0], bh[nh - 1], bpy, B("BOREP")], writes=[bEP1c])
                    if c.DBG and s == 0 and g == 0:
                        em.dma("gpsimd",
                               bass.AP(dbg2_scr.tensor, tc * 128 * D, [[D, 128], [1, D]]),
                               e1, reads=[bEP1c], writes=[B("dbg2")])
                    ln_bn(e1, D, bEP1c, k4)
                    ln_sqrt(k4)
                    ln_recip(k4)
                    ln_apply(e2, e1, k4, [bEP1c], [bEP2c])
                    tt("vector", e1, e2, GREP[:], ALU.mult, reads=[bEP2c, B("GREP")],
                       writes=[bEP1c])
                    tt("vector", e2, e1, BREP[:], ALU.add, reads=[bEP1c, B("BREP")],
                       writes=[bEP2c])
                    for j in range(DC):
                        transp(PT[:, (j % 8) * 128:(j % 8) * 128 + 128],
                               HBF[:, j * S + g * c.AG + tc * 128:
                                   j * S + g * c.AG + (tc + 1) * 128],
                               reads=[bHBF], writes=[bPT2[j % 2]])
                    tt("vector", e1, e2, PT[:, 0:D], ALU.add,
                       reads=[bEP2c, bPT2[0], bPT2[1]],
                       writes=[bEP1c])
                    tt("vector", e2, e1, GAR[:], ALU.mult, reads=[bEP1c, bGAR],
                       writes=[bEP2c])
                    tt("gpsimd", xa, xa, e2, ALU.add, reads=[bEP2c], writes=[bxa])
                    em.dma("scalar",
                           bass.AP(xmid_scr.tensor, (gbase + tc * 128) * D,
                                   [[D, 128], [1, D]]),
                           xa, reads=[bxa], writes=[bXMS])

            em.barrier()
            ar.close()

            br = contextlib.ExitStack()
            W1S = br.enter_context(nc.sbuf_tensor(f"W1S{s}", [128, DC * HID], BF16))
            CB1C1 = br.enter_context(nc.sbuf_tensor(f"CB1{s}", [128, D], BF16))
            SBC1 = br.enter_context(nc.sbuf_tensor(f"SBC1{s}", [128, D], BF16))
            ZC = br.enter_context(nc.sbuf_tensor(f"ZC{s}", [128, (c.T1 // 128) * D], F32))
            DS1 = br.enter_context(nc.sbuf_tensor(f"DS1{s}", [128, D], F32))
            DS2 = br.enter_context(nc.sbuf_tensor(f"DS2{s}", [128, D], BF16))
            DI8 = br.enter_context(nc.sbuf_tensor(f"DI8{s}", [128, D], I8))
            XQ1 = br.enter_context(nc.sbuf_tensor(f"XQ1{s}", [128, 3 * DC * c.T1], BF16))
            XMB = br.enter_context(nc.sbuf_tensor(f"XMB{s}", [128, 2 * (c.T1 // 128) * D], F32))
            HA = br.enter_context(nc.sbuf_tensor(f"HA{s}", [128, (c.T1 // 128) * HID], BF16))
            HBt = br.enter_context(nc.sbuf_tensor(f"HBt{s}", [128, (c.T1 // 128) * HID], BF16))
            HI8 = br.enter_context(nc.sbuf_tensor(f"HI8{s}", [128, HID], I8))
            XQ2S = br.enter_context(nc.sbuf_tensor(f"XQ2S{s}", [128, 2048], BF16))

            bW1, bCB, bSB = B(f"W1S{s}"), B(f"CB1{s}"), B(f"SBC1{s}")
            bXQ1, bXMB = B(f"XQ1{s}"), B(f"XMB{s}")
            bXQscr, bS2 = B("xq2_scr"), B("S2COL")

            bcast("gpsimd", CB1C1[:], mod_scr.tensor, (s * 6 + 4) * D, D,
                  reads=[B("mod_scr")], writes=[bCB])
            ts("vector", CB1C1[:], CB1C1[:], 1.0, None, ALU.add, reads=[bCB],
               writes=[bCB])
            tt("vector", CB1C1[:], CB1C1[:], C1REP[:], ALU.mult, reads=[B("C1REP")],
               writes=[bCB])
            bcast("sync", DS1[:, 0:D], mod_scr.tensor, (s * 6 + 3) * D, D,
                  reads=[B("mod_scr")], writes=[B(f"DS1{s}_0")])
            tt("vector", SBC1[:], DS1[:, 0:D], C1REP[:], ALU.mult,
               reads=[B(f"DS1{s}_0"), B("C1REP")], writes=[bSB])
            for hh, q_ in ((0, "sync"), (1, "gpsimd")):
                em.dma(q_,
                       bass.AP(W1S, hh * (DC // 2) * HID,
                               [[DC * HID, 128], [HID, DC // 2], [1, HID]]),
                       bass.AP(w1q_scr.tensor, hh * (DC // 2) * 128 * HID,
                               [[HID, 128], [128 * HID, DC // 2], [1, HID]]),
                       reads=[B("qS0")], writes=[bW1])

            def xq2_finish(t1, tc):
                bHBc = B(f"HBt{s}_{tc}")
                ua = HBt[:, tc * HID:(tc + 1) * HID]
                g8 = min(8, HC)
                for quad in range(HC // g8):
                    for j in range(g8):
                        transp(PT[:, j * 128:(j + 1) * 128],
                               ua[:, (quad * g8 + j) * 128:(quad * g8 + j + 1) * 128],
                               reads=[bHBc], writes=[bPT2[j % 2]])
                    qs = quad % 2
                    bXQ2c = B(f"XQ2S{s}_{qs}")
                    cp("scalar",
                       bass.AP(XQ2S, qs * 1024, [[2048, 128], [128, g8], [1, 128]]),
                       bass.AP(PT, 0, [[1024, 128], [128, g8], [1, 128]]),
                       reads=[bPT2[0], bPT2[1]], writes=[bXQ2c])
                    em.dma("scalar",
                           bass.AP(xq2_scr.tensor,
                                   (quad * g8 * 128) * S + t1 * c.T1 + tc * 128,
                                   [[S, 128], [128 * S, g8], [1, 128]]),
                           bass.AP(XQ2S, qs * 1024, [[2048, 128], [128, g8], [1, 128]]),
                           reads=[bXQ2c], writes=[bXQscr])

            NC1 = c.T1 // 128

            def b1_dstage(t1):
                ts_ = t1 % 2
                sl = t1 % 3
                tb = s * S + t1 * c.T1
                em.dma("sync",
                       bass.AP(XMB, ts_ * NC1 * D,
                               [[2 * NC1 * D, 128], [D, NC1], [1, D]]),
                       bass.AP(xmid_scr.tensor, tb * D,
                               [[D, 128], [128 * D, NC1], [1, D]]),
                       reads=[bXMS], writes=[B(f"XMB{s}_{ts_}")])
                for tc in range(NC1):
                    k4 = (t1 * NC1 + tc) % 4
                    ln_bn(XMB[:, (ts_ * NC1 + tc) * D:(ts_ * NC1 + tc + 1) * D], D,
                          B(f"XMB{s}_{ts_}"), k4)
                for tc in range(NC1):
                    ln_sqrt((t1 * NC1 + tc) % 4)
                for tc in range(NC1):
                    ln_recip((t1 * NC1 + tc) % 4)
                for tc in range(NC1):
                    k4 = (t1 * NC1 + tc) % 4
                    o = 32 * k4
                    bSk = bSTs[k4]
                    tcol = t1 * NC1 + tc
                    bZCc = B(f"ZC{s}_{tc}")
                    bD1c, bD2c = B(f"DS1{s}"), B(f"DS2{s}")
                    bDIc = B(f"DI8{s}")
                    xa = XMB[:, (ts_ * NC1 + tc) * D:(ts_ * NC1 + tc + 1) * D]
                    za = ZC[:, tc * D:(tc + 1) * D]
                    d1 = DS1[:, 0:D]
                    d2 = DS2[:, 0:D]
                    di = DI8[:, 0:D]
                    xqt = d2
                    ak = 22 + (tcol // 4) % 2
                    ln_apply(za, xa, k4, [B(f"XMB{s}_{ts_}")], [bZCc])
                    tt("vector", za, za, CB1C1[:], ALU.mult, reads=[bCB], writes=[bZCc])
                    tt("vector", za, za, SBC1[:], ALU.add, reads=[bSB], writes=[bZCc])
                    tt("vector", d1, za, C1IREP[:], ALU.mult,
                       reads=[bZCc, B("C1IREP")], writes=[bD1c])
                    act("scalar", d2, d1, AF.Square, accum=STATC[:, o + 15:o + 16],
                        reads=[bD1c], writes=[bD2c, bSk])
                    ts("vector", STATC[:, o + 15:o + 16], STATC[:, o + 15:o + 16],
                       1.0 / D, EPS_NORM, ALU.mult, ALU.add, reads=[bSk], writes=[bSk])
                    em.emit("vector", lambda e, o=o: e.reciprocal(
                        STATC[:, o + 16:o + 17], STATC[:, o + 15:o + 16]),
                        reads=[bSk], writes=[bSk])
                    act("scalar", STATC[:, o + 16:o + 17], STATC[:, o + 16:o + 17],
                        AF.Sqrt, reads=[bSk], writes=[bSk])
                    red("vector", STATC[:, o + 17:o + 18], za, ALU.max, absval=True,
                        reads=[bZCc], writes=[bSk])
                    tt("vector", STATC[:, o + 17:o + 18], STATC[:, o + 17:o + 18],
                       STATC[:, o + 16:o + 17], ALU.mult, reads=[bSk], writes=[bSk])
                    ts("vector", STATC[:, o + 17:o + 18], STATC[:, o + 17:o + 18],
                       EPS_Q, None, ALU.max, reads=[bSk], writes=[bSk])
                    em.emit("vector", lambda e, o=o: e.reciprocal(
                        STATC[:, o + 18:o + 19], STATC[:, o + 17:o + 18]),
                        reads=[bSk], writes=[bSk])
                    tt("vector", STATC[:, o + 19:o + 20], STATC[:, o + 18:o + 19],
                       STATC[:, o + 16:o + 17], ALU.mult, reads=[bSk], writes=[bSk])
                    ts("vector", STATC[:, o + 19:o + 20], STATC[:, o + 19:o + 20],
                       127.0, None, ALU.mult, reads=[bSk], writes=[bSk])
                    act("scalar", za, za, AF.Identity, bias=ZCOL[:, 0:1],
                        scale=STATC[:, o + 19:o + 20],
                        reads=[bSk, bZCc, B("ZCOL")], writes=[bZCc])
                    ts("vector", STATC[:, o + 20:o + 21], STATC[:, o + 17:o + 18],
                       1.0 / 127.0, None, ALU.mult, reads=[bSk], writes=[bSk])
                    tt("vector", STATC[:, o + 20:o + 21], STATC[:, o + 20:o + 21],
                       SWCOL[:, 0:1], ALU.mult, reads=[B("SWCOL")], writes=[bSk])
                    tt("vector", STATC[:, o + ak:o + ak + 1], STATC[:, o + 20:o + 21],
                       STATC[:, o + 20:o + 21], ALU.mult, reads=[bSk], writes=[bAK])
                    tt("vector", STATC[:, o + ak:o + ak + 1], STATC[:, o + ak:o + ak + 1],
                       ABCOL[:, 0:1], ALU.mult, reads=[B("ABCOL")], writes=[bAK])
                    cp("scalar", di, za, reads=[bZCc], writes=[bDIc])
                    tt("vector", d1, za, di, ALU.subtract, reads=[bZCc, bDIc],
                       writes=[bD1c])
                    stt("vector", d2, d1, 0.5, di, ALU.is_ge, ALU.add,
                        reads=[bD1c, bDIc], writes=[bD2c])
                    stt("vector", xqt, d1, -0.5, d2, ALU.is_lt, ALU.subtract,
                        reads=[bD1c, bD2c], writes=[bD2c])
                    for j in range(DC):
                        transp(PT[:, (j % 8) * 128:(j % 8) * 128 + 128],
                               xqt[:, j * 128:(j + 1) * 128],
                               reads=[bD2c], writes=[bPT2[j % 2]])
                    g4 = min(4, DC)
                    for half in range(DC // g4):
                        cp("scalar",
                           bass.AP(XQ1, sl * DC * c.T1 + half * g4 * c.T1 + tc * 128,
                                   [[3 * DC * c.T1, 128], [c.T1, g4], [1, 128]]),
                           bass.AP(PT, half * g4 * 128, [[1024, 128], [128, g4], [1, 128]]),
                           reads=[bPT2[0], bPT2[1]], writes=[B(f"XQ1{s}_{sl}")])

            def b1_mstage(t1):
                sl = t1 % 3
                for tc in range(NC1):
                    bHAc, bHBc = B(f"HA{s}_{tc}"), B(f"HBt{s}_{tc}")
                    for hb2 in range(0, HID // 512, 2):
                        pairs = [(0, hb2)]
                        if hb2 + 1 < HID // 512:
                            pairs.append((1, hb2 + 1))
                        for dc in range(DC):
                            for pi, hb in pairs:
                                ph, bph = (P0, bP0) if pi == 0 else (P1, bP1)
                                mm(ph[:, 0:512],
                                   XQ1[:, sl * DC * c.T1 + dc * c.T1 + tc * 128:
                                       sl * DC * c.T1 + dc * c.T1 + (tc + 1) * 128],
                                   W1S[:, dc * HID + hb * 512:dc * HID + (hb + 1) * 512],
                                   start=(dc == 0), stop=(dc == DC - 1),
                                   reads=[B(f"XQ1{s}_{sl}"), bW1], writes=[bph])
                        for pi, hb in pairs:
                            ph, bph = (P0, bP0) if pi == 0 else (P1, bP1)
                            hs = HA[:, tc * HID + hb * 512:tc * HID + (hb + 1) * 512]
                            act("scalar", hs, ph[:, 0:512], AF.Relu, reads=[bph],
                                writes=[bHAc])
                            us = HBt[:, tc * HID + hb * 512:tc * HID + (hb + 1) * 512]
                            act("scalar", us, hs, AF.Square, reads=[bHAc],
                                writes=[bHBc])
                    k4 = (t1 * NC1 + tc) % 4
                    o = 32 * k4
                    tcol = t1 * NC1 + tc
                    ak = 22 + (tcol // 4) % 2
                    bSm = bSTm[k4]
                    bHAc, bHBc, bHIc = B(f"HA{s}_{tc}"), B(f"HBt{s}_{tc}"), B(f"HI8{s}")
                    ua = HBt[:, tc * HID:(tc + 1) * HID]
                    aa = HA[:, tc * HID:(tc + 1) * HID]
                    ia = HI8[:, 0:HID]
                    act("scalar", aa, ua, AF.Identity, bias=ABCOL[:, 1:2],
                        scale=STATC[:, o + ak:o + ak + 1],
                        reads=[bHBc, bAK, B("ABCOL")], writes=[bHAc])
                    act("scalar", ua, aa, AF.Square, accum=STATC[:, o + 24:o + 25],
                        reads=[bHAc], writes=[bHBc, bSm])
                    ts("vector", STATC[:, o + 24:o + 25], STATC[:, o + 24:o + 25],
                       1.0 / HID, EPS_NORM, ALU.mult, ALU.add, reads=[bSm], writes=[bSm])
                    em.emit("vector", lambda e, o=o: e.reciprocal(
                        STATC[:, o + 25:o + 26], STATC[:, o + 24:o + 25]),
                        reads=[bSm], writes=[bSm])
                    act("scalar", STATC[:, o + 25:o + 26], STATC[:, o + 25:o + 26],
                        AF.Sqrt, reads=[bSm], writes=[bSm])
                    tt("vector", ua, aa, C2REP[:], ALU.mult, reads=[bHAc, B("C2REP")],
                       writes=[bHBc])
                    red("vector", STATC[:, o + 26:o + 27], ua, ALU.max, absval=True,
                        reads=[bHBc], writes=[bSm])
                    tt("vector", STATC[:, o + 26:o + 27], STATC[:, o + 26:o + 27],
                       STATC[:, o + 25:o + 26], ALU.mult, reads=[bSm], writes=[bSm])
                    ts("vector", STATC[:, o + 26:o + 27], STATC[:, o + 26:o + 27],
                       EPS_Q, None, ALU.max, reads=[bSm], writes=[bSm])
                    em.emit("vector", lambda e, o=o: e.reciprocal(
                        STATC[:, o + 27:o + 28], STATC[:, o + 26:o + 27]),
                        reads=[bSm], writes=[bSm])
                    tt("vector", STATC[:, o + 28:o + 29], STATC[:, o + 27:o + 28],
                       STATC[:, o + 25:o + 26], ALU.mult, reads=[bSm], writes=[bSm])
                    ts("vector", STATC[:, o + 28:o + 29], STATC[:, o + 28:o + 29],
                       127.0, None, ALU.mult, reads=[bSm], writes=[bSm])
                    act("scalar", aa, ua, AF.Identity, bias=ZCOL[:, 0:1],
                        scale=STATC[:, o + 28:o + 29],
                        reads=[bHBc, bSm, B("ZCOL")], writes=[bHAc])
                    ts("vector", S2COL[:, tcol:tcol + 1], STATC[:, o + 26:o + 27],
                       1.0 / 127.0, None, ALU.mult, reads=[bSm], writes=[bS2])
                    tt("vector", S2COL[:, tcol:tcol + 1], S2COL[:, tcol:tcol + 1],
                       SWCOL[:, 1:2], ALU.mult, reads=[B("SWCOL")], writes=[bS2])
                    cp("scalar", ia, aa, reads=[bHAc], writes=[bHIc])
                    tt("vector", ua, aa, ia, ALU.subtract, reads=[bHAc, bHIc],
                       writes=[bHBc])
                    stt("vector", aa, ua, 0.5, ia, ALU.is_ge, ALU.add,
                        reads=[bHBc, bHIc], writes=[bHAc])
                    stt("vector", ua, ua, -0.5, aa, ALU.is_lt, ALU.subtract,
                        reads=[bHAc], writes=[bHBc])

            b1_dstage(0)
            if c.NT1 > 1:
                b1_dstage(1)
            for t1 in range(c.NT1):
                if t1 >= 1:
                    for tc in range(NC1):
                        xq2_finish(t1 - 1, tc)
                if t1 + 2 < c.NT1:
                    b1_dstage(t1 + 2)
                b1_mstage(t1)
            for tc in range(c.T1 // 128):
                xq2_finish(c.NT1 - 1, tc)

            em.barrier()
            br.close()

            b2 = contextlib.ExitStack()
            W2S = b2.enter_context(nc.sbuf_tensor(f"W2S{s}", [128, HC * D], BF16))
            GBR = b2.enter_context(nc.sbuf_tensor(f"GBR{s}", [128, D], BF16))
            GB2 = b2.enter_context(nc.sbuf_tensor(f"GB2{s}", [128, D], F32))
            XQ2F = b2.enter_context(nc.sbuf_tensor(f"XQ2F{s}", [128, 2 * HC * c.T2], BF16))
            XMT2 = b2.enter_context(nc.sbuf_tensor(f"XMT2{s}", [128, (c.T2 // 128) * D], F32))
            OUTS = b2.enter_context(nc.sbuf_tensor(f"OUTS{s}", [128, (c.T2 // 128) * D], F32))
            bW2, bGBR, bGB2 = B(f"W2S{s}"), B(f"GBR{s}"), B(f"GB2{s}")
            bXM2 = B(f"XMT2{s}")
            bXQFh = {(t2s, hh): B(f"XQ2F{s}_{t2s}_{hh}")
                     for t2s in range(2) for hh in range(2)}

            bcast("sync", GB2[:], mod_scr.tensor, (s * 6 + 5) * D, D,
                  reads=[B("mod_scr")], writes=[bGB2])
            ts("vector", GBR[:], GB2[:], 0.0, 2.0, ALU.is_ge, ALU.mult, reads=[bGB2],
               writes=[bGBR])
            ts("vector", GBR[:], GBR[:], -1.0, None, ALU.add, reads=[bGBR],
               writes=[bGBR])
            act("scalar", GB2[:], GB2[:], AF.Abs, reads=[bGB2], writes=[bGB2])
            ts("vector", GB2[:], GB2[:], 0.25, None, ALU.max, reads=[bGB2],
               writes=[bGB2])
            tt("vector", GB2[:], GB2[:], GBR[:], ALU.mult, reads=[bGBR], writes=[bGB2])
            cp("vector", GBR[:], GB2[:], reads=[bGB2], writes=[bGBR])
            for hh, q_ in ((0, "sync"), (1, "gpsimd")):
                em.dma(q_,
                       bass.AP(W2S, hh * (HC // 2) * D,
                               [[HC * D, 128], [D, HC // 2], [1, D]]),
                       bass.AP(w2q_scr.tensor, hh * (HC // 2) * 128 * D,
                               [[D, 128], [128 * D, HC // 2], [1, D]]),
                       reads=[B("qS1")], writes=[bW2])

            HC2 = HC // 2
            def xq2f_load(t2):
                t2s = t2 % 2
                for hh in range(2):
                    em.dma("sync" if hh == 0 else "gpsimd",
                           bass.AP(XQ2F, t2s * HC * c.T2 + hh * HC2 * c.T2,
                                   [[2 * HC * c.T2, 128], [c.T2, HC2], [1, c.T2]]),
                           bass.AP(xq2_scr.tensor, hh * HC2 * 128 * S + t2 * c.T2,
                                   [[S, 128], [128 * S, HC2], [1, c.T2]]),
                           reads=[bXQscr], writes=[bXQFh[(t2s, hh)]])

            xq2f_load(0)
            for t2 in range(c.NT2):
                t2s = t2 % 2
                tb = s * S + t2 * c.T2
                if t2 + 1 < c.NT2:
                    xq2f_load(t2 + 1)
                em.dma("scalar",
                       bass.AP(XMT2, 0, [[(c.T2 // 128) * D, 128], [D, c.T2 // 128], [1, D]]),
                       bass.AP(xmid_scr.tensor, tb * D,
                               [[D, 128], [128 * D, c.T2 // 128], [1, D]]),
                       reads=[bXMS], writes=[bXM2])
                for tc in range(c.T2 // 128):
                    pi2 = tc % 2
                    py = P0 if pi2 == 0 else P1
                    bh = [B(f"PB2{s}_{pi2}_0"), B(f"PB2{s}_{pi2}_1")]
                    nh = max(1, D // 512)
                    dw = D // nh
                    for hc in range(HC):
                        for half in range(nh):
                            mm(py[:, half * dw:(half + 1) * dw],
                               XQ2F[:, t2s * HC * c.T2 + hc * c.T2 + tc * 128:
                                   t2s * HC * c.T2 + hc * c.T2 + (tc + 1) * 128],
                               W2S[:, hc * D + half * dw:hc * D + (half + 1) * dw],
                               start=(hc == 0), stop=(hc == HC - 1),
                               reads=[bXQFh[(t2s, hc // HC2)], bW2],
                               writes=[bh[half % nh]])
                    oa = OUTS[:, tc * D:(tc + 1) * D]
                    bOUc = B(f"OUTS{s}_{tc % 2}")
                    tcol = (t2 * c.T2) // 128 + tc
                    ts("vector", oa, py[:, 0:D], S2COL[:, tcol:tcol + 1], None,
                       ALU.mult, reads=[bh[0], bh[nh - 1], bS2], writes=[bOUc])
                    tt("vector", oa, oa, GBR[:], ALU.mult, reads=[bGBR], writes=[bOUc])
                    tt("vector", oa, oa, XMT2[:, tc * D:(tc + 1) * D], ALU.add,
                       reads=[bXM2], writes=[bOUc])
                    em.dma("scalar",
                           bass.AP(out_t.tensor, (tb + tc * 128) * D,
                                   [[D, 128], [1, D]]),
                           oa, reads=[bOUc], writes=[B("out")])

            em.barrier()
            b2.close()

        em.finish()

    return nc


_NC_CACHE = {}


def _get_nc():
    if "full" not in _NC_CACHE:
        _NC_CACHE["full"] = build_program(CFG)
    return _NC_CACHE["full"]


def make_inputs(core, cfg, L0_attn_x, L0_attn_W_qk, L0_attn_b_qk, L0_attn_W_v,
                L0_attn_b_v, L0_attn_W_out, L0_attn_b_out, L0_attn_ln_gamma,
                L0_attn_ln_beta, L0_ffn_norm1_w, L0_ffn_w1, L0_ffn_act_scale,
                L0_ffn_act_bias, L0_ffn_norm2_w, L0_ffn_w2, L0_c, L0_wmod):
    c = cfg
    D, QK = c.D, c.QK
    f32 = np.float32
    perm = np.arange(QK)
    h_idx, d_idx = perm // c.DH, perm % c.DH
    qcols = h_idx * 2 * c.DH + 2 * d_idx
    kcols = qcols + 1
    W_qk = np.asarray(L0_attn_W_qk, f32)
    b_qk = np.asarray(L0_attn_b_qk, f32)
    wqk_p = np.concatenate([W_qk[:, qcols], W_qk[:, kcols]], axis=1)
    bqk2 = np.stack([b_qk[qcols], b_qk[kcols]], axis=0)
    b0 = core * c.NSEQ
    x = np.asarray(L0_attn_x, f32)
    return {
        "x": np.ascontiguousarray(x[b0:b0 + c.NSEQ].reshape(c.NSEQ * c.S, D)),
        "wqk": np.ascontiguousarray(wqk_p),
        "bqk2": np.ascontiguousarray(bqk2),
        "wv": np.ascontiguousarray(np.asarray(L0_attn_W_v, f32)),
        "bv": np.ascontiguousarray(np.asarray(L0_attn_b_v, f32).reshape(1, QK)),
        "wout": np.ascontiguousarray(np.asarray(L0_attn_W_out, f32)),
        "bout": np.ascontiguousarray(np.asarray(L0_attn_b_out, f32).reshape(1, D)),
        "ln_g": np.ascontiguousarray(np.asarray(L0_attn_ln_gamma, f32).reshape(1, D)),
        "ln_b": np.ascontiguousarray(np.asarray(L0_attn_ln_beta, f32).reshape(1, D)),
        "n1w": np.ascontiguousarray(np.asarray(L0_ffn_norm1_w, f32).reshape(1, D)),
        "w1t": np.ascontiguousarray(np.asarray(L0_ffn_w1, f32).T),
        "actsc": np.ascontiguousarray(np.array(
            [[np.asarray(L0_ffn_act_scale, f32).reshape(()),
              np.asarray(L0_ffn_act_bias, f32).reshape(())]], f32)),
        "n2w": np.ascontiguousarray(np.asarray(L0_ffn_norm2_w, f32).reshape(1, c.HID)),
        "w2t": np.ascontiguousarray(np.asarray(L0_ffn_w2, f32).T),
        "c_t": np.ascontiguousarray(np.asarray(L0_c, f32)[b0:b0 + c.NSEQ].T),
        "wmod": np.ascontiguousarray(np.asarray(L0_wmod, f32)),
    }


def kernel(**inputs):
    c = CFG
    nc = _get_nc()
    in_maps = [make_inputs(core, c, **inputs) for core in range(N_CORES)]
    res = run_bass_kernel_spmd(nc, in_maps, core_ids=list(range(N_CORES)))
    outs = [np.asarray(r["out"]).reshape(c.NSEQ, c.S, c.D) for r in res.results]
    return np.concatenate(outs, axis=0)
